# version 46
# baseline (speedup 1.0000x reference)
import numpy as np
from contextlib import ExitStack
import concourse.bass as bass
import concourse.mybir as mybir
from concourse.bass_utils import run_bass_kernel_spmd
from concourse.bass_core_rust import IndirectOffsetOnAxis

F32 = mybir.dt.float32
BF16 = mybir.dt.bfloat16
I32 = mybir.dt.int32
AF = mybir.ActivationFunctionType
ALU = mybir.AluOpType
AX = mybir.AxisListType
ds = bass.ds
RC = 1.5 * 2 ** 23

H = 1024
E = 8
IM = 2048
IF = 2816
NCORE = 8


class Buf:
    __slots__ = ("name", "writer", "readers")

    def __init__(self, name=""):
        self.name = name
        self.writer = None
        self.readers = {}


class Eng:
    def __init__(self, eng, sem, name):
        self.e = eng
        self.sem = sem
        self.name = name
        self.n = 0
        self.hwm = {}
        self.pending = []
        self.fence = []

    def _wait(self, sem, val):
        if self.hwm.get(sem.name, -1) >= val:
            return
        self.e.wait_ge(sem, val)
        self.hwm[sem.name] = val

    def _pre(self, ins, outs):
        for s, v in self.fence:
            self._wait(s, v)
        self.fence = []
        for b in ins:
            if b.writer is not None:
                self._wait(*b.writer)
        for b in outs:
            if b.writer is not None:
                self._wait(*b.writer)
            for s, v in b.readers.items():
                self._wait(s, v)

    @staticmethod
    def _post(ins, outs, tok):
        sem, val = tok
        for b in ins:
            if b.readers.get(sem, -1) < val:
                b.readers[sem] = val
        for b in outs:
            b.writer = tok
            b.readers = {}

    def op(self, name, *args, ins=(), outs=(), defer=False, **kw):
        self._pre(ins, outs)
        inst = getattr(self.e, name)(*args, **kw)
        if defer:
            self.pending.append((tuple(ins), tuple(outs)))
            return inst
        self.n += 1
        inst.then_inc(self.sem, 1)
        tok = (self.sem, self.n)
        for pins, pouts in self.pending:
            self._post(pins, pouts, tok)
        self.pending = []
        self._post(ins, outs, tok)
        return inst

    def token(self):
        assert not self.pending
        return (self.sem, self.n)


class Q:
    def __init__(self, eng, sems, name, step=16):
        self.eng = eng
        self.sems = sems if isinstance(sems, (list, tuple)) else [sems]
        self.name = name
        self.step = step
        self.n = 0
        self.cnt = [0] * len(self.sems)

    def _issue(self, fn, ins, outs):
        e = self.eng
        i = self.n % len(self.sems)
        if self.cnt[i] > 0:
            e._wait(self.sems[i], self.step * self.cnt[i])
        e._pre(ins, outs)
        inst = fn()
        self.n += 1
        self.cnt[i] += 1
        inst.then_inc(self.sems[i], self.step)
        tok = (self.sems[i], self.step * self.cnt[i])
        Eng._post(ins, outs, tok)
        return tok

    def dma(self, out_ap, in_ap, ins=(), outs=(), **kw):
        return self._issue(lambda: self.eng.e.dma_start(out=out_ap, in_=in_ap, **kw),
                           ins, outs)

    def dma_t(self, out_ap, in_ap, ins=(), outs=()):
        return self._issue(
            lambda: self.eng.e.dma_start(out=out_ap, in_=in_ap, transpose=True),
            ins, outs)

    def indirect(self, out_ap, out_off, in_ap, in_off, ins=(), outs=(), **kw):
        return self._issue(
            lambda: self.eng.e.indirect_dma_start(out_ap, out_off, in_ap, in_off, **kw),
            ins, outs)

    def collective(self, kind, op, rg, ins_aps, outs_aps, ins=(), outs=()):
        return self._issue(
            lambda: self.eng.e.collective_compute(kind, op, rg, ins_aps, outs_aps),
            ins, outs)

    def wait_all(self, eng):
        for s, c in zip(self.sems, self.cnt):
            if c:
                eng._wait(s, self.step * c)


class KB:
    def __init__(self, nc, st):
        self.nc = nc
        self.st = st
        self.bufs = {}
        sem = lambda n: st.enter_context(nc.semaphore(n))
        self.v = Eng(nc.vector, sem("se_v"), "v")
        self.s = Eng(nc.scalar, sem("se_s"), "s")
        self.t = Eng(nc.tensor, sem("se_t"), "t")
        self.g = Eng(nc.gpsimd, sem("se_g"), "g")
        self.y = Eng(nc.sync, sem("se_y"), "y")
        self.engs = [self.v, self.s, self.t, self.g, self.y]
        self.qy = Q(self.y, [sem(f"qy{i}") for i in range(6)], "qy")
        self.qs = Q(self.s, [sem(f"qs{i}") for i in range(6)], "qs")
        self.qg = Q(self.g, [sem(f"qg{i}") for i in range(6)], "qg")
        self.qo = Q(self.g, [sem(f"qo{i}") for i in range(2)], "qo")
        self.qc = Q(self.g, [sem(f"qc{i}") for i in range(4)], "qc", step=1)
        self.queues = [self.qy, self.qs, self.qg, self.qo, self.qc]

    def sb(self, name, shape, dtype=F32, scope=None):
        h = (scope or self.st).enter_context(self.nc.sbuf_tensor(name, shape, dtype))
        self.bufs[name] = Buf(name)
        return h

    def ps(self, name, shape, dtype=F32):
        h = self.st.enter_context(self.nc.psum_tensor(name, shape, dtype))
        self.bufs[name] = Buf(name)
        return h

    def dram(self, name, shape, dtype=F32, kind="Internal"):
        h = self.nc.dram_tensor(name, shape, dtype, kind=kind)
        self.bufs[name] = Buf(name)
        return h

    def b(self, name):
        if name not in self.bufs:
            self.bufs[name] = Buf(name)
        return self.bufs[name]

    def fence(self, skip=()):
        toks = [(e.sem, e.n) for e in self.engs if e.n > 0]
        for q in self.queues:
            if q in skip:
                continue
            toks += [(s, q.step * c) for s, c in zip(q.sems, q.cnt) if c]
        for e in self.engs:
            e.fence = list(toks)


def cdiv(a, b):
    return (a + b - 1) // b


def build(nc, st, T):
    k = KB(nc, st)
    B = k.b
    TL = T // NCORE
    CAP = min(int(IM * 1.5), T // E)
    NT = TL // 128
    NJ = T // 128
    NS = CAP // 128
    NCH_AG = min(8, NS)
    CHS = CAP // NCH_AG
    W6 = 6 * H // NCORE
    RG = [list(range(NCORE))]
    IFS = IF // NCORE
    HS = H // NCORE

    x_full = k.dram("x_full", [T, H], F32, kind="ExternalInput")
    xown = k.dram("xown", [TL, H], F32, kind="ExternalInput")
    c_vec = k.dram("c_vec", [1, H], F32, kind="ExternalInput")
    wmod_sl = k.dram("wmod_sl", [H, W6], F32, kind="ExternalInput")
    gnw_in = k.dram("gnw_in", [1, H], F32, kind="ExternalInput")
    ginw_in = k.dram("ginw_in", [1, H], F32, kind="ExternalInput")
    gatew_in = k.dram("gatew_in", [E, H], F32, kind="ExternalInput")
    gpn_in = k.dram("gpn_in", [1, H], F32, kind="ExternalInput")
    gpw_in = k.dram("gpw_in", [2 * IM, H], F32, kind="ExternalInput")
    dpn_in = k.dram("dpn_in", [1, IM], F32, kind="ExternalInput")
    dpw_in = k.dram("dpw_in", [H, IM], F32, kind="ExternalInput")
    clipx_in = k.dram("clipx_in", [1, 1], F32, kind="ExternalInput")
    clipd_in = k.dram("clipd_in", [1, 1], F32, kind="ExternalInput")
    wg_sl = k.dram("wg_sl", [IFS, H], F32, kind="ExternalInput")
    wu_sl = k.dram("wu_sl", [IFS, H], F32, kind="ExternalInput")
    wd_sl = k.dram("wd_sl", [HS, IF], F32, kind="ExternalInput")
    out = k.dram("out", [TL, H], F32, kind="ExternalOutput")

    modP = k.dram("modP", [1, W6], F32)
    modF = k.dram("modF", [NCORE, W6], F32)
    ct_d = k.dram("ct_d", [NJ * 128, 1], F32)
    fl_d = k.dram("fl_d", [1, CAP], F32)
    jr_d = k.dram("jr_d", [1, CAP], F32)
    gpq_d = k.dram("gpq_d", [2 * IM, H], BF16)
    dpq_d = k.dram("dpq_d", [H, IM], BF16)
    ng6_d = k.dram("ng6_d", [CAP, H], F32)
    contribP = k.dram("contribP", [CAP, H], F32)
    contribF = k.dram("contribF", [E * CAP, H], F32)
    fqgT_p = k.dram("fqgT_p", [H, IFS], BF16)
    fquT_p = k.dram("fquT_p", [H, IFS], BF16)
    fqd_p = k.dram("fqd_p", [HS, IF], BF16)
    fqgT_d = k.dram("fqgT_d", [NCORE * H, IFS], BF16)
    fquT_d = k.dram("fquT_d", [NCORE * H, IFS], BF16)
    fqd_d = k.dram("fqd_d", [H, IF], BF16)
    x1_d = k.dram("x1_d", [TL, H], F32)
    h2_d = k.dram("h2_d", [TL, H], F32)
    n1_d = k.dram("n1_d", [TL, H], BF16)
    prod_d = k.dram("prod_d", [TL, IF], BF16)
    n2f_d = k.dram("n2f_d", [TL, IF], BF16)
    gm1P = k.dram("gm1P", [1, 8], F32)
    gm1F = k.dram("gm1F", [NCORE, 8], F32)
    gm2P = k.dram("gm2P", [1, 8], F32)
    gm2F = k.dram("gm2F", [NCORE, 8], F32)

    pid = {}
    for e in (k.y, k.g):
        pid[e.name] = e.e.partition_id()


    rc_t = k.sb("rc_t", [128, 1])
    nc.const_aps.register(F32, RC, rc_t[:])
    k.v.op("memset", rc_t[:], RC, outs=[B("rc_t")])
    mrc_t = k.sb("mrc_t", [128, 1])
    nc.const_aps.register(F32, -RC, mrc_t[:])
    k.v.op("memset", mrc_t[:], -RC, outs=[B("mrc_t")])

    iot_pf = k.sb("iot_pf", [128, 128])
    k.g.op("iota", iot_pf[:], [[-1, 128]], base=0, channel_multiplier=1,
           allow_small_or_imprecise_dtypes=True, outs=[B("iot_pf")])
    tri_b = k.sb("tri_b", [128, 128], BF16)
    tri_f = k.sb("tri_f", [128, 128], F32)
    id_b = k.sb("id_b", [128, 128], BF16)
    id_f = k.sb("id_f", [128, 128], F32)
    k.v.op("tensor_scalar", tri_b[:], iot_pf[:], 0.0, None, ALU.is_lt,
           ins=[B("iot_pf")], outs=[B("tri_b")])
    k.v.op("tensor_scalar", tri_f[:], iot_pf[:], 0.0, None, ALU.is_lt,
           ins=[B("iot_pf")], outs=[B("tri_f")])
    k.v.op("tensor_scalar", id_b[:], iot_pf[:], 0.0, None, ALU.is_equal,
           ins=[B("iot_pf")], outs=[B("id_b")])
    k.v.op("tensor_scalar", id_f[:], iot_pf[:], 0.0, None, ALU.is_equal,
           ins=[B("iot_pf")], outs=[B("id_f")])
    ones_r = k.sb("ones_r", [1, 128], BF16)
    k.v.op("memset", ones_r[:], 1.0, outs=[B("ones_r")])
    ones_rf = k.sb("ones_rf", [1, 128], F32)
    k.v.op("memset", ones_rf[:], 1.0, outs=[B("ones_rf")])
    ones_c = k.sb("ones_c", [128, 1], BF16)
    k.v.op("memset", ones_c[:], 1.0, outs=[B("ones_c")])
    ones_cf = k.sb("ones_cf", [128, 1], F32)
    k.v.op("memset", ones_cf[:], 1.0, outs=[B("ones_cf")])
    iop1_b = k.sb("iop1_b", [128, 1], BF16)
    iop1_f = k.sb("iop1_f", [128, 1])
    k.g.op("iota", iop1_f[:], [[1, 1]], base=1, channel_multiplier=1,
           allow_small_or_imprecise_dtypes=True, outs=[B("iop1_f")])
    k.v.op("tensor_copy", iop1_b[:], iop1_f[:], ins=[B("iop1_f")], outs=[B("iop1_b")])
    io_e = k.sb("io_e", [128, E])
    k.g.op("iota", io_e[:], [[1, E]], base=0, channel_multiplier=0,
           allow_small_or_imprecise_dtypes=True, outs=[B("io_e")])
    io_r = k.sb("io_r", [128, 128])
    k.g.op("iota", io_r[:], [[1, 128]], base=0, channel_multiplier=0,
           allow_small_or_imprecise_dtypes=True, outs=[B("io_r")])
    io_s = k.sb("io_s", [128, CAP])
    k.g.op("iota", io_s[:], [[1, CAP]], base=0, channel_multiplier=0,
           allow_small_or_imprecise_dtypes=True, outs=[B("io_s")])

    pk = [k.ps(f"pk{i}", [128, 512]) for i in range(8)]

    def bc_fill(dst, name, src_ap, src_buf, width):
        for j in range(cdiv(width, 512)):
            w = min(512, width - 512 * j)
            p = pk[j % 2]
            k.t.op("matmul", p[:, 0:w], ones_rf[:], src_ap[0:1, 512 * j:512 * j + w],
                   start=True, stop=True, ins=[B("ones_rf"), src_buf],
                   outs=[B(f"pk{j % 2}")])
            k.v.op("tensor_copy", dst[:, 512 * j:512 * j + w], p[:, 0:w],
                   ins=[B(f"pk{j % 2}")], outs=[B(name)])
        return dst

    def bcast_row(name, src_ap, src_buf, width, scope=None):
        dst = k.sb(name, [128, width], scope=scope)
        return bc_fill(dst, name, src_ap, src_buf, width)

    ca1b = k.sb("ca1b", [128, H])
    sab = k.sb("sab", [128, H])
    gaxb = k.sb("gaxb", [128, H])
    cb1b = k.sb("cb1b", [128, H])
    sbb = k.sb("sbb", [128, H])
    gbxb = k.sb("gbxb", [128, H])
    ggb = k.sb("ggb", [128, H])
    gnwb = k.sb("gnwb", [128, H])
    gpnb = k.sb("gpnb", [128, H])
    dpnb = k.sb("dpnb", [128, IM])
    gamc_b = k.sb("gamc_b", [128, 1])
    mgt = k.sb("mgt", [128, E * 8], BF16)

    sc0 = ExitStack()
    cv8 = k.sb("cv8", [8, 128], scope=sc0)
    k.qy.dma(cv8[:], c_vec.ap().rearrange("o (a b) -> a (o b)", a=8),
             ins=[B("c_vec")], outs=[B("cv8")])
    sc8 = k.sb("sc8", [8, 128], scope=sc0)
    k.s.op("activation", sc8[:], cv8[:], AF.Silu, bias=0.0, scale=1.0,
           ins=[B("cv8")], outs=[B("sc8")])
    k.t.op("transpose", pk[0][:, 0:8], sc8[:], id_f[0:8, 0:8],
           ins=[B("sc8"), B("id_f")], outs=[B("pk0")])
    scT = k.sb("scT", [128, 8], scope=sc0)
    k.v.op("tensor_copy", scT[:], pk[0][:, 0:8], ins=[B("pk0")], outs=[B("scT")])
    wm_t = k.sb("wm_t", [128, 8 * W6], scope=sc0)
    modp_t = k.sb("modp_t", [1, W6], scope=sc0)
    NB6 = cdiv(W6, 512)
    for kk in range(8):
        k.qy.dma(wm_t[:, W6 * kk:W6 * (kk + 1)],
                 wmod_sl.ap()[128 * kk:128 * (kk + 1), :],
                 ins=[B("wmod_sl")], outs=[B("wm_t")])
    for kk in range(8):
        for j in range(NB6):
            w = min(512, W6 - 512 * j)
            k.t.op("matmul", pk[2 + j][0:1, 0:w], scT[:, kk:kk + 1],
                   wm_t[:, W6 * kk + 512 * j:W6 * kk + 512 * j + w],
                   start=(kk == 0), stop=(kk == 7),
                   ins=[B("scT"), B("wm_t")], outs=[B(f"pk{2 + j}")],
                   defer=not (kk == 7 and j == NB6 - 1))
    for j in range(NB6):
        w = min(512, W6 - 512 * j)
        k.v.op("tensor_copy", modp_t[0:1, 512 * j:512 * j + w], pk[2 + j][0:1, 0:w],
               ins=[B(f"pk{2 + j}")], outs=[B("modp_t")])
    k.qg.dma(modP.ap(), modp_t[:], ins=[B("modp_t")], outs=[B("modP")])
    k.qc.collective("AllGather", ALU.bypass, RG, [modP.ap()], [modF.ap()],
                    ins=[B("modP")], outs=[B("modF")])
    modf_t = k.sb("modf_t", [1, 6 * H], scope=sc0)
    k.qy.dma(modf_t[:], modF.ap().rearrange("a b -> () (a b)"),
             ins=[B("modF")], outs=[B("modf_t")])
    mrow = lambda i: modf_t[0:1, H * i:H * (i + 1)]

    ca1 = k.sb("ca1", [1, H], scope=sc0)
    cb1 = k.sb("cb1", [1, H], scope=sc0)
    k.v.op("tensor_scalar", ca1[:], mrow(1), 1.0, None, ALU.add,
           ins=[B("modf_t")], outs=[B("ca1")])
    k.v.op("tensor_scalar", cb1[:], mrow(4), 1.0, None, ALU.add,
           ins=[B("modf_t")], outs=[B("cb1")])

    def gclamp(name, src):
        ab = k.sb(name + "_a", [1, H], scope=sc0)
        sg = k.sb(name + "_s", [1, H], scope=sc0)
        res = k.sb(name, [1, H], scope=sc0)
        k.s.op("activation", ab[:], src, AF.Abs, bias=0.0, scale=1.0,
               ins=[B("modf_t")], outs=[B(name + "_a")])
        k.v.op("tensor_scalar", ab[:], ab[:], 0.25, None, ALU.max,
               ins=[B(name + "_a")], outs=[B(name + "_a")])
        k.v.op("tensor_scalar", sg[:], src, 0.0, None, ALU.is_ge,
               ins=[B("modf_t")], outs=[B(name + "_s")])
        k.v.op("tensor_scalar", sg[:], sg[:], 2.0, -1.0, ALU.mult, ALU.add,
               ins=[B(name + "_s")], outs=[B(name + "_s")])
        k.v.op("tensor_tensor", res[:], ab[:], sg[:], ALU.mult,
               ins=[B(name + "_a"), B(name + "_s")], outs=[B(name)])
        return res

    gax = gclamp("gax", mrow(2))
    gbx = gclamp("gbx", mrow(5))

    bc_fill(ca1b, "ca1b", ca1[:], B("ca1"), H)
    bc_fill(sab, "sab", mrow(0), B("modf_t"), H)
    bc_fill(gaxb, "gaxb", gax[:], B("gax"), H)
    bc_fill(cb1b, "cb1b", cb1[:], B("cb1"), H)
    bc_fill(sbb, "sbb", mrow(3), B("modf_t"), H)
    bc_fill(gbxb, "gbxb", gbx[:], B("gbx"), H)

    gnw_t = k.sb("gnw_t", [1, H], scope=sc0)
    ginw_t = k.sb("ginw_t", [1, H], scope=sc0)
    k.qy.dma(gnw_t[:], gnw_in.ap(), ins=[B("gnw_in")], outs=[B("gnw_t")])
    k.qy.dma(ginw_t[:], ginw_in.ap(), ins=[B("ginw_in")], outs=[B("ginw_t")])
    gg_t = k.sb("gg_t", [1, H], scope=sc0)
    k.v.op("tensor_tensor", gg_t[:], gnw_t[:], ginw_t[:], ALU.mult,
           ins=[B("gnw_t"), B("ginw_t")], outs=[B("gg_t")])
    bc_fill(ggb, "ggb", gg_t[:], B("gg_t"), H)
    bc_fill(gnwb, "gnwb", gnw_t[:], B("gnw_t"), H)
    gpn_t = k.sb("gpn_t", [1, H], scope=sc0)
    k.qy.dma(gpn_t[:], gpn_in.ap(), ins=[B("gpn_in")], outs=[B("gpn_t")])
    bc_fill(gpnb, "gpnb", gpn_t[:], B("gpn_t"), H)
    dpn_t = k.sb("dpn_t", [1, IM], scope=sc0)
    k.qy.dma(dpn_t[:], dpn_in.ap(), ins=[B("dpn_in")], outs=[B("dpn_t")])
    bc_fill(dpnb, "dpnb", dpn_t[:], B("dpn_t"), IM)

    gw_t = k.sb("gw_t", [E, H], scope=sc0)
    k.qy.dma(gw_t[:], gatew_in.ap(), ins=[B("gatew_in")], outs=[B("gw_t")])
    gw_abs = k.sb("gw_abs", [E, H], scope=sc0)
    gw_sum = k.sb("gw_sum", [E, 1], scope=sc0)
    k.s.op("activation", gw_abs[:], gw_t[:], AF.Abs, bias=0.0, scale=1.0,
           accum_out=gw_sum[:], ins=[B("gw_t")], outs=[B("gw_abs"), B("gw_sum")])
    k.t.op("matmul", pk[0][0:1, 0:1], ones_cf[0:E, :], gw_sum[:],
           start=True, stop=True, ins=[B("ones_cf"), B("gw_sum")], outs=[B("pk0")])
    wsg = k.sb("wsg", [1, 1], scope=sc0)
    k.v.op("tensor_scalar", wsg[:], pk[0][0:1, 0:1], 1.0 / (E * H), 1e-5,
           ALU.mult, ALU.max, ins=[B("pk0")], outs=[B("wsg")])
    wsg_r = k.sb("wsg_r", [1, 1], scope=sc0)
    k.v.op("reciprocal", wsg_r[:], wsg[:], ins=[B("wsg")], outs=[B("wsg_r")])
    bc_fill(gamc_b, "gamc_b", wsg[:], B("wsg"), 1)
    k.v.op("tensor_scalar", gamc_b[:], gamc_b[:], 1.0 / 127.0, None, ALU.mult,
           ins=[B("gamc_b")], outs=[B("gamc_b")])
    wsgb = bcast_row("wsgb", wsg_r[:], B("wsg_r"), 1, scope=sc0)
    mg_t = k.sb("mg_t", [E, H], scope=sc0)
    k.s.op("activation", mg_t[:], gw_t[:], AF.Identity, bias=RC,
           scale=wsgb[0:E, 0:1], ins=[B("gw_t"), B("wsgb"), B("rc_t")],
           outs=[B("mg_t")])
    k.v.op("tensor_scalar", mg_t[:], mg_t[:], -RC, None, ALU.add,
           ins=[B("mg_t")], outs=[B("mg_t")])
    k.v.op("tensor_scalar", mg_t[:], mg_t[:], -1.0, 1.0, ALU.max, ALU.min,
           ins=[B("mg_t")], outs=[B("mg_t")])
    for kk in range(8):
        k.t.op("transpose", pk[kk % 2][:, 0:E], mg_t[:, 128 * kk:128 * (kk + 1)],
               id_f[0:E, 0:E], ins=[B("mg_t"), B("id_f")], outs=[B(f"pk{kk % 2}")])
        k.v.op("tensor_copy", mgt[:, E * kk:E * (kk + 1)], pk[kk % 2][:, 0:E],
               ins=[B(f"pk{kk % 2}")], outs=[B("mgt")])

    sc0.close()
    k.fence()

    stat_g = k.sb("stat_g", [128, NT])
    cid1_s = k.sb("cid1_s", [128, NT], I32)
    cid2_s = k.sb("cid2_s", [128, NT], I32)
    wk1_s = k.sb("wk1_s", [128, NT])
    wk2_s = k.sb("wk2_s", [128, NT])
    xidx_s = k.sb("xidx_s", [128, NS], I32)
    agp_s = k.sb("agp_s", [128, NS])
    adp_s = k.sb("adp_s", [128, NS])
    sxb = k.sb("sxb", [128, 1])
    sxv = k.sb("sxv", [128, 1])
    s2b = k.sb("s2b", [128, 1])
    s2v = k.sb("s2v", [128, 1])
    mdpc_b = k.sb("mdpc_b", [128, 1])
    mgpc_b = k.sb("mgpc_b", [128, 1])
    IL = k.sb("IL", [128, NT * E])

    class Stream:
        def __init__(self, si, widths):
            self.si = si
            self.h = {}
            for nm, shape, dt in widths:
                self.h[nm] = k.sb(f"{nm}_u{si}", shape, dt)
        def __getitem__(self, nm):
            return self.h[nm]
        def b(self, nm):
            return B(f"{nm}_u{self.si}")

    base_w = ([("scr", [128, H], F32)] +
              [(f"st{i}", [128, 1], F32) for i in range(8)])
    SS = [Stream(0, base_w), Stream(1, base_w), Stream(2, base_w)]

    scr = k.sb("scr", [128, 1])
    stt = [k.sb(f"stt{i}", [128, 1]) for i in range(8)]

    class _LegacyS:
        def __init__(self):
            self.h = {"scr": scr}
            for i in range(8):
                self.h[f"st{i}"] = stt[i]
        def __getitem__(self, nm):
            return self.h[nm]
        def b(self, nm):
            return B("scr" if nm == "scr" else "stt" + nm[2:])
    LS = _LegacyS()

    def drive(gen):
        for _ in gen:
            pass

    def interleave(gens):
        gens = list(gens)
        while gens:
            alive = []
            for g in gens:
                try:
                    next(g)
                    alive.append(g)
                except StopIteration:
                    pass
            gens = alive

    def rsqrt_col(S, dst, src, scale, eps):
        d, ds_, db, sb_ = S[f"st{dst}"], S[f"st{src}"], S.b(f"st{dst}"), S.b(f"st{src}")
        k.v.op("tensor_scalar", d[:], ds_[:], scale, eps, ALU.mult, ALU.add,
               ins=[sb_], outs=[db])
        yield
        k.s.op("activation", d[:], d[:], AF.Sqrt, bias=0.0, scale=1.0,
               ins=[db], outs=[db])
        yield
        k.v.op("reciprocal", d[:], d[:], ins=[db], outs=[db])
        yield

    def adaln_tile(S, xt_ap, xt_buf, h_ap, h_buf, vtile, vbuf):
        st0, st2, st3 = S["st0"], S["st2"], S["st3"]
        b0, b1, b2, b3 = S.b("st0"), S.b("st1"), S.b("st2"), S.b("st3")
        scr_, bscr = S["scr"], S.b("scr")
        k.v.op("tensor_reduce", st0[:], xt_ap, AX.X, ALU.add,
               ins=[xt_buf], outs=[b0])
        yield
        k.s.op("activation", scr_[:], xt_ap, AF.Square, bias=0.0, scale=1.0,
               accum_out=S["st1"][:], ins=[xt_buf], outs=[bscr, b1])
        yield
        k.v.op("tensor_scalar", st0[:], st0[:], 1.0 / H, None, ALU.mult,
               ins=[b0], outs=[b0])
        k.v.op("tensor_tensor", st2[:], st0[:], st0[:], ALU.mult,
               ins=[b0], outs=[b2])
        yield
        k.v.op("tensor_scalar", st3[:], S["st1"][:], 1.0 / H, None, ALU.mult,
               ins=[b1], outs=[b3])
        k.v.op("tensor_tensor", st3[:], st3[:], st2[:], ALU.subtract,
               ins=[b3, b2], outs=[b3])
        k.v.op("tensor_scalar", st3[:], st3[:], 1e-5, None, ALU.add,
               ins=[b3], outs=[b3])
        yield
        k.s.op("activation", st3[:], st3[:], AF.Sqrt, bias=0.0, scale=1.0,
               ins=[b3], outs=[b3])
        yield
        k.v.op("reciprocal", st3[:], st3[:], ins=[b3], outs=[b3])
        k.v.op("tensor_tensor", st2[:], st0[:], st3[:], ALU.mult,
               ins=[b0, b3], outs=[b2])
        k.v.op("tensor_scalar", st2[:], st2[:], -1.0, None, ALU.mult,
               ins=[b2], outs=[b2])
        yield
        k.s.op("activation", vtile[:], xt_ap, AF.Identity, bias=st2[:, 0:1],
               scale=st3[:, 0:1], ins=[xt_buf, b2, b3], outs=[vbuf])
        yield
        k.v.op("tensor_tensor", h_ap, vtile[:], ca1b[:], ALU.mult,
               ins=[vbuf, B("ca1b")], outs=[h_buf])
        k.v.op("tensor_tensor", h_ap, h_ap, sab[:], ALU.add,
               ins=[h_buf, B("sab")], outs=[h_buf])
        yield

    scw = ExitStack()
    PW = Stream(4, [])
    PW.h["wrow"] = k.sb("wrow_u4", [128, H], scope=scw)
    PW.h["fqs"] = k.sb("fqs_u4", [128, H], BF16, scope=scw)
    PW.h["wrow2"] = k.sb("wrow2_u4", [128, IM], scope=scw)
    PW.h["wq2"] = k.sb("wq2_u4", [128, IM], BF16, scope=scw)
    for i in range(3):
        PW.h[f"st{i}"] = k.sb(f"st{i}_u4", [128, 1], scope=scw)
    fqT_dst = [(fqgT_p, "fqgT_p"), (fquT_p, "fquT_p")]

    def wq_chain(job, S):
        src_t, m, r0, rr = job
        wrow = S["wrow"]
        bw = S.b("wrow")
        st = [S[f"st{i}"] for i in range(3)]
        sb_ = [S.b(f"st{i}") for i in range(3)]
        k.qy.dma(wrow[0:rr, :], src_t.ap()[r0:r0 + rr, :],
                 ins=[B(src_t.name)], outs=[bw])
        yield
        k.v.op("tensor_reduce", st[0][0:rr, :], wrow[0:rr, :], AX.X, ALU.max,
               apply_absolute_value=True, ins=[bw], outs=[sb_[0]])
        yield
        k.v.op("reciprocal", st[1][0:rr, :], st[0][0:rr, :],
               ins=[sb_[0]], outs=[sb_[1]])
        k.v.op("tensor_scalar", st[1][0:rr, :], st[1][0:rr, :], 127.0, None,
               ALU.mult, ins=[sb_[1]], outs=[sb_[1]])
        k.v.op("tensor_scalar", st[2][0:rr, :], st[0][0:rr, :], 1.0 / 127.0,
               None, ALU.mult, ins=[sb_[0]], outs=[sb_[2]])
        yield
        k.s.op("activation", wrow[0:rr, :], wrow[0:rr, :], AF.Identity,
               bias=RC, scale=st[1][0:rr, 0:1],
               ins=[bw, sb_[1], B("rc_t")], outs=[bw])
        yield
        k.v.op("tensor_scalar", wrow[0:rr, :], wrow[0:rr, :], -RC, None,
               ALU.add, ins=[bw], outs=[bw])
        k.v.op("tensor_scalar", wrow[0:rr, :], wrow[0:rr, :],
               st[2][0:rr, 0:1], None, ALU.mult, ins=[bw, sb_[2]], outs=[bw])
        yield
        fqs, bfq = S["fqs"], S.b("fqs")
        for kk in range(8):
            tb = 6 + kk % 2
            k.t.op("transpose", pk[tb][:, 0:rr],
                   wrow[0:rr, 128 * kk:128 * (kk + 1)],
                   id_f[0:rr, 0:rr] if rr < 128 else id_f[:],
                   ins=[bw, B("id_f")], outs=[B(f"pk{tb}")])
            k.s.op("activation", fqs[:, rr * kk:rr * (kk + 1)],
                   pk[tb][:, 0:rr], AF.Copy, bias=0.0, scale=1.0,
                   ins=[B(f"pk{tb}")], outs=[bfq])
            if kk % 4 == 3:
                yield
        dstp, dstpb = fqT_dst[m]
        k.qg.dma(dstp.ap()[:, r0:r0 + rr].rearrange("(a p) r -> p a r", p=128),
                 fqs.ap()[0:128, 0:8 * rr].rearrange("p (a r) -> p a r", r=rr),
                 ins=[bfq], outs=[B(dstpb)])
        yield

    wq_jobs = []
    for m, src_t in enumerate((wg_sl, wu_sl)):
        for r0 in range(0, IFS, 128):
            wq_jobs.append((src_t, m, r0, min(128, IFS - r0)))
    wrowd = k.sb("wrowd", [HS, IF], scope=scw)
    wqd_b = k.sb("wqd_b", [HS, IF], BF16, scope=scw)
    wabs_acc = k.sb("wabs_acc", [128, 48], scope=scw)
    n_gp = 2 * IM // 128
    n_dp = H // 128

    def wabs_chain(win, wid, i, S):
        wrow2, wscr2 = S["wrow2"], S["wq2"]
        bw, bs = S.b("wrow2"), S.b("wq2")
        k.qy.dma(wrow2[:, 0:wid], win.ap()[128 * i:128 * (i + 1), :],
                 ins=[B(win.name)], outs=[bw])
        yield
        k.s.op("activation", wscr2[:, 0:wid], wrow2[:, 0:wid], AF.Abs, bias=0.0,
               scale=1.0, accum_out=S["st0"][:],
               ins=[bw], outs=[bs, S.b("st0")])
        yield
        k.v.op("tensor_tensor", wabs_acc[:, 40 + S.si:41 + S.si],
               wabs_acc[:, 40 + S.si:41 + S.si], S["st0"][:], ALU.add,
               ins=[B("wabs_acc"), S.b("st0")], outs=[B("wabs_acc")])
        yield

    def wtern_chain(win, wout, wid, i, S, sc_ptr, sc_buf):
        wrow2, wq2 = S["wrow2"], S["wq2"]
        bw, bq = S.b("wrow2"), S.b("wq2")
        k.qy.dma(wrow2[:, 0:wid], win.ap()[128 * i:128 * (i + 1), :],
                 ins=[B(win.name)], outs=[bw])
        yield
        k.s.op("activation", wrow2[:, 0:wid], wrow2[:, 0:wid], AF.Identity,
               bias=RC, scale=sc_ptr, ins=[bw, sc_buf, B("rc_t")], outs=[bw])
        yield
        k.v.op("tensor_scalar", wrow2[:, 0:wid], wrow2[:, 0:wid], -RC, None,
               ALU.add, ins=[bw], outs=[bw])
        k.v.op("tensor_scalar", wq2[:, 0:wid], wrow2[:, 0:wid], -1.0, 1.0,
               ALU.max, ALU.min, ins=[bw], outs=[bq])
        yield
        k.qg.dma(wout.ap()[128 * i:128 * (i + 1), :], wq2[:, 0:wid],
                 ins=[bq], outs=[B(wout.name)])
        yield

    asum = k.sb("asum", [128, 1], scope=scw)
    wsgp = k.sb("wsgp", [1, 1], scope=scw)
    wsgp_r = k.sb("wsgp_r", [1, 1], scope=scw)
    wsdp = k.sb("wsdp", [1, 1], scope=scw)
    wsdp_r = k.sb("wsdp_r", [1, 1], scope=scw)
    t1 = k.sb("wsgp_rb", [128, 1], scope=scw)
    t2 = k.sb("wsgp_b", [128, 1], scope=scw)
    t3 = k.sb("wsdp_rb", [128, 1], scope=scw)
    t4 = k.sb("wsdp_b", [128, 1], scope=scw)

    def p0w_gen():
        for j in range(len(wq_jobs)):
            yield from wq_chain(wq_jobs[j], PW)
        k.qy.dma(wrowd[:], wd_sl.ap(), ins=[B("wd_sl")], outs=[B("wrowd")])
        yield
        k.v.op("tensor_reduce", stt[0][0:HS, :], wrowd[:], AX.X, ALU.max,
               apply_absolute_value=True, ins=[B("wrowd")], outs=[B("stt0")])
        yield
        k.v.op("reciprocal", stt[1][0:HS, :], stt[0][0:HS, :],
               ins=[B("stt0")], outs=[B("stt1")])
        k.v.op("tensor_scalar", stt[1][0:HS, :], stt[1][0:HS, :], 127.0, None,
               ALU.mult, ins=[B("stt1")], outs=[B("stt1")])
        k.v.op("tensor_scalar", stt[2][0:HS, :], stt[0][0:HS, :], 1.0 / 127.0,
               None, ALU.mult, ins=[B("stt0")], outs=[B("stt2")])
        yield
        k.s.op("activation", wrowd[:], wrowd[:], AF.Identity, bias=RC,
               scale=stt[1][0:HS, 0:1], ins=[B("wrowd"), B("stt1"), B("rc_t")],
               outs=[B("wrowd")])
        yield
        k.v.op("tensor_scalar", wrowd[:], wrowd[:], -RC, None, ALU.add,
               ins=[B("wrowd")], outs=[B("wrowd")])
        k.v.op("tensor_scalar", wqd_b[:], wrowd[:], stt[2][0:HS, 0:1], None,
               ALU.mult, ins=[B("wrowd"), B("stt2")], outs=[B("wqd_b")])
        yield
        k.qg.dma(fqd_p.ap(), wqd_b[:], ins=[B("wqd_b")], outs=[B("fqd_p")])
        k.qg.wait_all(k.g)
        k.qc.collective("AllGather", ALU.bypass, RG, [fqgT_p.ap()],
                        [fqgT_d.ap()], ins=[B("fqgT_p")], outs=[B("fqgT_d")])
        k.qc.collective("AllGather", ALU.bypass, RG, [fquT_p.ap()],
                        [fquT_d.ap()], ins=[B("fquT_p")], outs=[B("fquT_d")])
        k.qc.collective("AllGather", ALU.bypass, RG, [fqd_p.ap()],
                        [fqd_d.ap()], ins=[B("fqd_p")], outs=[B("fqd_d")])
        yield
        k.v.op("memset", wabs_acc[:], 0.0, outs=[B("wabs_acc")])
        for job in [(gpw_in, H, i) for i in range(n_gp)]:
            yield from wabs_chain(*job, PW)
        k.v.op("tensor_copy", asum[:], wabs_acc[:, 44:45],
               ins=[B("wabs_acc")], outs=[B("asum")])
        k.t.op("matmul", pk[0][0:1, 0:1], ones_cf[:], asum[:], start=True,
               stop=True, ins=[B("ones_cf"), B("asum")], outs=[B("pk0")])
        k.v.op("tensor_scalar", wsgp[:], pk[0][0:1, 0:1], 1.0 / (2 * IM * H),
               1e-5, ALU.mult, ALU.max, ins=[B("pk0")], outs=[B("wsgp")])
        yield
        k.v.op("reciprocal", wsgp_r[:], wsgp[:], ins=[B("wsgp")],
               outs=[B("wsgp_r")])
        bc_fill(t1, "wsgp_rb", wsgp_r[:], B("wsgp_r"), 1)
        bc_fill(t2, "wsgp_b", wsgp[:], B("wsgp"), 1)
        k.v.op("tensor_scalar", mgpc_b[:], t2[:], 1.0 / 127.0, None, ALU.mult,
               ins=[B("wsgp_b")], outs=[B("mgpc_b")])
        yield
        for job in [(gpw_in, gpq_d, H, i) for i in range(n_gp)]:
            yield from wtern_chain(*job, PW, t1[:, 0:1], B("wsgp_rb"))
        k.v.op("memset", wabs_acc[:, 44:45], 0.0, outs=[B("wabs_acc")])
        for job in [(dpw_in, IM, i) for i in range(n_dp)]:
            yield from wabs_chain(*job, PW)
        k.v.op("tensor_copy", asum[:], wabs_acc[:, 44:45],
               ins=[B("wabs_acc")], outs=[B("asum")])
        k.t.op("matmul", pk[0][0:1, 0:1], ones_cf[:], asum[:], start=True,
               stop=True, ins=[B("ones_cf"), B("asum")], outs=[B("pk0")])
        k.v.op("tensor_scalar", wsdp[:], pk[0][0:1, 0:1], 1.0 / (H * IM), 1e-5,
               ALU.mult, ALU.max, ins=[B("pk0")], outs=[B("wsdp")])
        yield
        k.v.op("reciprocal", wsdp_r[:], wsdp[:], ins=[B("wsdp")],
               outs=[B("wsdp_r")])
        bc_fill(t3, "wsdp_rb", wsdp_r[:], B("wsdp_r"), 1)
        bc_fill(t4, "wsdp_b", wsdp[:], B("wsdp"), 1)
        k.v.op("tensor_scalar", mdpc_b[:], t4[:], 1.0 / 127.0, None, ALU.mult,
               ins=[B("wsdp_b")], outs=[B("mdpc_b")])
        yield
        for job in [(dpw_in, dpq_d, IM, i) for i in range(n_dp)]:
            yield from wtern_chain(*job, PW, t3[:, 0:1], B("wsdp_rb"))

    def paced(gen, n):
        it = iter(gen)
        alive = True
        while alive:
            for _ in range(n):
                try:
                    next(it)
                except StopIteration:
                    alive = False
                    break
            yield

    sc_ng = ExitStack()
    ngt = k.sb("ngt", [128, 8 * TL], BF16, scope=sc_ng)
    with ExitStack() as sc1:
        p1w = [("p1x", [128, H], F32), ("p1h", [128, H], F32),
               ("p1tg", [128, H], F32)]
        for S in SS:
            for nm, shape, dt in p1w:
                S.h[nm] = k.sb(f"{nm}_u{S.si}", shape, dt, scope=sc1)

        def p1_chain(t, S):
            p1x, p1h, p1tg = (S["p1x"], S["p1h"], S["p1tg"])
            p1v = p1tg
            bx, bh, btg = (S.b("p1x"), S.b("p1h"), S.b("p1tg"))
            bv = btg
            scr_, bscr = S["scr"], S.b("scr")
            st = [S[f"st{i}"] for i in range(8)]
            sb_ = [S.b(f"st{i}") for i in range(8)]
            k.qy.dma(p1x[:], xown.ap()[128 * t:128 * (t + 1), :],
                     ins=[B("xown")], outs=[bx])
            yield
            yield from adaln_tile(S, p1x[:], bx, p1h[:], bh, p1v, bv)
            k.s.op("activation", scr_[:], p1h[:], AF.Square, bias=0.0, scale=1.0,
                   accum_out=st[0][:], ins=[bh], outs=[bscr, sb_[0]])
            yield
            yield from rsqrt_col(S, 1, 0, 1.0 / H, 1e-6)
            k.v.op("tensor_tensor", p1tg[:], p1h[:], gnwb[:], ALU.mult,
                   ins=[bh, B("gnwb")], outs=[btg])
            yield
            k.s.op("activation", scr_[:], p1tg[:], AF.Square, bias=0.0, scale=1.0,
                   accum_out=st[2][:], ins=[btg], outs=[bscr, sb_[2]])
            yield
            k.v.op("tensor_tensor", st[3][:], st[1][:], st[1][:], ALU.mult,
                   ins=[sb_[1]], outs=[sb_[3]])
            k.v.op("tensor_tensor", st[3][:], st[3][:], st[2][:], ALU.mult,
                   ins=[sb_[3], sb_[2]], outs=[sb_[3]])
            yield
            yield from rsqrt_col(S, 4, 3, 1.0 / H, 1e-8)
            k.v.op("tensor_tensor", p1tg[:], p1h[:], ggb[:], ALU.mult,
                   ins=[bh, B("ggb")], outs=[btg])
            yield
            k.v.op("tensor_reduce", st[5][:], p1tg[:], AX.X, ALU.max,
                   apply_absolute_value=True, ins=[btg], outs=[sb_[5]])
            yield
            k.v.op("tensor_scalar", st[6][:], st[5][:], 1e-30, None, ALU.max,
                   ins=[sb_[5]], outs=[sb_[6]])
            k.v.op("reciprocal", st[6][:], st[6][:], ins=[sb_[6]], outs=[sb_[6]])
            k.v.op("tensor_scalar", st[6][:], st[6][:], 127.0, None, ALU.mult,
                   ins=[sb_[6]], outs=[sb_[6]])
            yield
            k.s.op("activation", scr_[:], p1tg[:], AF.Identity, bias=RC,
                   scale=st[6][:, 0:1], ins=[btg, sb_[6], B("rc_t")], outs=[bscr])
            yield
            ngt3 = ngt.ap().rearrange("p (a c) -> p a c", c=TL)
            for kk in range(8):
                j = kk // 4
                pbn = f"pk{(2 * S.si + j) % 6}"
                pbank = pk[(2 * S.si + j) % 6]
                k.t.op("transpose", pbank[:, 128 * (kk % 4):128 * (kk % 4) + 128],
                       scr_[:, 128 * kk:128 * (kk + 1)], id_f[:],
                       ins=[bscr, B("id_f")], outs=[B(pbn)],
                       defer=(kk % 4 != 3))
                if kk % 4 == 3:
                    k.s.op("activation",
                           ngt3[:, 4 * j:4 * j + 4, 128 * t:128 * (t + 1)],
                           pk[(2 * S.si + j) % 6][:, 0:512]
                           .rearrange("p (a c) -> p a c", c=128),
                           AF.Copy, bias=-RC, scale=1.0,
                           ins=[B(pbn), B("mrc_t")], outs=[B("ngt")])
                    yield
            k.v.op("tensor_tensor", st[7][:], st[1][:], st[4][:], ALU.mult,
                   ins=[sb_[1], sb_[4]], outs=[sb_[7]])
            k.v.op("tensor_tensor", st[7][:], st[7][:], st[5][:], ALU.mult,
                   ins=[sb_[7], sb_[5]], outs=[sb_[7]])
            k.v.op("tensor_scalar", st[7][:], st[7][:], 1e-5, None, ALU.max,
                   ins=[sb_[7]], outs=[sb_[7]])
            k.v.op("tensor_tensor", stat_g[:, t:t + 1], st[7][:], gamc_b[:],
                   ALU.mult, ins=[sb_[7], B("gamc_b")], outs=[B("stat_g")])
            yield

        def p1_stream(si):
            for t in range(si, NT, len(SS)):
                yield from p1_chain(t, SS[si])

        interleave([p1_stream(si) for si in range(len(SS))] +
                   [paced(p0w_gen(), 5)])
    k.fence()

    with ExitStack() as sc2:
        for t in range(NT):
            for kk in range(8):
                k.t.op("matmul", pk[t % 4][:, 0:E],
                       ngt[:, TL * kk + 128 * t:TL * kk + 128 * (t + 1)],
                       mgt[:, E * kk:E * (kk + 1)], start=(kk == 0), stop=(kk == 7),
                       ins=[B("ngt"), B("mgt")], outs=[B(f"pk{t % 4}")],
                       defer=(kk != 7))
            k.v.op("tensor_copy", IL[:, E * t:E * (t + 1)], pk[t % 4][:, 0:E],
                   ins=[B(f"pk{t % 4}")], outs=[B("IL")])
    sc_ng.close()
    k.fence()

    NCL = NT * E
    selP_d = k.dram("selP_d", [128, E * NT], F32)
    selF_d = k.dram("selF_d", [128 * NCORE, E * NT], F32)
    tot_d = k.dram("tot_d", [1, 8], F32)
    totF_d = k.dram("totF_d", [NCORE, 8], F32)
    lcar_d = k.dram("lcar_d", [NT, E], F32)
    ccar_d = k.dram("ccar_d", [NCORE, E], F32)
    cs_d = k.dram("cs_d", [1, NCL], F32)
    with ExitStack() as sc3:
        def sb3(name, dtype=F32):
            tile = k.sb(name, [128, NCL], dtype, scope=sc3)
            return tile, tile.ap().rearrange("p (a e) -> p a e", e=E)

        ioe3 = io_e.ap()[0:128, 0:E].rearrange("p e -> p () e").broadcast_to(
            (128, NT, E))
        IL3 = IL.ap().rearrange("p (a e) -> p a e", e=E)
        keys, keys3 = sb3("keys")
        k.v.op("tensor_scalar", keys[:], IL[:], 16.0, None, ALU.mult,
               ins=[B("IL")], outs=[B("keys")])
        k.v.op("tensor_tensor", keys3, keys3, ioe3, ALU.subtract,
               ins=[B("keys"), B("io_e")], outs=[B("keys")])
        m1 = k.sb("m1", [128, NT], scope=sc3)
        m13 = m1.ap().rearrange("p (a o) -> p a o", o=1)
        k.v.op("tensor_reduce", m13, keys3, AX.X, ALU.max,
               ins=[B("keys")], outs=[B("m1")])
        is1, is13 = sb3("is1")
        k.v.op("tensor_tensor", is13, keys3, m13.broadcast_to((128, NT, E)),
               ALU.is_ge, ins=[B("keys"), B("m1")], outs=[B("is1")])
        keys2, keys23 = sb3("keys2")
        k.v.op("scalar_tensor_tensor", keys2[:], is1[:], -2.0 ** 30, keys[:],
               ALU.mult, ALU.add, ins=[B("is1"), B("keys")], outs=[B("keys2")])
        m2 = k.sb("m2", [128, NT], scope=sc3)
        m23 = m2.ap().rearrange("p (a o) -> p a o", o=1)
        k.v.op("tensor_reduce", m23, keys23, AX.X, ALU.max,
               ins=[B("keys2")], outs=[B("m2")])
        sel, sel3 = sb3("sel")
        k.v.op("tensor_tensor", sel3, keys3, m23.broadcast_to((128, NT, E)),
               ALU.is_ge, ins=[B("keys"), B("m2")], outs=[B("sel")])
        lg, lg3 = sb3("lg")
        g3 = stat_g.ap().rearrange("p (a o) -> p a o", o=1)
        k.v.op("tensor_tensor", lg3, IL3, g3.broadcast_to((128, NT, E)), ALU.mult,
               ins=[B("IL"), B("stat_g")], outs=[B("lg")])
        lm = k.sb("lm", [128, NT], scope=sc3)
        lm3 = lm.ap().rearrange("p (a o) -> p a o", o=1)
        k.v.op("tensor_reduce", lm3, lg3, AX.X, ALU.max, ins=[B("lg")], outs=[B("lm")])
        k.v.op("tensor_tensor", lg3, lg3, lm3.broadcast_to((128, NT, E)),
               ALU.subtract, ins=[B("lg"), B("lm")], outs=[B("lg")])
        k.s.op("activation", lg[:], lg[:], AF.Exp, bias=0.0, scale=1.0,
               ins=[B("lg")], outs=[B("lg")])
        ssum = k.sb("ssum", [128, NT], scope=sc3)
        ssum3 = ssum.ap().rearrange("p (a o) -> p a o", o=1)
        k.v.op("tensor_reduce", ssum3, lg3, AX.X, ALU.add,
               ins=[B("lg")], outs=[B("ssum")])
        k.v.op("reciprocal", ssum[:], ssum[:], ins=[B("ssum")], outs=[B("ssum")])
        wprob, wprob3 = sb3("wprob")
        k.v.op("tensor_tensor", wprob3, lg3, ssum3.broadcast_to((128, NT, E)),
               ALU.mult, ins=[B("lg"), B("ssum")], outs=[B("wprob")])
        k.v.op("tensor_tensor", wprob[:], wprob[:], sel[:], ALU.mult,
               ins=[B("wprob"), B("sel")], outs=[B("wprob")])
        selb = k.sb("selb", [128, NCL], BF16, scope=sc3)
        k.v.op("tensor_copy", selb[:], sel[:], ins=[B("sel")], outs=[B("selb")])
        wpx, _ = sb3("wpx")
        for j in range(cdiv(NCL, 512)):
            w = min(512, NCL - 512 * j)
            p = pk[j % 4]
            k.t.op("matmul", p[:, 0:w], tri_b[:], selb[:, 512 * j:512 * j + w],
                   start=True, stop=True, ins=[B("tri_b"), B("selb")],
                   outs=[B(f"pk{j % 4}")])
            k.v.op("tensor_copy", wpx[:, 512 * j:512 * j + w], p[:, 0:w],
                   ins=[B(f"pk{j % 4}")], outs=[B("wpx")])
        cs_row = k.sb("cs_row", [1, NCL], scope=sc3)
        for j in range(cdiv(NCL, 512)):
            w = min(512, NCL - 512 * j)
            p = pk[4 + (j % 2)]
            k.t.op("matmul", p[0:1, 0:w], ones_c[:], selb[:, 512 * j:512 * j + w],
                   start=True, stop=True, ins=[B("ones_c"), B("selb")],
                   outs=[B(f"pk{4 + (j % 2)}")])
            k.v.op("tensor_copy", cs_row[0:1, 512 * j:512 * j + w], p[0:1, 0:w],
                   ins=[B(f"pk{4 + (j % 2)}")], outs=[B("cs_row")])
        k.qg.dma(cs_d.ap(), cs_row[:], ins=[B("cs_row")], outs=[B("cs_d")])
        csje = k.sb("csje", [NT, E], BF16, scope=sc3)
        csje_f = k.sb("csje_f", [NT, E], scope=sc3)
        k.qy.dma(csje_f[:], cs_d.ap().rearrange("o (a e) -> (o a) e", e=E),
                 ins=[B("cs_d")], outs=[B("csje_f")])
        k.v.op("tensor_copy", csje[:], csje_f[:], ins=[B("csje_f")], outs=[B("csje")])
        k.t.op("matmul", pk[6][0:NT, 0:E], tri_b[0:NT, 0:NT], csje[:],
               start=True, stop=True, ins=[B("tri_b"), B("csje")], outs=[B("pk6")])
        lcar = k.sb("lcar", [NT, E], scope=sc3)
        k.v.op("tensor_copy", lcar[:], pk[6][0:NT, 0:E], ins=[B("pk6")], outs=[B("lcar")])
        k.qg.dma(lcar_d.ap(), lcar[:], ins=[B("lcar")], outs=[B("lcar_d")])
        k.t.op("matmul", pk[6][0:1, 0:E], ones_c[0:NT, :], csje[:],
               start=True, stop=True, ins=[B("ones_c"), B("csje")], outs=[B("pk6")])
        totr = k.sb("totr", [1, E], scope=sc3)
        k.v.op("tensor_copy", totr[:], pk[6][0:1, 0:E], ins=[B("pk6")], outs=[B("totr")])
        k.qg.dma(tot_d.ap(), totr[:], ins=[B("totr")], outs=[B("tot_d")])
        k.qc.collective("AllGather", ALU.bypass, RG, [tot_d.ap()], [totF_d.ap()],
                        ins=[B("tot_d")], outs=[B("totF_d")])
        selx = k.sb("selx", [128, E * NT], scope=sc3)
        k.v.op("tensor_copy", selx.ap().rearrange("p (e a) -> p e a", a=NT),
               sel.ap().rearrange("p (a e) -> p e a", e=E),
               ins=[B("sel")], outs=[B("selx")])
        k.qg.dma(selP_d.ap(), selx[:], ins=[B("selx")], outs=[B("selP_d")])
        k.qg.wait_all(k.g)
        k.qc.collective("AllGather", ALU.bypass, RG, [selP_d.ap()], [selF_d.ap()],
                        ins=[B("selP_d")], outs=[B("selF_d")])
        totF = k.sb("totF", [NCORE, E], scope=sc3)
        k.qy.dma(totF[:], totF_d.ap(), ins=[B("totF_d")], outs=[B("totF")])
        k.t.op("matmul", pk[6][0:NCORE, 0:E], tri_f[0:NCORE, 0:NCORE], totF[:],
               start=True, stop=True, ins=[B("tri_f"), B("totF")], outs=[B("pk6")])
        ccar = k.sb("ccar", [NCORE, E], scope=sc3)
        k.v.op("tensor_copy", ccar[:], pk[6][0:NCORE, 0:E], ins=[B("pk6")],
               outs=[B("ccar")])
        k.qg.dma(ccar_d.ap(), ccar[:], ins=[B("ccar")], outs=[B("ccar_d")])
        ccrow = k.sb("ccrow", [1, E], scope=sc3)
        k.qg.wait_all(k.y)
        k.qy.dma(ccrow[:], ccar_d.ap()[ds(pid["y"], 1), :],
                 ins=[B("ccar_d")], outs=[B("ccrow")])
        ccb = k.sb("ccb", [128, E], scope=sc3)
        k.t.op("matmul", pk[5][:, 0:E], ones_rf[:], ccrow[0:1, :],
               start=True, stop=True, ins=[B("ones_rf"), B("ccrow")], outs=[B("pk5")])
        k.v.op("tensor_copy", ccb[:], pk[5][:, 0:E], ins=[B("pk5")], outs=[B("ccb")])
        lcrow = k.sb("lcrow", [1, NCL], scope=sc3)
        k.qy.dma(lcrow[:], lcar_d.ap().rearrange("a e -> () (a e)"),
                 ins=[B("lcar_d")], outs=[B("lcrow")])
        lcb = k.sb("lcb", [128, NCL], scope=sc3)
        for j in range(cdiv(NCL, 512)):
            w = min(512, NCL - 512 * j)
            p = pk[4 + (j % 2)]
            k.t.op("matmul", p[:, 0:w], ones_rf[:], lcrow[0:1, 512 * j:512 * j + w],
                   start=True, stop=True, ins=[B("ones_rf"), B("lcrow")],
                   outs=[B(f"pk{4 + (j % 2)}")])
            k.v.op("tensor_copy", lcb[:, 512 * j:512 * j + w], p[:, 0:w],
                   ins=[B(f"pk{4 + (j % 2)}")], outs=[B("lcb")])
        pos, pos3 = sb3("pos")
        k.v.op("tensor_tensor", pos[:], wpx[:], lcb[:], ALU.add,
               ins=[B("wpx"), B("lcb")], outs=[B("pos")])
        ccb3 = ccb.ap().rearrange("p e -> p () e").broadcast_to((128, NT, E))
        k.v.op("tensor_tensor", pos3, pos3, ccb3, ALU.add,
               ins=[B("pos"), B("ccb")], outs=[B("pos")])
        keep, keep3 = sb3("keep")
        k.v.op("tensor_scalar", keep[:], pos[:], float(CAP), None, ALU.is_lt,
               ins=[B("pos")], outs=[B("keep")])
        k.v.op("tensor_tensor", keep[:], keep[:], sel[:], ALU.mult,
               ins=[B("keep"), B("sel")], outs=[B("keep")])
        is2 = k.sb("is2", [128, NCL], scope=sc3)
        k.v.op("tensor_tensor", is2[:], sel[:], is1[:], ALU.subtract,
               ins=[B("sel"), B("is1")], outs=[B("is2")])
        tmp3t = k.sb("tmp3t", [128, NCL], scope=sc3)
        tmp3 = tmp3t.ap().rearrange("p (a e) -> p a e", e=E)
        red1 = k.sb("red1", [128, NT], scope=sc3)
        red13 = red1.ap().rearrange("p (a o) -> p a o", o=1)
        cidf = k.sb("cidf", [128, NT], scope=sc3)

        def extract(mask_t, mask_buf, cid_dst, cid_buf, wk_dst, wk_buf):
            m3 = mask_t.ap().rearrange("p (a e) -> p a e", e=E)
            k.v.op("tensor_tensor", tmp3, m3, ioe3, ALU.mult,
                   ins=[mask_buf, B("io_e")], outs=[B("tmp3t")])
            k.v.op("tensor_reduce", red13, tmp3, AX.X, ALU.add,
                   ins=[B("tmp3t")], outs=[B("red1")])
            k.v.op("tensor_scalar", cidf[:], red1[:], float(CHS), None, ALU.mult,
                   ins=[B("red1")], outs=[B("cidf")])
            k.v.op("tensor_tensor", tmp3t[:], mask_t[:], pos[:], ALU.mult,
                   ins=[mask_buf, B("pos")], outs=[B("tmp3t")])
            k.v.op("tensor_reduce", red13, tmp3, AX.X, ALU.add,
                   ins=[B("tmp3t")], outs=[B("red1")])
            k.v.op("tensor_tensor", cidf[:], cidf[:], red1[:], ALU.add,
                   ins=[B("cidf"), B("red1")], outs=[B("cidf")])
            for j in range(1, NCH_AG):
                k.v.op("tensor_scalar", tmp3t[:, 0:NT], red1[:], float(CHS * j),
                       None, ALU.is_ge, ins=[B("red1")], outs=[B("tmp3t")])
                k.v.op("scalar_tensor_tensor", cidf[:], tmp3t[:, 0:NT],
                       float((E - 1) * CHS), cidf[:], ALU.mult, ALU.add,
                       ins=[B("tmp3t"), B("cidf")], outs=[B("cidf")])
            k.v.op("tensor_tensor", tmp3t[:], mask_t[:], keep[:], ALU.mult,
                   ins=[mask_buf, B("keep")], outs=[B("tmp3t")])
            k.v.op("tensor_reduce", red13, tmp3, AX.X, ALU.add,
                   ins=[B("tmp3t")], outs=[B("red1")])
            k.v.op("tensor_scalar", tmp3t[:, 0:NT], red1[:], -1e6, 1e6,
                   ALU.mult, ALU.add, ins=[B("red1")], outs=[B("tmp3t")])
            k.v.op("tensor_tensor", cidf[:], cidf[:], tmp3t[:, 0:NT], ALU.add,
                   ins=[B("cidf"), B("tmp3t")], outs=[B("cidf")])
            k.v.op("tensor_scalar", cidf[:], cidf[:], float(E * CAP - 1), None,
                   ALU.min, ins=[B("cidf")], outs=[B("cidf")])
            k.v.op("tensor_copy", cid_dst[:], cidf[:], ins=[B("cidf")],
                   outs=[cid_buf])
            k.v.op("tensor_tensor", tmp3t[:], mask_t[:], wprob[:], ALU.mult,
                   ins=[mask_buf, B("wprob")], outs=[B("tmp3t")])
            k.v.op("tensor_reduce",
                   wk_dst.ap()[0:128, 0:NT].rearrange("p (a o) -> p a o", o=1),
                   tmp3, AX.X, ALU.add, ins=[B("tmp3t")], outs=[wk_buf])
            k.v.op("tensor_tensor", wk_dst[:], wk_dst[:], red1[:], ALU.mult,
                   ins=[wk_buf, B("red1")], outs=[wk_buf])

        extract(is1, B("is1"), cid1_s, B("cid1_s"), wk1_s, B("wk1_s"))
        extract(is2, B("is2"), cid2_s, B("cid2_s"), wk2_s, B("wk2_s"))
    k.fence()

    scw.close()
    k.fence()

    sc6a = ExitStack()
    gpwt = k.sb("gpwt", [128, 8 * 4096], BF16, scope=sc6a)
    k.qg.wait_all(k.y)
    for kk in range(8):
        k.qy.dma_t(gpwt[:, 4096 * kk:4096 * (kk + 1)],
                   gpq_d.ap()[:, 128 * kk:128 * (kk + 1)],
                   ins=[B("gpq_d")], outs=[B("gpwt")])

    with ExitStack() as sc5:
        selg = k.sb("selg", [128, NJ], scope=sc5)
        ey = pid["y"]
        for c in range(NCORE):
            k.qy.dma(selg[:, NT * c:NT * (c + 1)],
                     selF_d.ap()[128 * c:128 * (c + 1)][:, ds(ey * NT, NT)],
                     ins=[B("selF_d")], outs=[B("selg")])
        selgb = k.sb("selgb", [128, NJ], BF16, scope=sc5)
        k.v.op("tensor_copy", selgb[:], selg[:], ins=[B("selg")], outs=[B("selgb")])
        csg = k.sb("csg", [1, NJ], scope=sc5)
        for j in range(cdiv(NJ, 512)):
            w = min(512, NJ - 512 * j)
            p = pk[j % 2]
            k.t.op("matmul", p[0:1, 0:w], ones_c[:], selgb[:, 512 * j:512 * j + w],
                   start=True, stop=True, ins=[B("ones_c"), B("selgb")],
                   outs=[B(f"pk{j % 2}")])
            k.v.op("tensor_copy", csg[0:1, 512 * j:512 * j + w], p[0:1, 0:w],
                   ins=[B(f"pk{j % 2}")], outs=[B("csg")])
        csgc = k.sb("csgc", [NJ, 1], BF16, scope=sc5)
        k.t.op("transpose", pk[2][0:NJ, 0:1], csg[:], id_f[0:1, 0:1],
               ins=[B("csg"), B("id_f")], outs=[B("pk2")])
        csgc_f = k.sb("csgc_f", [NJ, 1], scope=sc5)
        k.v.op("tensor_copy", csgc_f[:], pk[2][0:NJ, 0:1], ins=[B("pk2")],
               outs=[B("csgc_f")])
        k.v.op("tensor_copy", csgc[:], csgc_f[:], ins=[B("csgc_f")], outs=[B("csgc")])
        k.t.op("matmul", pk[3][0:NJ, 0:1], tri_b[0:NJ, 0:NJ], csgc[:],
               start=True, stop=True, ins=[B("tri_b"), B("csgc")], outs=[B("pk3")])
        carsel = k.sb("carsel", [NJ, 1], scope=sc5)
        k.v.op("tensor_copy", carsel[:], pk[3][0:NJ, 0:1], ins=[B("pk3")],
               outs=[B("carsel")])
        carK = k.sb("carK", [NJ, 1], scope=sc5)
        k.v.op("tensor_scalar", carK[:], carsel[:], float(CAP), None, ALU.min,
               ins=[B("carsel")], outs=[B("carK")])
        k.t.op("transpose", pk[2][0:1, 0:NJ], carsel[:], id_f[0:NJ, 0:NJ],
               ins=[B("carsel"), B("id_f")], outs=[B("pk2")])
        carselr = k.sb("carselr", [1, NJ], scope=sc5)
        k.v.op("tensor_copy", carselr[:], pk[2][0:1, 0:NJ], ins=[B("pk2")],
               outs=[B("carselr")])
        carKr = k.sb("carKr", [1, NJ], scope=sc5)
        k.v.op("tensor_scalar", carKr[:], carselr[:], float(CAP), None, ALU.min,
               ins=[B("carselr")], outs=[B("carKr")])
        wpe = k.sb("wpe", [128, NJ], scope=sc5)
        for j in range(cdiv(NJ, 512)):
            w = min(512, NJ - 512 * j)
            p = pk[j % 2]
            k.t.op("matmul", p[:, 0:w], tri_b[:], selgb[:, 512 * j:512 * j + w],
                   start=True, stop=True, ins=[B("tri_b"), B("selgb")],
                   outs=[B(f"pk{j % 2}")])
            k.v.op("tensor_copy", wpe[:, 512 * j:512 * j + w], p[:, 0:w],
                   ins=[B(f"pk{j % 2}")], outs=[B("wpe")])
        carsb = k.sb("carsb", [128, NJ], scope=sc5)
        for j in range(cdiv(NJ, 512)):
            w = min(512, NJ - 512 * j)
            p = pk[2 + (j % 2)]
            k.t.op("matmul", p[:, 0:w], ones_rf[:], carselr[0:1, 512 * j:512 * j + w],
                   start=True, stop=True, ins=[B("ones_rf"), B("carselr")],
                   outs=[B(f"pk{2 + (j % 2)}")])
            k.v.op("tensor_copy", carsb[:, 512 * j:512 * j + w], p[:, 0:w],
                   ins=[B(f"pk{2 + (j % 2)}")], outs=[B("carsb")])
        pose = k.sb("pose", [128, NJ], scope=sc5)
        k.v.op("tensor_tensor", pose[:], wpe[:], carsb[:], ALU.add,
               ins=[B("wpe"), B("carsb")], outs=[B("pose")])
        keep_e = k.sb("keep_e", [128, NJ], scope=sc5)
        k.v.op("tensor_scalar", keep_e[:], pose[:], float(CAP), None, ALU.is_lt,
               ins=[B("pose")], outs=[B("keep_e")])
        k.v.op("tensor_tensor", keep_e[:], keep_e[:], selg[:], ALU.mult,
               ins=[B("keep_e"), B("selg")], outs=[B("keep_e")])
        keb = k.sb("keb", [128, NJ], BF16, scope=sc5)
        k.v.op("tensor_copy", keb[:], keep_e[:], ins=[B("keep_e")], outs=[B("keb")])
        rw = k.sb("rw", [128, NJ], scope=sc5)
        for j in range(cdiv(NJ, 512)):
            w = min(512, NJ - 512 * j)
            p = pk[j % 2]
            k.t.op("matmul", p[:, 0:w], tri_b[:], keb[:, 512 * j:512 * j + w],
                   start=True, stop=True, ins=[B("tri_b"), B("keb")],
                   outs=[B(f"pk{j % 2}")])
            k.v.op("tensor_copy", rw[:, 512 * j:512 * j + w], p[:, 0:w],
                   ins=[B(f"pk{j % 2}")], outs=[B("rw")])
        k.v.op("tensor_scalar", rw[:], rw[:], 1.0, None, ALU.add,
               ins=[B("rw")], outs=[B("rw")])
        k.v.op("tensor_tensor", rw[:], rw[:], keep_e[:], ALU.mult,
               ins=[B("rw"), B("keep_e")], outs=[B("rw")])
        k.v.op("tensor_scalar", rw[:], rw[:], -1.0, None, ALU.add,
               ins=[B("rw")], outs=[B("rw")])
        NJH = min(64, NJ)
        pall = k.sb("pall", [128, NJH * 128], BF16, scope=sc5)
        pall3 = pall.ap().rearrange("p (a r) -> p a r", r=128)
        ior3 = io_r.ap().rearrange("p r -> p () r").broadcast_to((128, NJH, 128))
        ct = k.sb("ct", [128, NJ], scope=sc5)
        for hh in range(NJ // NJH):
            rw3 = rw.ap()[0:128, NJH * hh:NJH * (hh + 1)].rearrange(
                "p (a o) -> p a o", o=1).broadcast_to((128, NJH, 128))
            k.v.op("tensor_tensor", pall3, rw3, ior3, ALU.is_equal,
                   ins=[B("rw"), B("io_r")], outs=[B("pall")])
            for jj in range(NJH):
                j = NJH * hh + jj
                p = pk[j % 4]
                k.t.op("matmul", p[:, 0:1], pall[:, 128 * jj:128 * (jj + 1)],
                       iop1_b[:], start=True, stop=True,
                       ins=[B("pall"), B("iop1_b")], outs=[B(f"pk{j % 4}")])
                k.v.op("tensor_copy", ct[:, j:j + 1], p[:, 0:1],
                       ins=[B(f"pk{j % 4}")], outs=[B("ct")])
        k.t.op("transpose", pk[4][0:NJ, 0:128], ct[:], id_f[:],
               ins=[B("ct"), B("id_f")], outs=[B("pk4")])
        ctt = k.sb("ctt", [128, 128], scope=sc5)
        k.v.op("tensor_copy", ctt[0:NJ, :], pk[4][0:NJ, 0:128],
               ins=[B("pk4")], outs=[B("ctt")])
        k.qg.dma(ct_d.ap().rearrange("(a r) o -> a (r o)", r=128), ctt[0:NJ, :],
                 ins=[B("ctt")], outs=[B("ct_d")])
        dcar = k.sb("dcar", [1, NJ], scope=sc5)
        k.v.op("memset", dcar[:], 0.0, outs=[B("dcar")])
        k.v.op("tensor_copy", dcar[0:1, 1:NJ], carKr[0:1, 0:NJ - 1],
               ins=[B("carKr")], outs=[B("dcar")])
        k.v.op("tensor_tensor", dcar[:], carKr[:], dcar[:], ALU.subtract,
               ins=[B("carKr"), B("dcar")], outs=[B("dcar")])
        k.t.op("transpose", pk[5][0:NJ, 0:1], dcar[:], id_f[0:1, 0:1],
               ins=[B("dcar"), B("id_f")], outs=[B("pk5")])
        dcarc = k.sb("dcarc", [NJ, 1], BF16, scope=sc5)
        k.v.op("tensor_copy", dcarc[:], pk[5][0:NJ, 0:1], ins=[B("pk5")],
               outs=[B("dcarc")])
        CH5 = min(1024, CAP)
        cm = k.sb("cm", [128, CH5], BF16, scope=sc5)
        j1r = k.sb("j1r", [1, CH5], scope=sc5)
        cofr = k.sb("cofr", [1, CH5], scope=sc5)
        flr = k.sb("flr", [1, CH5], scope=sc5)
        for ph in range(CAP // CH5):
            k.v.op("tensor_scalar", cm[0:NJ, :],
                   io_s[0:NJ, CH5 * ph:CH5 * (ph + 1)], carK[:, 0:1], None,
                   ALU.is_ge, ins=[B("io_s"), B("carK")], outs=[B("cm")])
            for j in range(cdiv(CH5, 512)):
                w = min(512, CH5 - 512 * j)
                c0 = 512 * j
                p0, p1 = pk[j % 2], pk[2 + j % 2]
                k.t.op("matmul", p0[0:1, 0:w], ones_c[0:NJ, :],
                       cm[0:NJ, c0:c0 + w], start=True, stop=True,
                       ins=[B("ones_c"), B("cm")], outs=[B(f"pk{j % 2}")])
                k.v.op("tensor_copy", j1r[0:1, 512 * j:512 * j + w], p0[0:1, 0:w],
                       ins=[B(f"pk{j % 2}")], outs=[B("j1r")])
                k.t.op("matmul", p1[0:1, 0:w], dcarc[:], cm[0:NJ, c0:c0 + w],
                       start=True, stop=True, ins=[B("dcarc"), B("cm")],
                       outs=[B(f"pk{2 + j % 2}")])
                k.v.op("tensor_copy", cofr[0:1, 512 * j:512 * j + w], p1[0:1, 0:w],
                       ins=[B(f"pk{2 + j % 2}")], outs=[B("cofr")])
            k.v.op("tensor_scalar", flr[:], j1r[:], 128.0, -128.0, ALU.mult,
                   ALU.add, ins=[B("j1r")], outs=[B("flr")])
            k.v.op("tensor_tensor", flr[:], flr[:],
                   io_s[0:1, CH5 * ph:CH5 * (ph + 1)], ALU.add,
                   ins=[B("flr"), B("io_s")], outs=[B("flr")])
            k.v.op("tensor_tensor", flr[:], flr[:], cofr[:], ALU.subtract,
                   ins=[B("flr"), B("cofr")], outs=[B("flr")])
            k.v.op("tensor_scalar", flr[:], flr[:], float(T - 1), None, ALU.min,
                   ins=[B("flr")], outs=[B("flr")])
            k.v.op("tensor_scalar", j1r[:], j1r[:], -1.0, None, ALU.add,
                   ins=[B("j1r")], outs=[B("j1r")])
            k.qg.dma(fl_d.ap()[0:1, CH5 * ph:CH5 * (ph + 1)], flr[:],
                     ins=[B("flr")], outs=[B("fl_d")])
            k.qg.dma(jr_d.ap()[0:1, CH5 * ph:CH5 * (ph + 1)], j1r[:],
                     ins=[B("j1r")], outs=[B("jr_d")])
        k.qg.wait_all(k.y)
        flv = k.sb("flv", [NS, 128], scope=sc5)
        jrv = k.sb("jrv", [NS, 128], scope=sc5)
        k.qy.dma(flv[:], fl_d.ap().rearrange("o (a p) -> (o a) p", p=128),
                 ins=[B("fl_d")], outs=[B("flv")])
        k.qy.dma(jrv[:], jr_d.ap().rearrange("o (a p) -> (o a) p", p=128),
                 ins=[B("jr_d")], outs=[B("jrv")])
        k.t.op("transpose", pk[0][:, 0:NS], flv[:], id_f[0:NS, 0:NS],
               ins=[B("flv"), B("id_f")], outs=[B("pk0")])
        flT = k.sb("flT", [128, NS], scope=sc5)
        k.v.op("tensor_copy", flT[:], pk[0][:, 0:NS], ins=[B("pk0")], outs=[B("flT")])
        k.t.op("transpose", pk[1][:, 0:NS], jrv[:], id_f[0:NS, 0:NS],
               ins=[B("jrv"), B("id_f")], outs=[B("pk1")])
        jrT = k.sb("jrT", [128, NS], scope=sc5)
        k.v.op("tensor_copy", jrT[:], pk[1][:, 0:NS], ins=[B("pk1")], outs=[B("jrT")])
        flTi = k.sb("flTi", [128, NS], I32, scope=sc5)
        k.v.op("tensor_copy", flTi[:], flT[:], ins=[B("flT")], outs=[B("flTi")])
        ctg = k.sb("ctg", [128, NS], scope=sc5)
        for s in range(NS):
            k.qg.indirect(ctg[:, s:s + 1], None, ct_d.ap(),
                          IndirectOffsetOnAxis(flTi[:, s:s + 1], 0),
                          ins=[B("flTi"), B("ct_d")], outs=[B("ctg")])
        xf_t = k.sb("xf_t", [128, NS], scope=sc5)
        k.v.op("tensor_scalar", xf_t[:], jrT[:], 128.0, None, ALU.mult,
               ins=[B("jrT")], outs=[B("xf_t")])
        k.v.op("tensor_tensor", xf_t[:], xf_t[:], ctg[:], ALU.add,
               ins=[B("xf_t"), B("ctg")], outs=[B("xf_t")])
        k.v.op("tensor_scalar", xf_t[:], xf_t[:], -1.0, 0.0, ALU.add, ALU.max,
               ins=[B("xf_t")], outs=[B("xf_t")])
        k.v.op("tensor_copy", xidx_s[:], xf_t[:], ins=[B("xf_t")], outs=[B("xidx_s")])
    k.fence()



    NG = min(8, NS, NT)
    with ExitStack() as sc6:
        sc6x = ExitStack()
        SS6 = SS
        p6w = [("xin", [128, H], F32), ("xinB", [128, H], F32),
               ("he", [128, H], F32), ("tgp", [128, H], F32)]
        for S in SS6:
            for nm, shape, dt in p6w:
                S.h[nm] = k.sb(f"{nm}_u{S.si}", shape, dt, scope=sc6x)
            k.v.op("memset", S["xin"][:], 0.0, outs=[S.b("xin")])
            k.v.op("memset", S["xinB"][:], 0.0, outs=[S.b("xinB")])

        def p6_gather(s, S, i):
            nm = "xin" if i % 2 == 0 else "xinB"
            k.qg.indirect(S[nm][:], None, x_full.ap(),
                          IndirectOffsetOnAxis(xidx_s[:, s:s + 1], 0),
                          ins=[B("xidx_s"), B("x_full")], outs=[S.b(nm)])

        def p6_chain(s, S, i):
            xin = S["xin"] if i % 2 == 0 else S["xinB"]
            bxin = S.b("xin") if i % 2 == 0 else S.b("xinB")
            he, tgp = S["he"], S["tgp"]
            hv = tgp
            bhe, btgp = S.b("he"), S.b("tgp")
            bhv = btgp
            st = [S[f"st{i}"] for i in range(8)]
            sb_ = [S.b(f"st{i}") for i in range(8)]
            scr_, bscr = S["scr"], S.b("scr")
            yield
            yield from adaln_tile(S, xin[:], bxin, he[:], bhe, hv, bhv)
            k.s.op("activation", scr_[:], he[:], AF.Square, bias=0.0, scale=1.0,
                   accum_out=st[0][:], ins=[bhe], outs=[bscr, sb_[0]])
            yield
            yield from rsqrt_col(S, 1, 0, 1.0 / H, 1e-8)
            k.v.op("tensor_tensor", tgp[:], he[:], gpnb[:], ALU.mult,
                   ins=[bhe, B("gpnb")], outs=[btgp])
            yield
            k.v.op("tensor_reduce", st[2][:], tgp[:], AX.X, ALU.max,
                   apply_absolute_value=True, ins=[btgp], outs=[sb_[2]])
            yield
            k.v.op("tensor_scalar", st[3][:], st[2][:], 1e-30, None, ALU.max,
                   ins=[sb_[2]], outs=[sb_[3]])
            k.v.op("reciprocal", st[3][:], st[3][:], ins=[sb_[3]], outs=[sb_[3]])
            k.v.op("tensor_scalar", st[3][:], st[3][:], 127.0, None, ALU.mult,
                   ins=[sb_[3]], outs=[sb_[3]])
            yield
            k.s.op("activation", scr_[:], tgp[:], AF.Identity, bias=RC,
                   scale=st[3][:, 0:1], ins=[btgp, sb_[3], B("rc_t")],
                   outs=[bscr])
            yield
            k.qg.dma(ng6_d.ap()[128 * s:128 * (s + 1), :], scr_[:],
                     ins=[bscr], outs=[B("ng6_d")])
            k.v.op("tensor_tensor", st[4][:], st[1][:], st[2][:], ALU.mult,
                   ins=[sb_[1], sb_[2]], outs=[sb_[4]])
            k.v.op("tensor_scalar", st[4][:], st[4][:], 1e-5, None, ALU.max,
                   ins=[sb_[4]], outs=[sb_[4]])
            k.v.op("tensor_tensor", agp_s[:, s:s + 1], st[4][:], mgpc_b[:],
                   ALU.mult, ins=[sb_[4], B("mgpc_b")], outs=[B("agp_s")])
            yield

        def p6_stream(si):
            S = SS6[si]
            tiles = list(range(si, NS, 3))
            p6_gather(tiles[0], S, 0)
            for i, s in enumerate(tiles):
                if i + 1 < len(tiles):
                    p6_gather(tiles[i + 1], S, i + 1)
                yield from p6_chain(s, S, i)

        interleave([p6_stream(si) for si in range(3)])
        sc6x.close()
        k.fence()
        k.qg.wait_all(k.y)
        dpwt = k.sb("dpwt", [128, 16 * 1024], BF16, scope=sc6a)
        for kk in range(16):
            k.qy.dma_t(dpwt[:, 1024 * kk:1024 * (kk + 1)],
                       dpq_d.ap()[:, 128 * kk:128 * (kk + 1)],
                       ins=[B("dpq_d")], outs=[B("dpwt")])
        prod = k.sb("prod", [128, IM], scope=sc6a)
        uu2 = [k.sb(f"uu{i}", [128, IM], scope=sc6a) for i in range(2)]
        n2tl = k.sb("n2tl", [128, 16 * 128], BF16, scope=sc6a)
        ctrb = k.sb("ctrb", [128, H], scope=sc6a)
        ngs2 = [k.sb(f"ngs{i}", [128, H], F32, scope=sc6a) for i in range(2)]
        nmTs = k.sb("nmTs", [128, 8 * 128], BF16, scope=sc6a)

        def load_ngs(s):
            k.qy.dma(ngs2[s % 2][:], ng6_d.ap()[128 * s:128 * (s + 1), :],
                     ins=[B("ng6_d")], outs=[B(f"ngs{s % 2}")])

        def gp_part(s, si):
            bngs = B(f"ngs{s % 2}")
            uu, buu = uu2[s % 2], B(f"uu{s % 2}")
            for kk in range(8):
                tb = kk // 4
                k.t.op("transpose", pk[tb][:, 128 * (kk % 4):128 * (kk % 4) + 128],
                       ngs2[s % 2][:, 128 * kk:128 * (kk + 1)], id_f[:],
                       ins=[bngs, B("id_f")], outs=[B(f"pk{tb}")],
                       defer=(kk % 4 != 3))
                if kk % 4 == 3:
                    k.s.op("activation", nmTs[:, 512 * tb:512 * (tb + 1)],
                           pk[tb][:, 0:512], AF.Copy, bias=-RC, scale=1.0,
                           ins=[B(f"pk{tb}"), B("mrc_t")], outs=[B("nmTs")])
            for half in range(2):
                pb = 4 * half
                for kk in range(8):
                    for nch in range(4):
                        k.t.op("matmul", pk[pb + nch][:, 0:512],
                               nmTs[:, 128 * kk:128 * (kk + 1)],
                               gpwt[:, 4096 * kk + 2048 * half + 512 * nch:
                                    4096 * kk + 2048 * half + 512 * (nch + 1)],
                               start=(kk == 0), stop=(kk == 7),
                               ins=[B("nmTs"), B("gpwt")], outs=[B(f"pk{pb + nch}")],
                               defer=not (kk == 7 and nch == 3))
                if half == 0:
                    for nch in range(4):
                        k.s.op("activation", prod[:, 512 * nch:512 * (nch + 1)],
                               pk[pb + nch][:, 0:512], AF.Silu,
                               bias=0.0, scale=agp_s[:, s:s + 1],
                               ins=[B(f"pk{pb + nch}"), B("agp_s")], outs=[B("prod")])
                else:
                    for nch in range(4):
                        k.v.op("scalar_tensor_tensor", prod[:, 512 * nch:512 * (nch + 1)],
                               pk[pb + nch][:, 0:512], agp_s[:, s:s + 1],
                               prod[:, 512 * nch:512 * (nch + 1)], ALU.mult, ALU.mult,
                               ins=[B(f"pk{pb + nch}"), B("agp_s"), B("prod")],
                               outs=[B("prod")])
            k.s.op("activation", uu[:], prod[:], AF.Square, bias=0.0, scale=1.0,
                   accum_out=stt[0][:], ins=[B("prod")],
                   outs=[buu, B("stt0")])
            drive(rsqrt_col(LS, 1, 0, 1.0 / IM, 1e-8))
            k.v.op("tensor_tensor", uu[:], prod[:], dpnb[:], ALU.mult,
                   ins=[B("prod"), B("dpnb")], outs=[buu])
            k.v.op("tensor_reduce", stt[2][:], uu[:], AX.X, ALU.max,
                   apply_absolute_value=True, ins=[buu], outs=[B("stt2")])
            k.v.op("tensor_scalar", stt[3][:], stt[2][:], 1e-30, None, ALU.max,
                   ins=[B("stt2")], outs=[B("stt3")])
            k.v.op("reciprocal", stt[3][:], stt[3][:], ins=[B("stt3")],
                   outs=[B("stt3")])
            k.v.op("tensor_scalar", stt[3][:], stt[3][:], 127.0, None, ALU.mult,
                   ins=[B("stt3")], outs=[B("stt3")])
            k.s.op("activation", uu[:], uu[:], AF.Identity, bias=RC,
                   scale=stt[3][:, 0:1], ins=[buu, B("stt3"), B("rc_t")],
                   outs=[buu])
            k.v.op("tensor_tensor", stt[4][:], stt[1][:], stt[2][:], ALU.mult,
                   ins=[B("stt1"), B("stt2")], outs=[B("stt4")])
            k.v.op("tensor_scalar", stt[4][:], stt[4][:], 1e-5, None, ALU.max,
                   ins=[B("stt4")], outs=[B("stt4")])
            k.v.op("tensor_tensor", adp_s[:, s:s + 1], stt[4][:], mdpc_b[:],
                   ALU.mult, ins=[B("stt4"), B("mdpc_b")], outs=[B("adp_s")])

        def dp_part(s):
            uu, buu = uu2[s % 2], B(f"uu{s % 2}")
            for kk in range(16):
                tb = 2 + kk // 4
                k.t.op("transpose", pk[tb][:, 128 * (kk % 4):128 * (kk % 4) + 128],
                       uu[:, 128 * kk:128 * (kk + 1)], id_f[:],
                       ins=[buu, B("id_f")], outs=[B(f"pk{tb}")],
                       defer=(kk % 4 != 3))
                if kk % 4 == 3:
                    k.s.op("activation", n2tl[:, 512 * (tb - 2):512 * (tb - 1)],
                           pk[tb][:, 0:512], AF.Copy, bias=-RC, scale=1.0,
                           ins=[B(f"pk{tb}"), B("mrc_t")], outs=[B("n2tl")])
            for kk in range(16):
                for nch in range(2):
                    k.t.op("matmul", pk[nch][:, 0:512],
                           n2tl[:, 128 * kk:128 * (kk + 1)],
                           dpwt[:, 1024 * kk + 512 * nch:1024 * kk + 512 * (nch + 1)],
                           start=(kk == 0), stop=(kk == 15),
                           ins=[B("n2tl"), B("dpwt")], outs=[B(f"pk{nch}")],
                           defer=not (kk == 15 and nch == 1))
            for nch in range(2):
                k.s.op("activation", ctrb[:, 512 * nch:512 * (nch + 1)],
                       pk[nch][:, 0:512], AF.Copy, bias=0.0,
                       scale=adp_s[:, s:s + 1],
                       ins=[B(f"pk{nch}"), B("adp_s")], outs=[B("ctrb")])
            k.qg.dma(contribP.ap()[128 * s:128 * (s + 1), :], ctrb[:],
                     ins=[B("ctrb")], outs=[B("contribP")])
            if (s + 1) % (NS // NCH_AG) == 0:
                c = (s + 1) // (NS // NCH_AG) - 1
                k.qg.wait_all(k.g)
                k.qc.collective(
                    "AllGather", ALU.bypass, RG,
                    [contribP.ap()[CHS * c:CHS * (c + 1), :]],
                    [contribF.ap()[E * CHS * c:E * CHS * (c + 1), :]],
                    ins=[B("contribP")], outs=[B("contribF")])

        load_ngs(0)
        for s in range(NS):
            if s + 1 < NS:
                load_ngs(s + 1)
            gp_part(s, s % NG)
            if s >= 1:
                dp_part(s - 1)
        dp_part(NS - 1)
        sc6a.close()
    k.fence(skip=(k.qc,))

    sc9 = ExitStack()
    wgt = k.sb("wgt", [128, 8 * IF], BF16, scope=sc9)
    wut = k.sb("wut", [128, 8 * IF], BF16, scope=sc9)

    def load_wgut():
        for srcT, srcb, dstt, dstb in ((fqgT_d, "fqgT_d", wgt, "wgt"),
                                       (fquT_d, "fquT_d", wut, "wut")):
            src4 = srcT.ap().rearrange("(c b p) r -> p b c r", c=NCORE, p=128)
            for kk in range(8):
                k.qy.dma(dstt[:, IF * kk:IF * (kk + 1)]
                         .rearrange("p (c r) -> p () c r", r=IFS),
                         src4[:, kk:kk + 1], ins=[B(srcb)], outs=[B(dstb)])

    with ExitStack() as sc7:
        SS7 = SS[:2]
        p7w = [("c1t", [128, H], F32), ("c2t", [128, H], F32),
               ("c1tB", [128, H], F32), ("c2tB", [128, H], F32),
               ("xt7", [128, H], F32)]
        for S in SS7:
            for nm, shape, dt in p7w:
                S.h[nm] = k.sb(f"{nm}_u{S.si}", shape, dt, scope=sc7)
        macc = k.sb("macc", [128, NT], scope=sc7)
        for S in SS7:
            for nm in ("c1t", "c2t", "c1tB", "c2tB"):
                k.v.op("memset", S[nm][:], 0.0, outs=[S.b(nm)])
        k.qc.wait_all(k.g)

        def p7_gather(t, S, i):
            sfx = "" if i % 2 == 0 else "B"
            k.qg.indirect(S["c1t" + sfx][:], None, contribF.ap(),
                          IndirectOffsetOnAxis(cid1_s[:, t:t + 1], 0),
                          ins=[B("cid1_s"), B("contribF")],
                          outs=[S.b("c1t" + sfx)])
            k.qg.indirect(S["c2t" + sfx][:], None, contribF.ap(),
                          IndirectOffsetOnAxis(cid2_s[:, t:t + 1], 0),
                          ins=[B("cid2_s"), B("contribF")],
                          outs=[S.b("c2t" + sfx)])

        def p7_chain(t, S, i):
            sfx = "" if i % 2 == 0 else "B"
            c1t, c2t, xt7 = (S["c1t" + sfx], S["c2t" + sfx], S["xt7"])
            h2t = c2t
            b1, b2, bx = (S.b("c1t" + sfx), S.b("c2t" + sfx), S.b("xt7"))
            bh2 = b2
            st = [S[f"st{i}"] for i in range(4)]
            sb_ = [S.b(f"st{i}") for i in range(4)]
            scr_, bscr = S["scr"], S.b("scr")
            k.qy.dma(xt7[:], xown.ap()[128 * t:128 * (t + 1), :],
                     ins=[B("xown")], outs=[bx])
            yield
            k.s.op("activation", scr_[:], c1t[:], AF.Copy, bias=0.0,
                   scale=wk1_s[:, t:t + 1], ins=[b1, B("wk1_s")], outs=[bscr])
            yield
            k.v.op("scalar_tensor_tensor", scr_[:], c2t[:], wk2_s[:, t:t + 1],
                   scr_[:], ALU.mult, ALU.add, ins=[b2, B("wk2_s"), bscr],
                   outs=[bscr])
            yield
            k.v.op("tensor_tensor", scr_[:], scr_[:], gaxb[:], ALU.mult,
                   ins=[bscr, B("gaxb")], outs=[bscr])
            k.v.op("tensor_tensor", xt7[:], scr_[:], xt7[:], ALU.add,
                   ins=[bscr, bx], outs=[bx])
            yield
            k.qg.dma(x1_d.ap()[128 * t:128 * (t + 1), :], xt7[:],
                     ins=[bx], outs=[B("x1_d")])
            k.v.op("tensor_reduce", st[0][:], xt7[:], AX.X, ALU.add,
                   ins=[bx], outs=[sb_[0]])
            yield
            k.s.op("activation", c1t[:], xt7[:], AF.Square, bias=0.0, scale=1.0,
                   accum_out=st[1][:], ins=[bx], outs=[b1, sb_[1]])
            yield
            k.v.op("tensor_scalar", st[0][:], st[0][:], 1.0 / H, None, ALU.mult,
                   ins=[sb_[0]], outs=[sb_[0]])
            k.v.op("tensor_tensor", st[2][:], st[0][:], st[0][:], ALU.mult,
                   ins=[sb_[0]], outs=[sb_[2]])
            k.v.op("tensor_scalar", st[3][:], st[1][:], 1.0 / H, None, ALU.mult,
                   ins=[sb_[1]], outs=[sb_[3]])
            k.v.op("tensor_tensor", st[3][:], st[3][:], st[2][:], ALU.subtract,
                   ins=[sb_[3], sb_[2]], outs=[sb_[3]])
            k.v.op("tensor_scalar", st[3][:], st[3][:], 1e-5, None, ALU.add,
                   ins=[sb_[3]], outs=[sb_[3]])
            yield
            k.s.op("activation", st[3][:], st[3][:], AF.Sqrt, bias=0.0, scale=1.0,
                   ins=[sb_[3]], outs=[sb_[3]])
            yield
            k.v.op("reciprocal", st[3][:], st[3][:], ins=[sb_[3]], outs=[sb_[3]])
            k.v.op("tensor_tensor", st[2][:], st[0][:], st[3][:], ALU.mult,
                   ins=[sb_[0], sb_[3]], outs=[sb_[2]])
            k.v.op("tensor_scalar", st[2][:], st[2][:], -1.0, None, ALU.mult,
                   ins=[sb_[2]], outs=[sb_[2]])
            yield
            k.s.op("activation", h2t[:], xt7[:], AF.Identity, bias=st[2][:, 0:1],
                   scale=st[3][:, 0:1], ins=[bx, sb_[2], sb_[3]], outs=[bh2])
            yield
            k.v.op("tensor_tensor", h2t[:], h2t[:], cb1b[:], ALU.mult,
                   ins=[bh2, B("cb1b")], outs=[bh2])
            k.v.op("tensor_tensor", h2t[:], h2t[:], sbb[:], ALU.add,
                   ins=[bh2, B("sbb")], outs=[bh2])
            yield
            k.qg.dma(h2_d.ap()[128 * t:128 * (t + 1), :], h2t[:],
                     ins=[bh2], outs=[B("h2_d")])
            k.v.op("tensor_reduce", macc[:, t:t + 1], h2t[:], AX.X, ALU.max,
                   apply_absolute_value=True, ins=[bh2], outs=[B("macc")])
            yield

        def p7_stream(si):
            S = SS7[si]
            tiles = list(range(si, NT, 2))
            p7_gather(tiles[0], S, 0)
            for i, t in enumerate(tiles):
                if i + 1 < len(tiles):
                    p7_gather(tiles[i + 1], S, i + 1)
                yield from p7_chain(t, S, i)

        interleave([p7_stream(si) for si in range(2)])
        load_wgut()
        k.v.op("tensor_reduce", stt[0][:], macc[:], AX.X, ALU.max,
               ins=[B("macc")], outs=[B("stt0")])
        k.t.op("transpose", pk[0][0:1, 0:128], stt[0][:], id_f[:],
               ins=[B("stt0"), B("id_f")], outs=[B("pk0")])
        mrow1 = k.sb("mrow1", [1, 128], scope=sc7)
        k.v.op("tensor_copy", mrow1[:], pk[0][0:1, 0:128], ins=[B("pk0")],
               outs=[B("mrow1")])
        gmx = k.sb("gmx", [1, 8], scope=sc7)
        k.v.op("memset", gmx[:], 0.0, outs=[B("gmx")])
        k.v.op("tensor_reduce", gmx[0:1, 0:1], mrow1[:], AX.X, ALU.max,
               ins=[B("mrow1")], outs=[B("gmx")])
        k.qg.dma(gm1P.ap(), gmx[:], ins=[B("gmx")], outs=[B("gm1P")])
        k.qc.collective("AllGather", ALU.bypass, RG, [gm1P.ap()], [gm1F.ap()],
                        ins=[B("gm1P")], outs=[B("gm1F")])
        gmall = k.sb("gmall", [1, 64], scope=sc7)
        k.qy.dma(gmall[:], gm1F.ap().rearrange("a b -> () (a b)"),
                 ins=[B("gm1F")], outs=[B("gmall")])
        k.v.op("tensor_reduce", gmx[0:1, 0:1], gmall[:], AX.X, ALU.max,
               ins=[B("gmall")], outs=[B("gmx")])
        clx = k.sb("clx", [1, 1], scope=sc7)
        k.qy.dma(clx[:], clipx_in.ap(), ins=[B("clipx_in")], outs=[B("clx")])
        k.s.op("activation", clx[:], clx[:], AF.Sigmoid, bias=0.0, scale=1.0,
               ins=[B("clx")], outs=[B("clx")])
        k.v.op("tensor_tensor", clx[:], clx[:], gmx[0:1, 0:1], ALU.mult,
               ins=[B("clx"), B("gmx")], outs=[B("clx")])
        k.v.op("tensor_scalar", clx[:], clx[:], 1.0 / 127.0, None, ALU.mult,
               ins=[B("clx")], outs=[B("clx")])
        sxv_ = bcast_row("sxv_", clx[:], B("clx"), 1, scope=sc7)
        k.v.op("tensor_copy", sxv[:], sxv_[:], ins=[B("sxv_")], outs=[B("sxv")])
        k.v.op("reciprocal", sxb[:], sxv[:], ins=[B("sxv")], outs=[B("sxb")])
        for S in SS7:
            S.h["n1t_"] = k.sb(f"n1t__u{S.si}", [128, H], BF16, scope=sc7)
        k.qg.wait_all(k.y)

        def n1_chain(t, S):
            h2t, n1t_, scr_ = S["c2t"], S["n1t_"], S["scr"]
            bh2, bn1, bscr = S.b("c2t"), S.b("n1t_"), S.b("scr")
            k.qy.dma(h2t[:], h2_d.ap()[128 * t:128 * (t + 1), :],
                     ins=[B("h2_d")], outs=[bh2])
            yield
            k.s.op("activation", scr_[:], h2t[:], AF.Identity, bias=RC,
                   scale=sxb[:, 0:1], ins=[bh2, B("sxb"), B("rc_t")],
                   outs=[bscr])
            yield
            k.v.op("tensor_scalar", scr_[:], scr_[:], -RC, None, ALU.add,
                   ins=[bscr], outs=[bscr])
            k.v.op("tensor_scalar", n1t_[:], scr_[:], -128.0, 127.0, ALU.max,
                   ALU.min, ins=[bscr], outs=[bn1])
            yield
            k.qg.dma(n1_d.ap()[128 * t:128 * (t + 1), :], n1t_[:],
                     ins=[bn1], outs=[B("n1_d")])
            yield

        interleave([(lambda si: (x for t in range(si, NT, 2)
                                 for x in n1_chain(t, SS7[si])))(si)
                    for si in range(2)])
    k.fence()

    ich = [(512 * i, min(512, IF - 512 * i)) for i in range(cdiv(IF, 512))]
    NI = len(ich)
    NK2 = cdiv(IF, 128)
    if True:
        n1tt2 = [k.sb(f"n1tt{i}", [128, 8 * 128 * NG], BF16, scope=sc9)
                 for i in range(2)]
        sgf = k.sb("sgf", [128, 512], scope=sc9)
        hyf = k.sb("hyf", [128, 512], scope=sc9)
        prodf = k.sb("prodf", [128, 512], BF16, scope=sc9)
        maccf = k.sb("maccf", [128, NT * NI], scope=sc9)

        def load_n1tt(g):
            for kk in range(8):
                k.qy.dma_t(n1tt2[g % 2][:, 128 * NG * kk:128 * NG * (kk + 1)],
                           n1_d.ap()[128 * NG * g:128 * NG * (g + 1),
                                     128 * kk:128 * (kk + 1)],
                           ins=[B("n1_d")], outs=[B(f"n1tt{g % 2}")])
        load_n1tt(0)
        for g in range(cdiv(NT, NG)):
            if g + 1 < cdiv(NT, NG):
                load_n1tt(g + 1)
            n1tt = n1tt2[g % 2]
            bn1tt = B(f"n1tt{g % 2}")
            for si in range(NG):
                t = NG * g + si
                for c0 in range(0, NI, 2):
                    nch_here = min(2, NI - c0)
                    pb = 4 * ((c0 // 2 + t) % 2)
                    for kk in range(8):
                        for ci in range(nch_here):
                            off, wdt = ich[c0 + ci]
                            k.t.op("matmul", pk[pb + 2 * ci][:, 0:wdt],
                                   n1tt[:, 128 * NG * kk + 128 * si:
                                        128 * NG * kk + 128 * (si + 1)],
                                   wgt[:, IF * kk + off:IF * kk + off + wdt],
                                   start=(kk == 0), stop=(kk == 7),
                                   ins=[bn1tt, B("wgt")],
                                   outs=[B(f"pk{pb + 2 * ci}")], defer=True)
                            k.t.op("matmul", pk[pb + 2 * ci + 1][:, 0:wdt],
                                   n1tt[:, 128 * NG * kk + 128 * si:
                                        128 * NG * kk + 128 * (si + 1)],
                                   wut[:, IF * kk + off:IF * kk + off + wdt],
                                   start=(kk == 0), stop=(kk == 7),
                                   ins=[bn1tt, B("wut")],
                                   outs=[B(f"pk{pb + 2 * ci + 1}")],
                                   defer=not (kk == 7 and ci == nch_here - 1))
                    for ci in range(nch_here):
                        off, wdt = ich[c0 + ci]
                        k.s.op("activation", sgf[:, 0:wdt], pk[pb + 2 * ci][:, 0:wdt],
                               AF.Silu, bias=0.0, scale=sxv[:, 0:1],
                               ins=[B(f"pk{pb + 2 * ci}"), B("sxv")], outs=[B("sgf")])
                        k.s.op("activation", hyf[:, 0:wdt], pk[pb + 2 * ci + 1][:, 0:wdt],
                               AF.Copy, bias=0.0, scale=sxv[:, 0:1],
                               ins=[B(f"pk{pb + 2 * ci + 1}"), B("sxv")], outs=[B("hyf")])
                        k.v.op("tensor_tensor", prodf[:, 0:wdt], sgf[:, 0:wdt],
                               hyf[:, 0:wdt], ALU.mult,
                               ins=[B("sgf"), B("hyf")], outs=[B("prodf")])
                        k.v.op("tensor_reduce", maccf[:, t * NI + c0 + ci:
                                                      t * NI + c0 + ci + 1],
                               prodf[:, 0:wdt], AX.X, ALU.max,
                               apply_absolute_value=True,
                               ins=[B("prodf")], outs=[B("maccf")])
                        k.qg.dma(prod_d.ap()[128 * t:128 * (t + 1),
                                             off:off + wdt], prodf[:, 0:wdt],
                                 ins=[B("prodf")], outs=[B("prod_d")])
        k.v.op("tensor_reduce", stt[0][:], maccf[:], AX.X, ALU.max,
               ins=[B("maccf")], outs=[B("stt0")])
        k.t.op("transpose", pk[0][0:1, 0:128], stt[0][:], id_f[:],
               ins=[B("stt0"), B("id_f")], outs=[B("pk0")])
        mrow2 = k.sb("mrow2", [1, 128], scope=sc9)
        k.v.op("tensor_copy", mrow2[:], pk[0][0:1, 0:128], ins=[B("pk0")],
               outs=[B("mrow2")])
        gmx2 = k.sb("gmx2", [1, 8], scope=sc9)
        k.v.op("memset", gmx2[:], 0.0, outs=[B("gmx2")])
        k.v.op("tensor_reduce", gmx2[0:1, 0:1], mrow2[:], AX.X, ALU.max,
               ins=[B("mrow2")], outs=[B("gmx2")])
        k.qg.dma(gm2P.ap(), gmx2[:], ins=[B("gmx2")], outs=[B("gm2P")])
        k.qc.collective("AllGather", ALU.bypass, RG, [gm2P.ap()], [gm2F.ap()],
                        ins=[B("gm2P")], outs=[B("gm2F")])
        gmall2 = k.sb("gmall2", [1, 64], scope=sc9)
        k.qy.dma(gmall2[:], gm2F.ap().rearrange("a b -> () (a b)"),
                 ins=[B("gm2F")], outs=[B("gmall2")])
        k.v.op("tensor_reduce", gmx2[0:1, 0:1], gmall2[:], AX.X, ALU.max,
               ins=[B("gmall2")], outs=[B("gmx2")])
        cld = k.sb("cld", [1, 1], scope=sc9)
        k.qy.dma(cld[:], clipd_in.ap(), ins=[B("clipd_in")], outs=[B("cld")])
        k.s.op("activation", cld[:], cld[:], AF.Sigmoid, bias=0.0, scale=1.0,
               ins=[B("cld")], outs=[B("cld")])
        k.v.op("tensor_tensor", cld[:], cld[:], gmx2[0:1, 0:1], ALU.mult,
               ins=[B("cld"), B("gmx2")], outs=[B("cld")])
        k.v.op("tensor_scalar", cld[:], cld[:], 1.0 / 127.0, None, ALU.mult,
               ins=[B("cld")], outs=[B("cld")])
        s2v_ = bcast_row("s2v_", cld[:], B("cld"), 1, scope=sc9)
        k.v.op("tensor_copy", s2v[:], s2v_[:], ins=[B("s2v_")], outs=[B("s2v")])
        k.v.op("reciprocal", s2b[:], s2v[:], ins=[B("s2v")], outs=[B("s2b")])
        sc9.close()
        k.fence()
        scd = ExitStack()
        wdt_t = k.sb("wdt_t", [128, NK2 * H], BF16, scope=scd)
        wdsg = k.sb("wdsg", [128, IF], BF16, scope=scd)
        wdsf = k.sb("wdsf", [128, IF], F32, scope=scd)
        for r in range(H // 128):
            k.qy.dma(wdsg[:], fqd_d.ap()[128 * r:128 * (r + 1), :],
                     ins=[B("fqd_d")], outs=[B("wdsg")])
            k.v.op("tensor_copy", wdsf[:], wdsg[:], ins=[B("wdsg")],
                   outs=[B("wdsf")])
            wdt3 = wdt_t.ap().rearrange("p (a c) -> p a c", c=H)
            for kk in range(NK2):
                j, jj = kk // 4, kk % 4
                gw = min(4, NK2 - 4 * j)
                tb = j % 4
                k.t.op("transpose", pk[tb][:, 128 * jj:128 * jj + 128],
                       wdsf[:, 128 * kk:128 * (kk + 1)], id_f[:],
                       ins=[B("wdsf"), B("id_f")], outs=[B(f"pk{tb}")],
                       defer=(jj != gw - 1))
                if jj == gw - 1:
                    k.s.op("activation",
                           wdt3[:, 4 * j:4 * j + gw, 128 * r:128 * (r + 1)],
                           pk[tb][:, 0:128 * gw]
                           .rearrange("p (a c) -> p a c", c=128),
                           AF.Copy, bias=0.0, scale=1.0,
                           ins=[B(f"pk{tb}")], outs=[B("wdt_t")])
        scq = ExitStack()
        for S in SS[:2]:
            S.h["prt"] = k.sb(f"prt_u{S.si}", [128, IF], BF16, scope=scq)
            S.h["prtf"] = k.sb(f"prtf_u{S.si}", [128, IF], F32, scope=scq)
            S.h["n2ft"] = k.sb(f"n2ft_u{S.si}", [128, IF], BF16, scope=scq)
        k.qg.wait_all(k.y)

        def n2f_chain(t, S):
            prt, prtf, n2ft = S["prt"], S["prtf"], S["n2ft"]
            bp, bpf, bn2 = S.b("prt"), S.b("prtf"), S.b("n2ft")
            k.qy.dma(prt[:], prod_d.ap()[128 * t:128 * (t + 1), :],
                     ins=[B("prod_d")], outs=[bp])
            yield
            k.s.op("activation", prtf[:], prt[:], AF.Identity, bias=RC,
                   scale=s2b[:, 0:1], ins=[bp, B("s2b"), B("rc_t")], outs=[bpf])
            yield
            k.v.op("tensor_scalar", prtf[:], prtf[:], -RC, None, ALU.add,
                   ins=[bpf], outs=[bpf])
            k.v.op("tensor_scalar", n2ft[:], prtf[:], -128.0, 127.0, ALU.max,
                   ALU.min, ins=[bpf], outs=[bn2])
            yield
            k.qg.dma(n2f_d.ap()[128 * t:128 * (t + 1), :], n2ft[:],
                     ins=[bn2], outs=[B("n2f_d")])
            yield

        interleave([(lambda si: (x for t in range(si, NT, 2)
                                 for x in n2f_chain(t, SS[si])))(si)
                    for si in range(2)])
    scq.close()
    k.fence()

    NGd = min(4, NG)
    if True:
        n2tt2 = [k.sb(f"n2tt{i}", [128, NK2 * 128 * NGd], BF16, scope=scd)
                 for i in range(2)]
        x1r2 = [k.sb(f"x1r{i}", [128, H], scope=scd) for i in range(2)]
        fot2 = [k.sb(f"fot{i}", [128, H], scope=scd) for i in range(2)]

        def load_n2tt(g):
            for kk in range(NK2):
                k.qy.dma_t(n2tt2[g % 2][:, 128 * NGd * kk:128 * NGd * (kk + 1)],
                           n2f_d.ap()[128 * NGd * g:128 * NGd * (g + 1),
                                      128 * kk:128 * (kk + 1)],
                           ins=[B("n2f_d")], outs=[B(f"n2tt{g % 2}")])
        load_n2tt(0)
        for g in range(NT // NGd):
            if g + 1 < NT // NGd:
                load_n2tt(g + 1)
            n2tt = n2tt2[g % 2]
            bn2tt = B(f"n2tt{g % 2}")
            for si in range(NGd):
                t = NGd * g + si
                pb = 2 * (si % 4)
                for kk in range(NK2):
                    for nch in range(2):
                        k.t.op("matmul", pk[pb + nch][:, 0:512],
                               n2tt[:, 128 * NGd * kk + 128 * si:
                                    128 * NGd * kk + 128 * (si + 1)],
                               wdt_t[:, H * kk + 512 * nch:H * kk + 512 * (nch + 1)],
                               start=(kk == 0), stop=(kk == NK2 - 1),
                               ins=[bn2tt, B("wdt_t")], outs=[B(f"pk{pb + nch}")],
                               defer=not (kk == NK2 - 1 and nch == 1))
                x1r = x1r2[si % 2]
                fot = fot2[si % 2]
                bx1r, bfot = B(f"x1r{si % 2}"), B(f"fot{si % 2}")
                k.qy.dma(x1r[:], x1_d.ap()[128 * t:128 * (t + 1), :],
                         ins=[B("x1_d")], outs=[bx1r])
                for nch in range(2):
                    k.s.op("activation", fot[:, 512 * nch:512 * (nch + 1)],
                           pk[pb + nch][:, 0:512], AF.Copy, bias=0.0,
                           scale=s2v[:, 0:1],
                           ins=[B(f"pk{pb + nch}"), B("s2v")], outs=[bfot])
                k.v.op("tensor_tensor", fot[:], fot[:], gbxb[:], ALU.mult,
                       ins=[bfot, B("gbxb")], outs=[bfot])
                k.v.op("tensor_tensor", fot[:], fot[:], x1r[:], ALU.add,
                       ins=[bfot, bx1r], outs=[bfot])
                k.qo.dma(out.ap()[128 * t:128 * (t + 1), :], fot[:],
                         ins=[bfot], outs=[B("out")])
        k.qo.wait_all(k.g)
        k.qo.wait_all(k.y)
        scd.close()
    return k


def _shard_inputs(inputs, T):
    x = np.ascontiguousarray(np.asarray(inputs["L0_attn_x"],
                                        dtype=np.float32).reshape(T, H))
    wmod = np.asarray(inputs["L0_wmod"], dtype=np.float32)
    W6 = 6 * H // NCORE
    gpw = np.asarray(inputs["L0_attn_gp_w"], dtype=np.float32)
    dpw = np.asarray(inputs["L0_attn_dp_w"], dtype=np.float32)
    wg = np.asarray(inputs["L0_ffn_w_gate"], dtype=np.float32)
    wu = np.asarray(inputs["L0_ffn_w_up"], dtype=np.float32)
    wd = np.asarray(inputs["L0_ffn_w_down"], dtype=np.float32)
    TL = T // NCORE
    IFS = IF // NCORE
    HS = H // NCORE
    in_maps = []
    for c in range(NCORE):
        in_maps.append({
            "x_full": x,
            "xown": np.ascontiguousarray(x[TL * c:TL * (c + 1)]),
            "c_vec": np.asarray(inputs["L0_c"], dtype=np.float32).reshape(1, H),
            "wmod_sl": np.ascontiguousarray(wmod[:, W6 * c:W6 * (c + 1)]),
            "gnw_in": np.asarray(inputs["L0_attn_gate_norm_w"],
                                 dtype=np.float32).reshape(1, H),
            "ginw_in": np.asarray(inputs["L0_attn_gate_in_norm_w"],
                                  dtype=np.float32).reshape(1, H),
            "gatew_in": np.asarray(inputs["L0_attn_gate_w"], dtype=np.float32),
            "gpn_in": np.ascontiguousarray(
                np.asarray(inputs["L0_attn_gp_norm_w"],
                           dtype=np.float32)[c].reshape(1, H)),
            "gpw_in": np.ascontiguousarray(gpw[c]),
            "dpn_in": np.ascontiguousarray(
                np.asarray(inputs["L0_attn_dp_norm_w"],
                           dtype=np.float32)[c].reshape(1, IM)),
            "dpw_in": np.ascontiguousarray(dpw[c]),
            "clipx_in": np.asarray(inputs["L0_ffn_clip_x"],
                                   dtype=np.float32).reshape(1, 1),
            "clipd_in": np.asarray(inputs["L0_ffn_clip_down"],
                                   dtype=np.float32).reshape(1, 1),
            "wg_sl": np.ascontiguousarray(wg[IFS * c:IFS * (c + 1)]),
            "wu_sl": np.ascontiguousarray(wu[IFS * c:IFS * (c + 1)]),
            "wd_sl": np.ascontiguousarray(wd[HS * c:HS * (c + 1)]),
        })
    return in_maps


def kernel(**inputs):
    T = int(np.asarray(inputs["L0_attn_x"]).shape[1])
    nc = bass.Bass("TRN2", target_bir_lowering=False)
    with ExitStack() as st:
        build(nc, st, T)
    in_maps = _shard_inputs(inputs, T)
    res = run_bass_kernel_spmd(nc, in_maps, core_ids=list(range(NCORE)))
    TL = T // NCORE
    full = np.concatenate([res.results[c]["out"] for c in range(NCORE)], axis=0)
    return full.reshape(1, T, H).astype(np.float32)


# version 47
# speedup vs baseline: 1.0707x; 1.0707x over previous
import numpy as np
from contextlib import ExitStack
import concourse.bass as bass
import concourse.mybir as mybir
from concourse.bass_utils import run_bass_kernel_spmd
from concourse.bass_core_rust import IndirectOffsetOnAxis

F32 = mybir.dt.float32
BF16 = mybir.dt.bfloat16
I32 = mybir.dt.int32
AF = mybir.ActivationFunctionType
ALU = mybir.AluOpType
AX = mybir.AxisListType
ds = bass.ds
RC = 1.5 * 2 ** 23

H = 1024
E = 8
IM = 2048
IF = 2816
NCORE = 8


class Buf:
    __slots__ = ("name", "writer", "readers")

    def __init__(self, name=""):
        self.name = name
        self.writer = None
        self.readers = {}


class Eng:
    def __init__(self, eng, sem, name):
        self.e = eng
        self.sem = sem
        self.name = name
        self.n = 0
        self.hwm = {}
        self.pending = []
        self.fence = []

    def _wait(self, sem, val):
        if self.hwm.get(sem.name, -1) >= val:
            return
        self.e.wait_ge(sem, val)
        self.hwm[sem.name] = val

    def _pre(self, ins, outs):
        for s, v in self.fence:
            self._wait(s, v)
        self.fence = []
        for b in ins:
            if b.writer is not None:
                self._wait(*b.writer)
        for b in outs:
            if b.writer is not None:
                self._wait(*b.writer)
            for s, v in b.readers.items():
                self._wait(s, v)

    @staticmethod
    def _post(ins, outs, tok):
        sem, val = tok
        for b in ins:
            if b.readers.get(sem, -1) < val:
                b.readers[sem] = val
        for b in outs:
            b.writer = tok
            b.readers = {}

    def op(self, name, *args, ins=(), outs=(), defer=False, **kw):
        self._pre(ins, outs)
        inst = getattr(self.e, name)(*args, **kw)
        if defer:
            self.pending.append((tuple(ins), tuple(outs)))
            return inst
        self.n += 1
        inst.then_inc(self.sem, 1)
        tok = (self.sem, self.n)
        for pins, pouts in self.pending:
            self._post(pins, pouts, tok)
        self.pending = []
        self._post(ins, outs, tok)
        return inst

    def token(self):
        assert not self.pending
        return (self.sem, self.n)


class Q:
    def __init__(self, eng, sems, name, step=16):
        self.eng = eng
        self.sems = sems if isinstance(sems, (list, tuple)) else [sems]
        self.name = name
        self.step = step
        self.n = 0
        self.cnt = [0] * len(self.sems)

    def _issue(self, fn, ins, outs):
        e = self.eng
        i = self.n % len(self.sems)
        if self.cnt[i] > 0:
            e._wait(self.sems[i], self.step * self.cnt[i])
        e._pre(ins, outs)
        inst = fn()
        self.n += 1
        self.cnt[i] += 1
        inst.then_inc(self.sems[i], self.step)
        tok = (self.sems[i], self.step * self.cnt[i])
        Eng._post(ins, outs, tok)
        return tok

    def dma(self, out_ap, in_ap, ins=(), outs=(), **kw):
        return self._issue(lambda: self.eng.e.dma_start(out=out_ap, in_=in_ap, **kw),
                           ins, outs)

    def dma_t(self, out_ap, in_ap, ins=(), outs=()):
        return self._issue(
            lambda: self.eng.e.dma_start(out=out_ap, in_=in_ap, transpose=True),
            ins, outs)

    def indirect(self, out_ap, out_off, in_ap, in_off, ins=(), outs=(), **kw):
        return self._issue(
            lambda: self.eng.e.indirect_dma_start(out_ap, out_off, in_ap, in_off, **kw),
            ins, outs)

    def collective(self, kind, op, rg, ins_aps, outs_aps, ins=(), outs=()):
        return self._issue(
            lambda: self.eng.e.collective_compute(kind, op, rg, ins_aps, outs_aps),
            ins, outs)

    def wait_all(self, eng):
        for s, c in zip(self.sems, self.cnt):
            if c:
                eng._wait(s, self.step * c)


class KB:
    def __init__(self, nc, st):
        self.nc = nc
        self.st = st
        self.bufs = {}
        sem = lambda n: st.enter_context(nc.semaphore(n))
        self.v = Eng(nc.vector, sem("se_v"), "v")
        self.s = Eng(nc.scalar, sem("se_s"), "s")
        self.t = Eng(nc.tensor, sem("se_t"), "t")
        self.g = Eng(nc.gpsimd, sem("se_g"), "g")
        self.y = Eng(nc.sync, sem("se_y"), "y")
        self.engs = [self.v, self.s, self.t, self.g, self.y]
        self.qy = Q(self.y, [sem(f"qy{i}") for i in range(6)], "qy")
        self.qs = Q(self.s, [sem(f"qs{i}") for i in range(6)], "qs")
        self.qg = Q(self.g, [sem(f"qg{i}") for i in range(6)], "qg")
        self.qo = Q(self.g, [sem(f"qo{i}") for i in range(2)], "qo")
        self.qc = Q(self.g, [sem(f"qc{i}") for i in range(4)], "qc", step=1)
        self.queues = [self.qy, self.qs, self.qg, self.qo, self.qc]

    def sb(self, name, shape, dtype=F32, scope=None):
        h = (scope or self.st).enter_context(self.nc.sbuf_tensor(name, shape, dtype))
        self.bufs[name] = Buf(name)
        return h

    def ps(self, name, shape, dtype=F32):
        h = self.st.enter_context(self.nc.psum_tensor(name, shape, dtype))
        self.bufs[name] = Buf(name)
        return h

    def dram(self, name, shape, dtype=F32, kind="Internal"):
        h = self.nc.dram_tensor(name, shape, dtype, kind=kind)
        self.bufs[name] = Buf(name)
        return h

    def b(self, name):
        if name not in self.bufs:
            self.bufs[name] = Buf(name)
        return self.bufs[name]

    def fence(self, skip=()):
        toks = [(e.sem, e.n) for e in self.engs if e.n > 0]
        for q in self.queues:
            if q in skip:
                continue
            toks += [(s, q.step * c) for s, c in zip(q.sems, q.cnt) if c]
        for e in self.engs:
            e.fence = list(toks)


def cdiv(a, b):
    return (a + b - 1) // b


def build(nc, st, T):
    k = KB(nc, st)
    B = k.b
    TL = T // NCORE
    CAP = min(int(IM * 1.5), T // E)
    NT = TL // 128
    NJ = T // 128
    NS = CAP // 128
    NCH_AG = min(8, NS)
    CHS = CAP // NCH_AG
    W6 = 6 * H // NCORE
    RG = [list(range(NCORE))]
    IFS = IF // NCORE
    HS = H // NCORE

    x_full = k.dram("x_full", [T, H], F32, kind="ExternalInput")
    xown = k.dram("xown", [TL, H], F32, kind="ExternalInput")
    c_vec = k.dram("c_vec", [1, H], F32, kind="ExternalInput")
    wmod_sl = k.dram("wmod_sl", [H, W6], F32, kind="ExternalInput")
    gnw_in = k.dram("gnw_in", [1, H], F32, kind="ExternalInput")
    ginw_in = k.dram("ginw_in", [1, H], F32, kind="ExternalInput")
    gatew_in = k.dram("gatew_in", [E, H], F32, kind="ExternalInput")
    gpn_in = k.dram("gpn_in", [1, H], F32, kind="ExternalInput")
    gpw_in = k.dram("gpw_in", [2 * IM, H], F32, kind="ExternalInput")
    dpn_in = k.dram("dpn_in", [1, IM], F32, kind="ExternalInput")
    dpw_in = k.dram("dpw_in", [H, IM], F32, kind="ExternalInput")
    clipx_in = k.dram("clipx_in", [1, 1], F32, kind="ExternalInput")
    clipd_in = k.dram("clipd_in", [1, 1], F32, kind="ExternalInput")
    wg_sl = k.dram("wg_sl", [IFS, H], F32, kind="ExternalInput")
    wu_sl = k.dram("wu_sl", [IFS, H], F32, kind="ExternalInput")
    wd_sl = k.dram("wd_sl", [HS, IF], F32, kind="ExternalInput")
    out = k.dram("out", [TL, H], F32, kind="ExternalOutput")

    modP = k.dram("modP", [1, W6], F32)
    modF = k.dram("modF", [NCORE, W6], F32)
    ct_d = k.dram("ct_d", [NJ * 128, 1], F32)
    fl_d = k.dram("fl_d", [1, CAP], F32)
    jr_d = k.dram("jr_d", [1, CAP], F32)
    gpq_d = k.dram("gpq_d", [2 * IM, H], BF16)
    dpq_d = k.dram("dpq_d", [H, IM], BF16)
    ng6_d = k.dram("ng6_d", [CAP, H], F32)
    contribP = k.dram("contribP", [CAP, H], F32)
    contribF = k.dram("contribF", [E * CAP, H], F32)
    fqgT_p = k.dram("fqgT_p", [H, IFS], BF16)
    fquT_p = k.dram("fquT_p", [H, IFS], BF16)
    fqd_p = k.dram("fqd_p", [HS, IF], BF16)
    fqgT_d = k.dram("fqgT_d", [NCORE * H, IFS], BF16)
    fquT_d = k.dram("fquT_d", [NCORE * H, IFS], BF16)
    fqd_d = k.dram("fqd_d", [H, IF], BF16)
    x1_d = k.dram("x1_d", [TL, H], F32)
    h2_d = k.dram("h2_d", [TL, H], F32)
    n1_d = k.dram("n1_d", [TL, H], BF16)
    prod_d = k.dram("prod_d", [TL, IF], BF16)
    n2f_d = k.dram("n2f_d", [TL, IF], BF16)
    gm1P = k.dram("gm1P", [1, 8], F32)
    gm1F = k.dram("gm1F", [NCORE, 8], F32)
    gm2P = k.dram("gm2P", [1, 8], F32)
    gm2F = k.dram("gm2F", [NCORE, 8], F32)

    pid = {}
    for e in (k.y, k.g):
        pid[e.name] = e.e.partition_id()


    rc_t = k.sb("rc_t", [128, 1])
    nc.const_aps.register(F32, RC, rc_t[:])
    k.v.op("memset", rc_t[:], RC, outs=[B("rc_t")])
    mrc_t = k.sb("mrc_t", [128, 1])
    nc.const_aps.register(F32, -RC, mrc_t[:])
    k.v.op("memset", mrc_t[:], -RC, outs=[B("mrc_t")])

    iot_pf = k.sb("iot_pf", [128, 128])
    k.g.op("iota", iot_pf[:], [[-1, 128]], base=0, channel_multiplier=1,
           allow_small_or_imprecise_dtypes=True, outs=[B("iot_pf")])
    tri_b = k.sb("tri_b", [128, 128], BF16)
    tri_f = k.sb("tri_f", [128, 128], F32)
    id_b = k.sb("id_b", [128, 128], BF16)
    id_f = k.sb("id_f", [128, 128], F32)
    k.v.op("tensor_scalar", tri_b[:], iot_pf[:], 0.0, None, ALU.is_lt,
           ins=[B("iot_pf")], outs=[B("tri_b")])
    k.v.op("tensor_scalar", tri_f[:], iot_pf[:], 0.0, None, ALU.is_lt,
           ins=[B("iot_pf")], outs=[B("tri_f")])
    k.v.op("tensor_scalar", id_b[:], iot_pf[:], 0.0, None, ALU.is_equal,
           ins=[B("iot_pf")], outs=[B("id_b")])
    k.v.op("tensor_scalar", id_f[:], iot_pf[:], 0.0, None, ALU.is_equal,
           ins=[B("iot_pf")], outs=[B("id_f")])
    ones_r = k.sb("ones_r", [1, 128], BF16)
    k.v.op("memset", ones_r[:], 1.0, outs=[B("ones_r")])
    ones_rf = k.sb("ones_rf", [1, 128], F32)
    k.v.op("memset", ones_rf[:], 1.0, outs=[B("ones_rf")])
    ones_c = k.sb("ones_c", [128, 1], BF16)
    k.v.op("memset", ones_c[:], 1.0, outs=[B("ones_c")])
    ones_cf = k.sb("ones_cf", [128, 1], F32)
    k.v.op("memset", ones_cf[:], 1.0, outs=[B("ones_cf")])
    iop1_b = k.sb("iop1_b", [128, 1], BF16)
    iop1_f = k.sb("iop1_f", [128, 1])
    k.g.op("iota", iop1_f[:], [[1, 1]], base=1, channel_multiplier=1,
           allow_small_or_imprecise_dtypes=True, outs=[B("iop1_f")])
    k.v.op("tensor_copy", iop1_b[:], iop1_f[:], ins=[B("iop1_f")], outs=[B("iop1_b")])
    io_e = k.sb("io_e", [128, E])
    k.g.op("iota", io_e[:], [[1, E]], base=0, channel_multiplier=0,
           allow_small_or_imprecise_dtypes=True, outs=[B("io_e")])
    io_r = k.sb("io_r", [128, 128])
    k.g.op("iota", io_r[:], [[1, 128]], base=0, channel_multiplier=0,
           allow_small_or_imprecise_dtypes=True, outs=[B("io_r")])
    io_s = k.sb("io_s", [128, CAP])
    k.g.op("iota", io_s[:], [[1, CAP]], base=0, channel_multiplier=0,
           allow_small_or_imprecise_dtypes=True, outs=[B("io_s")])

    pk = [k.ps(f"pk{i}", [128, 512]) for i in range(8)]

    def bc_fill(dst, name, src_ap, src_buf, width):
        for j in range(cdiv(width, 512)):
            w = min(512, width - 512 * j)
            p = pk[j % 2]
            k.t.op("matmul", p[:, 0:w], ones_rf[:], src_ap[0:1, 512 * j:512 * j + w],
                   start=True, stop=True, ins=[B("ones_rf"), src_buf],
                   outs=[B(f"pk{j % 2}")])
            k.v.op("tensor_copy", dst[:, 512 * j:512 * j + w], p[:, 0:w],
                   ins=[B(f"pk{j % 2}")], outs=[B(name)])
        return dst

    def bcast_row(name, src_ap, src_buf, width, scope=None):
        dst = k.sb(name, [128, width], scope=scope)
        return bc_fill(dst, name, src_ap, src_buf, width)

    ca1b = k.sb("ca1b", [128, H])
    sab = k.sb("sab", [128, H])
    gaxb = k.sb("gaxb", [128, H])
    cb1b = k.sb("cb1b", [128, H])
    sbb = k.sb("sbb", [128, H])
    gbxb = k.sb("gbxb", [128, H])
    ggb = k.sb("ggb", [128, H])
    gnwb = k.sb("gnwb", [128, H])
    gpnb = k.sb("gpnb", [128, H])
    dpnb = k.sb("dpnb", [128, IM])
    gamc_b = k.sb("gamc_b", [128, 1])
    mgt = k.sb("mgt", [128, E * 8], BF16)

    sc0 = ExitStack()
    cv8 = k.sb("cv8", [8, 128], scope=sc0)
    k.qy.dma(cv8[:], c_vec.ap().rearrange("o (a b) -> a (o b)", a=8),
             ins=[B("c_vec")], outs=[B("cv8")])
    sc8 = k.sb("sc8", [8, 128], scope=sc0)
    k.s.op("activation", sc8[:], cv8[:], AF.Silu, bias=0.0, scale=1.0,
           ins=[B("cv8")], outs=[B("sc8")])
    k.t.op("transpose", pk[0][:, 0:8], sc8[:], id_f[0:8, 0:8],
           ins=[B("sc8"), B("id_f")], outs=[B("pk0")])
    scT = k.sb("scT", [128, 8], scope=sc0)
    k.v.op("tensor_copy", scT[:], pk[0][:, 0:8], ins=[B("pk0")], outs=[B("scT")])
    wm_t = k.sb("wm_t", [128, 8 * W6], scope=sc0)
    modp_t = k.sb("modp_t", [1, W6], scope=sc0)
    NB6 = cdiv(W6, 512)
    for kk in range(8):
        k.qy.dma(wm_t[:, W6 * kk:W6 * (kk + 1)],
                 wmod_sl.ap()[128 * kk:128 * (kk + 1), :],
                 ins=[B("wmod_sl")], outs=[B("wm_t")])
    for kk in range(8):
        for j in range(NB6):
            w = min(512, W6 - 512 * j)
            k.t.op("matmul", pk[2 + j][0:1, 0:w], scT[:, kk:kk + 1],
                   wm_t[:, W6 * kk + 512 * j:W6 * kk + 512 * j + w],
                   start=(kk == 0), stop=(kk == 7),
                   ins=[B("scT"), B("wm_t")], outs=[B(f"pk{2 + j}")],
                   defer=not (kk == 7 and j == NB6 - 1))
    for j in range(NB6):
        w = min(512, W6 - 512 * j)
        k.v.op("tensor_copy", modp_t[0:1, 512 * j:512 * j + w], pk[2 + j][0:1, 0:w],
               ins=[B(f"pk{2 + j}")], outs=[B("modp_t")])
    k.qg.dma(modP.ap(), modp_t[:], ins=[B("modp_t")], outs=[B("modP")])
    k.qc.collective("AllGather", ALU.bypass, RG, [modP.ap()], [modF.ap()],
                    ins=[B("modP")], outs=[B("modF")])
    modf_t = k.sb("modf_t", [1, 6 * H], scope=sc0)
    k.qy.dma(modf_t[:], modF.ap().rearrange("a b -> () (a b)"),
             ins=[B("modF")], outs=[B("modf_t")])
    mrow = lambda i: modf_t[0:1, H * i:H * (i + 1)]

    ca1 = k.sb("ca1", [1, H], scope=sc0)
    cb1 = k.sb("cb1", [1, H], scope=sc0)
    k.v.op("tensor_scalar", ca1[:], mrow(1), 1.0, None, ALU.add,
           ins=[B("modf_t")], outs=[B("ca1")])
    k.v.op("tensor_scalar", cb1[:], mrow(4), 1.0, None, ALU.add,
           ins=[B("modf_t")], outs=[B("cb1")])

    def gclamp(name, src):
        ab = k.sb(name + "_a", [1, H], scope=sc0)
        sg = k.sb(name + "_s", [1, H], scope=sc0)
        res = k.sb(name, [1, H], scope=sc0)
        k.s.op("activation", ab[:], src, AF.Abs, bias=0.0, scale=1.0,
               ins=[B("modf_t")], outs=[B(name + "_a")])
        k.v.op("tensor_scalar", ab[:], ab[:], 0.25, None, ALU.max,
               ins=[B(name + "_a")], outs=[B(name + "_a")])
        k.v.op("tensor_scalar", sg[:], src, 0.0, None, ALU.is_ge,
               ins=[B("modf_t")], outs=[B(name + "_s")])
        k.v.op("tensor_scalar", sg[:], sg[:], 2.0, -1.0, ALU.mult, ALU.add,
               ins=[B(name + "_s")], outs=[B(name + "_s")])
        k.v.op("tensor_tensor", res[:], ab[:], sg[:], ALU.mult,
               ins=[B(name + "_a"), B(name + "_s")], outs=[B(name)])
        return res

    gax = gclamp("gax", mrow(2))
    gbx = gclamp("gbx", mrow(5))

    bc_fill(ca1b, "ca1b", ca1[:], B("ca1"), H)
    bc_fill(sab, "sab", mrow(0), B("modf_t"), H)
    bc_fill(gaxb, "gaxb", gax[:], B("gax"), H)
    bc_fill(cb1b, "cb1b", cb1[:], B("cb1"), H)
    bc_fill(sbb, "sbb", mrow(3), B("modf_t"), H)
    bc_fill(gbxb, "gbxb", gbx[:], B("gbx"), H)

    gnw_t = k.sb("gnw_t", [1, H], scope=sc0)
    ginw_t = k.sb("ginw_t", [1, H], scope=sc0)
    k.qy.dma(gnw_t[:], gnw_in.ap(), ins=[B("gnw_in")], outs=[B("gnw_t")])
    k.qy.dma(ginw_t[:], ginw_in.ap(), ins=[B("ginw_in")], outs=[B("ginw_t")])
    gg_t = k.sb("gg_t", [1, H], scope=sc0)
    k.v.op("tensor_tensor", gg_t[:], gnw_t[:], ginw_t[:], ALU.mult,
           ins=[B("gnw_t"), B("ginw_t")], outs=[B("gg_t")])
    bc_fill(ggb, "ggb", gg_t[:], B("gg_t"), H)
    bc_fill(gnwb, "gnwb", gnw_t[:], B("gnw_t"), H)
    gpn_t = k.sb("gpn_t", [1, H], scope=sc0)
    k.qy.dma(gpn_t[:], gpn_in.ap(), ins=[B("gpn_in")], outs=[B("gpn_t")])
    bc_fill(gpnb, "gpnb", gpn_t[:], B("gpn_t"), H)
    dpn_t = k.sb("dpn_t", [1, IM], scope=sc0)
    k.qy.dma(dpn_t[:], dpn_in.ap(), ins=[B("dpn_in")], outs=[B("dpn_t")])
    bc_fill(dpnb, "dpnb", dpn_t[:], B("dpn_t"), IM)

    gw_t = k.sb("gw_t", [E, H], scope=sc0)
    k.qy.dma(gw_t[:], gatew_in.ap(), ins=[B("gatew_in")], outs=[B("gw_t")])
    gw_abs = k.sb("gw_abs", [E, H], scope=sc0)
    gw_sum = k.sb("gw_sum", [E, 1], scope=sc0)
    k.s.op("activation", gw_abs[:], gw_t[:], AF.Abs, bias=0.0, scale=1.0,
           accum_out=gw_sum[:], ins=[B("gw_t")], outs=[B("gw_abs"), B("gw_sum")])
    k.t.op("matmul", pk[0][0:1, 0:1], ones_cf[0:E, :], gw_sum[:],
           start=True, stop=True, ins=[B("ones_cf"), B("gw_sum")], outs=[B("pk0")])
    wsg = k.sb("wsg", [1, 1], scope=sc0)
    k.v.op("tensor_scalar", wsg[:], pk[0][0:1, 0:1], 1.0 / (E * H), 1e-5,
           ALU.mult, ALU.max, ins=[B("pk0")], outs=[B("wsg")])
    wsg_r = k.sb("wsg_r", [1, 1], scope=sc0)
    k.v.op("reciprocal", wsg_r[:], wsg[:], ins=[B("wsg")], outs=[B("wsg_r")])
    bc_fill(gamc_b, "gamc_b", wsg[:], B("wsg"), 1)
    k.v.op("tensor_scalar", gamc_b[:], gamc_b[:], 1.0 / 127.0, None, ALU.mult,
           ins=[B("gamc_b")], outs=[B("gamc_b")])
    wsgb = bcast_row("wsgb", wsg_r[:], B("wsg_r"), 1, scope=sc0)
    mg_t = k.sb("mg_t", [E, H], scope=sc0)
    k.s.op("activation", mg_t[:], gw_t[:], AF.Identity, bias=RC,
           scale=wsgb[0:E, 0:1], ins=[B("gw_t"), B("wsgb"), B("rc_t")],
           outs=[B("mg_t")])
    k.v.op("tensor_scalar", mg_t[:], mg_t[:], -RC, None, ALU.add,
           ins=[B("mg_t")], outs=[B("mg_t")])
    k.v.op("tensor_scalar", mg_t[:], mg_t[:], -1.0, 1.0, ALU.max, ALU.min,
           ins=[B("mg_t")], outs=[B("mg_t")])
    for kk in range(8):
        k.t.op("transpose", pk[kk % 2][:, 0:E], mg_t[:, 128 * kk:128 * (kk + 1)],
               id_f[0:E, 0:E], ins=[B("mg_t"), B("id_f")], outs=[B(f"pk{kk % 2}")])
        k.v.op("tensor_copy", mgt[:, E * kk:E * (kk + 1)], pk[kk % 2][:, 0:E],
               ins=[B(f"pk{kk % 2}")], outs=[B("mgt")])

    sc0.close()
    k.fence()

    stat_g = k.sb("stat_g", [128, NT])
    cid1_s = k.sb("cid1_s", [128, NT], I32)
    cid2_s = k.sb("cid2_s", [128, NT], I32)
    wk1_s = k.sb("wk1_s", [128, NT])
    wk2_s = k.sb("wk2_s", [128, NT])
    xidx_s = k.sb("xidx_s", [128, NS], I32)
    agp_s = k.sb("agp_s", [128, NS])
    adp_s = k.sb("adp_s", [128, NS])
    sxb = k.sb("sxb", [128, 1])
    sxv = k.sb("sxv", [128, 1])
    s2b = k.sb("s2b", [128, 1])
    s2v = k.sb("s2v", [128, 1])
    mdpc_b = k.sb("mdpc_b", [128, 1])
    mgpc_b = k.sb("mgpc_b", [128, 1])
    IL = k.sb("IL", [128, NT * E])

    class Stream:
        def __init__(self, si, widths):
            self.si = si
            self.h = {}
            for nm, shape, dt in widths:
                self.h[nm] = k.sb(f"{nm}_u{si}", shape, dt)
        def __getitem__(self, nm):
            return self.h[nm]
        def b(self, nm):
            return B(f"{nm}_u{self.si}")

    base_w = ([("scr", [128, H], F32)] +
              [(f"st{i}", [128, 1], F32) for i in range(8)])
    SS = [Stream(0, base_w), Stream(1, base_w), Stream(2, base_w)]

    scr = k.sb("scr", [128, 1])
    stt = [k.sb(f"stt{i}", [128, 1]) for i in range(8)]

    class _LegacyS:
        def __init__(self):
            self.h = {"scr": scr}
            for i in range(8):
                self.h[f"st{i}"] = stt[i]
        def __getitem__(self, nm):
            return self.h[nm]
        def b(self, nm):
            return B("scr" if nm == "scr" else "stt" + nm[2:])
    LS = _LegacyS()

    def drive(gen):
        for _ in gen:
            pass

    def interleave(gens):
        gens = list(gens)
        while gens:
            alive = []
            for g in gens:
                try:
                    next(g)
                    alive.append(g)
                except StopIteration:
                    pass
            gens = alive

    def rsqrt_col(S, dst, src, scale, eps):
        d, ds_, db, sb_ = S[f"st{dst}"], S[f"st{src}"], S.b(f"st{dst}"), S.b(f"st{src}")
        k.v.op("tensor_scalar", d[:], ds_[:], scale, eps, ALU.mult, ALU.add,
               ins=[sb_], outs=[db])
        yield
        k.s.op("activation", d[:], d[:], AF.Sqrt, bias=0.0, scale=1.0,
               ins=[db], outs=[db])
        yield
        k.v.op("reciprocal", d[:], d[:], ins=[db], outs=[db])
        yield

    def adaln_tile(S, xt_ap, xt_buf, h_ap, h_buf, vtile, vbuf):
        st0, st2, st3 = S["st0"], S["st2"], S["st3"]
        b0, b1, b2, b3 = S.b("st0"), S.b("st1"), S.b("st2"), S.b("st3")
        scr_, bscr = S["scr"], S.b("scr")
        k.v.op("tensor_reduce", st0[:], xt_ap, AX.X, ALU.add,
               ins=[xt_buf], outs=[b0])
        yield
        k.s.op("activation", scr_[:], xt_ap, AF.Square, bias=0.0, scale=1.0,
               accum_out=S["st1"][:], ins=[xt_buf], outs=[bscr, b1])
        yield
        k.v.op("tensor_scalar", st0[:], st0[:], 1.0 / H, None, ALU.mult,
               ins=[b0], outs=[b0])
        k.v.op("tensor_tensor", st2[:], st0[:], st0[:], ALU.mult,
               ins=[b0], outs=[b2])
        yield
        k.v.op("tensor_scalar", st3[:], S["st1"][:], 1.0 / H, None, ALU.mult,
               ins=[b1], outs=[b3])
        k.v.op("tensor_tensor", st3[:], st3[:], st2[:], ALU.subtract,
               ins=[b3, b2], outs=[b3])
        k.v.op("tensor_scalar", st3[:], st3[:], 1e-5, None, ALU.add,
               ins=[b3], outs=[b3])
        yield
        k.s.op("activation", st3[:], st3[:], AF.Sqrt, bias=0.0, scale=1.0,
               ins=[b3], outs=[b3])
        yield
        k.v.op("reciprocal", st3[:], st3[:], ins=[b3], outs=[b3])
        k.v.op("tensor_tensor", st2[:], st0[:], st3[:], ALU.mult,
               ins=[b0, b3], outs=[b2])
        k.v.op("tensor_scalar", st2[:], st2[:], -1.0, None, ALU.mult,
               ins=[b2], outs=[b2])
        yield
        k.s.op("activation", vtile[:], xt_ap, AF.Identity, bias=st2[:, 0:1],
               scale=st3[:, 0:1], ins=[xt_buf, b2, b3], outs=[vbuf])
        yield
        k.v.op("tensor_tensor", h_ap, vtile[:], ca1b[:], ALU.mult,
               ins=[vbuf, B("ca1b")], outs=[h_buf])
        k.v.op("tensor_tensor", h_ap, h_ap, sab[:], ALU.add,
               ins=[h_buf, B("sab")], outs=[h_buf])
        yield

    scw = ExitStack()
    for S in SS[:2]:
        S.h["wrow"] = k.sb(f"wrow_u{S.si}", [128, H], scope=scw)
        S.h["fqs"] = k.sb(f"fqs_u{S.si}", [128, H], BF16, scope=scw)
    fqT_dst = [(fqgT_p, "fqgT_p"), (fquT_p, "fquT_p")]

    def wq_chain(job, S):
        src_t, m, r0, rr = job
        wrow = S["wrow"]
        bw = S.b("wrow")
        st = [S[f"st{i}"] for i in range(3)]
        sb_ = [S.b(f"st{i}") for i in range(3)]
        k.qy.dma(wrow[0:rr, :], src_t.ap()[r0:r0 + rr, :],
                 ins=[B(src_t.name)], outs=[bw])
        yield
        k.v.op("tensor_reduce", st[0][0:rr, :], wrow[0:rr, :], AX.X, ALU.max,
               apply_absolute_value=True, ins=[bw], outs=[sb_[0]])
        yield
        k.v.op("reciprocal", st[1][0:rr, :], st[0][0:rr, :],
               ins=[sb_[0]], outs=[sb_[1]])
        k.v.op("tensor_scalar", st[1][0:rr, :], st[1][0:rr, :], 127.0, None,
               ALU.mult, ins=[sb_[1]], outs=[sb_[1]])
        k.v.op("tensor_scalar", st[2][0:rr, :], st[0][0:rr, :], 1.0 / 127.0,
               None, ALU.mult, ins=[sb_[0]], outs=[sb_[2]])
        yield
        k.s.op("activation", wrow[0:rr, :], wrow[0:rr, :], AF.Identity,
               bias=RC, scale=st[1][0:rr, 0:1],
               ins=[bw, sb_[1], B("rc_t")], outs=[bw])
        yield
        k.v.op("tensor_scalar", wrow[0:rr, :], wrow[0:rr, :], -RC, None,
               ALU.add, ins=[bw], outs=[bw])
        k.v.op("tensor_scalar", wrow[0:rr, :], wrow[0:rr, :],
               st[2][0:rr, 0:1], None, ALU.mult, ins=[bw, sb_[2]], outs=[bw])
        yield
        fqs, bfq = S["fqs"], S.b("fqs")
        for kk in range(8):
            tb = 6 + kk % 2
            k.t.op("transpose", pk[tb][:, 0:rr],
                   wrow[0:rr, 128 * kk:128 * (kk + 1)],
                   id_f[0:rr, 0:rr] if rr < 128 else id_f[:],
                   ins=[bw, B("id_f")], outs=[B(f"pk{tb}")])
            k.s.op("activation", fqs[:, rr * kk:rr * (kk + 1)],
                   pk[tb][:, 0:rr], AF.Copy, bias=0.0, scale=1.0,
                   ins=[B(f"pk{tb}")], outs=[bfq])
            if kk % 4 == 3:
                yield
        dstp, dstpb = fqT_dst[m]
        k.qg.dma(dstp.ap()[:, r0:r0 + rr].rearrange("(a p) r -> p a r", p=128),
                 fqs.ap()[0:128, 0:8 * rr].rearrange("p (a r) -> p a r", r=rr),
                 ins=[bfq], outs=[B(dstpb)])
        yield

    wq_jobs = []
    for m, src_t in enumerate((wg_sl, wu_sl)):
        for r0 in range(0, IFS, 128):
            wq_jobs.append((src_t, m, r0, min(128, IFS - r0)))
    interleave([(lambda si: (x for j in range(si, len(wq_jobs), 2)
                             for x in wq_chain(wq_jobs[j], SS[si])))(si)
                for si in range(2)])
    wrowd = k.sb("wrowd", [HS, IF], scope=scw)
    wqd_b = k.sb("wqd_b", [HS, IF], BF16, scope=scw)
    k.qy.dma(wrowd[:], wd_sl.ap(), ins=[B("wd_sl")], outs=[B("wrowd")])
    k.v.op("tensor_reduce", stt[0][0:HS, :], wrowd[:], AX.X, ALU.max,
           apply_absolute_value=True, ins=[B("wrowd")], outs=[B("stt0")])
    k.v.op("reciprocal", stt[1][0:HS, :], stt[0][0:HS, :], ins=[B("stt0")], outs=[B("stt1")])
    k.v.op("tensor_scalar", stt[1][0:HS, :], stt[1][0:HS, :], 127.0, None, ALU.mult,
           ins=[B("stt1")], outs=[B("stt1")])
    k.v.op("tensor_scalar", stt[2][0:HS, :], stt[0][0:HS, :], 1.0 / 127.0, None,
           ALU.mult, ins=[B("stt0")], outs=[B("stt2")])
    k.s.op("activation", wrowd[:], wrowd[:], AF.Identity, bias=RC,
           scale=stt[1][0:HS, 0:1], ins=[B("wrowd"), B("stt1"), B("rc_t")],
           outs=[B("wrowd")])
    k.v.op("tensor_scalar", wrowd[:], wrowd[:], -RC, None, ALU.add,
           ins=[B("wrowd")], outs=[B("wrowd")])
    k.v.op("tensor_scalar", wqd_b[:], wrowd[:], stt[2][0:HS, 0:1], None, ALU.mult,
           ins=[B("wrowd"), B("stt2")], outs=[B("wqd_b")])
    k.qg.dma(fqd_p.ap(), wqd_b[:], ins=[B("wqd_b")], outs=[B("fqd_p")])
    k.qg.wait_all(k.g)
    k.qc.collective("AllGather", ALU.bypass, RG, [fqgT_p.ap()], [fqgT_d.ap()],
                    ins=[B("fqgT_p")], outs=[B("fqgT_d")])
    k.qc.collective("AllGather", ALU.bypass, RG, [fquT_p.ap()], [fquT_d.ap()],
                    ins=[B("fquT_p")], outs=[B("fquT_d")])
    k.qc.collective("AllGather", ALU.bypass, RG, [fqd_p.ap()], [fqd_d.ap()],
                    ins=[B("fqd_p")], outs=[B("fqd_d")])

    wabs_acc = k.sb("wabs_acc", [128, 48], scope=scw)
    for S in SS[:2]:
        S.h["wrow2"] = k.sb(f"wrow2_u{S.si}", [128, IM], scope=scw)
        S.h["wq2"] = k.sb(f"wq2_u{S.si}", [128, IM], BF16, scope=scw)
    n_gp = 2 * IM // 128
    n_dp = H // 128

    def wabs_chain(win, wid, i, S):
        wrow2, wscr2 = S["wrow2"], S["wq2"]
        bw, bs = S.b("wrow2"), S.b("wq2")
        k.qy.dma(wrow2[:, 0:wid], win.ap()[128 * i:128 * (i + 1), :],
                 ins=[B(win.name)], outs=[bw])
        yield
        k.s.op("activation", wscr2[:, 0:wid], wrow2[:, 0:wid], AF.Abs, bias=0.0,
               scale=1.0, accum_out=S["st0"][:],
               ins=[bw], outs=[bs, S.b("st0")])
        yield
        k.v.op("tensor_tensor", wabs_acc[:, 40 + S.si:41 + S.si],
               wabs_acc[:, 40 + S.si:41 + S.si], S["st0"][:], ALU.add,
               ins=[B("wabs_acc"), S.b("st0")], outs=[B("wabs_acc")])
        yield

    def wtern_chain(win, wout, wid, i, S, sc_ptr, sc_buf):
        wrow2, wq2 = S["wrow2"], S["wq2"]
        bw, bq = S.b("wrow2"), S.b("wq2")
        k.qy.dma(wrow2[:, 0:wid], win.ap()[128 * i:128 * (i + 1), :],
                 ins=[B(win.name)], outs=[bw])
        yield
        k.s.op("activation", wrow2[:, 0:wid], wrow2[:, 0:wid], AF.Identity,
               bias=RC, scale=sc_ptr, ins=[bw, sc_buf, B("rc_t")], outs=[bw])
        yield
        k.v.op("tensor_scalar", wrow2[:, 0:wid], wrow2[:, 0:wid], -RC, None,
               ALU.add, ins=[bw], outs=[bw])
        k.v.op("tensor_scalar", wq2[:, 0:wid], wrow2[:, 0:wid], -1.0, 1.0,
               ALU.max, ALU.min, ins=[bw], outs=[bq])
        yield
        k.qg.dma(wout.ap()[128 * i:128 * (i + 1), :], wq2[:, 0:wid],
                 ins=[bq], outs=[B(wout.name)])
        yield

    k.v.op("memset", wabs_acc[:], 0.0, outs=[B("wabs_acc")])
    jobs_a = [(gpw_in, H, i) for i in range(n_gp)]
    interleave([(lambda si: (x for j in range(si, len(jobs_a), 2)
                             for x in wabs_chain(*jobs_a[j], SS[si])))(si)
                for si in range(2)])
    asum = k.sb("asum", [128, 1], scope=scw)
    k.v.op("tensor_tensor", asum[:], wabs_acc[:, 40:41], wabs_acc[:, 41:42],
           ALU.add, ins=[B("wabs_acc")], outs=[B("asum")])
    k.v.op("tensor_tensor", asum[:], asum[:], wabs_acc[:, 42:43],
           ALU.add, ins=[B("asum"), B("wabs_acc")], outs=[B("asum")])
    k.t.op("matmul", pk[0][0:1, 0:1], ones_cf[:], asum[:], start=True,
           stop=True, ins=[B("ones_cf"), B("asum")], outs=[B("pk0")])
    wsgp = k.sb("wsgp", [1, 1], scope=scw)
    k.v.op("tensor_scalar", wsgp[:], pk[0][0:1, 0:1], 1.0 / (2 * IM * H), 1e-5,
           ALU.mult, ALU.max, ins=[B("pk0")], outs=[B("wsgp")])
    wsgp_r = k.sb("wsgp_r", [1, 1], scope=scw)
    k.v.op("reciprocal", wsgp_r[:], wsgp[:], ins=[B("wsgp")], outs=[B("wsgp_r")])
    t1 = bcast_row("wsgp_rb", wsgp_r[:], B("wsgp_r"), 1, scope=scw)
    t2 = bcast_row("wsgp_b", wsgp[:], B("wsgp"), 1, scope=scw)
    k.v.op("tensor_scalar", mgpc_b[:], t2[:], 1.0 / 127.0, None, ALU.mult,
           ins=[B("wsgp_b")], outs=[B("mgpc_b")])
    jobs_q = [(gpw_in, gpq_d, H, i) for i in range(n_gp)]
    interleave([(lambda si: (x for j in range(si, len(jobs_q), 2)
                             for x in wtern_chain(*jobs_q[j], SS[si],
                                                  t1[:, 0:1], B("wsgp_rb"))))(si)
                for si in range(2)])
    k.v.op("memset", wabs_acc[:, 40:43], 0.0, outs=[B("wabs_acc")])
    jobs_a2 = [(dpw_in, IM, i) for i in range(n_dp)]
    interleave([(lambda si: (x for j in range(si, len(jobs_a2), 2)
                             for x in wabs_chain(*jobs_a2[j], SS[si])))(si)
                for si in range(2)])
    k.v.op("tensor_tensor", asum[:], wabs_acc[:, 40:41], wabs_acc[:, 41:42],
           ALU.add, ins=[B("wabs_acc")], outs=[B("asum")])
    k.v.op("tensor_tensor", asum[:], asum[:], wabs_acc[:, 42:43],
           ALU.add, ins=[B("asum"), B("wabs_acc")], outs=[B("asum")])
    k.t.op("matmul", pk[0][0:1, 0:1], ones_cf[:], asum[:], start=True,
           stop=True, ins=[B("ones_cf"), B("asum")], outs=[B("pk0")])
    wsdp = k.sb("wsdp", [1, 1], scope=scw)
    k.v.op("tensor_scalar", wsdp[:], pk[0][0:1, 0:1], 1.0 / (H * IM), 1e-5,
           ALU.mult, ALU.max, ins=[B("pk0")], outs=[B("wsdp")])
    wsdp_r = k.sb("wsdp_r", [1, 1], scope=scw)
    k.v.op("reciprocal", wsdp_r[:], wsdp[:], ins=[B("wsdp")], outs=[B("wsdp_r")])
    t3 = bcast_row("wsdp_rb", wsdp_r[:], B("wsdp_r"), 1, scope=scw)
    t4 = bcast_row("wsdp_b", wsdp[:], B("wsdp"), 1, scope=scw)
    k.v.op("tensor_scalar", mdpc_b[:], t4[:], 1.0 / 127.0, None, ALU.mult,
           ins=[B("wsdp_b")], outs=[B("mdpc_b")])
    jobs_q2 = [(dpw_in, dpq_d, IM, i) for i in range(n_dp)]
    interleave([(lambda si: (x for j in range(si, len(jobs_q2), 2)
                             for x in wtern_chain(*jobs_q2[j], SS[si],
                                                  t3[:, 0:1], B("wsdp_rb"))))(si)
                for si in range(2)])

    sc_ng = ExitStack()
    ngt = k.sb("ngt", [128, 8 * TL], BF16, scope=sc_ng)
    with ExitStack() as sc1:
        p1w = [("p1x", [128, H], F32), ("p1h", [128, H], F32),
               ("p1tg", [128, H], F32)]
        for S in SS:
            for nm, shape, dt in p1w:
                S.h[nm] = k.sb(f"{nm}_u{S.si}", shape, dt, scope=sc1)

        def p1_chain(t, S):
            p1x, p1h, p1tg = (S["p1x"], S["p1h"], S["p1tg"])
            p1v = p1tg
            bx, bh, btg = (S.b("p1x"), S.b("p1h"), S.b("p1tg"))
            bv = btg
            scr_, bscr = S["scr"], S.b("scr")
            st = [S[f"st{i}"] for i in range(8)]
            sb_ = [S.b(f"st{i}") for i in range(8)]
            k.qy.dma(p1x[:], xown.ap()[128 * t:128 * (t + 1), :],
                     ins=[B("xown")], outs=[bx])
            yield
            yield from adaln_tile(S, p1x[:], bx, p1h[:], bh, p1v, bv)
            k.s.op("activation", scr_[:], p1h[:], AF.Square, bias=0.0, scale=1.0,
                   accum_out=st[0][:], ins=[bh], outs=[bscr, sb_[0]])
            yield
            yield from rsqrt_col(S, 1, 0, 1.0 / H, 1e-6)
            k.v.op("tensor_tensor", p1tg[:], p1h[:], gnwb[:], ALU.mult,
                   ins=[bh, B("gnwb")], outs=[btg])
            yield
            k.s.op("activation", scr_[:], p1tg[:], AF.Square, bias=0.0, scale=1.0,
                   accum_out=st[2][:], ins=[btg], outs=[bscr, sb_[2]])
            yield
            k.v.op("tensor_tensor", st[3][:], st[1][:], st[1][:], ALU.mult,
                   ins=[sb_[1]], outs=[sb_[3]])
            k.v.op("tensor_tensor", st[3][:], st[3][:], st[2][:], ALU.mult,
                   ins=[sb_[3], sb_[2]], outs=[sb_[3]])
            yield
            yield from rsqrt_col(S, 4, 3, 1.0 / H, 1e-8)
            k.v.op("tensor_tensor", p1tg[:], p1h[:], ggb[:], ALU.mult,
                   ins=[bh, B("ggb")], outs=[btg])
            yield
            k.v.op("tensor_reduce", st[5][:], p1tg[:], AX.X, ALU.max,
                   apply_absolute_value=True, ins=[btg], outs=[sb_[5]])
            yield
            k.v.op("tensor_scalar", st[6][:], st[5][:], 1e-30, None, ALU.max,
                   ins=[sb_[5]], outs=[sb_[6]])
            k.v.op("reciprocal", st[6][:], st[6][:], ins=[sb_[6]], outs=[sb_[6]])
            k.v.op("tensor_scalar", st[6][:], st[6][:], 127.0, None, ALU.mult,
                   ins=[sb_[6]], outs=[sb_[6]])
            yield
            k.s.op("activation", scr_[:], p1tg[:], AF.Identity, bias=RC,
                   scale=st[6][:, 0:1], ins=[btg, sb_[6], B("rc_t")], outs=[bscr])
            yield
            ngt3 = ngt.ap().rearrange("p (a c) -> p a c", c=TL)
            for kk in range(8):
                j = kk // 4
                pbn = f"pk{(2 * S.si + j) % 6}"
                pbank = pk[(2 * S.si + j) % 6]
                k.t.op("transpose", pbank[:, 128 * (kk % 4):128 * (kk % 4) + 128],
                       scr_[:, 128 * kk:128 * (kk + 1)], id_f[:],
                       ins=[bscr, B("id_f")], outs=[B(pbn)],
                       defer=(kk % 4 != 3))
                if kk % 4 == 3:
                    k.s.op("activation",
                           ngt3[:, 4 * j:4 * j + 4, 128 * t:128 * (t + 1)],
                           pk[(2 * S.si + j) % 6][:, 0:512]
                           .rearrange("p (a c) -> p a c", c=128),
                           AF.Copy, bias=-RC, scale=1.0,
                           ins=[B(pbn), B("mrc_t")], outs=[B("ngt")])
                    yield
            k.v.op("tensor_tensor", st[7][:], st[1][:], st[4][:], ALU.mult,
                   ins=[sb_[1], sb_[4]], outs=[sb_[7]])
            k.v.op("tensor_tensor", st[7][:], st[7][:], st[5][:], ALU.mult,
                   ins=[sb_[7], sb_[5]], outs=[sb_[7]])
            k.v.op("tensor_scalar", st[7][:], st[7][:], 1e-5, None, ALU.max,
                   ins=[sb_[7]], outs=[sb_[7]])
            k.v.op("tensor_tensor", stat_g[:, t:t + 1], st[7][:], gamc_b[:],
                   ALU.mult, ins=[sb_[7], B("gamc_b")], outs=[B("stat_g")])
            yield

        def p1_stream(si):
            for t in range(si, NT, len(SS)):
                yield from p1_chain(t, SS[si])

        interleave([p1_stream(si) for si in range(len(SS))])
    k.fence()

    with ExitStack() as sc2:
        for t in range(NT):
            for kk in range(8):
                k.t.op("matmul", pk[t % 4][:, 0:E],
                       ngt[:, TL * kk + 128 * t:TL * kk + 128 * (t + 1)],
                       mgt[:, E * kk:E * (kk + 1)], start=(kk == 0), stop=(kk == 7),
                       ins=[B("ngt"), B("mgt")], outs=[B(f"pk{t % 4}")],
                       defer=(kk != 7))
            k.v.op("tensor_copy", IL[:, E * t:E * (t + 1)], pk[t % 4][:, 0:E],
                   ins=[B(f"pk{t % 4}")], outs=[B("IL")])
    sc_ng.close()
    k.fence()

    NCL = NT * E
    selP_d = k.dram("selP_d", [128, E * NT], F32)
    selF_d = k.dram("selF_d", [128 * NCORE, E * NT], F32)
    tot_d = k.dram("tot_d", [1, 8], F32)
    totF_d = k.dram("totF_d", [NCORE, 8], F32)
    lcar_d = k.dram("lcar_d", [NT, E], F32)
    ccar_d = k.dram("ccar_d", [NCORE, E], F32)
    cs_d = k.dram("cs_d", [1, NCL], F32)
    with ExitStack() as sc3:
        def sb3(name, dtype=F32):
            tile = k.sb(name, [128, NCL], dtype, scope=sc3)
            return tile, tile.ap().rearrange("p (a e) -> p a e", e=E)

        ioe3 = io_e.ap()[0:128, 0:E].rearrange("p e -> p () e").broadcast_to(
            (128, NT, E))
        IL3 = IL.ap().rearrange("p (a e) -> p a e", e=E)
        keys, keys3 = sb3("keys")
        k.v.op("tensor_scalar", keys[:], IL[:], 16.0, None, ALU.mult,
               ins=[B("IL")], outs=[B("keys")])
        k.v.op("tensor_tensor", keys3, keys3, ioe3, ALU.subtract,
               ins=[B("keys"), B("io_e")], outs=[B("keys")])
        m1 = k.sb("m1", [128, NT], scope=sc3)
        m13 = m1.ap().rearrange("p (a o) -> p a o", o=1)
        k.v.op("tensor_reduce", m13, keys3, AX.X, ALU.max,
               ins=[B("keys")], outs=[B("m1")])
        is1, is13 = sb3("is1")
        k.v.op("tensor_tensor", is13, keys3, m13.broadcast_to((128, NT, E)),
               ALU.is_ge, ins=[B("keys"), B("m1")], outs=[B("is1")])
        keys2, keys23 = sb3("keys2")
        k.v.op("scalar_tensor_tensor", keys2[:], is1[:], -2.0 ** 30, keys[:],
               ALU.mult, ALU.add, ins=[B("is1"), B("keys")], outs=[B("keys2")])
        m2 = k.sb("m2", [128, NT], scope=sc3)
        m23 = m2.ap().rearrange("p (a o) -> p a o", o=1)
        k.v.op("tensor_reduce", m23, keys23, AX.X, ALU.max,
               ins=[B("keys2")], outs=[B("m2")])
        sel, sel3 = sb3("sel")
        k.v.op("tensor_tensor", sel3, keys3, m23.broadcast_to((128, NT, E)),
               ALU.is_ge, ins=[B("keys"), B("m2")], outs=[B("sel")])
        lg, lg3 = sb3("lg")
        g3 = stat_g.ap().rearrange("p (a o) -> p a o", o=1)
        k.v.op("tensor_tensor", lg3, IL3, g3.broadcast_to((128, NT, E)), ALU.mult,
               ins=[B("IL"), B("stat_g")], outs=[B("lg")])
        lm = k.sb("lm", [128, NT], scope=sc3)
        lm3 = lm.ap().rearrange("p (a o) -> p a o", o=1)
        k.v.op("tensor_reduce", lm3, lg3, AX.X, ALU.max, ins=[B("lg")], outs=[B("lm")])
        k.v.op("tensor_tensor", lg3, lg3, lm3.broadcast_to((128, NT, E)),
               ALU.subtract, ins=[B("lg"), B("lm")], outs=[B("lg")])
        k.s.op("activation", lg[:], lg[:], AF.Exp, bias=0.0, scale=1.0,
               ins=[B("lg")], outs=[B("lg")])
        ssum = k.sb("ssum", [128, NT], scope=sc3)
        ssum3 = ssum.ap().rearrange("p (a o) -> p a o", o=1)
        k.v.op("tensor_reduce", ssum3, lg3, AX.X, ALU.add,
               ins=[B("lg")], outs=[B("ssum")])
        k.v.op("reciprocal", ssum[:], ssum[:], ins=[B("ssum")], outs=[B("ssum")])
        wprob, wprob3 = sb3("wprob")
        k.v.op("tensor_tensor", wprob3, lg3, ssum3.broadcast_to((128, NT, E)),
               ALU.mult, ins=[B("lg"), B("ssum")], outs=[B("wprob")])
        k.v.op("tensor_tensor", wprob[:], wprob[:], sel[:], ALU.mult,
               ins=[B("wprob"), B("sel")], outs=[B("wprob")])
        selb = k.sb("selb", [128, NCL], BF16, scope=sc3)
        k.v.op("tensor_copy", selb[:], sel[:], ins=[B("sel")], outs=[B("selb")])
        wpx, _ = sb3("wpx")
        for j in range(cdiv(NCL, 512)):
            w = min(512, NCL - 512 * j)
            p = pk[j % 4]
            k.t.op("matmul", p[:, 0:w], tri_b[:], selb[:, 512 * j:512 * j + w],
                   start=True, stop=True, ins=[B("tri_b"), B("selb")],
                   outs=[B(f"pk{j % 4}")])
            k.v.op("tensor_copy", wpx[:, 512 * j:512 * j + w], p[:, 0:w],
                   ins=[B(f"pk{j % 4}")], outs=[B("wpx")])
        cs_row = k.sb("cs_row", [1, NCL], scope=sc3)
        for j in range(cdiv(NCL, 512)):
            w = min(512, NCL - 512 * j)
            p = pk[4 + (j % 2)]
            k.t.op("matmul", p[0:1, 0:w], ones_c[:], selb[:, 512 * j:512 * j + w],
                   start=True, stop=True, ins=[B("ones_c"), B("selb")],
                   outs=[B(f"pk{4 + (j % 2)}")])
            k.v.op("tensor_copy", cs_row[0:1, 512 * j:512 * j + w], p[0:1, 0:w],
                   ins=[B(f"pk{4 + (j % 2)}")], outs=[B("cs_row")])
        k.qg.dma(cs_d.ap(), cs_row[:], ins=[B("cs_row")], outs=[B("cs_d")])
        csje = k.sb("csje", [NT, E], BF16, scope=sc3)
        csje_f = k.sb("csje_f", [NT, E], scope=sc3)
        k.qy.dma(csje_f[:], cs_d.ap().rearrange("o (a e) -> (o a) e", e=E),
                 ins=[B("cs_d")], outs=[B("csje_f")])
        k.v.op("tensor_copy", csje[:], csje_f[:], ins=[B("csje_f")], outs=[B("csje")])
        k.t.op("matmul", pk[6][0:NT, 0:E], tri_b[0:NT, 0:NT], csje[:],
               start=True, stop=True, ins=[B("tri_b"), B("csje")], outs=[B("pk6")])
        lcar = k.sb("lcar", [NT, E], scope=sc3)
        k.v.op("tensor_copy", lcar[:], pk[6][0:NT, 0:E], ins=[B("pk6")], outs=[B("lcar")])
        k.qg.dma(lcar_d.ap(), lcar[:], ins=[B("lcar")], outs=[B("lcar_d")])
        k.t.op("matmul", pk[6][0:1, 0:E], ones_c[0:NT, :], csje[:],
               start=True, stop=True, ins=[B("ones_c"), B("csje")], outs=[B("pk6")])
        totr = k.sb("totr", [1, E], scope=sc3)
        k.v.op("tensor_copy", totr[:], pk[6][0:1, 0:E], ins=[B("pk6")], outs=[B("totr")])
        k.qg.dma(tot_d.ap(), totr[:], ins=[B("totr")], outs=[B("tot_d")])
        k.qc.collective("AllGather", ALU.bypass, RG, [tot_d.ap()], [totF_d.ap()],
                        ins=[B("tot_d")], outs=[B("totF_d")])
        selx = k.sb("selx", [128, E * NT], scope=sc3)
        k.v.op("tensor_copy", selx.ap().rearrange("p (e a) -> p e a", a=NT),
               sel.ap().rearrange("p (a e) -> p e a", e=E),
               ins=[B("sel")], outs=[B("selx")])
        k.qg.dma(selP_d.ap(), selx[:], ins=[B("selx")], outs=[B("selP_d")])
        k.qg.wait_all(k.g)
        k.qc.collective("AllGather", ALU.bypass, RG, [selP_d.ap()], [selF_d.ap()],
                        ins=[B("selP_d")], outs=[B("selF_d")])
        totF = k.sb("totF", [NCORE, E], scope=sc3)
        k.qy.dma(totF[:], totF_d.ap(), ins=[B("totF_d")], outs=[B("totF")])
        k.t.op("matmul", pk[6][0:NCORE, 0:E], tri_f[0:NCORE, 0:NCORE], totF[:],
               start=True, stop=True, ins=[B("tri_f"), B("totF")], outs=[B("pk6")])
        ccar = k.sb("ccar", [NCORE, E], scope=sc3)
        k.v.op("tensor_copy", ccar[:], pk[6][0:NCORE, 0:E], ins=[B("pk6")],
               outs=[B("ccar")])
        k.qg.dma(ccar_d.ap(), ccar[:], ins=[B("ccar")], outs=[B("ccar_d")])
        ccrow = k.sb("ccrow", [1, E], scope=sc3)
        k.qg.wait_all(k.y)
        k.qy.dma(ccrow[:], ccar_d.ap()[ds(pid["y"], 1), :],
                 ins=[B("ccar_d")], outs=[B("ccrow")])
        ccb = k.sb("ccb", [128, E], scope=sc3)
        k.t.op("matmul", pk[5][:, 0:E], ones_rf[:], ccrow[0:1, :],
               start=True, stop=True, ins=[B("ones_rf"), B("ccrow")], outs=[B("pk5")])
        k.v.op("tensor_copy", ccb[:], pk[5][:, 0:E], ins=[B("pk5")], outs=[B("ccb")])
        lcrow = k.sb("lcrow", [1, NCL], scope=sc3)
        k.qy.dma(lcrow[:], lcar_d.ap().rearrange("a e -> () (a e)"),
                 ins=[B("lcar_d")], outs=[B("lcrow")])
        lcb = k.sb("lcb", [128, NCL], scope=sc3)
        for j in range(cdiv(NCL, 512)):
            w = min(512, NCL - 512 * j)
            p = pk[4 + (j % 2)]
            k.t.op("matmul", p[:, 0:w], ones_rf[:], lcrow[0:1, 512 * j:512 * j + w],
                   start=True, stop=True, ins=[B("ones_rf"), B("lcrow")],
                   outs=[B(f"pk{4 + (j % 2)}")])
            k.v.op("tensor_copy", lcb[:, 512 * j:512 * j + w], p[:, 0:w],
                   ins=[B(f"pk{4 + (j % 2)}")], outs=[B("lcb")])
        pos, pos3 = sb3("pos")
        k.v.op("tensor_tensor", pos[:], wpx[:], lcb[:], ALU.add,
               ins=[B("wpx"), B("lcb")], outs=[B("pos")])
        ccb3 = ccb.ap().rearrange("p e -> p () e").broadcast_to((128, NT, E))
        k.v.op("tensor_tensor", pos3, pos3, ccb3, ALU.add,
               ins=[B("pos"), B("ccb")], outs=[B("pos")])
        keep, keep3 = sb3("keep")
        k.v.op("tensor_scalar", keep[:], pos[:], float(CAP), None, ALU.is_lt,
               ins=[B("pos")], outs=[B("keep")])
        k.v.op("tensor_tensor", keep[:], keep[:], sel[:], ALU.mult,
               ins=[B("keep"), B("sel")], outs=[B("keep")])
        is2 = k.sb("is2", [128, NCL], scope=sc3)
        k.v.op("tensor_tensor", is2[:], sel[:], is1[:], ALU.subtract,
               ins=[B("sel"), B("is1")], outs=[B("is2")])
        tmp3t = k.sb("tmp3t", [128, NCL], scope=sc3)
        tmp3 = tmp3t.ap().rearrange("p (a e) -> p a e", e=E)
        red1 = k.sb("red1", [128, NT], scope=sc3)
        red13 = red1.ap().rearrange("p (a o) -> p a o", o=1)
        cidf = k.sb("cidf", [128, NT], scope=sc3)

        def extract(mask_t, mask_buf, cid_dst, cid_buf, wk_dst, wk_buf):
            m3 = mask_t.ap().rearrange("p (a e) -> p a e", e=E)
            k.v.op("tensor_tensor", tmp3, m3, ioe3, ALU.mult,
                   ins=[mask_buf, B("io_e")], outs=[B("tmp3t")])
            k.v.op("tensor_reduce", red13, tmp3, AX.X, ALU.add,
                   ins=[B("tmp3t")], outs=[B("red1")])
            k.v.op("tensor_scalar", cidf[:], red1[:], float(CHS), None, ALU.mult,
                   ins=[B("red1")], outs=[B("cidf")])
            k.v.op("tensor_tensor", tmp3t[:], mask_t[:], pos[:], ALU.mult,
                   ins=[mask_buf, B("pos")], outs=[B("tmp3t")])
            k.v.op("tensor_reduce", red13, tmp3, AX.X, ALU.add,
                   ins=[B("tmp3t")], outs=[B("red1")])
            k.v.op("tensor_tensor", cidf[:], cidf[:], red1[:], ALU.add,
                   ins=[B("cidf"), B("red1")], outs=[B("cidf")])
            for j in range(1, NCH_AG):
                k.v.op("tensor_scalar", tmp3t[:, 0:NT], red1[:], float(CHS * j),
                       None, ALU.is_ge, ins=[B("red1")], outs=[B("tmp3t")])
                k.v.op("scalar_tensor_tensor", cidf[:], tmp3t[:, 0:NT],
                       float((E - 1) * CHS), cidf[:], ALU.mult, ALU.add,
                       ins=[B("tmp3t"), B("cidf")], outs=[B("cidf")])
            k.v.op("tensor_tensor", tmp3t[:], mask_t[:], keep[:], ALU.mult,
                   ins=[mask_buf, B("keep")], outs=[B("tmp3t")])
            k.v.op("tensor_reduce", red13, tmp3, AX.X, ALU.add,
                   ins=[B("tmp3t")], outs=[B("red1")])
            k.v.op("tensor_scalar", tmp3t[:, 0:NT], red1[:], -1e6, 1e6,
                   ALU.mult, ALU.add, ins=[B("red1")], outs=[B("tmp3t")])
            k.v.op("tensor_tensor", cidf[:], cidf[:], tmp3t[:, 0:NT], ALU.add,
                   ins=[B("cidf"), B("tmp3t")], outs=[B("cidf")])
            k.v.op("tensor_scalar", cidf[:], cidf[:], float(E * CAP - 1), None,
                   ALU.min, ins=[B("cidf")], outs=[B("cidf")])
            k.v.op("tensor_copy", cid_dst[:], cidf[:], ins=[B("cidf")],
                   outs=[cid_buf])
            k.v.op("tensor_tensor", tmp3t[:], mask_t[:], wprob[:], ALU.mult,
                   ins=[mask_buf, B("wprob")], outs=[B("tmp3t")])
            k.v.op("tensor_reduce",
                   wk_dst.ap()[0:128, 0:NT].rearrange("p (a o) -> p a o", o=1),
                   tmp3, AX.X, ALU.add, ins=[B("tmp3t")], outs=[wk_buf])
            k.v.op("tensor_tensor", wk_dst[:], wk_dst[:], red1[:], ALU.mult,
                   ins=[wk_buf, B("red1")], outs=[wk_buf])

        extract(is1, B("is1"), cid1_s, B("cid1_s"), wk1_s, B("wk1_s"))
        extract(is2, B("is2"), cid2_s, B("cid2_s"), wk2_s, B("wk2_s"))
    k.fence()

    scw.close()
    k.fence()

    sc6a = ExitStack()
    gpwt = k.sb("gpwt", [128, 8 * 4096], BF16, scope=sc6a)
    k.qg.wait_all(k.y)
    for kk in range(8):
        k.qy.dma_t(gpwt[:, 4096 * kk:4096 * (kk + 1)],
                   gpq_d.ap()[:, 128 * kk:128 * (kk + 1)],
                   ins=[B("gpq_d")], outs=[B("gpwt")])

    with ExitStack() as sc5:
        selg = k.sb("selg", [128, NJ], scope=sc5)
        ey = pid["y"]
        for c in range(NCORE):
            k.qy.dma(selg[:, NT * c:NT * (c + 1)],
                     selF_d.ap()[128 * c:128 * (c + 1)][:, ds(ey * NT, NT)],
                     ins=[B("selF_d")], outs=[B("selg")])
        selgb = k.sb("selgb", [128, NJ], BF16, scope=sc5)
        k.v.op("tensor_copy", selgb[:], selg[:], ins=[B("selg")], outs=[B("selgb")])
        csg = k.sb("csg", [1, NJ], scope=sc5)
        for j in range(cdiv(NJ, 512)):
            w = min(512, NJ - 512 * j)
            p = pk[j % 2]
            k.t.op("matmul", p[0:1, 0:w], ones_c[:], selgb[:, 512 * j:512 * j + w],
                   start=True, stop=True, ins=[B("ones_c"), B("selgb")],
                   outs=[B(f"pk{j % 2}")])
            k.v.op("tensor_copy", csg[0:1, 512 * j:512 * j + w], p[0:1, 0:w],
                   ins=[B(f"pk{j % 2}")], outs=[B("csg")])
        csgc = k.sb("csgc", [NJ, 1], BF16, scope=sc5)
        k.t.op("transpose", pk[2][0:NJ, 0:1], csg[:], id_f[0:1, 0:1],
               ins=[B("csg"), B("id_f")], outs=[B("pk2")])
        csgc_f = k.sb("csgc_f", [NJ, 1], scope=sc5)
        k.v.op("tensor_copy", csgc_f[:], pk[2][0:NJ, 0:1], ins=[B("pk2")],
               outs=[B("csgc_f")])
        k.v.op("tensor_copy", csgc[:], csgc_f[:], ins=[B("csgc_f")], outs=[B("csgc")])
        k.t.op("matmul", pk[3][0:NJ, 0:1], tri_b[0:NJ, 0:NJ], csgc[:],
               start=True, stop=True, ins=[B("tri_b"), B("csgc")], outs=[B("pk3")])
        carsel = k.sb("carsel", [NJ, 1], scope=sc5)
        k.v.op("tensor_copy", carsel[:], pk[3][0:NJ, 0:1], ins=[B("pk3")],
               outs=[B("carsel")])
        carK = k.sb("carK", [NJ, 1], scope=sc5)
        k.v.op("tensor_scalar", carK[:], carsel[:], float(CAP), None, ALU.min,
               ins=[B("carsel")], outs=[B("carK")])
        k.t.op("transpose", pk[2][0:1, 0:NJ], carsel[:], id_f[0:NJ, 0:NJ],
               ins=[B("carsel"), B("id_f")], outs=[B("pk2")])
        carselr = k.sb("carselr", [1, NJ], scope=sc5)
        k.v.op("tensor_copy", carselr[:], pk[2][0:1, 0:NJ], ins=[B("pk2")],
               outs=[B("carselr")])
        carKr = k.sb("carKr", [1, NJ], scope=sc5)
        k.v.op("tensor_scalar", carKr[:], carselr[:], float(CAP), None, ALU.min,
               ins=[B("carselr")], outs=[B("carKr")])
        wpe = k.sb("wpe", [128, NJ], scope=sc5)
        for j in range(cdiv(NJ, 512)):
            w = min(512, NJ - 512 * j)
            p = pk[j % 2]
            k.t.op("matmul", p[:, 0:w], tri_b[:], selgb[:, 512 * j:512 * j + w],
                   start=True, stop=True, ins=[B("tri_b"), B("selgb")],
                   outs=[B(f"pk{j % 2}")])
            k.v.op("tensor_copy", wpe[:, 512 * j:512 * j + w], p[:, 0:w],
                   ins=[B(f"pk{j % 2}")], outs=[B("wpe")])
        carsb = k.sb("carsb", [128, NJ], scope=sc5)
        for j in range(cdiv(NJ, 512)):
            w = min(512, NJ - 512 * j)
            p = pk[2 + (j % 2)]
            k.t.op("matmul", p[:, 0:w], ones_rf[:], carselr[0:1, 512 * j:512 * j + w],
                   start=True, stop=True, ins=[B("ones_rf"), B("carselr")],
                   outs=[B(f"pk{2 + (j % 2)}")])
            k.v.op("tensor_copy", carsb[:, 512 * j:512 * j + w], p[:, 0:w],
                   ins=[B(f"pk{2 + (j % 2)}")], outs=[B("carsb")])
        pose = k.sb("pose", [128, NJ], scope=sc5)
        k.v.op("tensor_tensor", pose[:], wpe[:], carsb[:], ALU.add,
               ins=[B("wpe"), B("carsb")], outs=[B("pose")])
        keep_e = k.sb("keep_e", [128, NJ], scope=sc5)
        k.v.op("tensor_scalar", keep_e[:], pose[:], float(CAP), None, ALU.is_lt,
               ins=[B("pose")], outs=[B("keep_e")])
        k.v.op("tensor_tensor", keep_e[:], keep_e[:], selg[:], ALU.mult,
               ins=[B("keep_e"), B("selg")], outs=[B("keep_e")])
        keb = k.sb("keb", [128, NJ], BF16, scope=sc5)
        k.v.op("tensor_copy", keb[:], keep_e[:], ins=[B("keep_e")], outs=[B("keb")])
        rw = k.sb("rw", [128, NJ], scope=sc5)
        for j in range(cdiv(NJ, 512)):
            w = min(512, NJ - 512 * j)
            p = pk[j % 2]
            k.t.op("matmul", p[:, 0:w], tri_b[:], keb[:, 512 * j:512 * j + w],
                   start=True, stop=True, ins=[B("tri_b"), B("keb")],
                   outs=[B(f"pk{j % 2}")])
            k.v.op("tensor_copy", rw[:, 512 * j:512 * j + w], p[:, 0:w],
                   ins=[B(f"pk{j % 2}")], outs=[B("rw")])
        k.v.op("tensor_scalar", rw[:], rw[:], 1.0, None, ALU.add,
               ins=[B("rw")], outs=[B("rw")])
        k.v.op("tensor_tensor", rw[:], rw[:], keep_e[:], ALU.mult,
               ins=[B("rw"), B("keep_e")], outs=[B("rw")])
        k.v.op("tensor_scalar", rw[:], rw[:], -1.0, None, ALU.add,
               ins=[B("rw")], outs=[B("rw")])
        NJH = min(64, NJ)
        pall = k.sb("pall", [128, NJH * 128], BF16, scope=sc5)
        pall3 = pall.ap().rearrange("p (a r) -> p a r", r=128)
        ior3 = io_r.ap().rearrange("p r -> p () r").broadcast_to((128, NJH, 128))
        ct = k.sb("ct", [128, NJ], scope=sc5)
        for hh in range(NJ // NJH):
            rw3 = rw.ap()[0:128, NJH * hh:NJH * (hh + 1)].rearrange(
                "p (a o) -> p a o", o=1).broadcast_to((128, NJH, 128))
            k.v.op("tensor_tensor", pall3, rw3, ior3, ALU.is_equal,
                   ins=[B("rw"), B("io_r")], outs=[B("pall")])
            for jj in range(NJH):
                j = NJH * hh + jj
                p = pk[j % 4]
                k.t.op("matmul", p[:, 0:1], pall[:, 128 * jj:128 * (jj + 1)],
                       iop1_b[:], start=True, stop=True,
                       ins=[B("pall"), B("iop1_b")], outs=[B(f"pk{j % 4}")])
                k.v.op("tensor_copy", ct[:, j:j + 1], p[:, 0:1],
                       ins=[B(f"pk{j % 4}")], outs=[B("ct")])
        k.t.op("transpose", pk[4][0:NJ, 0:128], ct[:], id_f[:],
               ins=[B("ct"), B("id_f")], outs=[B("pk4")])
        ctt = k.sb("ctt", [128, 128], scope=sc5)
        k.v.op("tensor_copy", ctt[0:NJ, :], pk[4][0:NJ, 0:128],
               ins=[B("pk4")], outs=[B("ctt")])
        k.qg.dma(ct_d.ap().rearrange("(a r) o -> a (r o)", r=128), ctt[0:NJ, :],
                 ins=[B("ctt")], outs=[B("ct_d")])
        dcar = k.sb("dcar", [1, NJ], scope=sc5)
        k.v.op("memset", dcar[:], 0.0, outs=[B("dcar")])
        k.v.op("tensor_copy", dcar[0:1, 1:NJ], carKr[0:1, 0:NJ - 1],
               ins=[B("carKr")], outs=[B("dcar")])
        k.v.op("tensor_tensor", dcar[:], carKr[:], dcar[:], ALU.subtract,
               ins=[B("carKr"), B("dcar")], outs=[B("dcar")])
        k.t.op("transpose", pk[5][0:NJ, 0:1], dcar[:], id_f[0:1, 0:1],
               ins=[B("dcar"), B("id_f")], outs=[B("pk5")])
        dcarc = k.sb("dcarc", [NJ, 1], BF16, scope=sc5)
        k.v.op("tensor_copy", dcarc[:], pk[5][0:NJ, 0:1], ins=[B("pk5")],
               outs=[B("dcarc")])
        CH5 = min(1024, CAP)
        cm = k.sb("cm", [128, CH5], BF16, scope=sc5)
        j1r = k.sb("j1r", [1, CH5], scope=sc5)
        cofr = k.sb("cofr", [1, CH5], scope=sc5)
        flr = k.sb("flr", [1, CH5], scope=sc5)
        for ph in range(CAP // CH5):
            k.v.op("tensor_scalar", cm[0:NJ, :],
                   io_s[0:NJ, CH5 * ph:CH5 * (ph + 1)], carK[:, 0:1], None,
                   ALU.is_ge, ins=[B("io_s"), B("carK")], outs=[B("cm")])
            for j in range(cdiv(CH5, 512)):
                w = min(512, CH5 - 512 * j)
                c0 = 512 * j
                p0, p1 = pk[j % 2], pk[2 + j % 2]
                k.t.op("matmul", p0[0:1, 0:w], ones_c[0:NJ, :],
                       cm[0:NJ, c0:c0 + w], start=True, stop=True,
                       ins=[B("ones_c"), B("cm")], outs=[B(f"pk{j % 2}")])
                k.v.op("tensor_copy", j1r[0:1, 512 * j:512 * j + w], p0[0:1, 0:w],
                       ins=[B(f"pk{j % 2}")], outs=[B("j1r")])
                k.t.op("matmul", p1[0:1, 0:w], dcarc[:], cm[0:NJ, c0:c0 + w],
                       start=True, stop=True, ins=[B("dcarc"), B("cm")],
                       outs=[B(f"pk{2 + j % 2}")])
                k.v.op("tensor_copy", cofr[0:1, 512 * j:512 * j + w], p1[0:1, 0:w],
                       ins=[B(f"pk{2 + j % 2}")], outs=[B("cofr")])
            k.v.op("tensor_scalar", flr[:], j1r[:], 128.0, -128.0, ALU.mult,
                   ALU.add, ins=[B("j1r")], outs=[B("flr")])
            k.v.op("tensor_tensor", flr[:], flr[:],
                   io_s[0:1, CH5 * ph:CH5 * (ph + 1)], ALU.add,
                   ins=[B("flr"), B("io_s")], outs=[B("flr")])
            k.v.op("tensor_tensor", flr[:], flr[:], cofr[:], ALU.subtract,
                   ins=[B("flr"), B("cofr")], outs=[B("flr")])
            k.v.op("tensor_scalar", flr[:], flr[:], float(T - 1), None, ALU.min,
                   ins=[B("flr")], outs=[B("flr")])
            k.v.op("tensor_scalar", j1r[:], j1r[:], -1.0, None, ALU.add,
                   ins=[B("j1r")], outs=[B("j1r")])
            k.qg.dma(fl_d.ap()[0:1, CH5 * ph:CH5 * (ph + 1)], flr[:],
                     ins=[B("flr")], outs=[B("fl_d")])
            k.qg.dma(jr_d.ap()[0:1, CH5 * ph:CH5 * (ph + 1)], j1r[:],
                     ins=[B("j1r")], outs=[B("jr_d")])
        k.qg.wait_all(k.y)
        flv = k.sb("flv", [NS, 128], scope=sc5)
        jrv = k.sb("jrv", [NS, 128], scope=sc5)
        k.qy.dma(flv[:], fl_d.ap().rearrange("o (a p) -> (o a) p", p=128),
                 ins=[B("fl_d")], outs=[B("flv")])
        k.qy.dma(jrv[:], jr_d.ap().rearrange("o (a p) -> (o a) p", p=128),
                 ins=[B("jr_d")], outs=[B("jrv")])
        k.t.op("transpose", pk[0][:, 0:NS], flv[:], id_f[0:NS, 0:NS],
               ins=[B("flv"), B("id_f")], outs=[B("pk0")])
        flT = k.sb("flT", [128, NS], scope=sc5)
        k.v.op("tensor_copy", flT[:], pk[0][:, 0:NS], ins=[B("pk0")], outs=[B("flT")])
        k.t.op("transpose", pk[1][:, 0:NS], jrv[:], id_f[0:NS, 0:NS],
               ins=[B("jrv"), B("id_f")], outs=[B("pk1")])
        jrT = k.sb("jrT", [128, NS], scope=sc5)
        k.v.op("tensor_copy", jrT[:], pk[1][:, 0:NS], ins=[B("pk1")], outs=[B("jrT")])
        flTi = k.sb("flTi", [128, NS], I32, scope=sc5)
        k.v.op("tensor_copy", flTi[:], flT[:], ins=[B("flT")], outs=[B("flTi")])
        ctg = k.sb("ctg", [128, NS], scope=sc5)
        for s in range(NS):
            k.qg.indirect(ctg[:, s:s + 1], None, ct_d.ap(),
                          IndirectOffsetOnAxis(flTi[:, s:s + 1], 0),
                          ins=[B("flTi"), B("ct_d")], outs=[B("ctg")])
        xf_t = k.sb("xf_t", [128, NS], scope=sc5)
        k.v.op("tensor_scalar", xf_t[:], jrT[:], 128.0, None, ALU.mult,
               ins=[B("jrT")], outs=[B("xf_t")])
        k.v.op("tensor_tensor", xf_t[:], xf_t[:], ctg[:], ALU.add,
               ins=[B("xf_t"), B("ctg")], outs=[B("xf_t")])
        k.v.op("tensor_scalar", xf_t[:], xf_t[:], -1.0, 0.0, ALU.add, ALU.max,
               ins=[B("xf_t")], outs=[B("xf_t")])
        k.v.op("tensor_copy", xidx_s[:], xf_t[:], ins=[B("xf_t")], outs=[B("xidx_s")])
    k.fence()



    NG = min(8, NS, NT)
    with ExitStack() as sc6:
        sc6x = ExitStack()
        SS6 = SS
        p6w = [("xin", [128, H], F32), ("xinB", [128, H], F32),
               ("he", [128, H], F32), ("tgp", [128, H], F32)]
        for S in SS6:
            for nm, shape, dt in p6w:
                S.h[nm] = k.sb(f"{nm}_u{S.si}", shape, dt, scope=sc6x)
            k.v.op("memset", S["xin"][:], 0.0, outs=[S.b("xin")])
            k.v.op("memset", S["xinB"][:], 0.0, outs=[S.b("xinB")])

        def p6_gather(s, S, i):
            nm = "xin" if i % 2 == 0 else "xinB"
            k.qg.indirect(S[nm][:], None, x_full.ap(),
                          IndirectOffsetOnAxis(xidx_s[:, s:s + 1], 0),
                          ins=[B("xidx_s"), B("x_full")], outs=[S.b(nm)])

        def p6_chain(s, S, i):
            xin = S["xin"] if i % 2 == 0 else S["xinB"]
            bxin = S.b("xin") if i % 2 == 0 else S.b("xinB")
            he, tgp = S["he"], S["tgp"]
            hv = tgp
            bhe, btgp = S.b("he"), S.b("tgp")
            bhv = btgp
            st = [S[f"st{i}"] for i in range(8)]
            sb_ = [S.b(f"st{i}") for i in range(8)]
            scr_, bscr = S["scr"], S.b("scr")
            yield
            yield from adaln_tile(S, xin[:], bxin, he[:], bhe, hv, bhv)
            k.s.op("activation", scr_[:], he[:], AF.Square, bias=0.0, scale=1.0,
                   accum_out=st[0][:], ins=[bhe], outs=[bscr, sb_[0]])
            yield
            yield from rsqrt_col(S, 1, 0, 1.0 / H, 1e-8)
            k.v.op("tensor_tensor", tgp[:], he[:], gpnb[:], ALU.mult,
                   ins=[bhe, B("gpnb")], outs=[btgp])
            yield
            k.v.op("tensor_reduce", st[2][:], tgp[:], AX.X, ALU.max,
                   apply_absolute_value=True, ins=[btgp], outs=[sb_[2]])
            yield
            k.v.op("tensor_scalar", st[3][:], st[2][:], 1e-30, None, ALU.max,
                   ins=[sb_[2]], outs=[sb_[3]])
            k.v.op("reciprocal", st[3][:], st[3][:], ins=[sb_[3]], outs=[sb_[3]])
            k.v.op("tensor_scalar", st[3][:], st[3][:], 127.0, None, ALU.mult,
                   ins=[sb_[3]], outs=[sb_[3]])
            yield
            k.s.op("activation", scr_[:], tgp[:], AF.Identity, bias=RC,
                   scale=st[3][:, 0:1], ins=[btgp, sb_[3], B("rc_t")],
                   outs=[bscr])
            yield
            k.qg.dma(ng6_d.ap()[128 * s:128 * (s + 1), :], scr_[:],
                     ins=[bscr], outs=[B("ng6_d")])
            k.v.op("tensor_tensor", st[4][:], st[1][:], st[2][:], ALU.mult,
                   ins=[sb_[1], sb_[2]], outs=[sb_[4]])
            k.v.op("tensor_scalar", st[4][:], st[4][:], 1e-5, None, ALU.max,
                   ins=[sb_[4]], outs=[sb_[4]])
            k.v.op("tensor_tensor", agp_s[:, s:s + 1], st[4][:], mgpc_b[:],
                   ALU.mult, ins=[sb_[4], B("mgpc_b")], outs=[B("agp_s")])
            yield

        def p6_stream(si):
            S = SS6[si]
            tiles = list(range(si, NS, 3))
            p6_gather(tiles[0], S, 0)
            for i, s in enumerate(tiles):
                if i + 1 < len(tiles):
                    p6_gather(tiles[i + 1], S, i + 1)
                yield from p6_chain(s, S, i)

        interleave([p6_stream(si) for si in range(3)])
        sc6x.close()
        k.fence()
        k.qg.wait_all(k.y)
        dpwt = k.sb("dpwt", [128, 16 * 1024], BF16, scope=sc6a)
        for kk in range(16):
            k.qy.dma_t(dpwt[:, 1024 * kk:1024 * (kk + 1)],
                       dpq_d.ap()[:, 128 * kk:128 * (kk + 1)],
                       ins=[B("dpq_d")], outs=[B("dpwt")])
        prod = k.sb("prod", [128, IM], scope=sc6a)
        uu2 = [k.sb(f"uu{i}", [128, IM], scope=sc6a) for i in range(2)]
        n2tl = k.sb("n2tl", [128, 16 * 128], BF16, scope=sc6a)
        ctrb = k.sb("ctrb", [128, H], scope=sc6a)
        ngs2 = [k.sb(f"ngs{i}", [128, H], F32, scope=sc6a) for i in range(2)]
        nmTs = k.sb("nmTs", [128, 8 * 128], BF16, scope=sc6a)

        def load_ngs(s):
            k.qs.dma(ngs2[s % 2][:], ng6_d.ap()[128 * s:128 * (s + 1), :],
                     ins=[B("ng6_d")], outs=[B(f"ngs{s % 2}")])

        def gp_part(s, si):
            bngs = B(f"ngs{s % 2}")
            uu, buu = uu2[s % 2], B(f"uu{s % 2}")
            for kk in range(8):
                tb = kk // 4
                k.t.op("transpose", pk[tb][:, 128 * (kk % 4):128 * (kk % 4) + 128],
                       ngs2[s % 2][:, 128 * kk:128 * (kk + 1)], id_f[:],
                       ins=[bngs, B("id_f")], outs=[B(f"pk{tb}")],
                       defer=(kk % 4 != 3))
                if kk % 4 == 3:
                    k.s.op("activation", nmTs[:, 512 * tb:512 * (tb + 1)],
                           pk[tb][:, 0:512], AF.Copy, bias=-RC, scale=1.0,
                           ins=[B(f"pk{tb}"), B("mrc_t")], outs=[B("nmTs")])
            for half in range(2):
                pb = 4 * half
                for kk in range(8):
                    for nch in range(4):
                        k.t.op("matmul", pk[pb + nch][:, 0:512],
                               nmTs[:, 128 * kk:128 * (kk + 1)],
                               gpwt[:, 4096 * kk + 2048 * half + 512 * nch:
                                    4096 * kk + 2048 * half + 512 * (nch + 1)],
                               start=(kk == 0), stop=(kk == 7),
                               ins=[B("nmTs"), B("gpwt")], outs=[B(f"pk{pb + nch}")],
                               defer=not (kk == 7 and nch == 3))
                if half == 0:
                    for nch in range(4):
                        k.s.op("activation", prod[:, 512 * nch:512 * (nch + 1)],
                               pk[pb + nch][:, 0:512], AF.Silu,
                               bias=0.0, scale=agp_s[:, s:s + 1],
                               ins=[B(f"pk{pb + nch}"), B("agp_s")], outs=[B("prod")])
                else:
                    for nch in range(4):
                        k.v.op("scalar_tensor_tensor", prod[:, 512 * nch:512 * (nch + 1)],
                               pk[pb + nch][:, 0:512], agp_s[:, s:s + 1],
                               prod[:, 512 * nch:512 * (nch + 1)], ALU.mult, ALU.mult,
                               ins=[B(f"pk{pb + nch}"), B("agp_s"), B("prod")],
                               outs=[B("prod")])
            k.s.op("activation", uu[:], prod[:], AF.Square, bias=0.0, scale=1.0,
                   accum_out=stt[0][:], ins=[B("prod")],
                   outs=[buu, B("stt0")])
            drive(rsqrt_col(LS, 1, 0, 1.0 / IM, 1e-8))
            k.v.op("tensor_tensor", uu[:], prod[:], dpnb[:], ALU.mult,
                   ins=[B("prod"), B("dpnb")], outs=[buu])
            k.v.op("tensor_reduce", stt[2][:], uu[:], AX.X, ALU.max,
                   apply_absolute_value=True, ins=[buu], outs=[B("stt2")])
            k.v.op("tensor_scalar", stt[3][:], stt[2][:], 1e-30, None, ALU.max,
                   ins=[B("stt2")], outs=[B("stt3")])
            k.v.op("reciprocal", stt[3][:], stt[3][:], ins=[B("stt3")],
                   outs=[B("stt3")])
            k.v.op("tensor_scalar", stt[3][:], stt[3][:], 127.0, None, ALU.mult,
                   ins=[B("stt3")], outs=[B("stt3")])
            k.s.op("activation", uu[:], uu[:], AF.Identity, bias=RC,
                   scale=stt[3][:, 0:1], ins=[buu, B("stt3"), B("rc_t")],
                   outs=[buu])
            k.v.op("tensor_tensor", stt[4][:], stt[1][:], stt[2][:], ALU.mult,
                   ins=[B("stt1"), B("stt2")], outs=[B("stt4")])
            k.v.op("tensor_scalar", stt[4][:], stt[4][:], 1e-5, None, ALU.max,
                   ins=[B("stt4")], outs=[B("stt4")])
            k.v.op("tensor_tensor", adp_s[:, s:s + 1], stt[4][:], mdpc_b[:],
                   ALU.mult, ins=[B("stt4"), B("mdpc_b")], outs=[B("adp_s")])

        def dp_part(s):
            uu, buu = uu2[s % 2], B(f"uu{s % 2}")
            for kk in range(16):
                tb = 2 + kk // 4
                k.t.op("transpose", pk[tb][:, 128 * (kk % 4):128 * (kk % 4) + 128],
                       uu[:, 128 * kk:128 * (kk + 1)], id_f[:],
                       ins=[buu, B("id_f")], outs=[B(f"pk{tb}")],
                       defer=(kk % 4 != 3))
                if kk % 4 == 3:
                    k.s.op("activation", n2tl[:, 512 * (tb - 2):512 * (tb - 1)],
                           pk[tb][:, 0:512], AF.Copy, bias=-RC, scale=1.0,
                           ins=[B(f"pk{tb}"), B("mrc_t")], outs=[B("n2tl")])
            for kk in range(16):
                for nch in range(2):
                    k.t.op("matmul", pk[nch][:, 0:512],
                           n2tl[:, 128 * kk:128 * (kk + 1)],
                           dpwt[:, 1024 * kk + 512 * nch:1024 * kk + 512 * (nch + 1)],
                           start=(kk == 0), stop=(kk == 15),
                           ins=[B("n2tl"), B("dpwt")], outs=[B(f"pk{nch}")],
                           defer=not (kk == 15 and nch == 1))
            for nch in range(2):
                k.s.op("activation", ctrb[:, 512 * nch:512 * (nch + 1)],
                       pk[nch][:, 0:512], AF.Copy, bias=0.0,
                       scale=adp_s[:, s:s + 1],
                       ins=[B(f"pk{nch}"), B("adp_s")], outs=[B("ctrb")])
            k.qg.dma(contribP.ap()[128 * s:128 * (s + 1), :], ctrb[:],
                     ins=[B("ctrb")], outs=[B("contribP")])
            if (s + 1) % (NS // NCH_AG) == 0:
                c = (s + 1) // (NS // NCH_AG) - 1
                k.qg.wait_all(k.g)
                k.qc.collective(
                    "AllGather", ALU.bypass, RG,
                    [contribP.ap()[CHS * c:CHS * (c + 1), :]],
                    [contribF.ap()[E * CHS * c:E * CHS * (c + 1), :]],
                    ins=[B("contribP")], outs=[B("contribF")])

        load_ngs(0)
        for s in range(NS):
            if s + 1 < NS:
                load_ngs(s + 1)
            gp_part(s, s % NG)
            if s >= 1:
                dp_part(s - 1)
        dp_part(NS - 1)
        sc6a.close()
    k.fence(skip=(k.qc,))

    sc9 = ExitStack()
    wgt = k.sb("wgt", [128, 8 * IF], BF16, scope=sc9)
    wut = k.sb("wut", [128, 8 * IF], BF16, scope=sc9)

    def load_wgut():
        for srcT, srcb, dstt, dstb in ((fqgT_d, "fqgT_d", wgt, "wgt"),
                                       (fquT_d, "fquT_d", wut, "wut")):
            src4 = srcT.ap().rearrange("(c b p) r -> p b c r", c=NCORE, p=128)
            for kk in range(8):
                k.qy.dma(dstt[:, IF * kk:IF * (kk + 1)]
                         .rearrange("p (c r) -> p () c r", r=IFS),
                         src4[:, kk:kk + 1], ins=[B(srcb)], outs=[B(dstb)])

    with ExitStack() as sc7:
        SS7 = SS[:2]
        p7w = [("c1t", [128, H], F32), ("c2t", [128, H], F32),
               ("c1tB", [128, H], F32), ("c2tB", [128, H], F32),
               ("xt7", [128, H], F32)]
        for S in SS7:
            for nm, shape, dt in p7w:
                S.h[nm] = k.sb(f"{nm}_u{S.si}", shape, dt, scope=sc7)
        macc = k.sb("macc", [128, NT], scope=sc7)
        for S in SS7:
            for nm in ("c1t", "c2t", "c1tB", "c2tB"):
                k.v.op("memset", S[nm][:], 0.0, outs=[S.b(nm)])
        k.qc.wait_all(k.g)

        def p7_gather(t, S, i):
            sfx = "" if i % 2 == 0 else "B"
            k.qg.indirect(S["c1t" + sfx][:], None, contribF.ap(),
                          IndirectOffsetOnAxis(cid1_s[:, t:t + 1], 0),
                          ins=[B("cid1_s"), B("contribF")],
                          outs=[S.b("c1t" + sfx)])
            k.qg.indirect(S["c2t" + sfx][:], None, contribF.ap(),
                          IndirectOffsetOnAxis(cid2_s[:, t:t + 1], 0),
                          ins=[B("cid2_s"), B("contribF")],
                          outs=[S.b("c2t" + sfx)])

        def p7_chain(t, S, i):
            sfx = "" if i % 2 == 0 else "B"
            c1t, c2t, xt7 = (S["c1t" + sfx], S["c2t" + sfx], S["xt7"])
            h2t = c2t
            b1, b2, bx = (S.b("c1t" + sfx), S.b("c2t" + sfx), S.b("xt7"))
            bh2 = b2
            st = [S[f"st{i}"] for i in range(4)]
            sb_ = [S.b(f"st{i}") for i in range(4)]
            scr_, bscr = S["scr"], S.b("scr")
            k.qs.dma(xt7[:], xown.ap()[128 * t:128 * (t + 1), :],
                     ins=[B("xown")], outs=[bx])
            yield
            k.s.op("activation", scr_[:], c1t[:], AF.Copy, bias=0.0,
                   scale=wk1_s[:, t:t + 1], ins=[b1, B("wk1_s")], outs=[bscr])
            yield
            k.v.op("scalar_tensor_tensor", scr_[:], c2t[:], wk2_s[:, t:t + 1],
                   scr_[:], ALU.mult, ALU.add, ins=[b2, B("wk2_s"), bscr],
                   outs=[bscr])
            yield
            k.v.op("tensor_tensor", scr_[:], scr_[:], gaxb[:], ALU.mult,
                   ins=[bscr, B("gaxb")], outs=[bscr])
            k.v.op("tensor_tensor", xt7[:], scr_[:], xt7[:], ALU.add,
                   ins=[bscr, bx], outs=[bx])
            yield
            k.qg.dma(x1_d.ap()[128 * t:128 * (t + 1), :], xt7[:],
                     ins=[bx], outs=[B("x1_d")])
            k.v.op("tensor_reduce", st[0][:], xt7[:], AX.X, ALU.add,
                   ins=[bx], outs=[sb_[0]])
            yield
            k.s.op("activation", c1t[:], xt7[:], AF.Square, bias=0.0, scale=1.0,
                   accum_out=st[1][:], ins=[bx], outs=[b1, sb_[1]])
            yield
            k.v.op("tensor_scalar", st[0][:], st[0][:], 1.0 / H, None, ALU.mult,
                   ins=[sb_[0]], outs=[sb_[0]])
            k.v.op("tensor_tensor", st[2][:], st[0][:], st[0][:], ALU.mult,
                   ins=[sb_[0]], outs=[sb_[2]])
            k.v.op("tensor_scalar", st[3][:], st[1][:], 1.0 / H, None, ALU.mult,
                   ins=[sb_[1]], outs=[sb_[3]])
            k.v.op("tensor_tensor", st[3][:], st[3][:], st[2][:], ALU.subtract,
                   ins=[sb_[3], sb_[2]], outs=[sb_[3]])
            k.v.op("tensor_scalar", st[3][:], st[3][:], 1e-5, None, ALU.add,
                   ins=[sb_[3]], outs=[sb_[3]])
            yield
            k.s.op("activation", st[3][:], st[3][:], AF.Sqrt, bias=0.0, scale=1.0,
                   ins=[sb_[3]], outs=[sb_[3]])
            yield
            k.v.op("reciprocal", st[3][:], st[3][:], ins=[sb_[3]], outs=[sb_[3]])
            k.v.op("tensor_tensor", st[2][:], st[0][:], st[3][:], ALU.mult,
                   ins=[sb_[0], sb_[3]], outs=[sb_[2]])
            k.v.op("tensor_scalar", st[2][:], st[2][:], -1.0, None, ALU.mult,
                   ins=[sb_[2]], outs=[sb_[2]])
            yield
            k.s.op("activation", h2t[:], xt7[:], AF.Identity, bias=st[2][:, 0:1],
                   scale=st[3][:, 0:1], ins=[bx, sb_[2], sb_[3]], outs=[bh2])
            yield
            k.v.op("tensor_tensor", h2t[:], h2t[:], cb1b[:], ALU.mult,
                   ins=[bh2, B("cb1b")], outs=[bh2])
            k.v.op("tensor_tensor", h2t[:], h2t[:], sbb[:], ALU.add,
                   ins=[bh2, B("sbb")], outs=[bh2])
            yield
            k.qg.dma(h2_d.ap()[128 * t:128 * (t + 1), :], h2t[:],
                     ins=[bh2], outs=[B("h2_d")])
            k.v.op("tensor_reduce", macc[:, t:t + 1], h2t[:], AX.X, ALU.max,
                   apply_absolute_value=True, ins=[bh2], outs=[B("macc")])
            yield

        def p7_stream(si):
            S = SS7[si]
            tiles = list(range(si, NT, 2))
            p7_gather(tiles[0], S, 0)
            for i, t in enumerate(tiles):
                if i + 1 < len(tiles):
                    p7_gather(tiles[i + 1], S, i + 1)
                yield from p7_chain(t, S, i)

        interleave([p7_stream(si) for si in range(2)])
        load_wgut()
        k.v.op("tensor_reduce", stt[0][:], macc[:], AX.X, ALU.max,
               ins=[B("macc")], outs=[B("stt0")])
        k.t.op("transpose", pk[0][0:1, 0:128], stt[0][:], id_f[:],
               ins=[B("stt0"), B("id_f")], outs=[B("pk0")])
        mrow1 = k.sb("mrow1", [1, 128], scope=sc7)
        k.v.op("tensor_copy", mrow1[:], pk[0][0:1, 0:128], ins=[B("pk0")],
               outs=[B("mrow1")])
        gmx = k.sb("gmx", [1, 8], scope=sc7)
        k.v.op("memset", gmx[:], 0.0, outs=[B("gmx")])
        k.v.op("tensor_reduce", gmx[0:1, 0:1], mrow1[:], AX.X, ALU.max,
               ins=[B("mrow1")], outs=[B("gmx")])
        k.qg.dma(gm1P.ap(), gmx[:], ins=[B("gmx")], outs=[B("gm1P")])
        k.qc.collective("AllGather", ALU.bypass, RG, [gm1P.ap()], [gm1F.ap()],
                        ins=[B("gm1P")], outs=[B("gm1F")])
        gmall = k.sb("gmall", [1, 64], scope=sc7)
        k.qy.dma(gmall[:], gm1F.ap().rearrange("a b -> () (a b)"),
                 ins=[B("gm1F")], outs=[B("gmall")])
        k.v.op("tensor_reduce", gmx[0:1, 0:1], gmall[:], AX.X, ALU.max,
               ins=[B("gmall")], outs=[B("gmx")])
        clx = k.sb("clx", [1, 1], scope=sc7)
        k.qy.dma(clx[:], clipx_in.ap(), ins=[B("clipx_in")], outs=[B("clx")])
        k.s.op("activation", clx[:], clx[:], AF.Sigmoid, bias=0.0, scale=1.0,
               ins=[B("clx")], outs=[B("clx")])
        k.v.op("tensor_tensor", clx[:], clx[:], gmx[0:1, 0:1], ALU.mult,
               ins=[B("clx"), B("gmx")], outs=[B("clx")])
        k.v.op("tensor_scalar", clx[:], clx[:], 1.0 / 127.0, None, ALU.mult,
               ins=[B("clx")], outs=[B("clx")])
        sxv_ = bcast_row("sxv_", clx[:], B("clx"), 1, scope=sc7)
        k.v.op("tensor_copy", sxv[:], sxv_[:], ins=[B("sxv_")], outs=[B("sxv")])
        k.v.op("reciprocal", sxb[:], sxv[:], ins=[B("sxv")], outs=[B("sxb")])
        for S in SS7:
            S.h["n1t_"] = k.sb(f"n1t__u{S.si}", [128, H], BF16, scope=sc7)
        k.qg.wait_all(k.y)
        k.qg.wait_all(k.s)

        def n1_chain(t, S):
            h2t, n1t_, scr_ = S["c2t"], S["n1t_"], S["scr"]
            bh2, bn1, bscr = S.b("c2t"), S.b("n1t_"), S.b("scr")
            k.qs.dma(h2t[:], h2_d.ap()[128 * t:128 * (t + 1), :],
                     ins=[B("h2_d")], outs=[bh2])
            yield
            k.s.op("activation", scr_[:], h2t[:], AF.Identity, bias=RC,
                   scale=sxb[:, 0:1], ins=[bh2, B("sxb"), B("rc_t")],
                   outs=[bscr])
            yield
            k.v.op("tensor_scalar", scr_[:], scr_[:], -RC, None, ALU.add,
                   ins=[bscr], outs=[bscr])
            k.v.op("tensor_scalar", n1t_[:], scr_[:], -128.0, 127.0, ALU.max,
                   ALU.min, ins=[bscr], outs=[bn1])
            yield
            k.qg.dma(n1_d.ap()[128 * t:128 * (t + 1), :], n1t_[:],
                     ins=[bn1], outs=[B("n1_d")])
            yield

        interleave([(lambda si: (x for t in range(si, NT, 2)
                                 for x in n1_chain(t, SS7[si])))(si)
                    for si in range(2)])
    k.fence()

    ich = [(512 * i, min(512, IF - 512 * i)) for i in range(cdiv(IF, 512))]
    NI = len(ich)
    NK2 = cdiv(IF, 128)
    if True:
        n1tt2 = [k.sb(f"n1tt{i}", [128, 8 * 128 * NG], BF16, scope=sc9)
                 for i in range(2)]
        sgf = k.sb("sgf", [128, 512], scope=sc9)
        hyf = k.sb("hyf", [128, 512], scope=sc9)
        prodf = k.sb("prodf", [128, 512], BF16, scope=sc9)
        maccf = k.sb("maccf", [128, NT * NI], scope=sc9)

        def load_n1tt(g):
            for kk in range(8):
                k.qy.dma_t(n1tt2[g % 2][:, 128 * NG * kk:128 * NG * (kk + 1)],
                           n1_d.ap()[128 * NG * g:128 * NG * (g + 1),
                                     128 * kk:128 * (kk + 1)],
                           ins=[B("n1_d")], outs=[B(f"n1tt{g % 2}")])
        load_n1tt(0)
        for g in range(cdiv(NT, NG)):
            if g + 1 < cdiv(NT, NG):
                load_n1tt(g + 1)
            n1tt = n1tt2[g % 2]
            bn1tt = B(f"n1tt{g % 2}")
            for si in range(NG):
                t = NG * g + si
                for c0 in range(0, NI, 2):
                    nch_here = min(2, NI - c0)
                    pb = 4 * ((c0 // 2 + t) % 2)
                    for kk in range(8):
                        for ci in range(nch_here):
                            off, wdt = ich[c0 + ci]
                            k.t.op("matmul", pk[pb + 2 * ci][:, 0:wdt],
                                   n1tt[:, 128 * NG * kk + 128 * si:
                                        128 * NG * kk + 128 * (si + 1)],
                                   wgt[:, IF * kk + off:IF * kk + off + wdt],
                                   start=(kk == 0), stop=(kk == 7),
                                   ins=[bn1tt, B("wgt")],
                                   outs=[B(f"pk{pb + 2 * ci}")], defer=True)
                            k.t.op("matmul", pk[pb + 2 * ci + 1][:, 0:wdt],
                                   n1tt[:, 128 * NG * kk + 128 * si:
                                        128 * NG * kk + 128 * (si + 1)],
                                   wut[:, IF * kk + off:IF * kk + off + wdt],
                                   start=(kk == 0), stop=(kk == 7),
                                   ins=[bn1tt, B("wut")],
                                   outs=[B(f"pk{pb + 2 * ci + 1}")],
                                   defer=not (kk == 7 and ci == nch_here - 1))
                    for ci in range(nch_here):
                        off, wdt = ich[c0 + ci]
                        k.s.op("activation", sgf[:, 0:wdt], pk[pb + 2 * ci][:, 0:wdt],
                               AF.Silu, bias=0.0, scale=sxv[:, 0:1],
                               ins=[B(f"pk{pb + 2 * ci}"), B("sxv")], outs=[B("sgf")])
                        k.s.op("activation", hyf[:, 0:wdt], pk[pb + 2 * ci + 1][:, 0:wdt],
                               AF.Copy, bias=0.0, scale=sxv[:, 0:1],
                               ins=[B(f"pk{pb + 2 * ci + 1}"), B("sxv")], outs=[B("hyf")])
                        k.v.op("tensor_tensor", prodf[:, 0:wdt], sgf[:, 0:wdt],
                               hyf[:, 0:wdt], ALU.mult,
                               ins=[B("sgf"), B("hyf")], outs=[B("prodf")])
                        k.v.op("tensor_reduce", maccf[:, t * NI + c0 + ci:
                                                      t * NI + c0 + ci + 1],
                               prodf[:, 0:wdt], AX.X, ALU.max,
                               apply_absolute_value=True,
                               ins=[B("prodf")], outs=[B("maccf")])
                        k.qg.dma(prod_d.ap()[128 * t:128 * (t + 1),
                                             off:off + wdt], prodf[:, 0:wdt],
                                 ins=[B("prodf")], outs=[B("prod_d")])
        k.v.op("tensor_reduce", stt[0][:], maccf[:], AX.X, ALU.max,
               ins=[B("maccf")], outs=[B("stt0")])
        k.t.op("transpose", pk[0][0:1, 0:128], stt[0][:], id_f[:],
               ins=[B("stt0"), B("id_f")], outs=[B("pk0")])
        mrow2 = k.sb("mrow2", [1, 128], scope=sc9)
        k.v.op("tensor_copy", mrow2[:], pk[0][0:1, 0:128], ins=[B("pk0")],
               outs=[B("mrow2")])
        gmx2 = k.sb("gmx2", [1, 8], scope=sc9)
        k.v.op("memset", gmx2[:], 0.0, outs=[B("gmx2")])
        k.v.op("tensor_reduce", gmx2[0:1, 0:1], mrow2[:], AX.X, ALU.max,
               ins=[B("mrow2")], outs=[B("gmx2")])
        k.qg.dma(gm2P.ap(), gmx2[:], ins=[B("gmx2")], outs=[B("gm2P")])
        k.qc.collective("AllGather", ALU.bypass, RG, [gm2P.ap()], [gm2F.ap()],
                        ins=[B("gm2P")], outs=[B("gm2F")])
        gmall2 = k.sb("gmall2", [1, 64], scope=sc9)
        k.qy.dma(gmall2[:], gm2F.ap().rearrange("a b -> () (a b)"),
                 ins=[B("gm2F")], outs=[B("gmall2")])
        k.v.op("tensor_reduce", gmx2[0:1, 0:1], gmall2[:], AX.X, ALU.max,
               ins=[B("gmall2")], outs=[B("gmx2")])
        cld = k.sb("cld", [1, 1], scope=sc9)
        k.qy.dma(cld[:], clipd_in.ap(), ins=[B("clipd_in")], outs=[B("cld")])
        k.s.op("activation", cld[:], cld[:], AF.Sigmoid, bias=0.0, scale=1.0,
               ins=[B("cld")], outs=[B("cld")])
        k.v.op("tensor_tensor", cld[:], cld[:], gmx2[0:1, 0:1], ALU.mult,
               ins=[B("cld"), B("gmx2")], outs=[B("cld")])
        k.v.op("tensor_scalar", cld[:], cld[:], 1.0 / 127.0, None, ALU.mult,
               ins=[B("cld")], outs=[B("cld")])
        s2v_ = bcast_row("s2v_", cld[:], B("cld"), 1, scope=sc9)
        k.v.op("tensor_copy", s2v[:], s2v_[:], ins=[B("s2v_")], outs=[B("s2v")])
        k.v.op("reciprocal", s2b[:], s2v[:], ins=[B("s2v")], outs=[B("s2b")])
        sc9.close()
        k.fence()
        scd = ExitStack()
        wdt_t = k.sb("wdt_t", [128, NK2 * H], BF16, scope=scd)
        wdsg = k.sb("wdsg", [128, IF], BF16, scope=scd)
        wdsf = k.sb("wdsf", [128, IF], F32, scope=scd)
        for r in range(H // 128):
            k.qy.dma(wdsg[:], fqd_d.ap()[128 * r:128 * (r + 1), :],
                     ins=[B("fqd_d")], outs=[B("wdsg")])
            k.v.op("tensor_copy", wdsf[:], wdsg[:], ins=[B("wdsg")],
                   outs=[B("wdsf")])
            wdt3 = wdt_t.ap().rearrange("p (a c) -> p a c", c=H)
            for kk in range(NK2):
                j, jj = kk // 4, kk % 4
                gw = min(4, NK2 - 4 * j)
                tb = j % 4
                k.t.op("transpose", pk[tb][:, 128 * jj:128 * jj + 128],
                       wdsf[:, 128 * kk:128 * (kk + 1)], id_f[:],
                       ins=[B("wdsf"), B("id_f")], outs=[B(f"pk{tb}")],
                       defer=(jj != gw - 1))
                if jj == gw - 1:
                    k.s.op("activation",
                           wdt3[:, 4 * j:4 * j + gw, 128 * r:128 * (r + 1)],
                           pk[tb][:, 0:128 * gw]
                           .rearrange("p (a c) -> p a c", c=128),
                           AF.Copy, bias=0.0, scale=1.0,
                           ins=[B(f"pk{tb}")], outs=[B("wdt_t")])
        scq = ExitStack()
        for S in SS[:2]:
            S.h["prt"] = k.sb(f"prt_u{S.si}", [128, IF], BF16, scope=scq)
            S.h["prtf"] = k.sb(f"prtf_u{S.si}", [128, IF], F32, scope=scq)
            S.h["n2ft"] = k.sb(f"n2ft_u{S.si}", [128, IF], BF16, scope=scq)
        k.qg.wait_all(k.y)
        k.qg.wait_all(k.s)

        def n2f_chain(t, S):
            prt, prtf, n2ft = S["prt"], S["prtf"], S["n2ft"]
            bp, bpf, bn2 = S.b("prt"), S.b("prtf"), S.b("n2ft")
            k.qs.dma(prt[:], prod_d.ap()[128 * t:128 * (t + 1), :],
                     ins=[B("prod_d")], outs=[bp])
            yield
            k.s.op("activation", prtf[:], prt[:], AF.Identity, bias=RC,
                   scale=s2b[:, 0:1], ins=[bp, B("s2b"), B("rc_t")], outs=[bpf])
            yield
            k.v.op("tensor_scalar", prtf[:], prtf[:], -RC, None, ALU.add,
                   ins=[bpf], outs=[bpf])
            k.v.op("tensor_scalar", n2ft[:], prtf[:], -128.0, 127.0, ALU.max,
                   ALU.min, ins=[bpf], outs=[bn2])
            yield
            k.qg.dma(n2f_d.ap()[128 * t:128 * (t + 1), :], n2ft[:],
                     ins=[bn2], outs=[B("n2f_d")])
            yield

        interleave([(lambda si: (x for t in range(si, NT, 2)
                                 for x in n2f_chain(t, SS[si])))(si)
                    for si in range(2)])
    scq.close()
    k.fence()

    NGd = min(4, NG)
    if True:
        n2tt2 = [k.sb(f"n2tt{i}", [128, NK2 * 128 * NGd], BF16, scope=scd)
                 for i in range(2)]
        x1r2 = [k.sb(f"x1r{i}", [128, H], scope=scd) for i in range(2)]
        fot2 = [k.sb(f"fot{i}", [128, H], scope=scd) for i in range(2)]

        def load_n2tt(g):
            for kk in range(NK2):
                k.qy.dma_t(n2tt2[g % 2][:, 128 * NGd * kk:128 * NGd * (kk + 1)],
                           n2f_d.ap()[128 * NGd * g:128 * NGd * (g + 1),
                                      128 * kk:128 * (kk + 1)],
                           ins=[B("n2f_d")], outs=[B(f"n2tt{g % 2}")])
        load_n2tt(0)
        for g in range(NT // NGd):
            if g + 1 < NT // NGd:
                load_n2tt(g + 1)
            n2tt = n2tt2[g % 2]
            bn2tt = B(f"n2tt{g % 2}")
            for si in range(NGd):
                t = NGd * g + si
                pb = 2 * (si % 4)
                for kk in range(NK2):
                    for nch in range(2):
                        k.t.op("matmul", pk[pb + nch][:, 0:512],
                               n2tt[:, 128 * NGd * kk + 128 * si:
                                    128 * NGd * kk + 128 * (si + 1)],
                               wdt_t[:, H * kk + 512 * nch:H * kk + 512 * (nch + 1)],
                               start=(kk == 0), stop=(kk == NK2 - 1),
                               ins=[bn2tt, B("wdt_t")], outs=[B(f"pk{pb + nch}")],
                               defer=not (kk == NK2 - 1 and nch == 1))
                x1r = x1r2[si % 2]
                fot = fot2[si % 2]
                bx1r, bfot = B(f"x1r{si % 2}"), B(f"fot{si % 2}")
                k.qs.dma(x1r[:], x1_d.ap()[128 * t:128 * (t + 1), :],
                         ins=[B("x1_d")], outs=[bx1r])
                for nch in range(2):
                    k.s.op("activation", fot[:, 512 * nch:512 * (nch + 1)],
                           pk[pb + nch][:, 0:512], AF.Copy, bias=0.0,
                           scale=s2v[:, 0:1],
                           ins=[B(f"pk{pb + nch}"), B("s2v")], outs=[bfot])
                k.v.op("tensor_tensor", fot[:], fot[:], gbxb[:], ALU.mult,
                       ins=[bfot, B("gbxb")], outs=[bfot])
                k.v.op("tensor_tensor", fot[:], fot[:], x1r[:], ALU.add,
                       ins=[bfot, bx1r], outs=[bfot])
                k.qo.dma(out.ap()[128 * t:128 * (t + 1), :], fot[:],
                         ins=[bfot], outs=[B("out")])
        k.qo.wait_all(k.g)
        k.qo.wait_all(k.y)
        scd.close()
    return k


def _shard_inputs(inputs, T):
    x = np.ascontiguousarray(np.asarray(inputs["L0_attn_x"],
                                        dtype=np.float32).reshape(T, H))
    wmod = np.asarray(inputs["L0_wmod"], dtype=np.float32)
    W6 = 6 * H // NCORE
    gpw = np.asarray(inputs["L0_attn_gp_w"], dtype=np.float32)
    dpw = np.asarray(inputs["L0_attn_dp_w"], dtype=np.float32)
    wg = np.asarray(inputs["L0_ffn_w_gate"], dtype=np.float32)
    wu = np.asarray(inputs["L0_ffn_w_up"], dtype=np.float32)
    wd = np.asarray(inputs["L0_ffn_w_down"], dtype=np.float32)
    TL = T // NCORE
    IFS = IF // NCORE
    HS = H // NCORE
    in_maps = []
    for c in range(NCORE):
        in_maps.append({
            "x_full": x,
            "xown": np.ascontiguousarray(x[TL * c:TL * (c + 1)]),
            "c_vec": np.asarray(inputs["L0_c"], dtype=np.float32).reshape(1, H),
            "wmod_sl": np.ascontiguousarray(wmod[:, W6 * c:W6 * (c + 1)]),
            "gnw_in": np.asarray(inputs["L0_attn_gate_norm_w"],
                                 dtype=np.float32).reshape(1, H),
            "ginw_in": np.asarray(inputs["L0_attn_gate_in_norm_w"],
                                  dtype=np.float32).reshape(1, H),
            "gatew_in": np.asarray(inputs["L0_attn_gate_w"], dtype=np.float32),
            "gpn_in": np.ascontiguousarray(
                np.asarray(inputs["L0_attn_gp_norm_w"],
                           dtype=np.float32)[c].reshape(1, H)),
            "gpw_in": np.ascontiguousarray(gpw[c]),
            "dpn_in": np.ascontiguousarray(
                np.asarray(inputs["L0_attn_dp_norm_w"],
                           dtype=np.float32)[c].reshape(1, IM)),
            "dpw_in": np.ascontiguousarray(dpw[c]),
            "clipx_in": np.asarray(inputs["L0_ffn_clip_x"],
                                   dtype=np.float32).reshape(1, 1),
            "clipd_in": np.asarray(inputs["L0_ffn_clip_down"],
                                   dtype=np.float32).reshape(1, 1),
            "wg_sl": np.ascontiguousarray(wg[IFS * c:IFS * (c + 1)]),
            "wu_sl": np.ascontiguousarray(wu[IFS * c:IFS * (c + 1)]),
            "wd_sl": np.ascontiguousarray(wd[HS * c:HS * (c + 1)]),
        })
    return in_maps


def kernel(**inputs):
    T = int(np.asarray(inputs["L0_attn_x"]).shape[1])
    nc = bass.Bass("TRN2", target_bir_lowering=False)
    with ExitStack() as st:
        build(nc, st, T)
    in_maps = _shard_inputs(inputs, T)
    res = run_bass_kernel_spmd(nc, in_maps, core_ids=list(range(NCORE)))
    TL = T // NCORE
    full = np.concatenate([res.results[c]["out"] for c in range(NCORE)], axis=0)
    return full.reshape(1, T, H).astype(np.float32)


# version 48 (final)
# speedup vs baseline: 1.0804x; 1.0091x over previous
import numpy as np
from contextlib import ExitStack
import concourse.bass as bass
import concourse.mybir as mybir
from concourse.bass_utils import run_bass_kernel_spmd
from concourse.bass_core_rust import IndirectOffsetOnAxis

F32 = mybir.dt.float32
BF16 = mybir.dt.bfloat16
I32 = mybir.dt.int32
AF = mybir.ActivationFunctionType
ALU = mybir.AluOpType
AX = mybir.AxisListType
ds = bass.ds
RC = 1.5 * 2 ** 23

H = 1024
E = 8
IM = 2048
IF = 2816
NCORE = 8


class Buf:
    __slots__ = ("name", "writer", "readers")

    def __init__(self, name=""):
        self.name = name
        self.writer = None
        self.readers = {}


class Eng:
    def __init__(self, eng, sem, name):
        self.e = eng
        self.sem = sem
        self.name = name
        self.n = 0
        self.hwm = {}
        self.pending = []
        self.fence = []

    def _wait(self, sem, val):
        if self.hwm.get(sem.name, -1) >= val:
            return
        self.e.wait_ge(sem, val)
        self.hwm[sem.name] = val

    def _pre(self, ins, outs):
        for s, v in self.fence:
            self._wait(s, v)
        self.fence = []
        for b in ins:
            if b.writer is not None:
                self._wait(*b.writer)
        for b in outs:
            if b.writer is not None:
                self._wait(*b.writer)
            for s, v in b.readers.items():
                self._wait(s, v)

    @staticmethod
    def _post(ins, outs, tok):
        sem, val = tok
        for b in ins:
            if b.readers.get(sem, -1) < val:
                b.readers[sem] = val
        for b in outs:
            b.writer = tok
            b.readers = {}

    def op(self, name, *args, ins=(), outs=(), defer=False, **kw):
        self._pre(ins, outs)
        inst = getattr(self.e, name)(*args, **kw)
        if defer:
            self.pending.append((tuple(ins), tuple(outs)))
            return inst
        self.n += 1
        inst.then_inc(self.sem, 1)
        tok = (self.sem, self.n)
        for pins, pouts in self.pending:
            self._post(pins, pouts, tok)
        self.pending = []
        self._post(ins, outs, tok)
        return inst

    def token(self):
        assert not self.pending
        return (self.sem, self.n)


class Q:
    def __init__(self, eng, sems, name, step=16):
        self.eng = eng
        self.sems = sems if isinstance(sems, (list, tuple)) else [sems]
        self.name = name
        self.step = step
        self.n = 0
        self.cnt = [0] * len(self.sems)

    def _issue(self, fn, ins, outs):
        e = self.eng
        i = self.n % len(self.sems)
        if self.cnt[i] > 0:
            e._wait(self.sems[i], self.step * self.cnt[i])
        e._pre(ins, outs)
        inst = fn()
        self.n += 1
        self.cnt[i] += 1
        inst.then_inc(self.sems[i], self.step)
        tok = (self.sems[i], self.step * self.cnt[i])
        Eng._post(ins, outs, tok)
        return tok

    def dma(self, out_ap, in_ap, ins=(), outs=(), **kw):
        return self._issue(lambda: self.eng.e.dma_start(out=out_ap, in_=in_ap, **kw),
                           ins, outs)

    def dma_t(self, out_ap, in_ap, ins=(), outs=()):
        return self._issue(
            lambda: self.eng.e.dma_start(out=out_ap, in_=in_ap, transpose=True),
            ins, outs)

    def indirect(self, out_ap, out_off, in_ap, in_off, ins=(), outs=(), **kw):
        return self._issue(
            lambda: self.eng.e.indirect_dma_start(out_ap, out_off, in_ap, in_off, **kw),
            ins, outs)

    def collective(self, kind, op, rg, ins_aps, outs_aps, ins=(), outs=()):
        return self._issue(
            lambda: self.eng.e.collective_compute(kind, op, rg, ins_aps, outs_aps),
            ins, outs)

    def wait_all(self, eng):
        for s, c in zip(self.sems, self.cnt):
            if c:
                eng._wait(s, self.step * c)


class KB:
    def __init__(self, nc, st):
        self.nc = nc
        self.st = st
        self.bufs = {}
        sem = lambda n: st.enter_context(nc.semaphore(n))
        self.v = Eng(nc.vector, sem("se_v"), "v")
        self.s = Eng(nc.scalar, sem("se_s"), "s")
        self.t = Eng(nc.tensor, sem("se_t"), "t")
        self.g = Eng(nc.gpsimd, sem("se_g"), "g")
        self.y = Eng(nc.sync, sem("se_y"), "y")
        self.engs = [self.v, self.s, self.t, self.g, self.y]
        self.qy = Q(self.y, [sem(f"qy{i}") for i in range(6)], "qy")
        self.qs = Q(self.s, [sem(f"qs{i}") for i in range(6)], "qs")
        self.qg = Q(self.g, [sem(f"qg{i}") for i in range(6)], "qg")
        self.qo = Q(self.g, [sem(f"qo{i}") for i in range(2)], "qo")
        self.qc = Q(self.g, [sem(f"qc{i}") for i in range(4)], "qc", step=1)
        self.queues = [self.qy, self.qs, self.qg, self.qo, self.qc]

    def sb(self, name, shape, dtype=F32, scope=None):
        h = (scope or self.st).enter_context(self.nc.sbuf_tensor(name, shape, dtype))
        self.bufs[name] = Buf(name)
        return h

    def ps(self, name, shape, dtype=F32):
        h = self.st.enter_context(self.nc.psum_tensor(name, shape, dtype))
        self.bufs[name] = Buf(name)
        return h

    def dram(self, name, shape, dtype=F32, kind="Internal"):
        h = self.nc.dram_tensor(name, shape, dtype, kind=kind)
        self.bufs[name] = Buf(name)
        return h

    def b(self, name):
        if name not in self.bufs:
            self.bufs[name] = Buf(name)
        return self.bufs[name]

    def fence(self, skip=()):
        toks = [(e.sem, e.n) for e in self.engs if e.n > 0]
        for q in self.queues:
            if q in skip:
                continue
            toks += [(s, q.step * c) for s, c in zip(q.sems, q.cnt) if c]
        for e in self.engs:
            e.fence = list(toks)


def cdiv(a, b):
    return (a + b - 1) // b


def build(nc, st, T):
    k = KB(nc, st)
    B = k.b
    TL = T // NCORE
    CAP = min(int(IM * 1.5), T // E)
    NT = TL // 128
    NJ = T // 128
    NS = CAP // 128
    NCH_AG = min(8, NS)
    CHS = CAP // NCH_AG
    W6 = 6 * H // NCORE
    RG = [list(range(NCORE))]
    IFS = IF // NCORE
    HS = H // NCORE

    x_full = k.dram("x_full", [T, H], F32, kind="ExternalInput")
    xown = k.dram("xown", [TL, H], F32, kind="ExternalInput")
    c_vec = k.dram("c_vec", [1, H], F32, kind="ExternalInput")
    wmod_sl = k.dram("wmod_sl", [H, W6], F32, kind="ExternalInput")
    gnw_in = k.dram("gnw_in", [1, H], F32, kind="ExternalInput")
    ginw_in = k.dram("ginw_in", [1, H], F32, kind="ExternalInput")
    gatew_in = k.dram("gatew_in", [E, H], F32, kind="ExternalInput")
    gpn_in = k.dram("gpn_in", [1, H], F32, kind="ExternalInput")
    gpw_in = k.dram("gpw_in", [2 * IM, H], F32, kind="ExternalInput")
    dpn_in = k.dram("dpn_in", [1, IM], F32, kind="ExternalInput")
    dpw_in = k.dram("dpw_in", [H, IM], F32, kind="ExternalInput")
    clipx_in = k.dram("clipx_in", [1, 1], F32, kind="ExternalInput")
    clipd_in = k.dram("clipd_in", [1, 1], F32, kind="ExternalInput")
    wg_sl = k.dram("wg_sl", [IFS, H], F32, kind="ExternalInput")
    wu_sl = k.dram("wu_sl", [IFS, H], F32, kind="ExternalInput")
    wd_sl = k.dram("wd_sl", [HS, IF], F32, kind="ExternalInput")
    out = k.dram("out", [TL, H], F32, kind="ExternalOutput")

    modP = k.dram("modP", [1, W6], F32)
    modF = k.dram("modF", [NCORE, W6], F32)
    ct_d = k.dram("ct_d", [NJ * 128, 1], F32)
    fl_d = k.dram("fl_d", [1, CAP], F32)
    jr_d = k.dram("jr_d", [1, CAP], F32)
    gpq_d = k.dram("gpq_d", [2 * IM, H], BF16)
    dpq_d = k.dram("dpq_d", [H, IM], BF16)
    ng6_d = k.dram("ng6_d", [CAP, H], F32)
    contribP = k.dram("contribP", [CAP, H], F32)
    contribF = k.dram("contribF", [E * CAP, H], F32)
    fqgT_p = k.dram("fqgT_p", [H, IFS], BF16)
    fquT_p = k.dram("fquT_p", [H, IFS], BF16)
    fqd_p = k.dram("fqd_p", [HS, IF], BF16)
    fqgT_d = k.dram("fqgT_d", [NCORE * H, IFS], BF16)
    fquT_d = k.dram("fquT_d", [NCORE * H, IFS], BF16)
    fqd_d = k.dram("fqd_d", [H, IF], BF16)
    x1_d = k.dram("x1_d", [TL, H], F32)
    h2_d = k.dram("h2_d", [TL, H], F32)
    n1_d = k.dram("n1_d", [TL, H], BF16)
    prod_d = k.dram("prod_d", [TL, IF], BF16)
    n2f_d = k.dram("n2f_d", [TL, IF], BF16)
    gm1P = k.dram("gm1P", [1, 8], F32)
    gm1F = k.dram("gm1F", [NCORE, 8], F32)
    gm2P = k.dram("gm2P", [1, 8], F32)
    gm2F = k.dram("gm2F", [NCORE, 8], F32)

    pid = {}
    for e in (k.y, k.g):
        pid[e.name] = e.e.partition_id()


    rc_t = k.sb("rc_t", [128, 1])
    nc.const_aps.register(F32, RC, rc_t[:])
    k.v.op("memset", rc_t[:], RC, outs=[B("rc_t")])
    mrc_t = k.sb("mrc_t", [128, 1])
    nc.const_aps.register(F32, -RC, mrc_t[:])
    k.v.op("memset", mrc_t[:], -RC, outs=[B("mrc_t")])

    iot_pf = k.sb("iot_pf", [128, 128])
    k.g.op("iota", iot_pf[:], [[-1, 128]], base=0, channel_multiplier=1,
           allow_small_or_imprecise_dtypes=True, outs=[B("iot_pf")])
    tri_b = k.sb("tri_b", [128, 128], BF16)
    tri_f = k.sb("tri_f", [128, 128], F32)
    id_b = k.sb("id_b", [128, 128], BF16)
    id_f = k.sb("id_f", [128, 128], F32)
    k.v.op("tensor_scalar", tri_b[:], iot_pf[:], 0.0, None, ALU.is_lt,
           ins=[B("iot_pf")], outs=[B("tri_b")])
    k.v.op("tensor_scalar", tri_f[:], iot_pf[:], 0.0, None, ALU.is_lt,
           ins=[B("iot_pf")], outs=[B("tri_f")])
    k.v.op("tensor_scalar", id_b[:], iot_pf[:], 0.0, None, ALU.is_equal,
           ins=[B("iot_pf")], outs=[B("id_b")])
    k.v.op("tensor_scalar", id_f[:], iot_pf[:], 0.0, None, ALU.is_equal,
           ins=[B("iot_pf")], outs=[B("id_f")])
    ones_r = k.sb("ones_r", [1, 128], BF16)
    k.v.op("memset", ones_r[:], 1.0, outs=[B("ones_r")])
    ones_rf = k.sb("ones_rf", [1, 128], F32)
    k.v.op("memset", ones_rf[:], 1.0, outs=[B("ones_rf")])
    ones_c = k.sb("ones_c", [128, 1], BF16)
    k.v.op("memset", ones_c[:], 1.0, outs=[B("ones_c")])
    ones_cf = k.sb("ones_cf", [128, 1], F32)
    k.v.op("memset", ones_cf[:], 1.0, outs=[B("ones_cf")])
    iop1_b = k.sb("iop1_b", [128, 1], BF16)
    iop1_f = k.sb("iop1_f", [128, 1])
    k.g.op("iota", iop1_f[:], [[1, 1]], base=1, channel_multiplier=1,
           allow_small_or_imprecise_dtypes=True, outs=[B("iop1_f")])
    k.v.op("tensor_copy", iop1_b[:], iop1_f[:], ins=[B("iop1_f")], outs=[B("iop1_b")])
    io_e = k.sb("io_e", [128, E])
    k.g.op("iota", io_e[:], [[1, E]], base=0, channel_multiplier=0,
           allow_small_or_imprecise_dtypes=True, outs=[B("io_e")])
    io_r = k.sb("io_r", [128, 128])
    k.g.op("iota", io_r[:], [[1, 128]], base=0, channel_multiplier=0,
           allow_small_or_imprecise_dtypes=True, outs=[B("io_r")])
    io_s = k.sb("io_s", [128, CAP])
    k.g.op("iota", io_s[:], [[1, CAP]], base=0, channel_multiplier=0,
           allow_small_or_imprecise_dtypes=True, outs=[B("io_s")])

    pk = [k.ps(f"pk{i}", [128, 512]) for i in range(8)]

    def bc_fill(dst, name, src_ap, src_buf, width):
        for j in range(cdiv(width, 512)):
            w = min(512, width - 512 * j)
            p = pk[j % 2]
            k.t.op("matmul", p[:, 0:w], ones_rf[:], src_ap[0:1, 512 * j:512 * j + w],
                   start=True, stop=True, ins=[B("ones_rf"), src_buf],
                   outs=[B(f"pk{j % 2}")])
            k.v.op("tensor_copy", dst[:, 512 * j:512 * j + w], p[:, 0:w],
                   ins=[B(f"pk{j % 2}")], outs=[B(name)])
        return dst

    def bcast_row(name, src_ap, src_buf, width, scope=None):
        dst = k.sb(name, [128, width], scope=scope)
        return bc_fill(dst, name, src_ap, src_buf, width)

    ca1b = k.sb("ca1b", [128, H])
    sab = k.sb("sab", [128, H])
    gaxb = k.sb("gaxb", [128, H])
    cb1b = k.sb("cb1b", [128, H])
    sbb = k.sb("sbb", [128, H])
    gbxb = k.sb("gbxb", [128, H])
    ggb = k.sb("ggb", [128, H])
    gnwb = k.sb("gnwb", [128, H])
    gpnb = k.sb("gpnb", [128, H])
    dpnb = k.sb("dpnb", [128, IM])
    gamc_b = k.sb("gamc_b", [128, 1])
    mgt = k.sb("mgt", [128, E * 8], BF16)

    sc0 = ExitStack()
    cv8 = k.sb("cv8", [8, 128], scope=sc0)
    k.qy.dma(cv8[:], c_vec.ap().rearrange("o (a b) -> a (o b)", a=8),
             ins=[B("c_vec")], outs=[B("cv8")])
    sc8 = k.sb("sc8", [8, 128], scope=sc0)
    k.s.op("activation", sc8[:], cv8[:], AF.Silu, bias=0.0, scale=1.0,
           ins=[B("cv8")], outs=[B("sc8")])
    k.t.op("transpose", pk[0][:, 0:8], sc8[:], id_f[0:8, 0:8],
           ins=[B("sc8"), B("id_f")], outs=[B("pk0")])
    scT = k.sb("scT", [128, 8], scope=sc0)
    k.v.op("tensor_copy", scT[:], pk[0][:, 0:8], ins=[B("pk0")], outs=[B("scT")])
    wm_t = k.sb("wm_t", [128, 8 * W6], scope=sc0)
    modp_t = k.sb("modp_t", [1, W6], scope=sc0)
    NB6 = cdiv(W6, 512)
    for kk in range(8):
        k.qy.dma(wm_t[:, W6 * kk:W6 * (kk + 1)],
                 wmod_sl.ap()[128 * kk:128 * (kk + 1), :],
                 ins=[B("wmod_sl")], outs=[B("wm_t")])
    for kk in range(8):
        for j in range(NB6):
            w = min(512, W6 - 512 * j)
            k.t.op("matmul", pk[2 + j][0:1, 0:w], scT[:, kk:kk + 1],
                   wm_t[:, W6 * kk + 512 * j:W6 * kk + 512 * j + w],
                   start=(kk == 0), stop=(kk == 7),
                   ins=[B("scT"), B("wm_t")], outs=[B(f"pk{2 + j}")],
                   defer=not (kk == 7 and j == NB6 - 1))
    for j in range(NB6):
        w = min(512, W6 - 512 * j)
        k.v.op("tensor_copy", modp_t[0:1, 512 * j:512 * j + w], pk[2 + j][0:1, 0:w],
               ins=[B(f"pk{2 + j}")], outs=[B("modp_t")])
    k.qg.dma(modP.ap(), modp_t[:], ins=[B("modp_t")], outs=[B("modP")])
    k.qc.collective("AllGather", ALU.bypass, RG, [modP.ap()], [modF.ap()],
                    ins=[B("modP")], outs=[B("modF")])
    modf_t = k.sb("modf_t", [1, 6 * H], scope=sc0)
    k.qy.dma(modf_t[:], modF.ap().rearrange("a b -> () (a b)"),
             ins=[B("modF")], outs=[B("modf_t")])
    mrow = lambda i: modf_t[0:1, H * i:H * (i + 1)]

    ca1 = k.sb("ca1", [1, H], scope=sc0)
    cb1 = k.sb("cb1", [1, H], scope=sc0)
    k.v.op("tensor_scalar", ca1[:], mrow(1), 1.0, None, ALU.add,
           ins=[B("modf_t")], outs=[B("ca1")])
    k.v.op("tensor_scalar", cb1[:], mrow(4), 1.0, None, ALU.add,
           ins=[B("modf_t")], outs=[B("cb1")])

    def gclamp(name, src):
        ab = k.sb(name + "_a", [1, H], scope=sc0)
        sg = k.sb(name + "_s", [1, H], scope=sc0)
        res = k.sb(name, [1, H], scope=sc0)
        k.s.op("activation", ab[:], src, AF.Abs, bias=0.0, scale=1.0,
               ins=[B("modf_t")], outs=[B(name + "_a")])
        k.v.op("tensor_scalar", ab[:], ab[:], 0.25, None, ALU.max,
               ins=[B(name + "_a")], outs=[B(name + "_a")])
        k.v.op("tensor_scalar", sg[:], src, 0.0, None, ALU.is_ge,
               ins=[B("modf_t")], outs=[B(name + "_s")])
        k.v.op("tensor_scalar", sg[:], sg[:], 2.0, -1.0, ALU.mult, ALU.add,
               ins=[B(name + "_s")], outs=[B(name + "_s")])
        k.v.op("tensor_tensor", res[:], ab[:], sg[:], ALU.mult,
               ins=[B(name + "_a"), B(name + "_s")], outs=[B(name)])
        return res

    gax = gclamp("gax", mrow(2))
    gbx = gclamp("gbx", mrow(5))

    bc_fill(ca1b, "ca1b", ca1[:], B("ca1"), H)
    bc_fill(sab, "sab", mrow(0), B("modf_t"), H)
    bc_fill(gaxb, "gaxb", gax[:], B("gax"), H)
    bc_fill(cb1b, "cb1b", cb1[:], B("cb1"), H)
    bc_fill(sbb, "sbb", mrow(3), B("modf_t"), H)
    bc_fill(gbxb, "gbxb", gbx[:], B("gbx"), H)

    gnw_t = k.sb("gnw_t", [1, H], scope=sc0)
    ginw_t = k.sb("ginw_t", [1, H], scope=sc0)
    k.qy.dma(gnw_t[:], gnw_in.ap(), ins=[B("gnw_in")], outs=[B("gnw_t")])
    k.qy.dma(ginw_t[:], ginw_in.ap(), ins=[B("ginw_in")], outs=[B("ginw_t")])
    gg_t = k.sb("gg_t", [1, H], scope=sc0)
    k.v.op("tensor_tensor", gg_t[:], gnw_t[:], ginw_t[:], ALU.mult,
           ins=[B("gnw_t"), B("ginw_t")], outs=[B("gg_t")])
    bc_fill(ggb, "ggb", gg_t[:], B("gg_t"), H)
    bc_fill(gnwb, "gnwb", gnw_t[:], B("gnw_t"), H)
    gpn_t = k.sb("gpn_t", [1, H], scope=sc0)
    k.qy.dma(gpn_t[:], gpn_in.ap(), ins=[B("gpn_in")], outs=[B("gpn_t")])
    bc_fill(gpnb, "gpnb", gpn_t[:], B("gpn_t"), H)
    dpn_t = k.sb("dpn_t", [1, IM], scope=sc0)
    k.qy.dma(dpn_t[:], dpn_in.ap(), ins=[B("dpn_in")], outs=[B("dpn_t")])
    bc_fill(dpnb, "dpnb", dpn_t[:], B("dpn_t"), IM)

    gw_t = k.sb("gw_t", [E, H], scope=sc0)
    k.qy.dma(gw_t[:], gatew_in.ap(), ins=[B("gatew_in")], outs=[B("gw_t")])
    gw_abs = k.sb("gw_abs", [E, H], scope=sc0)
    gw_sum = k.sb("gw_sum", [E, 1], scope=sc0)
    k.s.op("activation", gw_abs[:], gw_t[:], AF.Abs, bias=0.0, scale=1.0,
           accum_out=gw_sum[:], ins=[B("gw_t")], outs=[B("gw_abs"), B("gw_sum")])
    k.t.op("matmul", pk[0][0:1, 0:1], ones_cf[0:E, :], gw_sum[:],
           start=True, stop=True, ins=[B("ones_cf"), B("gw_sum")], outs=[B("pk0")])
    wsg = k.sb("wsg", [1, 1], scope=sc0)
    k.v.op("tensor_scalar", wsg[:], pk[0][0:1, 0:1], 1.0 / (E * H), 1e-5,
           ALU.mult, ALU.max, ins=[B("pk0")], outs=[B("wsg")])
    wsg_r = k.sb("wsg_r", [1, 1], scope=sc0)
    k.v.op("reciprocal", wsg_r[:], wsg[:], ins=[B("wsg")], outs=[B("wsg_r")])
    bc_fill(gamc_b, "gamc_b", wsg[:], B("wsg"), 1)
    k.v.op("tensor_scalar", gamc_b[:], gamc_b[:], 1.0 / 127.0, None, ALU.mult,
           ins=[B("gamc_b")], outs=[B("gamc_b")])
    wsgb = bcast_row("wsgb", wsg_r[:], B("wsg_r"), 1, scope=sc0)
    mg_t = k.sb("mg_t", [E, H], scope=sc0)
    k.s.op("activation", mg_t[:], gw_t[:], AF.Identity, bias=RC,
           scale=wsgb[0:E, 0:1], ins=[B("gw_t"), B("wsgb"), B("rc_t")],
           outs=[B("mg_t")])
    k.v.op("tensor_scalar", mg_t[:], mg_t[:], -RC, None, ALU.add,
           ins=[B("mg_t")], outs=[B("mg_t")])
    k.v.op("tensor_scalar", mg_t[:], mg_t[:], -1.0, 1.0, ALU.max, ALU.min,
           ins=[B("mg_t")], outs=[B("mg_t")])
    for kk in range(8):
        k.t.op("transpose", pk[kk % 2][:, 0:E], mg_t[:, 128 * kk:128 * (kk + 1)],
               id_f[0:E, 0:E], ins=[B("mg_t"), B("id_f")], outs=[B(f"pk{kk % 2}")])
        k.v.op("tensor_copy", mgt[:, E * kk:E * (kk + 1)], pk[kk % 2][:, 0:E],
               ins=[B(f"pk{kk % 2}")], outs=[B("mgt")])

    sc0.close()
    k.fence()

    stat_g = k.sb("stat_g", [128, NT])
    cid1_s = k.sb("cid1_s", [128, NT], I32)
    cid2_s = k.sb("cid2_s", [128, NT], I32)
    wk1_s = k.sb("wk1_s", [128, NT])
    wk2_s = k.sb("wk2_s", [128, NT])
    xidx_s = k.sb("xidx_s", [128, NS], I32)
    agp_s = k.sb("agp_s", [128, NS])
    adp_s = k.sb("adp_s", [128, NS])
    sxb = k.sb("sxb", [128, 1])
    sxv = k.sb("sxv", [128, 1])
    s2b = k.sb("s2b", [128, 1])
    s2v = k.sb("s2v", [128, 1])
    mdpc_b = k.sb("mdpc_b", [128, 1])
    mgpc_b = k.sb("mgpc_b", [128, 1])
    IL = k.sb("IL", [128, NT * E])

    class Stream:
        def __init__(self, si, widths):
            self.si = si
            self.h = {}
            for nm, shape, dt in widths:
                self.h[nm] = k.sb(f"{nm}_u{si}", shape, dt)
        def __getitem__(self, nm):
            return self.h[nm]
        def b(self, nm):
            return B(f"{nm}_u{self.si}")

    base_w = ([("scr", [128, H], F32)] +
              [(f"st{i}", [128, 1], F32) for i in range(8)])
    SS = [Stream(0, base_w), Stream(1, base_w), Stream(2, base_w)]

    scr = k.sb("scr", [128, 1])
    stt = [k.sb(f"stt{i}", [128, 1]) for i in range(8)]

    class _LegacyS:
        def __init__(self):
            self.h = {"scr": scr}
            for i in range(8):
                self.h[f"st{i}"] = stt[i]
        def __getitem__(self, nm):
            return self.h[nm]
        def b(self, nm):
            return B("scr" if nm == "scr" else "stt" + nm[2:])
    LS = _LegacyS()

    def drive(gen):
        for _ in gen:
            pass

    def interleave(gens):
        gens = list(gens)
        while gens:
            alive = []
            for g in gens:
                try:
                    next(g)
                    alive.append(g)
                except StopIteration:
                    pass
            gens = alive

    def rsqrt_col(S, dst, src, scale, eps):
        d, ds_, db, sb_ = S[f"st{dst}"], S[f"st{src}"], S.b(f"st{dst}"), S.b(f"st{src}")
        k.v.op("tensor_scalar", d[:], ds_[:], scale, eps, ALU.mult, ALU.add,
               ins=[sb_], outs=[db])
        yield
        k.s.op("activation", d[:], d[:], AF.Sqrt, bias=0.0, scale=1.0,
               ins=[db], outs=[db])
        yield
        k.v.op("reciprocal", d[:], d[:], ins=[db], outs=[db])
        yield

    def adaln_tile(S, xt_ap, xt_buf, h_ap, h_buf, vtile, vbuf):
        st0, st2, st3 = S["st0"], S["st2"], S["st3"]
        b0, b1, b2, b3 = S.b("st0"), S.b("st1"), S.b("st2"), S.b("st3")
        scr_, bscr = S["scr"], S.b("scr")
        k.v.op("tensor_reduce", st0[:], xt_ap, AX.X, ALU.add,
               ins=[xt_buf], outs=[b0])
        yield
        k.s.op("activation", scr_[:], xt_ap, AF.Square, bias=0.0, scale=1.0,
               accum_out=S["st1"][:], ins=[xt_buf], outs=[bscr, b1])
        yield
        k.v.op("tensor_scalar", st0[:], st0[:], 1.0 / H, None, ALU.mult,
               ins=[b0], outs=[b0])
        k.v.op("tensor_tensor", st2[:], st0[:], st0[:], ALU.mult,
               ins=[b0], outs=[b2])
        yield
        k.v.op("tensor_scalar", st3[:], S["st1"][:], 1.0 / H, None, ALU.mult,
               ins=[b1], outs=[b3])
        k.v.op("tensor_tensor", st3[:], st3[:], st2[:], ALU.subtract,
               ins=[b3, b2], outs=[b3])
        k.v.op("tensor_scalar", st3[:], st3[:], 1e-5, None, ALU.add,
               ins=[b3], outs=[b3])
        yield
        k.s.op("activation", st3[:], st3[:], AF.Sqrt, bias=0.0, scale=1.0,
               ins=[b3], outs=[b3])
        yield
        k.v.op("reciprocal", st3[:], st3[:], ins=[b3], outs=[b3])
        k.v.op("tensor_tensor", st2[:], st0[:], st3[:], ALU.mult,
               ins=[b0, b3], outs=[b2])
        k.v.op("tensor_scalar", st2[:], st2[:], -1.0, None, ALU.mult,
               ins=[b2], outs=[b2])
        yield
        k.s.op("activation", vtile[:], xt_ap, AF.Identity, bias=st2[:, 0:1],
               scale=st3[:, 0:1], ins=[xt_buf, b2, b3], outs=[vbuf])
        yield
        k.v.op("tensor_tensor", h_ap, vtile[:], ca1b[:], ALU.mult,
               ins=[vbuf, B("ca1b")], outs=[h_buf])
        k.v.op("tensor_tensor", h_ap, h_ap, sab[:], ALU.add,
               ins=[h_buf, B("sab")], outs=[h_buf])
        yield

    scw = ExitStack()
    for S in SS[:2]:
        S.h["wrow"] = k.sb(f"wrow_u{S.si}", [128, H], scope=scw)
        S.h["fqs"] = k.sb(f"fqs_u{S.si}", [128, H], BF16, scope=scw)
    fqT_dst = [(fqgT_p, "fqgT_p"), (fquT_p, "fquT_p")]

    def wq_chain(job, S):
        src_t, m, r0, rr = job
        wrow = S["wrow"]
        bw = S.b("wrow")
        st = [S[f"st{i}"] for i in range(3)]
        sb_ = [S.b(f"st{i}") for i in range(3)]
        k.qy.dma(wrow[0:rr, :], src_t.ap()[r0:r0 + rr, :],
                 ins=[B(src_t.name)], outs=[bw])
        yield
        k.v.op("tensor_reduce", st[0][0:rr, :], wrow[0:rr, :], AX.X, ALU.max,
               apply_absolute_value=True, ins=[bw], outs=[sb_[0]])
        yield
        k.v.op("reciprocal", st[1][0:rr, :], st[0][0:rr, :],
               ins=[sb_[0]], outs=[sb_[1]])
        k.v.op("tensor_scalar", st[1][0:rr, :], st[1][0:rr, :], 127.0, None,
               ALU.mult, ins=[sb_[1]], outs=[sb_[1]])
        k.v.op("tensor_scalar", st[2][0:rr, :], st[0][0:rr, :], 1.0 / 127.0,
               None, ALU.mult, ins=[sb_[0]], outs=[sb_[2]])
        yield
        k.s.op("activation", wrow[0:rr, :], wrow[0:rr, :], AF.Identity,
               bias=RC, scale=st[1][0:rr, 0:1],
               ins=[bw, sb_[1], B("rc_t")], outs=[bw])
        yield
        k.v.op("tensor_scalar", wrow[0:rr, :], wrow[0:rr, :], -RC, None,
               ALU.add, ins=[bw], outs=[bw])
        k.v.op("tensor_scalar", wrow[0:rr, :], wrow[0:rr, :],
               st[2][0:rr, 0:1], None, ALU.mult, ins=[bw, sb_[2]], outs=[bw])
        yield
        fqs, bfq = S["fqs"], S.b("fqs")
        for kk in range(8):
            tb = 6 + kk % 2
            k.t.op("transpose", pk[tb][:, 0:rr],
                   wrow[0:rr, 128 * kk:128 * (kk + 1)],
                   id_f[0:rr, 0:rr] if rr < 128 else id_f[:],
                   ins=[bw, B("id_f")], outs=[B(f"pk{tb}")])
            k.s.op("activation", fqs[:, rr * kk:rr * (kk + 1)],
                   pk[tb][:, 0:rr], AF.Copy, bias=0.0, scale=1.0,
                   ins=[B(f"pk{tb}")], outs=[bfq])
            if kk % 4 == 3:
                yield
        dstp, dstpb = fqT_dst[m]
        k.qg.dma(dstp.ap()[:, r0:r0 + rr].rearrange("(a p) r -> p a r", p=128),
                 fqs.ap()[0:128, 0:8 * rr].rearrange("p (a r) -> p a r", r=rr),
                 ins=[bfq], outs=[B(dstpb)])
        yield

    wq_jobs = []
    for m, src_t in enumerate((wg_sl, wu_sl)):
        for r0 in range(0, IFS, 128):
            wq_jobs.append((src_t, m, r0, min(128, IFS - r0)))
    interleave([(lambda si: (x for j in range(si, len(wq_jobs), 2)
                             for x in wq_chain(wq_jobs[j], SS[si])))(si)
                for si in range(2)])
    wrowd = k.sb("wrowd", [HS, IF], scope=scw)
    wqd_b = k.sb("wqd_b", [HS, IF], BF16, scope=scw)
    k.qy.dma(wrowd[:], wd_sl.ap(), ins=[B("wd_sl")], outs=[B("wrowd")])
    k.v.op("tensor_reduce", stt[0][0:HS, :], wrowd[:], AX.X, ALU.max,
           apply_absolute_value=True, ins=[B("wrowd")], outs=[B("stt0")])
    k.v.op("reciprocal", stt[1][0:HS, :], stt[0][0:HS, :], ins=[B("stt0")], outs=[B("stt1")])
    k.v.op("tensor_scalar", stt[1][0:HS, :], stt[1][0:HS, :], 127.0, None, ALU.mult,
           ins=[B("stt1")], outs=[B("stt1")])
    k.v.op("tensor_scalar", stt[2][0:HS, :], stt[0][0:HS, :], 1.0 / 127.0, None,
           ALU.mult, ins=[B("stt0")], outs=[B("stt2")])
    k.s.op("activation", wrowd[:], wrowd[:], AF.Identity, bias=RC,
           scale=stt[1][0:HS, 0:1], ins=[B("wrowd"), B("stt1"), B("rc_t")],
           outs=[B("wrowd")])
    k.v.op("tensor_scalar", wrowd[:], wrowd[:], -RC, None, ALU.add,
           ins=[B("wrowd")], outs=[B("wrowd")])
    k.v.op("tensor_scalar", wqd_b[:], wrowd[:], stt[2][0:HS, 0:1], None, ALU.mult,
           ins=[B("wrowd"), B("stt2")], outs=[B("wqd_b")])
    k.qg.dma(fqd_p.ap(), wqd_b[:], ins=[B("wqd_b")], outs=[B("fqd_p")])
    k.qg.wait_all(k.g)
    k.qc.collective("AllGather", ALU.bypass, RG, [fqgT_p.ap()], [fqgT_d.ap()],
                    ins=[B("fqgT_p")], outs=[B("fqgT_d")])
    k.qc.collective("AllGather", ALU.bypass, RG, [fquT_p.ap()], [fquT_d.ap()],
                    ins=[B("fquT_p")], outs=[B("fquT_d")])
    k.qc.collective("AllGather", ALU.bypass, RG, [fqd_p.ap()], [fqd_d.ap()],
                    ins=[B("fqd_p")], outs=[B("fqd_d")])

    wabs_acc = k.sb("wabs_acc", [128, 48], scope=scw)
    for S in SS[:2]:
        S.h["wrow2"] = k.sb(f"wrow2_u{S.si}", [128, IM], scope=scw)
        S.h["wq2"] = k.sb(f"wq2_u{S.si}", [128, IM], BF16, scope=scw)
    n_gp = 2 * IM // 128
    n_dp = H // 128

    def wabs_chain(win, wid, i, S):
        wrow2, wscr2 = S["wrow2"], S["wq2"]
        bw, bs = S.b("wrow2"), S.b("wq2")
        k.qy.dma(wrow2[:, 0:wid], win.ap()[128 * i:128 * (i + 1), :],
                 ins=[B(win.name)], outs=[bw])
        yield
        k.s.op("activation", wscr2[:, 0:wid], wrow2[:, 0:wid], AF.Abs, bias=0.0,
               scale=1.0, accum_out=S["st0"][:],
               ins=[bw], outs=[bs, S.b("st0")])
        yield
        k.v.op("tensor_tensor", wabs_acc[:, 40 + S.si:41 + S.si],
               wabs_acc[:, 40 + S.si:41 + S.si], S["st0"][:], ALU.add,
               ins=[B("wabs_acc"), S.b("st0")], outs=[B("wabs_acc")])
        yield

    def wtern_chain(win, wout, wid, i, S, sc_ptr, sc_buf):
        wrow2, wq2 = S["wrow2"], S["wq2"]
        bw, bq = S.b("wrow2"), S.b("wq2")
        k.qy.dma(wrow2[:, 0:wid], win.ap()[128 * i:128 * (i + 1), :],
                 ins=[B(win.name)], outs=[bw])
        yield
        k.s.op("activation", wrow2[:, 0:wid], wrow2[:, 0:wid], AF.Identity,
               bias=RC, scale=sc_ptr, ins=[bw, sc_buf, B("rc_t")], outs=[bw])
        yield
        k.v.op("tensor_scalar", wrow2[:, 0:wid], wrow2[:, 0:wid], -RC, None,
               ALU.add, ins=[bw], outs=[bw])
        k.v.op("tensor_scalar", wq2[:, 0:wid], wrow2[:, 0:wid], -1.0, 1.0,
               ALU.max, ALU.min, ins=[bw], outs=[bq])
        yield
        k.qg.dma(wout.ap()[128 * i:128 * (i + 1), :], wq2[:, 0:wid],
                 ins=[bq], outs=[B(wout.name)])
        yield

    k.v.op("memset", wabs_acc[:], 0.0, outs=[B("wabs_acc")])
    jobs_a = [(gpw_in, H, i) for i in range(n_gp)]
    interleave([(lambda si: (x for j in range(si, len(jobs_a), 2)
                             for x in wabs_chain(*jobs_a[j], SS[si])))(si)
                for si in range(2)])
    asum = k.sb("asum", [128, 1], scope=scw)
    k.v.op("tensor_tensor", asum[:], wabs_acc[:, 40:41], wabs_acc[:, 41:42],
           ALU.add, ins=[B("wabs_acc")], outs=[B("asum")])
    k.v.op("tensor_tensor", asum[:], asum[:], wabs_acc[:, 42:43],
           ALU.add, ins=[B("asum"), B("wabs_acc")], outs=[B("asum")])
    k.t.op("matmul", pk[0][0:1, 0:1], ones_cf[:], asum[:], start=True,
           stop=True, ins=[B("ones_cf"), B("asum")], outs=[B("pk0")])
    wsgp = k.sb("wsgp", [1, 1], scope=scw)
    k.v.op("tensor_scalar", wsgp[:], pk[0][0:1, 0:1], 1.0 / (2 * IM * H), 1e-5,
           ALU.mult, ALU.max, ins=[B("pk0")], outs=[B("wsgp")])
    wsgp_r = k.sb("wsgp_r", [1, 1], scope=scw)
    k.v.op("reciprocal", wsgp_r[:], wsgp[:], ins=[B("wsgp")], outs=[B("wsgp_r")])
    t1 = bcast_row("wsgp_rb", wsgp_r[:], B("wsgp_r"), 1, scope=scw)
    t2 = bcast_row("wsgp_b", wsgp[:], B("wsgp"), 1, scope=scw)
    k.v.op("tensor_scalar", mgpc_b[:], t2[:], 1.0 / 127.0, None, ALU.mult,
           ins=[B("wsgp_b")], outs=[B("mgpc_b")])
    jobs_q = [(gpw_in, gpq_d, H, i) for i in range(n_gp)]
    interleave([(lambda si: (x for j in range(si, len(jobs_q), 2)
                             for x in wtern_chain(*jobs_q[j], SS[si],
                                                  t1[:, 0:1], B("wsgp_rb"))))(si)
                for si in range(2)])
    k.v.op("memset", wabs_acc[:, 40:43], 0.0, outs=[B("wabs_acc")])
    jobs_a2 = [(dpw_in, IM, i) for i in range(n_dp)]
    interleave([(lambda si: (x for j in range(si, len(jobs_a2), 2)
                             for x in wabs_chain(*jobs_a2[j], SS[si])))(si)
                for si in range(2)])
    k.v.op("tensor_tensor", asum[:], wabs_acc[:, 40:41], wabs_acc[:, 41:42],
           ALU.add, ins=[B("wabs_acc")], outs=[B("asum")])
    k.v.op("tensor_tensor", asum[:], asum[:], wabs_acc[:, 42:43],
           ALU.add, ins=[B("asum"), B("wabs_acc")], outs=[B("asum")])
    k.t.op("matmul", pk[0][0:1, 0:1], ones_cf[:], asum[:], start=True,
           stop=True, ins=[B("ones_cf"), B("asum")], outs=[B("pk0")])
    wsdp = k.sb("wsdp", [1, 1], scope=scw)
    k.v.op("tensor_scalar", wsdp[:], pk[0][0:1, 0:1], 1.0 / (H * IM), 1e-5,
           ALU.mult, ALU.max, ins=[B("pk0")], outs=[B("wsdp")])
    wsdp_r = k.sb("wsdp_r", [1, 1], scope=scw)
    k.v.op("reciprocal", wsdp_r[:], wsdp[:], ins=[B("wsdp")], outs=[B("wsdp_r")])
    t3 = bcast_row("wsdp_rb", wsdp_r[:], B("wsdp_r"), 1, scope=scw)
    t4 = bcast_row("wsdp_b", wsdp[:], B("wsdp"), 1, scope=scw)
    k.v.op("tensor_scalar", mdpc_b[:], t4[:], 1.0 / 127.0, None, ALU.mult,
           ins=[B("wsdp_b")], outs=[B("mdpc_b")])
    jobs_q2 = [(dpw_in, dpq_d, IM, i) for i in range(n_dp)]
    interleave([(lambda si: (x for j in range(si, len(jobs_q2), 2)
                             for x in wtern_chain(*jobs_q2[j], SS[si],
                                                  t3[:, 0:1], B("wsdp_rb"))))(si)
                for si in range(2)])

    sc_ng = ExitStack()
    ngt = k.sb("ngt", [128, 8 * TL], BF16, scope=sc_ng)
    with ExitStack() as sc1:
        p1w = [("p1x", [128, H], F32), ("p1h", [128, H], F32),
               ("p1tg", [128, H], F32)]
        for S in SS:
            for nm, shape, dt in p1w:
                S.h[nm] = k.sb(f"{nm}_u{S.si}", shape, dt, scope=sc1)

        def p1_chain(t, S):
            p1x, p1h, p1tg = (S["p1x"], S["p1h"], S["p1tg"])
            p1v = p1tg
            bx, bh, btg = (S.b("p1x"), S.b("p1h"), S.b("p1tg"))
            bv = btg
            scr_, bscr = S["scr"], S.b("scr")
            st = [S[f"st{i}"] for i in range(8)]
            sb_ = [S.b(f"st{i}") for i in range(8)]
            k.qy.dma(p1x[:], xown.ap()[128 * t:128 * (t + 1), :],
                     ins=[B("xown")], outs=[bx])
            yield
            yield from adaln_tile(S, p1x[:], bx, p1h[:], bh, p1v, bv)
            k.s.op("activation", scr_[:], p1h[:], AF.Square, bias=0.0, scale=1.0,
                   accum_out=st[0][:], ins=[bh], outs=[bscr, sb_[0]])
            yield
            yield from rsqrt_col(S, 1, 0, 1.0 / H, 1e-6)
            k.v.op("tensor_tensor", p1tg[:], p1h[:], gnwb[:], ALU.mult,
                   ins=[bh, B("gnwb")], outs=[btg])
            yield
            k.s.op("activation", scr_[:], p1tg[:], AF.Square, bias=0.0, scale=1.0,
                   accum_out=st[2][:], ins=[btg], outs=[bscr, sb_[2]])
            yield
            k.v.op("tensor_tensor", st[3][:], st[1][:], st[1][:], ALU.mult,
                   ins=[sb_[1]], outs=[sb_[3]])
            k.v.op("tensor_tensor", st[3][:], st[3][:], st[2][:], ALU.mult,
                   ins=[sb_[3], sb_[2]], outs=[sb_[3]])
            yield
            yield from rsqrt_col(S, 4, 3, 1.0 / H, 1e-8)
            k.v.op("tensor_tensor", p1tg[:], p1h[:], ggb[:], ALU.mult,
                   ins=[bh, B("ggb")], outs=[btg])
            yield
            k.v.op("tensor_reduce", st[5][:], p1tg[:], AX.X, ALU.max,
                   apply_absolute_value=True, ins=[btg], outs=[sb_[5]])
            yield
            k.v.op("tensor_scalar", st[6][:], st[5][:], 1e-30, None, ALU.max,
                   ins=[sb_[5]], outs=[sb_[6]])
            k.v.op("reciprocal", st[6][:], st[6][:], ins=[sb_[6]], outs=[sb_[6]])
            k.v.op("tensor_scalar", st[6][:], st[6][:], 127.0, None, ALU.mult,
                   ins=[sb_[6]], outs=[sb_[6]])
            yield
            k.s.op("activation", scr_[:], p1tg[:], AF.Identity, bias=RC,
                   scale=st[6][:, 0:1], ins=[btg, sb_[6], B("rc_t")], outs=[bscr])
            yield
            ngt3 = ngt.ap().rearrange("p (a c) -> p a c", c=TL)
            for kk in range(8):
                j = kk // 4
                pbn = f"pk{(2 * S.si + j) % 6}"
                pbank = pk[(2 * S.si + j) % 6]
                k.t.op("transpose", pbank[:, 128 * (kk % 4):128 * (kk % 4) + 128],
                       scr_[:, 128 * kk:128 * (kk + 1)], id_f[:],
                       ins=[bscr, B("id_f")], outs=[B(pbn)],
                       defer=(kk % 4 != 3))
                if kk % 4 == 3:
                    k.s.op("activation",
                           ngt3[:, 4 * j:4 * j + 4, 128 * t:128 * (t + 1)],
                           pk[(2 * S.si + j) % 6][:, 0:512]
                           .rearrange("p (a c) -> p a c", c=128),
                           AF.Copy, bias=-RC, scale=1.0,
                           ins=[B(pbn), B("mrc_t")], outs=[B("ngt")])
                    yield
            k.v.op("tensor_tensor", st[7][:], st[1][:], st[4][:], ALU.mult,
                   ins=[sb_[1], sb_[4]], outs=[sb_[7]])
            k.v.op("tensor_tensor", st[7][:], st[7][:], st[5][:], ALU.mult,
                   ins=[sb_[7], sb_[5]], outs=[sb_[7]])
            k.v.op("tensor_scalar", st[7][:], st[7][:], 1e-5, None, ALU.max,
                   ins=[sb_[7]], outs=[sb_[7]])
            k.v.op("tensor_tensor", stat_g[:, t:t + 1], st[7][:], gamc_b[:],
                   ALU.mult, ins=[sb_[7], B("gamc_b")], outs=[B("stat_g")])
            yield

        def p1_stream(si):
            for t in range(si, NT, len(SS)):
                yield from p1_chain(t, SS[si])

        interleave([p1_stream(si) for si in range(len(SS))])
    k.fence()

    with ExitStack() as sc2:
        for t in range(NT):
            for kk in range(8):
                k.t.op("matmul", pk[t % 4][:, 0:E],
                       ngt[:, TL * kk + 128 * t:TL * kk + 128 * (t + 1)],
                       mgt[:, E * kk:E * (kk + 1)], start=(kk == 0), stop=(kk == 7),
                       ins=[B("ngt"), B("mgt")], outs=[B(f"pk{t % 4}")],
                       defer=(kk != 7))
            k.v.op("tensor_copy", IL[:, E * t:E * (t + 1)], pk[t % 4][:, 0:E],
                   ins=[B(f"pk{t % 4}")], outs=[B("IL")])
    sc_ng.close()
    k.fence()

    NCL = NT * E
    selP_d = k.dram("selP_d", [128, E * NT], F32)
    selF_d = k.dram("selF_d", [128 * NCORE, E * NT], F32)
    tot_d = k.dram("tot_d", [1, 8], F32)
    totF_d = k.dram("totF_d", [NCORE, 8], F32)
    lcar_d = k.dram("lcar_d", [NT, E], F32)
    ccar_d = k.dram("ccar_d", [NCORE, E], F32)
    cs_d = k.dram("cs_d", [1, NCL], F32)
    with ExitStack() as sc3:
        def sb3(name, dtype=F32):
            tile = k.sb(name, [128, NCL], dtype, scope=sc3)
            return tile, tile.ap().rearrange("p (a e) -> p a e", e=E)

        ioe3 = io_e.ap()[0:128, 0:E].rearrange("p e -> p () e").broadcast_to(
            (128, NT, E))
        IL3 = IL.ap().rearrange("p (a e) -> p a e", e=E)
        keys, keys3 = sb3("keys")
        k.v.op("tensor_scalar", keys[:], IL[:], 16.0, None, ALU.mult,
               ins=[B("IL")], outs=[B("keys")])
        k.v.op("tensor_tensor", keys3, keys3, ioe3, ALU.subtract,
               ins=[B("keys"), B("io_e")], outs=[B("keys")])
        m1 = k.sb("m1", [128, NT], scope=sc3)
        m13 = m1.ap().rearrange("p (a o) -> p a o", o=1)
        k.v.op("tensor_reduce", m13, keys3, AX.X, ALU.max,
               ins=[B("keys")], outs=[B("m1")])
        is1, is13 = sb3("is1")
        k.v.op("tensor_tensor", is13, keys3, m13.broadcast_to((128, NT, E)),
               ALU.is_ge, ins=[B("keys"), B("m1")], outs=[B("is1")])
        keys2, keys23 = sb3("keys2")
        k.v.op("scalar_tensor_tensor", keys2[:], is1[:], -2.0 ** 30, keys[:],
               ALU.mult, ALU.add, ins=[B("is1"), B("keys")], outs=[B("keys2")])
        m2 = k.sb("m2", [128, NT], scope=sc3)
        m23 = m2.ap().rearrange("p (a o) -> p a o", o=1)
        k.v.op("tensor_reduce", m23, keys23, AX.X, ALU.max,
               ins=[B("keys2")], outs=[B("m2")])
        sel, sel3 = sb3("sel")
        k.v.op("tensor_tensor", sel3, keys3, m23.broadcast_to((128, NT, E)),
               ALU.is_ge, ins=[B("keys"), B("m2")], outs=[B("sel")])
        lg, lg3 = sb3("lg")
        g3 = stat_g.ap().rearrange("p (a o) -> p a o", o=1)
        k.v.op("tensor_tensor", lg3, IL3, g3.broadcast_to((128, NT, E)), ALU.mult,
               ins=[B("IL"), B("stat_g")], outs=[B("lg")])
        lm = k.sb("lm", [128, NT], scope=sc3)
        lm3 = lm.ap().rearrange("p (a o) -> p a o", o=1)
        k.v.op("tensor_reduce", lm3, lg3, AX.X, ALU.max, ins=[B("lg")], outs=[B("lm")])
        k.v.op("tensor_tensor", lg3, lg3, lm3.broadcast_to((128, NT, E)),
               ALU.subtract, ins=[B("lg"), B("lm")], outs=[B("lg")])
        k.s.op("activation", lg[:], lg[:], AF.Exp, bias=0.0, scale=1.0,
               ins=[B("lg")], outs=[B("lg")])
        ssum = k.sb("ssum", [128, NT], scope=sc3)
        ssum3 = ssum.ap().rearrange("p (a o) -> p a o", o=1)
        k.v.op("tensor_reduce", ssum3, lg3, AX.X, ALU.add,
               ins=[B("lg")], outs=[B("ssum")])
        k.v.op("reciprocal", ssum[:], ssum[:], ins=[B("ssum")], outs=[B("ssum")])
        wprob, wprob3 = sb3("wprob")
        k.v.op("tensor_tensor", wprob3, lg3, ssum3.broadcast_to((128, NT, E)),
               ALU.mult, ins=[B("lg"), B("ssum")], outs=[B("wprob")])
        k.v.op("tensor_tensor", wprob[:], wprob[:], sel[:], ALU.mult,
               ins=[B("wprob"), B("sel")], outs=[B("wprob")])
        selb = k.sb("selb", [128, NCL], BF16, scope=sc3)
        k.v.op("tensor_copy", selb[:], sel[:], ins=[B("sel")], outs=[B("selb")])
        wpx, _ = sb3("wpx")
        for j in range(cdiv(NCL, 512)):
            w = min(512, NCL - 512 * j)
            p = pk[j % 4]
            k.t.op("matmul", p[:, 0:w], tri_b[:], selb[:, 512 * j:512 * j + w],
                   start=True, stop=True, ins=[B("tri_b"), B("selb")],
                   outs=[B(f"pk{j % 4}")])
            k.v.op("tensor_copy", wpx[:, 512 * j:512 * j + w], p[:, 0:w],
                   ins=[B(f"pk{j % 4}")], outs=[B("wpx")])
        cs_row = k.sb("cs_row", [1, NCL], scope=sc3)
        for j in range(cdiv(NCL, 512)):
            w = min(512, NCL - 512 * j)
            p = pk[4 + (j % 2)]
            k.t.op("matmul", p[0:1, 0:w], ones_c[:], selb[:, 512 * j:512 * j + w],
                   start=True, stop=True, ins=[B("ones_c"), B("selb")],
                   outs=[B(f"pk{4 + (j % 2)}")])
            k.v.op("tensor_copy", cs_row[0:1, 512 * j:512 * j + w], p[0:1, 0:w],
                   ins=[B(f"pk{4 + (j % 2)}")], outs=[B("cs_row")])
        k.qg.dma(cs_d.ap(), cs_row[:], ins=[B("cs_row")], outs=[B("cs_d")])
        csje = k.sb("csje", [NT, E], BF16, scope=sc3)
        csje_f = k.sb("csje_f", [NT, E], scope=sc3)
        k.qy.dma(csje_f[:], cs_d.ap().rearrange("o (a e) -> (o a) e", e=E),
                 ins=[B("cs_d")], outs=[B("csje_f")])
        k.v.op("tensor_copy", csje[:], csje_f[:], ins=[B("csje_f")], outs=[B("csje")])
        k.t.op("matmul", pk[6][0:NT, 0:E], tri_b[0:NT, 0:NT], csje[:],
               start=True, stop=True, ins=[B("tri_b"), B("csje")], outs=[B("pk6")])
        lcar = k.sb("lcar", [NT, E], scope=sc3)
        k.v.op("tensor_copy", lcar[:], pk[6][0:NT, 0:E], ins=[B("pk6")], outs=[B("lcar")])
        k.qg.dma(lcar_d.ap(), lcar[:], ins=[B("lcar")], outs=[B("lcar_d")])
        k.t.op("matmul", pk[6][0:1, 0:E], ones_c[0:NT, :], csje[:],
               start=True, stop=True, ins=[B("ones_c"), B("csje")], outs=[B("pk6")])
        totr = k.sb("totr", [1, E], scope=sc3)
        k.v.op("tensor_copy", totr[:], pk[6][0:1, 0:E], ins=[B("pk6")], outs=[B("totr")])
        k.qg.dma(tot_d.ap(), totr[:], ins=[B("totr")], outs=[B("tot_d")])
        k.qc.collective("AllGather", ALU.bypass, RG, [tot_d.ap()], [totF_d.ap()],
                        ins=[B("tot_d")], outs=[B("totF_d")])
        selx = k.sb("selx", [128, E * NT], scope=sc3)
        k.v.op("tensor_copy", selx.ap().rearrange("p (e a) -> p e a", a=NT),
               sel.ap().rearrange("p (a e) -> p e a", e=E),
               ins=[B("sel")], outs=[B("selx")])
        k.qg.dma(selP_d.ap(), selx[:], ins=[B("selx")], outs=[B("selP_d")])
        k.qg.wait_all(k.g)
        k.qc.collective("AllGather", ALU.bypass, RG, [selP_d.ap()], [selF_d.ap()],
                        ins=[B("selP_d")], outs=[B("selF_d")])
        totF = k.sb("totF", [NCORE, E], scope=sc3)
        k.qy.dma(totF[:], totF_d.ap(), ins=[B("totF_d")], outs=[B("totF")])
        k.t.op("matmul", pk[6][0:NCORE, 0:E], tri_f[0:NCORE, 0:NCORE], totF[:],
               start=True, stop=True, ins=[B("tri_f"), B("totF")], outs=[B("pk6")])
        ccar = k.sb("ccar", [NCORE, E], scope=sc3)
        k.v.op("tensor_copy", ccar[:], pk[6][0:NCORE, 0:E], ins=[B("pk6")],
               outs=[B("ccar")])
        k.qg.dma(ccar_d.ap(), ccar[:], ins=[B("ccar")], outs=[B("ccar_d")])
        ccrow = k.sb("ccrow", [1, E], scope=sc3)
        k.qg.wait_all(k.y)
        k.qy.dma(ccrow[:], ccar_d.ap()[ds(pid["y"], 1), :],
                 ins=[B("ccar_d")], outs=[B("ccrow")])
        ccb = k.sb("ccb", [128, E], scope=sc3)
        k.t.op("matmul", pk[5][:, 0:E], ones_rf[:], ccrow[0:1, :],
               start=True, stop=True, ins=[B("ones_rf"), B("ccrow")], outs=[B("pk5")])
        k.v.op("tensor_copy", ccb[:], pk[5][:, 0:E], ins=[B("pk5")], outs=[B("ccb")])
        lcrow = k.sb("lcrow", [1, NCL], scope=sc3)
        k.qy.dma(lcrow[:], lcar_d.ap().rearrange("a e -> () (a e)"),
                 ins=[B("lcar_d")], outs=[B("lcrow")])
        lcb = k.sb("lcb", [128, NCL], scope=sc3)
        for j in range(cdiv(NCL, 512)):
            w = min(512, NCL - 512 * j)
            p = pk[4 + (j % 2)]
            k.t.op("matmul", p[:, 0:w], ones_rf[:], lcrow[0:1, 512 * j:512 * j + w],
                   start=True, stop=True, ins=[B("ones_rf"), B("lcrow")],
                   outs=[B(f"pk{4 + (j % 2)}")])
            k.v.op("tensor_copy", lcb[:, 512 * j:512 * j + w], p[:, 0:w],
                   ins=[B(f"pk{4 + (j % 2)}")], outs=[B("lcb")])
        pos, pos3 = sb3("pos")
        k.v.op("tensor_tensor", pos[:], wpx[:], lcb[:], ALU.add,
               ins=[B("wpx"), B("lcb")], outs=[B("pos")])
        ccb3 = ccb.ap().rearrange("p e -> p () e").broadcast_to((128, NT, E))
        k.v.op("tensor_tensor", pos3, pos3, ccb3, ALU.add,
               ins=[B("pos"), B("ccb")], outs=[B("pos")])
        keep, keep3 = sb3("keep")
        k.v.op("tensor_scalar", keep[:], pos[:], float(CAP), None, ALU.is_lt,
               ins=[B("pos")], outs=[B("keep")])
        k.v.op("tensor_tensor", keep[:], keep[:], sel[:], ALU.mult,
               ins=[B("keep"), B("sel")], outs=[B("keep")])
        is2 = k.sb("is2", [128, NCL], scope=sc3)
        k.v.op("tensor_tensor", is2[:], sel[:], is1[:], ALU.subtract,
               ins=[B("sel"), B("is1")], outs=[B("is2")])
        tmp3t = k.sb("tmp3t", [128, NCL], scope=sc3)
        tmp3 = tmp3t.ap().rearrange("p (a e) -> p a e", e=E)
        red1 = k.sb("red1", [128, NT], scope=sc3)
        red13 = red1.ap().rearrange("p (a o) -> p a o", o=1)
        cidf = k.sb("cidf", [128, NT], scope=sc3)

        def extract(mask_t, mask_buf, cid_dst, cid_buf, wk_dst, wk_buf):
            m3 = mask_t.ap().rearrange("p (a e) -> p a e", e=E)
            k.v.op("tensor_tensor", tmp3, m3, ioe3, ALU.mult,
                   ins=[mask_buf, B("io_e")], outs=[B("tmp3t")])
            k.v.op("tensor_reduce", red13, tmp3, AX.X, ALU.add,
                   ins=[B("tmp3t")], outs=[B("red1")])
            k.v.op("tensor_scalar", cidf[:], red1[:], float(CHS), None, ALU.mult,
                   ins=[B("red1")], outs=[B("cidf")])
            k.v.op("tensor_tensor", tmp3t[:], mask_t[:], pos[:], ALU.mult,
                   ins=[mask_buf, B("pos")], outs=[B("tmp3t")])
            k.v.op("tensor_reduce", red13, tmp3, AX.X, ALU.add,
                   ins=[B("tmp3t")], outs=[B("red1")])
            k.v.op("tensor_tensor", cidf[:], cidf[:], red1[:], ALU.add,
                   ins=[B("cidf"), B("red1")], outs=[B("cidf")])
            for j in range(1, NCH_AG):
                k.v.op("tensor_scalar", tmp3t[:, 0:NT], red1[:], float(CHS * j),
                       None, ALU.is_ge, ins=[B("red1")], outs=[B("tmp3t")])
                k.v.op("scalar_tensor_tensor", cidf[:], tmp3t[:, 0:NT],
                       float((E - 1) * CHS), cidf[:], ALU.mult, ALU.add,
                       ins=[B("tmp3t"), B("cidf")], outs=[B("cidf")])
            k.v.op("tensor_tensor", tmp3t[:], mask_t[:], keep[:], ALU.mult,
                   ins=[mask_buf, B("keep")], outs=[B("tmp3t")])
            k.v.op("tensor_reduce", red13, tmp3, AX.X, ALU.add,
                   ins=[B("tmp3t")], outs=[B("red1")])
            k.v.op("tensor_scalar", tmp3t[:, 0:NT], red1[:], -1e6, 1e6,
                   ALU.mult, ALU.add, ins=[B("red1")], outs=[B("tmp3t")])
            k.v.op("tensor_tensor", cidf[:], cidf[:], tmp3t[:, 0:NT], ALU.add,
                   ins=[B("cidf"), B("tmp3t")], outs=[B("cidf")])
            k.v.op("tensor_scalar", cidf[:], cidf[:], float(E * CAP - 1), None,
                   ALU.min, ins=[B("cidf")], outs=[B("cidf")])
            k.v.op("tensor_copy", cid_dst[:], cidf[:], ins=[B("cidf")],
                   outs=[cid_buf])
            k.v.op("tensor_tensor", tmp3t[:], mask_t[:], wprob[:], ALU.mult,
                   ins=[mask_buf, B("wprob")], outs=[B("tmp3t")])
            k.v.op("tensor_reduce",
                   wk_dst.ap()[0:128, 0:NT].rearrange("p (a o) -> p a o", o=1),
                   tmp3, AX.X, ALU.add, ins=[B("tmp3t")], outs=[wk_buf])
            k.v.op("tensor_tensor", wk_dst[:], wk_dst[:], red1[:], ALU.mult,
                   ins=[wk_buf, B("red1")], outs=[wk_buf])

        extract(is1, B("is1"), cid1_s, B("cid1_s"), wk1_s, B("wk1_s"))
        extract(is2, B("is2"), cid2_s, B("cid2_s"), wk2_s, B("wk2_s"))
    k.fence()

    scw.close()
    k.fence()

    sc6a = ExitStack()
    gpwt = k.sb("gpwt", [128, 8 * 4096], BF16, scope=sc6a)
    k.qg.wait_all(k.y)
    for kk in range(8):
        k.qy.dma_t(gpwt[:, 4096 * kk:4096 * (kk + 1)],
                   gpq_d.ap()[:, 128 * kk:128 * (kk + 1)],
                   ins=[B("gpq_d")], outs=[B("gpwt")])

    with ExitStack() as sc5:
        selg = k.sb("selg", [128, NJ], scope=sc5)
        ey = pid["y"]
        for c in range(NCORE):
            k.qy.dma(selg[:, NT * c:NT * (c + 1)],
                     selF_d.ap()[128 * c:128 * (c + 1)][:, ds(ey * NT, NT)],
                     ins=[B("selF_d")], outs=[B("selg")])
        selgb = k.sb("selgb", [128, NJ], BF16, scope=sc5)
        k.v.op("tensor_copy", selgb[:], selg[:], ins=[B("selg")], outs=[B("selgb")])
        csg = k.sb("csg", [1, NJ], scope=sc5)
        for j in range(cdiv(NJ, 512)):
            w = min(512, NJ - 512 * j)
            p = pk[j % 2]
            k.t.op("matmul", p[0:1, 0:w], ones_c[:], selgb[:, 512 * j:512 * j + w],
                   start=True, stop=True, ins=[B("ones_c"), B("selgb")],
                   outs=[B(f"pk{j % 2}")])
            k.v.op("tensor_copy", csg[0:1, 512 * j:512 * j + w], p[0:1, 0:w],
                   ins=[B(f"pk{j % 2}")], outs=[B("csg")])
        csgc = k.sb("csgc", [NJ, 1], BF16, scope=sc5)
        k.t.op("transpose", pk[2][0:NJ, 0:1], csg[:], id_f[0:1, 0:1],
               ins=[B("csg"), B("id_f")], outs=[B("pk2")])
        csgc_f = k.sb("csgc_f", [NJ, 1], scope=sc5)
        k.v.op("tensor_copy", csgc_f[:], pk[2][0:NJ, 0:1], ins=[B("pk2")],
               outs=[B("csgc_f")])
        k.v.op("tensor_copy", csgc[:], csgc_f[:], ins=[B("csgc_f")], outs=[B("csgc")])
        k.t.op("matmul", pk[3][0:NJ, 0:1], tri_b[0:NJ, 0:NJ], csgc[:],
               start=True, stop=True, ins=[B("tri_b"), B("csgc")], outs=[B("pk3")])
        carsel = k.sb("carsel", [NJ, 1], scope=sc5)
        k.v.op("tensor_copy", carsel[:], pk[3][0:NJ, 0:1], ins=[B("pk3")],
               outs=[B("carsel")])
        carK = k.sb("carK", [NJ, 1], scope=sc5)
        k.v.op("tensor_scalar", carK[:], carsel[:], float(CAP), None, ALU.min,
               ins=[B("carsel")], outs=[B("carK")])
        k.t.op("transpose", pk[2][0:1, 0:NJ], carsel[:], id_f[0:NJ, 0:NJ],
               ins=[B("carsel"), B("id_f")], outs=[B("pk2")])
        carselr = k.sb("carselr", [1, NJ], scope=sc5)
        k.v.op("tensor_copy", carselr[:], pk[2][0:1, 0:NJ], ins=[B("pk2")],
               outs=[B("carselr")])
        carKr = k.sb("carKr", [1, NJ], scope=sc5)
        k.v.op("tensor_scalar", carKr[:], carselr[:], float(CAP), None, ALU.min,
               ins=[B("carselr")], outs=[B("carKr")])
        wpe = k.sb("wpe", [128, NJ], scope=sc5)
        for j in range(cdiv(NJ, 512)):
            w = min(512, NJ - 512 * j)
            p = pk[j % 2]
            k.t.op("matmul", p[:, 0:w], tri_b[:], selgb[:, 512 * j:512 * j + w],
                   start=True, stop=True, ins=[B("tri_b"), B("selgb")],
                   outs=[B(f"pk{j % 2}")])
            k.v.op("tensor_copy", wpe[:, 512 * j:512 * j + w], p[:, 0:w],
                   ins=[B(f"pk{j % 2}")], outs=[B("wpe")])
        carsb = k.sb("carsb", [128, NJ], scope=sc5)
        for j in range(cdiv(NJ, 512)):
            w = min(512, NJ - 512 * j)
            p = pk[2 + (j % 2)]
            k.t.op("matmul", p[:, 0:w], ones_rf[:], carselr[0:1, 512 * j:512 * j + w],
                   start=True, stop=True, ins=[B("ones_rf"), B("carselr")],
                   outs=[B(f"pk{2 + (j % 2)}")])
            k.v.op("tensor_copy", carsb[:, 512 * j:512 * j + w], p[:, 0:w],
                   ins=[B(f"pk{2 + (j % 2)}")], outs=[B("carsb")])
        pose = k.sb("pose", [128, NJ], scope=sc5)
        k.v.op("tensor_tensor", pose[:], wpe[:], carsb[:], ALU.add,
               ins=[B("wpe"), B("carsb")], outs=[B("pose")])
        keep_e = k.sb("keep_e", [128, NJ], scope=sc5)
        k.v.op("tensor_scalar", keep_e[:], pose[:], float(CAP), None, ALU.is_lt,
               ins=[B("pose")], outs=[B("keep_e")])
        k.v.op("tensor_tensor", keep_e[:], keep_e[:], selg[:], ALU.mult,
               ins=[B("keep_e"), B("selg")], outs=[B("keep_e")])
        keb = k.sb("keb", [128, NJ], BF16, scope=sc5)
        k.v.op("tensor_copy", keb[:], keep_e[:], ins=[B("keep_e")], outs=[B("keb")])
        rw = k.sb("rw", [128, NJ], scope=sc5)
        for j in range(cdiv(NJ, 512)):
            w = min(512, NJ - 512 * j)
            p = pk[j % 2]
            k.t.op("matmul", p[:, 0:w], tri_b[:], keb[:, 512 * j:512 * j + w],
                   start=True, stop=True, ins=[B("tri_b"), B("keb")],
                   outs=[B(f"pk{j % 2}")])
            k.v.op("tensor_copy", rw[:, 512 * j:512 * j + w], p[:, 0:w],
                   ins=[B(f"pk{j % 2}")], outs=[B("rw")])
        k.v.op("tensor_scalar", rw[:], rw[:], 1.0, None, ALU.add,
               ins=[B("rw")], outs=[B("rw")])
        k.v.op("tensor_tensor", rw[:], rw[:], keep_e[:], ALU.mult,
               ins=[B("rw"), B("keep_e")], outs=[B("rw")])
        k.v.op("tensor_scalar", rw[:], rw[:], -1.0, None, ALU.add,
               ins=[B("rw")], outs=[B("rw")])
        NJH = min(64, NJ)
        pall = k.sb("pall", [128, NJH * 128], BF16, scope=sc5)
        pall3 = pall.ap().rearrange("p (a r) -> p a r", r=128)
        ior3 = io_r.ap().rearrange("p r -> p () r").broadcast_to((128, NJH, 128))
        ct = k.sb("ct", [128, NJ], scope=sc5)
        for hh in range(NJ // NJH):
            rw3 = rw.ap()[0:128, NJH * hh:NJH * (hh + 1)].rearrange(
                "p (a o) -> p a o", o=1).broadcast_to((128, NJH, 128))
            k.v.op("tensor_tensor", pall3, rw3, ior3, ALU.is_equal,
                   ins=[B("rw"), B("io_r")], outs=[B("pall")])
            for jj in range(NJH):
                j = NJH * hh + jj
                p = pk[j % 4]
                k.t.op("matmul", p[:, 0:1], pall[:, 128 * jj:128 * (jj + 1)],
                       iop1_b[:], start=True, stop=True,
                       ins=[B("pall"), B("iop1_b")], outs=[B(f"pk{j % 4}")])
                k.v.op("tensor_copy", ct[:, j:j + 1], p[:, 0:1],
                       ins=[B(f"pk{j % 4}")], outs=[B("ct")])
        k.t.op("transpose", pk[4][0:NJ, 0:128], ct[:], id_f[:],
               ins=[B("ct"), B("id_f")], outs=[B("pk4")])
        ctt = k.sb("ctt", [128, 128], scope=sc5)
        k.v.op("tensor_copy", ctt[0:NJ, :], pk[4][0:NJ, 0:128],
               ins=[B("pk4")], outs=[B("ctt")])
        k.qg.dma(ct_d.ap().rearrange("(a r) o -> a (r o)", r=128), ctt[0:NJ, :],
                 ins=[B("ctt")], outs=[B("ct_d")])
        dcar = k.sb("dcar", [1, NJ], scope=sc5)
        k.v.op("memset", dcar[:], 0.0, outs=[B("dcar")])
        k.v.op("tensor_copy", dcar[0:1, 1:NJ], carKr[0:1, 0:NJ - 1],
               ins=[B("carKr")], outs=[B("dcar")])
        k.v.op("tensor_tensor", dcar[:], carKr[:], dcar[:], ALU.subtract,
               ins=[B("carKr"), B("dcar")], outs=[B("dcar")])
        k.t.op("transpose", pk[5][0:NJ, 0:1], dcar[:], id_f[0:1, 0:1],
               ins=[B("dcar"), B("id_f")], outs=[B("pk5")])
        dcarc = k.sb("dcarc", [NJ, 1], BF16, scope=sc5)
        k.v.op("tensor_copy", dcarc[:], pk[5][0:NJ, 0:1], ins=[B("pk5")],
               outs=[B("dcarc")])
        CH5 = min(1024, CAP)
        cm = k.sb("cm", [128, CH5], BF16, scope=sc5)
        j1r = k.sb("j1r", [1, CH5], scope=sc5)
        cofr = k.sb("cofr", [1, CH5], scope=sc5)
        flr = k.sb("flr", [1, CH5], scope=sc5)
        for ph in range(CAP // CH5):
            k.v.op("tensor_scalar", cm[0:NJ, :],
                   io_s[0:NJ, CH5 * ph:CH5 * (ph + 1)], carK[:, 0:1], None,
                   ALU.is_ge, ins=[B("io_s"), B("carK")], outs=[B("cm")])
            for j in range(cdiv(CH5, 512)):
                w = min(512, CH5 - 512 * j)
                c0 = 512 * j
                p0, p1 = pk[j % 2], pk[2 + j % 2]
                k.t.op("matmul", p0[0:1, 0:w], ones_c[0:NJ, :],
                       cm[0:NJ, c0:c0 + w], start=True, stop=True,
                       ins=[B("ones_c"), B("cm")], outs=[B(f"pk{j % 2}")])
                k.v.op("tensor_copy", j1r[0:1, 512 * j:512 * j + w], p0[0:1, 0:w],
                       ins=[B(f"pk{j % 2}")], outs=[B("j1r")])
                k.t.op("matmul", p1[0:1, 0:w], dcarc[:], cm[0:NJ, c0:c0 + w],
                       start=True, stop=True, ins=[B("dcarc"), B("cm")],
                       outs=[B(f"pk{2 + j % 2}")])
                k.v.op("tensor_copy", cofr[0:1, 512 * j:512 * j + w], p1[0:1, 0:w],
                       ins=[B(f"pk{2 + j % 2}")], outs=[B("cofr")])
            k.v.op("tensor_scalar", flr[:], j1r[:], 128.0, -128.0, ALU.mult,
                   ALU.add, ins=[B("j1r")], outs=[B("flr")])
            k.v.op("tensor_tensor", flr[:], flr[:],
                   io_s[0:1, CH5 * ph:CH5 * (ph + 1)], ALU.add,
                   ins=[B("flr"), B("io_s")], outs=[B("flr")])
            k.v.op("tensor_tensor", flr[:], flr[:], cofr[:], ALU.subtract,
                   ins=[B("flr"), B("cofr")], outs=[B("flr")])
            k.v.op("tensor_scalar", flr[:], flr[:], float(T - 1), None, ALU.min,
                   ins=[B("flr")], outs=[B("flr")])
            k.v.op("tensor_scalar", j1r[:], j1r[:], -1.0, None, ALU.add,
                   ins=[B("j1r")], outs=[B("j1r")])
            k.qg.dma(fl_d.ap()[0:1, CH5 * ph:CH5 * (ph + 1)], flr[:],
                     ins=[B("flr")], outs=[B("fl_d")])
            k.qg.dma(jr_d.ap()[0:1, CH5 * ph:CH5 * (ph + 1)], j1r[:],
                     ins=[B("j1r")], outs=[B("jr_d")])
        k.qg.wait_all(k.y)
        flv = k.sb("flv", [NS, 128], scope=sc5)
        jrv = k.sb("jrv", [NS, 128], scope=sc5)
        k.qy.dma(flv[:], fl_d.ap().rearrange("o (a p) -> (o a) p", p=128),
                 ins=[B("fl_d")], outs=[B("flv")])
        k.qy.dma(jrv[:], jr_d.ap().rearrange("o (a p) -> (o a) p", p=128),
                 ins=[B("jr_d")], outs=[B("jrv")])
        k.t.op("transpose", pk[0][:, 0:NS], flv[:], id_f[0:NS, 0:NS],
               ins=[B("flv"), B("id_f")], outs=[B("pk0")])
        flT = k.sb("flT", [128, NS], scope=sc5)
        k.v.op("tensor_copy", flT[:], pk[0][:, 0:NS], ins=[B("pk0")], outs=[B("flT")])
        k.t.op("transpose", pk[1][:, 0:NS], jrv[:], id_f[0:NS, 0:NS],
               ins=[B("jrv"), B("id_f")], outs=[B("pk1")])
        jrT = k.sb("jrT", [128, NS], scope=sc5)
        k.v.op("tensor_copy", jrT[:], pk[1][:, 0:NS], ins=[B("pk1")], outs=[B("jrT")])
        flTi = k.sb("flTi", [128, NS], I32, scope=sc5)
        k.v.op("tensor_copy", flTi[:], flT[:], ins=[B("flT")], outs=[B("flTi")])
        ctg = k.sb("ctg", [128, NS], scope=sc5)
        for s in range(NS):
            k.qg.indirect(ctg[:, s:s + 1], None, ct_d.ap(),
                          IndirectOffsetOnAxis(flTi[:, s:s + 1], 0),
                          ins=[B("flTi"), B("ct_d")], outs=[B("ctg")])
        xf_t = k.sb("xf_t", [128, NS], scope=sc5)
        k.v.op("tensor_scalar", xf_t[:], jrT[:], 128.0, None, ALU.mult,
               ins=[B("jrT")], outs=[B("xf_t")])
        k.v.op("tensor_tensor", xf_t[:], xf_t[:], ctg[:], ALU.add,
               ins=[B("xf_t"), B("ctg")], outs=[B("xf_t")])
        k.v.op("tensor_scalar", xf_t[:], xf_t[:], -1.0, 0.0, ALU.add, ALU.max,
               ins=[B("xf_t")], outs=[B("xf_t")])
        k.v.op("tensor_copy", xidx_s[:], xf_t[:], ins=[B("xf_t")], outs=[B("xidx_s")])
    k.fence()



    NG = min(8, NS, NT)
    with ExitStack() as sc6:
        sc6x = ExitStack()
        SS6 = SS
        p6w = [("xin", [128, H], F32), ("xinB", [128, H], F32),
               ("he", [128, H], F32), ("tgp", [128, H], F32)]
        for S in SS6:
            for nm, shape, dt in p6w:
                S.h[nm] = k.sb(f"{nm}_u{S.si}", shape, dt, scope=sc6x)
            k.v.op("memset", S["xin"][:], 0.0, outs=[S.b("xin")])
            k.v.op("memset", S["xinB"][:], 0.0, outs=[S.b("xinB")])

        def p6_gather(s, S, i):
            nm = "xin" if i % 2 == 0 else "xinB"
            k.qg.indirect(S[nm][:], None, x_full.ap(),
                          IndirectOffsetOnAxis(xidx_s[:, s:s + 1], 0),
                          ins=[B("xidx_s"), B("x_full")], outs=[S.b(nm)])

        def p6_chain(s, S, i):
            xin = S["xin"] if i % 2 == 0 else S["xinB"]
            bxin = S.b("xin") if i % 2 == 0 else S.b("xinB")
            he, tgp = S["he"], S["tgp"]
            hv = tgp
            bhe, btgp = S.b("he"), S.b("tgp")
            bhv = btgp
            st = [S[f"st{i}"] for i in range(8)]
            sb_ = [S.b(f"st{i}") for i in range(8)]
            scr_, bscr = S["scr"], S.b("scr")
            yield
            yield from adaln_tile(S, xin[:], bxin, he[:], bhe, hv, bhv)
            k.s.op("activation", scr_[:], he[:], AF.Square, bias=0.0, scale=1.0,
                   accum_out=st[0][:], ins=[bhe], outs=[bscr, sb_[0]])
            yield
            yield from rsqrt_col(S, 1, 0, 1.0 / H, 1e-8)
            k.v.op("tensor_tensor", tgp[:], he[:], gpnb[:], ALU.mult,
                   ins=[bhe, B("gpnb")], outs=[btgp])
            yield
            k.v.op("tensor_reduce", st[2][:], tgp[:], AX.X, ALU.max,
                   apply_absolute_value=True, ins=[btgp], outs=[sb_[2]])
            yield
            k.v.op("tensor_scalar", st[3][:], st[2][:], 1e-30, None, ALU.max,
                   ins=[sb_[2]], outs=[sb_[3]])
            k.v.op("reciprocal", st[3][:], st[3][:], ins=[sb_[3]], outs=[sb_[3]])
            k.v.op("tensor_scalar", st[3][:], st[3][:], 127.0, None, ALU.mult,
                   ins=[sb_[3]], outs=[sb_[3]])
            yield
            k.s.op("activation", scr_[:], tgp[:], AF.Identity, bias=RC,
                   scale=st[3][:, 0:1], ins=[btgp, sb_[3], B("rc_t")],
                   outs=[bscr])
            yield
            k.qg.dma(ng6_d.ap()[128 * s:128 * (s + 1), :], scr_[:],
                     ins=[bscr], outs=[B("ng6_d")])
            k.v.op("tensor_tensor", st[4][:], st[1][:], st[2][:], ALU.mult,
                   ins=[sb_[1], sb_[2]], outs=[sb_[4]])
            k.v.op("tensor_scalar", st[4][:], st[4][:], 1e-5, None, ALU.max,
                   ins=[sb_[4]], outs=[sb_[4]])
            k.v.op("tensor_tensor", agp_s[:, s:s + 1], st[4][:], mgpc_b[:],
                   ALU.mult, ins=[sb_[4], B("mgpc_b")], outs=[B("agp_s")])
            yield

        def p6_stream(si):
            S = SS6[si]
            tiles = list(range(si, NS, 3))
            p6_gather(tiles[0], S, 0)
            for i, s in enumerate(tiles):
                if i + 1 < len(tiles):
                    p6_gather(tiles[i + 1], S, i + 1)
                yield from p6_chain(s, S, i)

        interleave([p6_stream(si) for si in range(3)])
        sc6x.close()
        k.fence()
        k.qg.wait_all(k.y)
        dpwt = k.sb("dpwt", [128, 16 * 1024], BF16, scope=sc6a)
        for kk in range(16):
            k.qy.dma_t(dpwt[:, 1024 * kk:1024 * (kk + 1)],
                       dpq_d.ap()[:, 128 * kk:128 * (kk + 1)],
                       ins=[B("dpq_d")], outs=[B("dpwt")])
        prod = k.sb("prod", [128, IM], scope=sc6a)
        uu2 = [k.sb(f"uu{i}", [128, IM], scope=sc6a) for i in range(2)]
        n2tl = k.sb("n2tl", [128, 16 * 128], BF16, scope=sc6a)
        ctrb = k.sb("ctrb", [128, H], scope=sc6a)
        ngs2 = [k.sb(f"ngs{i}", [128, H], F32, scope=sc6a) for i in range(2)]
        nmTs = k.sb("nmTs", [128, 8 * 128], BF16, scope=sc6a)

        def load_ngs(s):
            k.qs.dma(ngs2[s % 2][:], ng6_d.ap()[128 * s:128 * (s + 1), :],
                     ins=[B("ng6_d")], outs=[B(f"ngs{s % 2}")])

        def gp_part(s, si):
            bngs = B(f"ngs{s % 2}")
            uu, buu = uu2[s % 2], B(f"uu{s % 2}")
            for kk in range(8):
                tb = kk // 4
                k.t.op("transpose", pk[tb][:, 128 * (kk % 4):128 * (kk % 4) + 128],
                       ngs2[s % 2][:, 128 * kk:128 * (kk + 1)], id_f[:],
                       ins=[bngs, B("id_f")], outs=[B(f"pk{tb}")],
                       defer=(kk % 4 != 3))
                if kk % 4 == 3:
                    k.s.op("activation", nmTs[:, 512 * tb:512 * (tb + 1)],
                           pk[tb][:, 0:512], AF.Copy, bias=-RC, scale=1.0,
                           ins=[B(f"pk{tb}"), B("mrc_t")], outs=[B("nmTs")])
            for half in range(2):
                pb = 4 * half
                for kk in range(8):
                    for nch in range(4):
                        k.t.op("matmul", pk[pb + nch][:, 0:512],
                               nmTs[:, 128 * kk:128 * (kk + 1)],
                               gpwt[:, 4096 * kk + 2048 * half + 512 * nch:
                                    4096 * kk + 2048 * half + 512 * (nch + 1)],
                               start=(kk == 0), stop=(kk == 7),
                               ins=[B("nmTs"), B("gpwt")], outs=[B(f"pk{pb + nch}")],
                               defer=not (kk == 7 and nch == 3))
                if half == 0:
                    for nch in range(4):
                        k.s.op("activation", prod[:, 512 * nch:512 * (nch + 1)],
                               pk[pb + nch][:, 0:512], AF.Silu,
                               bias=0.0, scale=agp_s[:, s:s + 1],
                               ins=[B(f"pk{pb + nch}"), B("agp_s")], outs=[B("prod")])
                else:
                    for nch in range(4):
                        k.v.op("scalar_tensor_tensor", prod[:, 512 * nch:512 * (nch + 1)],
                               pk[pb + nch][:, 0:512], agp_s[:, s:s + 1],
                               prod[:, 512 * nch:512 * (nch + 1)], ALU.mult, ALU.mult,
                               ins=[B(f"pk{pb + nch}"), B("agp_s"), B("prod")],
                               outs=[B("prod")])
            k.s.op("activation", uu[:], prod[:], AF.Square, bias=0.0, scale=1.0,
                   accum_out=stt[0][:], ins=[B("prod")],
                   outs=[buu, B("stt0")])
            drive(rsqrt_col(LS, 1, 0, 1.0 / IM, 1e-8))
            k.v.op("tensor_tensor", uu[:], prod[:], dpnb[:], ALU.mult,
                   ins=[B("prod"), B("dpnb")], outs=[buu])
            k.v.op("tensor_reduce", stt[2][:], uu[:], AX.X, ALU.max,
                   apply_absolute_value=True, ins=[buu], outs=[B("stt2")])
            k.v.op("tensor_scalar", stt[3][:], stt[2][:], 1e-30, None, ALU.max,
                   ins=[B("stt2")], outs=[B("stt3")])
            k.v.op("reciprocal", stt[3][:], stt[3][:], ins=[B("stt3")],
                   outs=[B("stt3")])
            k.v.op("tensor_scalar", stt[3][:], stt[3][:], 127.0, None, ALU.mult,
                   ins=[B("stt3")], outs=[B("stt3")])
            k.s.op("activation", uu[:], uu[:], AF.Identity, bias=RC,
                   scale=stt[3][:, 0:1], ins=[buu, B("stt3"), B("rc_t")],
                   outs=[buu])
            k.v.op("tensor_tensor", stt[4][:], stt[1][:], stt[2][:], ALU.mult,
                   ins=[B("stt1"), B("stt2")], outs=[B("stt4")])
            k.v.op("tensor_scalar", stt[4][:], stt[4][:], 1e-5, None, ALU.max,
                   ins=[B("stt4")], outs=[B("stt4")])
            k.v.op("tensor_tensor", adp_s[:, s:s + 1], stt[4][:], mdpc_b[:],
                   ALU.mult, ins=[B("stt4"), B("mdpc_b")], outs=[B("adp_s")])

        def dp_part(s):
            uu, buu = uu2[s % 2], B(f"uu{s % 2}")
            for kk in range(16):
                tb = 2 + kk // 4
                k.t.op("transpose", pk[tb][:, 128 * (kk % 4):128 * (kk % 4) + 128],
                       uu[:, 128 * kk:128 * (kk + 1)], id_f[:],
                       ins=[buu, B("id_f")], outs=[B(f"pk{tb}")],
                       defer=(kk % 4 != 3))
                if kk % 4 == 3:
                    k.s.op("activation", n2tl[:, 512 * (tb - 2):512 * (tb - 1)],
                           pk[tb][:, 0:512], AF.Copy, bias=-RC, scale=1.0,
                           ins=[B(f"pk{tb}"), B("mrc_t")], outs=[B("n2tl")])
            for kk in range(16):
                for nch in range(2):
                    k.t.op("matmul", pk[nch][:, 0:512],
                           n2tl[:, 128 * kk:128 * (kk + 1)],
                           dpwt[:, 1024 * kk + 512 * nch:1024 * kk + 512 * (nch + 1)],
                           start=(kk == 0), stop=(kk == 15),
                           ins=[B("n2tl"), B("dpwt")], outs=[B(f"pk{nch}")],
                           defer=not (kk == 15 and nch == 1))
            for nch in range(2):
                k.s.op("activation", ctrb[:, 512 * nch:512 * (nch + 1)],
                       pk[nch][:, 0:512], AF.Copy, bias=0.0,
                       scale=adp_s[:, s:s + 1],
                       ins=[B(f"pk{nch}"), B("adp_s")], outs=[B("ctrb")])
            k.qg.dma(contribP.ap()[128 * s:128 * (s + 1), :], ctrb[:],
                     ins=[B("ctrb")], outs=[B("contribP")])
            if (s + 1) % (NS // NCH_AG) == 0:
                c = (s + 1) // (NS // NCH_AG) - 1
                k.qg.wait_all(k.g)
                k.qc.collective(
                    "AllGather", ALU.bypass, RG,
                    [contribP.ap()[CHS * c:CHS * (c + 1), :]],
                    [contribF.ap()[E * CHS * c:E * CHS * (c + 1), :]],
                    ins=[B("contribP")], outs=[B("contribF")])

        load_ngs(0)
        for s in range(NS):
            if s + 1 < NS:
                load_ngs(s + 1)
            gp_part(s, s % NG)
            if s >= 1:
                dp_part(s - 1)
        dp_part(NS - 1)
        sc6a.close()
    k.fence(skip=(k.qc,))

    sc9 = ExitStack()
    wgt = k.sb("wgt", [128, 8 * IF], BF16, scope=sc9)
    wut = k.sb("wut", [128, 8 * IF], BF16, scope=sc9)

    def load_wgut():
        for srcT, srcb, dstt, dstb in ((fqgT_d, "fqgT_d", wgt, "wgt"),
                                       (fquT_d, "fquT_d", wut, "wut")):
            src4 = srcT.ap().rearrange("(c b p) r -> p b c r", c=NCORE, p=128)
            for kk in range(8):
                k.qy.dma(dstt[:, IF * kk:IF * (kk + 1)]
                         .rearrange("p (c r) -> p () c r", r=IFS),
                         src4[:, kk:kk + 1], ins=[B(srcb)], outs=[B(dstb)])

    with ExitStack() as sc7:
        SS7 = SS[:2]
        p7w = [("c1t", [128, H], F32), ("c2t", [128, H], F32),
               ("c1tB", [128, H], F32), ("c2tB", [128, H], F32),
               ("xt7", [128, H], F32)]
        for S in SS7:
            for nm, shape, dt in p7w:
                S.h[nm] = k.sb(f"{nm}_u{S.si}", shape, dt, scope=sc7)
        macc = k.sb("macc", [128, NT], scope=sc7)
        for S in SS7:
            for nm in ("c1t", "c2t", "c1tB", "c2tB"):
                k.v.op("memset", S[nm][:], 0.0, outs=[S.b(nm)])
        k.qc.wait_all(k.g)

        def p7_gather(t, S, i):
            sfx = "" if i % 2 == 0 else "B"
            k.qg.indirect(S["c1t" + sfx][:], None, contribF.ap(),
                          IndirectOffsetOnAxis(cid1_s[:, t:t + 1], 0),
                          ins=[B("cid1_s"), B("contribF")],
                          outs=[S.b("c1t" + sfx)])
            k.qg.indirect(S["c2t" + sfx][:], None, contribF.ap(),
                          IndirectOffsetOnAxis(cid2_s[:, t:t + 1], 0),
                          ins=[B("cid2_s"), B("contribF")],
                          outs=[S.b("c2t" + sfx)])

        def p7_chain(t, S, i):
            sfx = "" if i % 2 == 0 else "B"
            c1t, c2t, xt7 = (S["c1t" + sfx], S["c2t" + sfx], S["xt7"])
            h2t = c2t
            b1, b2, bx = (S.b("c1t" + sfx), S.b("c2t" + sfx), S.b("xt7"))
            bh2 = b2
            st = [S[f"st{i}"] for i in range(4)]
            sb_ = [S.b(f"st{i}") for i in range(4)]
            scr_, bscr = S["scr"], S.b("scr")
            k.qs.dma(xt7[:], xown.ap()[128 * t:128 * (t + 1), :],
                     ins=[B("xown")], outs=[bx])
            yield
            k.s.op("activation", scr_[:], c1t[:], AF.Copy, bias=0.0,
                   scale=wk1_s[:, t:t + 1], ins=[b1, B("wk1_s")], outs=[bscr])
            yield
            k.v.op("scalar_tensor_tensor", scr_[:], c2t[:], wk2_s[:, t:t + 1],
                   scr_[:], ALU.mult, ALU.add, ins=[b2, B("wk2_s"), bscr],
                   outs=[bscr])
            yield
            k.v.op("tensor_tensor", scr_[:], scr_[:], gaxb[:], ALU.mult,
                   ins=[bscr, B("gaxb")], outs=[bscr])
            k.v.op("tensor_tensor", xt7[:], scr_[:], xt7[:], ALU.add,
                   ins=[bscr, bx], outs=[bx])
            yield
            k.qg.dma(x1_d.ap()[128 * t:128 * (t + 1), :], xt7[:],
                     ins=[bx], outs=[B("x1_d")])
            k.v.op("tensor_reduce", st[0][:], xt7[:], AX.X, ALU.add,
                   ins=[bx], outs=[sb_[0]])
            yield
            k.s.op("activation", c1t[:], xt7[:], AF.Square, bias=0.0, scale=1.0,
                   accum_out=st[1][:], ins=[bx], outs=[b1, sb_[1]])
            yield
            k.v.op("tensor_scalar", st[0][:], st[0][:], 1.0 / H, None, ALU.mult,
                   ins=[sb_[0]], outs=[sb_[0]])
            k.v.op("tensor_tensor", st[2][:], st[0][:], st[0][:], ALU.mult,
                   ins=[sb_[0]], outs=[sb_[2]])
            k.v.op("tensor_scalar", st[3][:], st[1][:], 1.0 / H, None, ALU.mult,
                   ins=[sb_[1]], outs=[sb_[3]])
            k.v.op("tensor_tensor", st[3][:], st[3][:], st[2][:], ALU.subtract,
                   ins=[sb_[3], sb_[2]], outs=[sb_[3]])
            k.v.op("tensor_scalar", st[3][:], st[3][:], 1e-5, None, ALU.add,
                   ins=[sb_[3]], outs=[sb_[3]])
            yield
            k.s.op("activation", st[3][:], st[3][:], AF.Sqrt, bias=0.0, scale=1.0,
                   ins=[sb_[3]], outs=[sb_[3]])
            yield
            k.v.op("reciprocal", st[3][:], st[3][:], ins=[sb_[3]], outs=[sb_[3]])
            k.v.op("tensor_tensor", st[2][:], st[0][:], st[3][:], ALU.mult,
                   ins=[sb_[0], sb_[3]], outs=[sb_[2]])
            k.v.op("tensor_scalar", st[2][:], st[2][:], -1.0, None, ALU.mult,
                   ins=[sb_[2]], outs=[sb_[2]])
            yield
            k.s.op("activation", h2t[:], xt7[:], AF.Identity, bias=st[2][:, 0:1],
                   scale=st[3][:, 0:1], ins=[bx, sb_[2], sb_[3]], outs=[bh2])
            yield
            k.v.op("tensor_tensor", h2t[:], h2t[:], cb1b[:], ALU.mult,
                   ins=[bh2, B("cb1b")], outs=[bh2])
            k.v.op("tensor_tensor", h2t[:], h2t[:], sbb[:], ALU.add,
                   ins=[bh2, B("sbb")], outs=[bh2])
            yield
            k.qg.dma(h2_d.ap()[128 * t:128 * (t + 1), :], h2t[:],
                     ins=[bh2], outs=[B("h2_d")])
            k.v.op("tensor_reduce", macc[:, t:t + 1], h2t[:], AX.X, ALU.max,
                   apply_absolute_value=True, ins=[bh2], outs=[B("macc")])
            yield

        def p7_stream(si):
            S = SS7[si]
            tiles = list(range(si, NT, 2))
            p7_gather(tiles[0], S, 0)
            for i, t in enumerate(tiles):
                if i + 1 < len(tiles):
                    p7_gather(tiles[i + 1], S, i + 1)
                yield from p7_chain(t, S, i)

        interleave([p7_stream(si) for si in range(2)])
        load_wgut()
        k.v.op("tensor_reduce", stt[0][:], macc[:], AX.X, ALU.max,
               ins=[B("macc")], outs=[B("stt0")])
        k.t.op("transpose", pk[0][0:1, 0:128], stt[0][:], id_f[:],
               ins=[B("stt0"), B("id_f")], outs=[B("pk0")])
        mrow1 = k.sb("mrow1", [1, 128], scope=sc7)
        k.v.op("tensor_copy", mrow1[:], pk[0][0:1, 0:128], ins=[B("pk0")],
               outs=[B("mrow1")])
        gmx = k.sb("gmx", [1, 8], scope=sc7)
        k.v.op("memset", gmx[:], 0.0, outs=[B("gmx")])
        k.v.op("tensor_reduce", gmx[0:1, 0:1], mrow1[:], AX.X, ALU.max,
               ins=[B("mrow1")], outs=[B("gmx")])
        k.qg.dma(gm1P.ap(), gmx[:], ins=[B("gmx")], outs=[B("gm1P")])
        k.qc.collective("AllGather", ALU.bypass, RG, [gm1P.ap()], [gm1F.ap()],
                        ins=[B("gm1P")], outs=[B("gm1F")])
        gmall = k.sb("gmall", [1, 64], scope=sc7)
        k.qy.dma(gmall[:], gm1F.ap().rearrange("a b -> () (a b)"),
                 ins=[B("gm1F")], outs=[B("gmall")])
        k.v.op("tensor_reduce", gmx[0:1, 0:1], gmall[:], AX.X, ALU.max,
               ins=[B("gmall")], outs=[B("gmx")])
        clx = k.sb("clx", [1, 1], scope=sc7)
        k.qy.dma(clx[:], clipx_in.ap(), ins=[B("clipx_in")], outs=[B("clx")])
        k.s.op("activation", clx[:], clx[:], AF.Sigmoid, bias=0.0, scale=1.0,
               ins=[B("clx")], outs=[B("clx")])
        k.v.op("tensor_tensor", clx[:], clx[:], gmx[0:1, 0:1], ALU.mult,
               ins=[B("clx"), B("gmx")], outs=[B("clx")])
        k.v.op("tensor_scalar", clx[:], clx[:], 1.0 / 127.0, None, ALU.mult,
               ins=[B("clx")], outs=[B("clx")])
        sxv_ = bcast_row("sxv_", clx[:], B("clx"), 1, scope=sc7)
        k.v.op("tensor_copy", sxv[:], sxv_[:], ins=[B("sxv_")], outs=[B("sxv")])
        k.v.op("reciprocal", sxb[:], sxv[:], ins=[B("sxv")], outs=[B("sxb")])
        for S in SS7:
            S.h["n1t_"] = k.sb(f"n1t__u{S.si}", [128, H], BF16, scope=sc7)
        k.qg.wait_all(k.y)
        k.qg.wait_all(k.s)

        def n1_chain(t, S):
            h2t, n1t_, scr_ = S["c2t"], S["n1t_"], S["scr"]
            bh2, bn1, bscr = S.b("c2t"), S.b("n1t_"), S.b("scr")
            k.qs.dma(h2t[:], h2_d.ap()[128 * t:128 * (t + 1), :],
                     ins=[B("h2_d")], outs=[bh2])
            yield
            k.s.op("activation", scr_[:], h2t[:], AF.Identity, bias=RC,
                   scale=sxb[:, 0:1], ins=[bh2, B("sxb"), B("rc_t")],
                   outs=[bscr])
            yield
            k.v.op("tensor_scalar", scr_[:], scr_[:], -RC, None, ALU.add,
                   ins=[bscr], outs=[bscr])
            k.v.op("tensor_scalar", n1t_[:], scr_[:], -128.0, 127.0, ALU.max,
                   ALU.min, ins=[bscr], outs=[bn1])
            yield
            k.qg.dma(n1_d.ap()[128 * t:128 * (t + 1), :], n1t_[:],
                     ins=[bn1], outs=[B("n1_d")])
            yield

        interleave([(lambda si: (x for t in range(si, NT, 2)
                                 for x in n1_chain(t, SS7[si])))(si)
                    for si in range(2)])
    k.fence()

    ich = [(512 * i, min(512, IF - 512 * i)) for i in range(cdiv(IF, 512))]
    NI = len(ich)
    NK2 = cdiv(IF, 128)
    if True:
        n1tt2 = [k.sb(f"n1tt{i}", [128, 8 * 128 * NG], BF16, scope=sc9)
                 for i in range(2)]
        sgf = k.sb("sgf", [128, 512], scope=sc9)
        hyf = k.sb("hyf", [128, 512], scope=sc9)
        prodf = k.sb("prodf", [128, 512], BF16, scope=sc9)
        maccf = k.sb("maccf", [128, NT * NI], scope=sc9)

        def load_n1tt(g):
            for kk in range(8):
                k.qy.dma_t(n1tt2[g % 2][:, 128 * NG * kk:128 * NG * (kk + 1)],
                           n1_d.ap()[128 * NG * g:128 * NG * (g + 1),
                                     128 * kk:128 * (kk + 1)],
                           ins=[B("n1_d")], outs=[B(f"n1tt{g % 2}")])
        load_n1tt(0)
        for g in range(cdiv(NT, NG)):
            if g + 1 < cdiv(NT, NG):
                load_n1tt(g + 1)
            n1tt = n1tt2[g % 2]
            bn1tt = B(f"n1tt{g % 2}")
            for si in range(NG):
                t = NG * g + si
                for c0 in range(0, NI, 2):
                    nch_here = min(2, NI - c0)
                    pb = 4 * ((c0 // 2 + t) % 2)
                    for kk in range(8):
                        for ci in range(nch_here):
                            off, wdt = ich[c0 + ci]
                            k.t.op("matmul", pk[pb + 2 * ci][:, 0:wdt],
                                   n1tt[:, 128 * NG * kk + 128 * si:
                                        128 * NG * kk + 128 * (si + 1)],
                                   wgt[:, IF * kk + off:IF * kk + off + wdt],
                                   start=(kk == 0), stop=(kk == 7),
                                   ins=[bn1tt, B("wgt")],
                                   outs=[B(f"pk{pb + 2 * ci}")], defer=True)
                            k.t.op("matmul", pk[pb + 2 * ci + 1][:, 0:wdt],
                                   n1tt[:, 128 * NG * kk + 128 * si:
                                        128 * NG * kk + 128 * (si + 1)],
                                   wut[:, IF * kk + off:IF * kk + off + wdt],
                                   start=(kk == 0), stop=(kk == 7),
                                   ins=[bn1tt, B("wut")],
                                   outs=[B(f"pk{pb + 2 * ci + 1}")],
                                   defer=not (kk == 7 and ci == nch_here - 1))
                    for ci in range(nch_here):
                        off, wdt = ich[c0 + ci]
                        k.s.op("activation", sgf[:, 0:wdt], pk[pb + 2 * ci][:, 0:wdt],
                               AF.Silu, bias=0.0, scale=sxv[:, 0:1],
                               ins=[B(f"pk{pb + 2 * ci}"), B("sxv")], outs=[B("sgf")])
                        k.s.op("activation", hyf[:, 0:wdt], pk[pb + 2 * ci + 1][:, 0:wdt],
                               AF.Copy, bias=0.0, scale=sxv[:, 0:1],
                               ins=[B(f"pk{pb + 2 * ci + 1}"), B("sxv")], outs=[B("hyf")])
                        k.v.op("tensor_tensor", prodf[:, 0:wdt], sgf[:, 0:wdt],
                               hyf[:, 0:wdt], ALU.mult,
                               ins=[B("sgf"), B("hyf")], outs=[B("prodf")])
                        k.v.op("tensor_reduce", maccf[:, t * NI + c0 + ci:
                                                      t * NI + c0 + ci + 1],
                               prodf[:, 0:wdt], AX.X, ALU.max,
                               apply_absolute_value=True,
                               ins=[B("prodf")], outs=[B("maccf")])
                        k.qg.dma(prod_d.ap()[128 * t:128 * (t + 1),
                                             off:off + wdt], prodf[:, 0:wdt],
                                 ins=[B("prodf")], outs=[B("prod_d")])
        k.v.op("tensor_reduce", stt[0][:], maccf[:], AX.X, ALU.max,
               ins=[B("maccf")], outs=[B("stt0")])
        k.t.op("transpose", pk[0][0:1, 0:128], stt[0][:], id_f[:],
               ins=[B("stt0"), B("id_f")], outs=[B("pk0")])
        mrow2 = k.sb("mrow2", [1, 128], scope=sc9)
        k.v.op("tensor_copy", mrow2[:], pk[0][0:1, 0:128], ins=[B("pk0")],
               outs=[B("mrow2")])
        gmx2 = k.sb("gmx2", [1, 8], scope=sc9)
        k.v.op("memset", gmx2[:], 0.0, outs=[B("gmx2")])
        k.v.op("tensor_reduce", gmx2[0:1, 0:1], mrow2[:], AX.X, ALU.max,
               ins=[B("mrow2")], outs=[B("gmx2")])
        k.qg.dma(gm2P.ap(), gmx2[:], ins=[B("gmx2")], outs=[B("gm2P")])
        k.qc.collective("AllGather", ALU.bypass, RG, [gm2P.ap()], [gm2F.ap()],
                        ins=[B("gm2P")], outs=[B("gm2F")])
        gmall2 = k.sb("gmall2", [1, 64], scope=sc9)
        k.qy.dma(gmall2[:], gm2F.ap().rearrange("a b -> () (a b)"),
                 ins=[B("gm2F")], outs=[B("gmall2")])
        k.v.op("tensor_reduce", gmx2[0:1, 0:1], gmall2[:], AX.X, ALU.max,
               ins=[B("gmall2")], outs=[B("gmx2")])
        cld = k.sb("cld", [1, 1], scope=sc9)
        k.qy.dma(cld[:], clipd_in.ap(), ins=[B("clipd_in")], outs=[B("cld")])
        k.s.op("activation", cld[:], cld[:], AF.Sigmoid, bias=0.0, scale=1.0,
               ins=[B("cld")], outs=[B("cld")])
        k.v.op("tensor_tensor", cld[:], cld[:], gmx2[0:1, 0:1], ALU.mult,
               ins=[B("cld"), B("gmx2")], outs=[B("cld")])
        k.v.op("tensor_scalar", cld[:], cld[:], 1.0 / 127.0, None, ALU.mult,
               ins=[B("cld")], outs=[B("cld")])
        s2v_ = bcast_row("s2v_", cld[:], B("cld"), 1, scope=sc9)
        k.v.op("tensor_copy", s2v[:], s2v_[:], ins=[B("s2v_")], outs=[B("s2v")])
        k.v.op("reciprocal", s2b[:], s2v[:], ins=[B("s2v")], outs=[B("s2b")])
        sc9.close()
        k.fence()
        scd = ExitStack()
        wdt_t = k.sb("wdt_t", [128, NK2 * H], BF16, scope=scd)
        wdsg2 = [k.sb(f"wdsg{i}", [128, IF], BF16, scope=scd) for i in range(2)]
        wdsf2 = [k.sb(f"wdsf{i}", [128, IF], F32, scope=scd) for i in range(2)]
        for r in range(H // 128):
            wdsg, bwdsg = wdsg2[r % 2], B(f"wdsg{r % 2}")
            wdsf, bwdsf = wdsf2[r % 2], B(f"wdsf{r % 2}")
            k.qy.dma(wdsg[:], fqd_d.ap()[128 * r:128 * (r + 1), :],
                     ins=[B("fqd_d")], outs=[bwdsg])
            k.v.op("tensor_copy", wdsf[:], wdsg[:], ins=[bwdsg],
                   outs=[bwdsf])
            wdt3 = wdt_t.ap().rearrange("p (a c) -> p a c", c=H)
            for kk in range(NK2):
                j, jj = kk // 4, kk % 4
                gw = min(4, NK2 - 4 * j)
                tb = (j + 3 * r) % 6
                k.t.op("transpose", pk[tb][:, 128 * jj:128 * jj + 128],
                       wdsf[:, 128 * kk:128 * (kk + 1)], id_f[:],
                       ins=[bwdsf, B("id_f")], outs=[B(f"pk{tb}")],
                       defer=(jj != gw - 1))
                if jj == gw - 1:
                    k.s.op("activation",
                           wdt3[:, 4 * j:4 * j + gw, 128 * r:128 * (r + 1)],
                           pk[tb][:, 0:128 * gw]
                           .rearrange("p (a c) -> p a c", c=128),
                           AF.Copy, bias=0.0, scale=1.0,
                           ins=[B(f"pk{tb}")], outs=[B("wdt_t")])
        scq = ExitStack()
        for S in SS[:2]:
            S.h["prt"] = k.sb(f"prt_u{S.si}", [128, IF], BF16, scope=scq)
            S.h["prtf"] = k.sb(f"prtf_u{S.si}", [128, IF], F32, scope=scq)
            S.h["n2ft"] = k.sb(f"n2ft_u{S.si}", [128, IF], BF16, scope=scq)
        k.qg.wait_all(k.y)
        k.qg.wait_all(k.s)

        def n2f_chain(t, S):
            prt, prtf, n2ft = S["prt"], S["prtf"], S["n2ft"]
            bp, bpf, bn2 = S.b("prt"), S.b("prtf"), S.b("n2ft")
            k.qs.dma(prt[:], prod_d.ap()[128 * t:128 * (t + 1), :],
                     ins=[B("prod_d")], outs=[bp])
            yield
            k.s.op("activation", prtf[:], prt[:], AF.Identity, bias=RC,
                   scale=s2b[:, 0:1], ins=[bp, B("s2b"), B("rc_t")], outs=[bpf])
            yield
            k.v.op("tensor_scalar", prtf[:], prtf[:], -RC, None, ALU.add,
                   ins=[bpf], outs=[bpf])
            k.v.op("tensor_scalar", n2ft[:], prtf[:], -128.0, 127.0, ALU.max,
                   ALU.min, ins=[bpf], outs=[bn2])
            yield
            k.qg.dma(n2f_d.ap()[128 * t:128 * (t + 1), :], n2ft[:],
                     ins=[bn2], outs=[B("n2f_d")])
            yield

        interleave([(lambda si: (x for t in range(si, NT, 2)
                                 for x in n2f_chain(t, SS[si])))(si)
                    for si in range(2)])
    scq.close()
    k.fence()

    NGd = min(4, NG)
    if True:
        n2tt2 = [k.sb(f"n2tt{i}", [128, NK2 * 128 * NGd], BF16, scope=scd)
                 for i in range(2)]
        x1r2 = [k.sb(f"x1r{i}", [128, H], scope=scd) for i in range(2)]
        fot2 = [k.sb(f"fot{i}", [128, H], scope=scd) for i in range(2)]

        def load_n2tt(g):
            for kk in range(NK2):
                k.qy.dma_t(n2tt2[g % 2][:, 128 * NGd * kk:128 * NGd * (kk + 1)],
                           n2f_d.ap()[128 * NGd * g:128 * NGd * (g + 1),
                                      128 * kk:128 * (kk + 1)],
                           ins=[B("n2f_d")], outs=[B(f"n2tt{g % 2}")])
        load_n2tt(0)
        for g in range(NT // NGd):
            if g + 1 < NT // NGd:
                load_n2tt(g + 1)
            n2tt = n2tt2[g % 2]
            bn2tt = B(f"n2tt{g % 2}")
            for si in range(NGd):
                t = NGd * g + si
                pb = 2 * (si % 4)
                for kk in range(NK2):
                    for nch in range(2):
                        k.t.op("matmul", pk[pb + nch][:, 0:512],
                               n2tt[:, 128 * NGd * kk + 128 * si:
                                    128 * NGd * kk + 128 * (si + 1)],
                               wdt_t[:, H * kk + 512 * nch:H * kk + 512 * (nch + 1)],
                               start=(kk == 0), stop=(kk == NK2 - 1),
                               ins=[bn2tt, B("wdt_t")], outs=[B(f"pk{pb + nch}")],
                               defer=not (kk == NK2 - 1 and nch == 1))
                x1r = x1r2[si % 2]
                fot = fot2[si % 2]
                bx1r, bfot = B(f"x1r{si % 2}"), B(f"fot{si % 2}")
                k.qs.dma(x1r[:], x1_d.ap()[128 * t:128 * (t + 1), :],
                         ins=[B("x1_d")], outs=[bx1r])
                for nch in range(2):
                    k.s.op("activation", fot[:, 512 * nch:512 * (nch + 1)],
                           pk[pb + nch][:, 0:512], AF.Copy, bias=0.0,
                           scale=s2v[:, 0:1],
                           ins=[B(f"pk{pb + nch}"), B("s2v")], outs=[bfot])
                k.v.op("tensor_tensor", fot[:], fot[:], gbxb[:], ALU.mult,
                       ins=[bfot, B("gbxb")], outs=[bfot])
                k.v.op("tensor_tensor", fot[:], fot[:], x1r[:], ALU.add,
                       ins=[bfot, bx1r], outs=[bfot])
                k.qo.dma(out.ap()[128 * t:128 * (t + 1), :], fot[:],
                         ins=[bfot], outs=[B("out")])
        k.qo.wait_all(k.g)
        k.qo.wait_all(k.y)
        scd.close()
    return k


def _shard_inputs(inputs, T):
    x = np.ascontiguousarray(np.asarray(inputs["L0_attn_x"],
                                        dtype=np.float32).reshape(T, H))
    wmod = np.asarray(inputs["L0_wmod"], dtype=np.float32)
    W6 = 6 * H // NCORE
    gpw = np.asarray(inputs["L0_attn_gp_w"], dtype=np.float32)
    dpw = np.asarray(inputs["L0_attn_dp_w"], dtype=np.float32)
    wg = np.asarray(inputs["L0_ffn_w_gate"], dtype=np.float32)
    wu = np.asarray(inputs["L0_ffn_w_up"], dtype=np.float32)
    wd = np.asarray(inputs["L0_ffn_w_down"], dtype=np.float32)
    TL = T // NCORE
    IFS = IF // NCORE
    HS = H // NCORE
    in_maps = []
    for c in range(NCORE):
        in_maps.append({
            "x_full": x,
            "xown": np.ascontiguousarray(x[TL * c:TL * (c + 1)]),
            "c_vec": np.asarray(inputs["L0_c"], dtype=np.float32).reshape(1, H),
            "wmod_sl": np.ascontiguousarray(wmod[:, W6 * c:W6 * (c + 1)]),
            "gnw_in": np.asarray(inputs["L0_attn_gate_norm_w"],
                                 dtype=np.float32).reshape(1, H),
            "ginw_in": np.asarray(inputs["L0_attn_gate_in_norm_w"],
                                  dtype=np.float32).reshape(1, H),
            "gatew_in": np.asarray(inputs["L0_attn_gate_w"], dtype=np.float32),
            "gpn_in": np.ascontiguousarray(
                np.asarray(inputs["L0_attn_gp_norm_w"],
                           dtype=np.float32)[c].reshape(1, H)),
            "gpw_in": np.ascontiguousarray(gpw[c]),
            "dpn_in": np.ascontiguousarray(
                np.asarray(inputs["L0_attn_dp_norm_w"],
                           dtype=np.float32)[c].reshape(1, IM)),
            "dpw_in": np.ascontiguousarray(dpw[c]),
            "clipx_in": np.asarray(inputs["L0_ffn_clip_x"],
                                   dtype=np.float32).reshape(1, 1),
            "clipd_in": np.asarray(inputs["L0_ffn_clip_down"],
                                   dtype=np.float32).reshape(1, 1),
            "wg_sl": np.ascontiguousarray(wg[IFS * c:IFS * (c + 1)]),
            "wu_sl": np.ascontiguousarray(wu[IFS * c:IFS * (c + 1)]),
            "wd_sl": np.ascontiguousarray(wd[HS * c:HS * (c + 1)]),
        })
    return in_maps


def kernel(**inputs):
    T = int(np.asarray(inputs["L0_attn_x"]).shape[1])
    nc = bass.Bass("TRN2", target_bir_lowering=False)
    with ExitStack() as st:
        build(nc, st, T)
    in_maps = _shard_inputs(inputs, T)
    res = run_bass_kernel_spmd(nc, in_maps, core_ids=list(range(NCORE)))
    TL = T // NCORE
    full = np.concatenate([res.results[c]["out"] for c in range(NCORE)], axis=0)
    return full.reshape(1, T, H).astype(np.float32)


# version 49
# speedup vs baseline: 1.0961x; 1.0145x over previous
import numpy as np
from contextlib import ExitStack
import concourse.bass as bass
import concourse.mybir as mybir
from concourse.bass_utils import run_bass_kernel_spmd
from concourse.bass_core_rust import IndirectOffsetOnAxis

F32 = mybir.dt.float32
BF16 = mybir.dt.bfloat16
I32 = mybir.dt.int32
AF = mybir.ActivationFunctionType
ALU = mybir.AluOpType
AX = mybir.AxisListType
ds = bass.ds
RC = 1.5 * 2 ** 23

H = 1024
E = 8
IM = 2048
IF = 2816
NCORE = 8


class Buf:
    __slots__ = ("name", "writer", "readers")

    def __init__(self, name=""):
        self.name = name
        self.writer = None
        self.readers = {}


class Eng:
    def __init__(self, eng, sem, name):
        self.e = eng
        self.sem = sem
        self.name = name
        self.n = 0
        self.hwm = {}
        self.pending = []
        self.fence = []

    def _wait(self, sem, val):
        if self.hwm.get(sem.name, -1) >= val:
            return
        self.e.wait_ge(sem, val)
        self.hwm[sem.name] = val

    def _pre(self, ins, outs):
        for s, v in self.fence:
            self._wait(s, v)
        self.fence = []
        for b in ins:
            if b.writer is not None:
                self._wait(*b.writer)
        for b in outs:
            if b.writer is not None:
                self._wait(*b.writer)
            for s, v in b.readers.items():
                self._wait(s, v)

    @staticmethod
    def _post(ins, outs, tok):
        sem, val = tok
        for b in ins:
            if b.readers.get(sem, -1) < val:
                b.readers[sem] = val
        for b in outs:
            b.writer = tok
            b.readers = {}

    def op(self, name, *args, ins=(), outs=(), defer=False, **kw):
        self._pre(ins, outs)
        inst = getattr(self.e, name)(*args, **kw)
        if defer:
            self.pending.append((tuple(ins), tuple(outs)))
            return inst
        self.n += 1
        inst.then_inc(self.sem, 1)
        tok = (self.sem, self.n)
        for pins, pouts in self.pending:
            self._post(pins, pouts, tok)
        self.pending = []
        self._post(ins, outs, tok)
        return inst

    def token(self):
        assert not self.pending
        return (self.sem, self.n)


class Q:
    def __init__(self, eng, sems, name, step=16):
        self.eng = eng
        self.sems = sems if isinstance(sems, (list, tuple)) else [sems]
        self.name = name
        self.step = step
        self.n = 0
        self.cnt = [0] * len(self.sems)

    def _issue(self, fn, ins, outs):
        e = self.eng
        i = self.n % len(self.sems)
        if self.cnt[i] > 0:
            e._wait(self.sems[i], self.step * self.cnt[i])
        e._pre(ins, outs)
        inst = fn()
        self.n += 1
        self.cnt[i] += 1
        inst.then_inc(self.sems[i], self.step)
        tok = (self.sems[i], self.step * self.cnt[i])
        Eng._post(ins, outs, tok)
        return tok

    def dma(self, out_ap, in_ap, ins=(), outs=(), **kw):
        return self._issue(lambda: self.eng.e.dma_start(out=out_ap, in_=in_ap, **kw),
                           ins, outs)

    def dma_t(self, out_ap, in_ap, ins=(), outs=()):
        return self._issue(
            lambda: self.eng.e.dma_start(out=out_ap, in_=in_ap, transpose=True),
            ins, outs)

    def indirect(self, out_ap, out_off, in_ap, in_off, ins=(), outs=(), **kw):
        return self._issue(
            lambda: self.eng.e.indirect_dma_start(out_ap, out_off, in_ap, in_off, **kw),
            ins, outs)

    def collective(self, kind, op, rg, ins_aps, outs_aps, ins=(), outs=()):
        return self._issue(
            lambda: self.eng.e.collective_compute(kind, op, rg, ins_aps, outs_aps),
            ins, outs)

    def wait_all(self, eng):
        for s, c in zip(self.sems, self.cnt):
            if c:
                eng._wait(s, self.step * c)


class KB:
    def __init__(self, nc, st):
        self.nc = nc
        self.st = st
        self.bufs = {}
        sem = lambda n: st.enter_context(nc.semaphore(n))
        self.v = Eng(nc.vector, sem("se_v"), "v")
        self.s = Eng(nc.scalar, sem("se_s"), "s")
        self.t = Eng(nc.tensor, sem("se_t"), "t")
        self.g = Eng(nc.gpsimd, sem("se_g"), "g")
        self.y = Eng(nc.sync, sem("se_y"), "y")
        self.engs = [self.v, self.s, self.t, self.g, self.y]
        self.qy = Q(self.y, [sem(f"qy{i}") for i in range(6)], "qy")
        self.qs = Q(self.s, [sem(f"qs{i}") for i in range(6)], "qs")
        self.qg = Q(self.g, [sem(f"qg{i}") for i in range(6)], "qg")
        self.qo = Q(self.g, [sem(f"qo{i}") for i in range(2)], "qo")
        self.qc = Q(self.g, [sem(f"qc{i}") for i in range(4)], "qc", step=1)
        self.queues = [self.qy, self.qs, self.qg, self.qo, self.qc]

    def sb(self, name, shape, dtype=F32, scope=None):
        h = (scope or self.st).enter_context(self.nc.sbuf_tensor(name, shape, dtype))
        self.bufs[name] = Buf(name)
        return h

    def ps(self, name, shape, dtype=F32):
        h = self.st.enter_context(self.nc.psum_tensor(name, shape, dtype))
        self.bufs[name] = Buf(name)
        return h

    def dram(self, name, shape, dtype=F32, kind="Internal"):
        h = self.nc.dram_tensor(name, shape, dtype, kind=kind)
        self.bufs[name] = Buf(name)
        return h

    def b(self, name):
        if name not in self.bufs:
            self.bufs[name] = Buf(name)
        return self.bufs[name]

    def fence(self, skip=()):
        toks = [(e.sem, e.n) for e in self.engs if e.n > 0]
        for q in self.queues:
            if q in skip:
                continue
            toks += [(s, q.step * c) for s, c in zip(q.sems, q.cnt) if c]
        for e in self.engs:
            e.fence = list(toks)


def cdiv(a, b):
    return (a + b - 1) // b


def build(nc, st, T):
    k = KB(nc, st)
    B = k.b
    TL = T // NCORE
    CAP = min(int(IM * 1.5), T // E)
    NT = TL // 128
    NJ = T // 128
    NS = CAP // 128
    NCH_AG = min(8, NS)
    CHS = CAP // NCH_AG
    W6 = 6 * H // NCORE
    RG = [list(range(NCORE))]
    IFS = IF // NCORE
    HS = H // NCORE

    x_full = k.dram("x_full", [T, H], F32, kind="ExternalInput")
    xown = k.dram("xown", [TL, H], F32, kind="ExternalInput")
    c_vec = k.dram("c_vec", [1, H], F32, kind="ExternalInput")
    wmod_sl = k.dram("wmod_sl", [H, W6], F32, kind="ExternalInput")
    gnw_in = k.dram("gnw_in", [1, H], F32, kind="ExternalInput")
    ginw_in = k.dram("ginw_in", [1, H], F32, kind="ExternalInput")
    gatew_in = k.dram("gatew_in", [E, H], F32, kind="ExternalInput")
    gpn_in = k.dram("gpn_in", [1, H], F32, kind="ExternalInput")
    gpw_in = k.dram("gpw_in", [2 * IM, H], F32, kind="ExternalInput")
    dpn_in = k.dram("dpn_in", [1, IM], F32, kind="ExternalInput")
    dpw_in = k.dram("dpw_in", [H, IM], F32, kind="ExternalInput")
    clipx_in = k.dram("clipx_in", [1, 1], F32, kind="ExternalInput")
    clipd_in = k.dram("clipd_in", [1, 1], F32, kind="ExternalInput")
    wg_sl = k.dram("wg_sl", [IFS, H], F32, kind="ExternalInput")
    wu_sl = k.dram("wu_sl", [IFS, H], F32, kind="ExternalInput")
    wd_sl = k.dram("wd_sl", [HS, IF], F32, kind="ExternalInput")
    out = k.dram("out", [TL, H], F32, kind="ExternalOutput")

    modP = k.dram("modP", [1, W6], F32)
    modF = k.dram("modF", [NCORE, W6], F32)
    ct_d = k.dram("ct_d", [NJ * 128, 1], F32)
    fl_d = k.dram("fl_d", [1, CAP], F32)
    jr_d = k.dram("jr_d", [1, CAP], F32)
    gpq_d = k.dram("gpq_d", [2 * IM, H], BF16)
    dpq_d = k.dram("dpq_d", [H, IM], BF16)
    ng6_d = k.dram("ng6_d", [CAP, H], F32)
    contribP = k.dram("contribP", [CAP, H], F32)
    contribF = k.dram("contribF", [E * CAP, H], F32)
    fqgT_p = k.dram("fqgT_p", [H, IFS], BF16)
    fquT_p = k.dram("fquT_p", [H, IFS], BF16)
    fqd_p = k.dram("fqd_p", [HS, IF], BF16)
    fqgT_d = k.dram("fqgT_d", [NCORE * H, IFS], BF16)
    fquT_d = k.dram("fquT_d", [NCORE * H, IFS], BF16)
    fqd_d = k.dram("fqd_d", [H, IF], BF16)
    x1_d = k.dram("x1_d", [TL, H], F32)
    h2_d = k.dram("h2_d", [TL, H], F32)
    n1_d = k.dram("n1_d", [TL, H], BF16)
    prod_d = k.dram("prod_d", [TL, IF], BF16)
    n2f_d = k.dram("n2f_d", [TL, IF], BF16)
    gm1P = k.dram("gm1P", [1, 8], F32)
    gm1F = k.dram("gm1F", [NCORE, 8], F32)
    gm2P = k.dram("gm2P", [1, 8], F32)
    gm2F = k.dram("gm2F", [NCORE, 8], F32)

    pid = {}
    for e in (k.y, k.g):
        pid[e.name] = e.e.partition_id()


    rc_t = k.sb("rc_t", [128, 1])
    nc.const_aps.register(F32, RC, rc_t[:])
    k.v.op("memset", rc_t[:], RC, outs=[B("rc_t")])
    mrc_t = k.sb("mrc_t", [128, 1])
    nc.const_aps.register(F32, -RC, mrc_t[:])
    k.v.op("memset", mrc_t[:], -RC, outs=[B("mrc_t")])

    iot_pf = k.sb("iot_pf", [128, 128])
    k.g.op("iota", iot_pf[:], [[-1, 128]], base=0, channel_multiplier=1,
           allow_small_or_imprecise_dtypes=True, outs=[B("iot_pf")])
    tri_b = k.sb("tri_b", [128, 128], BF16)
    tri_f = k.sb("tri_f", [128, 128], F32)
    id_b = k.sb("id_b", [128, 128], BF16)
    id_f = k.sb("id_f", [128, 128], F32)
    k.v.op("tensor_scalar", tri_b[:], iot_pf[:], 0.0, None, ALU.is_lt,
           ins=[B("iot_pf")], outs=[B("tri_b")])
    k.v.op("tensor_scalar", tri_f[:], iot_pf[:], 0.0, None, ALU.is_lt,
           ins=[B("iot_pf")], outs=[B("tri_f")])
    k.v.op("tensor_scalar", id_b[:], iot_pf[:], 0.0, None, ALU.is_equal,
           ins=[B("iot_pf")], outs=[B("id_b")])
    k.v.op("tensor_scalar", id_f[:], iot_pf[:], 0.0, None, ALU.is_equal,
           ins=[B("iot_pf")], outs=[B("id_f")])
    ones_r = k.sb("ones_r", [1, 128], BF16)
    k.v.op("memset", ones_r[:], 1.0, outs=[B("ones_r")])
    ones_rf = k.sb("ones_rf", [1, 128], F32)
    k.v.op("memset", ones_rf[:], 1.0, outs=[B("ones_rf")])
    ones_c = k.sb("ones_c", [128, 1], BF16)
    k.v.op("memset", ones_c[:], 1.0, outs=[B("ones_c")])
    ones_cf = k.sb("ones_cf", [128, 1], F32)
    k.v.op("memset", ones_cf[:], 1.0, outs=[B("ones_cf")])
    iop1_b = k.sb("iop1_b", [128, 1], BF16)
    iop1_f = k.sb("iop1_f", [128, 1])
    k.g.op("iota", iop1_f[:], [[1, 1]], base=1, channel_multiplier=1,
           allow_small_or_imprecise_dtypes=True, outs=[B("iop1_f")])
    k.v.op("tensor_copy", iop1_b[:], iop1_f[:], ins=[B("iop1_f")], outs=[B("iop1_b")])
    io_e = k.sb("io_e", [128, E])
    k.g.op("iota", io_e[:], [[1, E]], base=0, channel_multiplier=0,
           allow_small_or_imprecise_dtypes=True, outs=[B("io_e")])
    io_r = k.sb("io_r", [128, 128])
    k.g.op("iota", io_r[:], [[1, 128]], base=0, channel_multiplier=0,
           allow_small_or_imprecise_dtypes=True, outs=[B("io_r")])
    io_s = k.sb("io_s", [128, CAP])
    k.g.op("iota", io_s[:], [[1, CAP]], base=0, channel_multiplier=0,
           allow_small_or_imprecise_dtypes=True, outs=[B("io_s")])

    pk = [k.ps(f"pk{i}", [128, 512]) for i in range(8)]

    def bc_fill(dst, name, src_ap, src_buf, width):
        for j in range(cdiv(width, 512)):
            w = min(512, width - 512 * j)
            p = pk[j % 2]
            k.t.op("matmul", p[:, 0:w], ones_rf[:], src_ap[0:1, 512 * j:512 * j + w],
                   start=True, stop=True, ins=[B("ones_rf"), src_buf],
                   outs=[B(f"pk{j % 2}")])
            k.v.op("tensor_copy", dst[:, 512 * j:512 * j + w], p[:, 0:w],
                   ins=[B(f"pk{j % 2}")], outs=[B(name)])
        return dst

    def bcast_row(name, src_ap, src_buf, width, scope=None):
        dst = k.sb(name, [128, width], scope=scope)
        return bc_fill(dst, name, src_ap, src_buf, width)

    ca1b = k.sb("ca1b", [128, H])
    sab = k.sb("sab", [128, H])
    gaxb = k.sb("gaxb", [128, H])
    cb1b = k.sb("cb1b", [128, H])
    sbb = k.sb("sbb", [128, H])
    gbxb = k.sb("gbxb", [128, H])
    ggb = k.sb("ggb", [128, H])
    gnwb = k.sb("gnwb", [128, H])
    gpnb = k.sb("gpnb", [128, H])
    dpnb = k.sb("dpnb", [128, IM])
    gamc_b = k.sb("gamc_b", [128, 1])
    mgt = k.sb("mgt", [128, E * 8], BF16)

    sc0 = ExitStack()
    cv8 = k.sb("cv8", [8, 128], scope=sc0)
    k.qy.dma(cv8[:], c_vec.ap().rearrange("o (a b) -> a (o b)", a=8),
             ins=[B("c_vec")], outs=[B("cv8")])
    sc8 = k.sb("sc8", [8, 128], scope=sc0)
    k.s.op("activation", sc8[:], cv8[:], AF.Silu, bias=0.0, scale=1.0,
           ins=[B("cv8")], outs=[B("sc8")])
    k.t.op("transpose", pk[0][:, 0:8], sc8[:], id_f[0:8, 0:8],
           ins=[B("sc8"), B("id_f")], outs=[B("pk0")])
    scT = k.sb("scT", [128, 8], scope=sc0)
    k.v.op("tensor_copy", scT[:], pk[0][:, 0:8], ins=[B("pk0")], outs=[B("scT")])
    wm_t = k.sb("wm_t", [128, 8 * W6], scope=sc0)
    modp_t = k.sb("modp_t", [1, W6], scope=sc0)
    NB6 = cdiv(W6, 512)
    for kk in range(8):
        k.qy.dma(wm_t[:, W6 * kk:W6 * (kk + 1)],
                 wmod_sl.ap()[128 * kk:128 * (kk + 1), :],
                 ins=[B("wmod_sl")], outs=[B("wm_t")])
    for kk in range(8):
        for j in range(NB6):
            w = min(512, W6 - 512 * j)
            k.t.op("matmul", pk[2 + j][0:1, 0:w], scT[:, kk:kk + 1],
                   wm_t[:, W6 * kk + 512 * j:W6 * kk + 512 * j + w],
                   start=(kk == 0), stop=(kk == 7),
                   ins=[B("scT"), B("wm_t")], outs=[B(f"pk{2 + j}")],
                   defer=not (kk == 7 and j == NB6 - 1))
    for j in range(NB6):
        w = min(512, W6 - 512 * j)
        k.v.op("tensor_copy", modp_t[0:1, 512 * j:512 * j + w], pk[2 + j][0:1, 0:w],
               ins=[B(f"pk{2 + j}")], outs=[B("modp_t")])
    k.qg.dma(modP.ap(), modp_t[:], ins=[B("modp_t")], outs=[B("modP")])
    k.qc.collective("AllGather", ALU.bypass, RG, [modP.ap()], [modF.ap()],
                    ins=[B("modP")], outs=[B("modF")])
    modf_t = k.sb("modf_t", [1, 6 * H], scope=sc0)
    k.qy.dma(modf_t[:], modF.ap().rearrange("a b -> () (a b)"),
             ins=[B("modF")], outs=[B("modf_t")])
    mrow = lambda i: modf_t[0:1, H * i:H * (i + 1)]

    ca1 = k.sb("ca1", [1, H], scope=sc0)
    cb1 = k.sb("cb1", [1, H], scope=sc0)
    k.v.op("tensor_scalar", ca1[:], mrow(1), 1.0, None, ALU.add,
           ins=[B("modf_t")], outs=[B("ca1")])
    k.v.op("tensor_scalar", cb1[:], mrow(4), 1.0, None, ALU.add,
           ins=[B("modf_t")], outs=[B("cb1")])

    def gclamp(name, src):
        ab = k.sb(name + "_a", [1, H], scope=sc0)
        sg = k.sb(name + "_s", [1, H], scope=sc0)
        res = k.sb(name, [1, H], scope=sc0)
        k.s.op("activation", ab[:], src, AF.Abs, bias=0.0, scale=1.0,
               ins=[B("modf_t")], outs=[B(name + "_a")])
        k.v.op("tensor_scalar", ab[:], ab[:], 0.25, None, ALU.max,
               ins=[B(name + "_a")], outs=[B(name + "_a")])
        k.v.op("tensor_scalar", sg[:], src, 0.0, None, ALU.is_ge,
               ins=[B("modf_t")], outs=[B(name + "_s")])
        k.v.op("tensor_scalar", sg[:], sg[:], 2.0, -1.0, ALU.mult, ALU.add,
               ins=[B(name + "_s")], outs=[B(name + "_s")])
        k.v.op("tensor_tensor", res[:], ab[:], sg[:], ALU.mult,
               ins=[B(name + "_a"), B(name + "_s")], outs=[B(name)])
        return res

    gax = gclamp("gax", mrow(2))
    gbx = gclamp("gbx", mrow(5))

    bc_fill(ca1b, "ca1b", ca1[:], B("ca1"), H)
    bc_fill(sab, "sab", mrow(0), B("modf_t"), H)
    bc_fill(gaxb, "gaxb", gax[:], B("gax"), H)
    bc_fill(cb1b, "cb1b", cb1[:], B("cb1"), H)
    bc_fill(sbb, "sbb", mrow(3), B("modf_t"), H)
    bc_fill(gbxb, "gbxb", gbx[:], B("gbx"), H)

    gnw_t = k.sb("gnw_t", [1, H], scope=sc0)
    ginw_t = k.sb("ginw_t", [1, H], scope=sc0)
    k.qy.dma(gnw_t[:], gnw_in.ap(), ins=[B("gnw_in")], outs=[B("gnw_t")])
    k.qy.dma(ginw_t[:], ginw_in.ap(), ins=[B("ginw_in")], outs=[B("ginw_t")])
    gg_t = k.sb("gg_t", [1, H], scope=sc0)
    k.v.op("tensor_tensor", gg_t[:], gnw_t[:], ginw_t[:], ALU.mult,
           ins=[B("gnw_t"), B("ginw_t")], outs=[B("gg_t")])
    bc_fill(ggb, "ggb", gg_t[:], B("gg_t"), H)
    bc_fill(gnwb, "gnwb", gnw_t[:], B("gnw_t"), H)
    gpn_t = k.sb("gpn_t", [1, H], scope=sc0)
    k.qy.dma(gpn_t[:], gpn_in.ap(), ins=[B("gpn_in")], outs=[B("gpn_t")])
    bc_fill(gpnb, "gpnb", gpn_t[:], B("gpn_t"), H)
    dpn_t = k.sb("dpn_t", [1, IM], scope=sc0)
    k.qy.dma(dpn_t[:], dpn_in.ap(), ins=[B("dpn_in")], outs=[B("dpn_t")])
    bc_fill(dpnb, "dpnb", dpn_t[:], B("dpn_t"), IM)

    gw_t = k.sb("gw_t", [E, H], scope=sc0)
    k.qy.dma(gw_t[:], gatew_in.ap(), ins=[B("gatew_in")], outs=[B("gw_t")])
    gw_abs = k.sb("gw_abs", [E, H], scope=sc0)
    gw_sum = k.sb("gw_sum", [E, 1], scope=sc0)
    k.s.op("activation", gw_abs[:], gw_t[:], AF.Abs, bias=0.0, scale=1.0,
           accum_out=gw_sum[:], ins=[B("gw_t")], outs=[B("gw_abs"), B("gw_sum")])
    k.t.op("matmul", pk[0][0:1, 0:1], ones_cf[0:E, :], gw_sum[:],
           start=True, stop=True, ins=[B("ones_cf"), B("gw_sum")], outs=[B("pk0")])
    wsg = k.sb("wsg", [1, 1], scope=sc0)
    k.v.op("tensor_scalar", wsg[:], pk[0][0:1, 0:1], 1.0 / (E * H), 1e-5,
           ALU.mult, ALU.max, ins=[B("pk0")], outs=[B("wsg")])
    wsg_r = k.sb("wsg_r", [1, 1], scope=sc0)
    k.v.op("reciprocal", wsg_r[:], wsg[:], ins=[B("wsg")], outs=[B("wsg_r")])
    bc_fill(gamc_b, "gamc_b", wsg[:], B("wsg"), 1)
    k.v.op("tensor_scalar", gamc_b[:], gamc_b[:], 1.0 / 127.0, None, ALU.mult,
           ins=[B("gamc_b")], outs=[B("gamc_b")])
    wsgb = bcast_row("wsgb", wsg_r[:], B("wsg_r"), 1, scope=sc0)
    mg_t = k.sb("mg_t", [E, H], scope=sc0)
    k.s.op("activation", mg_t[:], gw_t[:], AF.Identity, bias=RC,
           scale=wsgb[0:E, 0:1], ins=[B("gw_t"), B("wsgb"), B("rc_t")],
           outs=[B("mg_t")])
    k.v.op("tensor_scalar", mg_t[:], mg_t[:], -RC, None, ALU.add,
           ins=[B("mg_t")], outs=[B("mg_t")])
    k.v.op("tensor_scalar", mg_t[:], mg_t[:], -1.0, 1.0, ALU.max, ALU.min,
           ins=[B("mg_t")], outs=[B("mg_t")])
    for kk in range(8):
        k.t.op("transpose", pk[kk % 2][:, 0:E], mg_t[:, 128 * kk:128 * (kk + 1)],
               id_f[0:E, 0:E], ins=[B("mg_t"), B("id_f")], outs=[B(f"pk{kk % 2}")])
        k.v.op("tensor_copy", mgt[:, E * kk:E * (kk + 1)], pk[kk % 2][:, 0:E],
               ins=[B(f"pk{kk % 2}")], outs=[B("mgt")])

    sc0.close()
    k.fence()

    stat_g = k.sb("stat_g", [128, NT])
    cid1_s = k.sb("cid1_s", [128, NT], I32)
    cid2_s = k.sb("cid2_s", [128, NT], I32)
    wk1_s = k.sb("wk1_s", [128, NT])
    wk2_s = k.sb("wk2_s", [128, NT])
    xidx_s = k.sb("xidx_s", [128, NS], I32)
    agp_s = k.sb("agp_s", [128, NS])
    adp_s = k.sb("adp_s", [128, NS])
    sxb = k.sb("sxb", [128, 1])
    sxv = k.sb("sxv", [128, 1])
    s2b = k.sb("s2b", [128, 1])
    s2v = k.sb("s2v", [128, 1])
    mdpc_b = k.sb("mdpc_b", [128, 1])
    mgpc_b = k.sb("mgpc_b", [128, 1])
    IL = k.sb("IL", [128, NT * E])

    class Stream:
        def __init__(self, si, widths):
            self.si = si
            self.h = {}
            for nm, shape, dt in widths:
                self.h[nm] = k.sb(f"{nm}_u{si}", shape, dt)
        def __getitem__(self, nm):
            return self.h[nm]
        def b(self, nm):
            return B(f"{nm}_u{self.si}")

    base_w = ([("scr", [128, H], F32)] +
              [(f"st{i}", [128, 1], F32) for i in range(8)])
    SS = [Stream(0, base_w), Stream(1, base_w), Stream(2, base_w)]

    scr = k.sb("scr", [128, 1])
    stt = [k.sb(f"stt{i}", [128, 1]) for i in range(8)]

    class _LegacyS:
        def __init__(self):
            self.h = {"scr": scr}
            for i in range(8):
                self.h[f"st{i}"] = stt[i]
        def __getitem__(self, nm):
            return self.h[nm]
        def b(self, nm):
            return B("scr" if nm == "scr" else "stt" + nm[2:])
    LS = _LegacyS()

    def drive(gen):
        for _ in gen:
            pass

    def interleave(gens):
        gens = list(gens)
        while gens:
            alive = []
            for g in gens:
                try:
                    next(g)
                    alive.append(g)
                except StopIteration:
                    pass
            gens = alive

    def rsqrt_col(S, dst, src, scale, eps):
        d, ds_, db, sb_ = S[f"st{dst}"], S[f"st{src}"], S.b(f"st{dst}"), S.b(f"st{src}")
        k.v.op("tensor_scalar", d[:], ds_[:], scale, eps, ALU.mult, ALU.add,
               ins=[sb_], outs=[db])
        yield
        k.s.op("activation", d[:], d[:], AF.Sqrt, bias=0.0, scale=1.0,
               ins=[db], outs=[db])
        yield
        k.v.op("reciprocal", d[:], d[:], ins=[db], outs=[db])
        yield

    def adaln_tile(S, xt_ap, xt_buf, h_ap, h_buf, vtile, vbuf):
        st0, st2, st3 = S["st0"], S["st2"], S["st3"]
        b0, b1, b2, b3 = S.b("st0"), S.b("st1"), S.b("st2"), S.b("st3")
        scr_, bscr = S["scr"], S.b("scr")
        k.v.op("tensor_reduce", st0[:], xt_ap, AX.X, ALU.add,
               ins=[xt_buf], outs=[b0])
        yield
        k.s.op("activation", scr_[:], xt_ap, AF.Square, bias=0.0, scale=1.0,
               accum_out=S["st1"][:], ins=[xt_buf], outs=[bscr, b1])
        yield
        k.v.op("tensor_scalar", st0[:], st0[:], 1.0 / H, None, ALU.mult,
               ins=[b0], outs=[b0])
        k.v.op("tensor_tensor", st2[:], st0[:], st0[:], ALU.mult,
               ins=[b0], outs=[b2])
        yield
        k.v.op("tensor_scalar", st3[:], S["st1"][:], 1.0 / H, None, ALU.mult,
               ins=[b1], outs=[b3])
        k.v.op("tensor_tensor", st3[:], st3[:], st2[:], ALU.subtract,
               ins=[b3, b2], outs=[b3])
        k.v.op("tensor_scalar", st3[:], st3[:], 1e-5, None, ALU.add,
               ins=[b3], outs=[b3])
        yield
        k.s.op("activation", st3[:], st3[:], AF.Sqrt, bias=0.0, scale=1.0,
               ins=[b3], outs=[b3])
        yield
        k.v.op("reciprocal", st3[:], st3[:], ins=[b3], outs=[b3])
        k.v.op("tensor_tensor", st2[:], st0[:], st3[:], ALU.mult,
               ins=[b0, b3], outs=[b2])
        k.v.op("tensor_scalar", st2[:], st2[:], -1.0, None, ALU.mult,
               ins=[b2], outs=[b2])
        yield
        k.s.op("activation", vtile[:], xt_ap, AF.Identity, bias=st2[:, 0:1],
               scale=st3[:, 0:1], ins=[xt_buf, b2, b3], outs=[vbuf])
        yield
        k.v.op("tensor_tensor", h_ap, vtile[:], ca1b[:], ALU.mult,
               ins=[vbuf, B("ca1b")], outs=[h_buf])
        k.v.op("tensor_tensor", h_ap, h_ap, sab[:], ALU.add,
               ins=[h_buf, B("sab")], outs=[h_buf])
        yield

    scw = ExitStack()
    for S in SS[:2]:
        S.h["wrow"] = k.sb(f"wrow_u{S.si}", [128, H], scope=scw)
        S.h["fqs"] = k.sb(f"fqs_u{S.si}", [128, H], BF16, scope=scw)
    fqT_dst = [(fqgT_p, "fqgT_p"), (fquT_p, "fquT_p")]

    def wq_chain(job, S):
        src_t, m, r0, rr = job
        wrow = S["wrow"]
        bw = S.b("wrow")
        st = [S[f"st{i}"] for i in range(3)]
        sb_ = [S.b(f"st{i}") for i in range(3)]
        k.qy.dma(wrow[0:rr, :], src_t.ap()[r0:r0 + rr, :],
                 ins=[B(src_t.name)], outs=[bw])
        yield
        k.v.op("tensor_reduce", st[0][0:rr, :], wrow[0:rr, :], AX.X, ALU.max,
               apply_absolute_value=True, ins=[bw], outs=[sb_[0]])
        yield
        k.v.op("reciprocal", st[1][0:rr, :], st[0][0:rr, :],
               ins=[sb_[0]], outs=[sb_[1]])
        k.v.op("tensor_scalar", st[1][0:rr, :], st[1][0:rr, :], 127.0, None,
               ALU.mult, ins=[sb_[1]], outs=[sb_[1]])
        k.v.op("tensor_scalar", st[2][0:rr, :], st[0][0:rr, :], 1.0 / 127.0,
               None, ALU.mult, ins=[sb_[0]], outs=[sb_[2]])
        yield
        k.s.op("activation", wrow[0:rr, :], wrow[0:rr, :], AF.Identity,
               bias=RC, scale=st[1][0:rr, 0:1],
               ins=[bw, sb_[1], B("rc_t")], outs=[bw])
        yield
        k.v.op("tensor_scalar", wrow[0:rr, :], wrow[0:rr, :], -RC, None,
               ALU.add, ins=[bw], outs=[bw])
        k.v.op("tensor_scalar", wrow[0:rr, :], wrow[0:rr, :],
               st[2][0:rr, 0:1], None, ALU.mult, ins=[bw, sb_[2]], outs=[bw])
        yield
        fqs, bfq = S["fqs"], S.b("fqs")
        for kk in range(8):
            tb = 6 + kk % 2
            k.t.op("transpose", pk[tb][:, 0:rr],
                   wrow[0:rr, 128 * kk:128 * (kk + 1)],
                   id_f[0:rr, 0:rr] if rr < 128 else id_f[:],
                   ins=[bw, B("id_f")], outs=[B(f"pk{tb}")])
            k.s.op("activation", fqs[:, rr * kk:rr * (kk + 1)],
                   pk[tb][:, 0:rr], AF.Copy, bias=0.0, scale=1.0,
                   ins=[B(f"pk{tb}")], outs=[bfq])
            if kk % 4 == 3:
                yield
        dstp, dstpb = fqT_dst[m]
        k.qg.dma(dstp.ap()[:, r0:r0 + rr].rearrange("(a p) r -> p a r", p=128),
                 fqs.ap()[0:128, 0:8 * rr].rearrange("p (a r) -> p a r", r=rr),
                 ins=[bfq], outs=[B(dstpb)])
        yield

    wq_jobs = []
    for m, src_t in enumerate((wg_sl, wu_sl)):
        for r0 in range(0, IFS, 128):
            wq_jobs.append((src_t, m, r0, min(128, IFS - r0)))
    interleave([(lambda si: (x for j in range(si, len(wq_jobs), 2)
                             for x in wq_chain(wq_jobs[j], SS[si])))(si)
                for si in range(2)])
    wrowd = k.sb("wrowd", [HS, IF], scope=scw)
    wqd_b = k.sb("wqd_b", [HS, IF], BF16, scope=scw)
    k.qy.dma(wrowd[:], wd_sl.ap(), ins=[B("wd_sl")], outs=[B("wrowd")])
    k.v.op("tensor_reduce", stt[0][0:HS, :], wrowd[:], AX.X, ALU.max,
           apply_absolute_value=True, ins=[B("wrowd")], outs=[B("stt0")])
    k.v.op("reciprocal", stt[1][0:HS, :], stt[0][0:HS, :], ins=[B("stt0")], outs=[B("stt1")])
    k.v.op("tensor_scalar", stt[1][0:HS, :], stt[1][0:HS, :], 127.0, None, ALU.mult,
           ins=[B("stt1")], outs=[B("stt1")])
    k.v.op("tensor_scalar", stt[2][0:HS, :], stt[0][0:HS, :], 1.0 / 127.0, None,
           ALU.mult, ins=[B("stt0")], outs=[B("stt2")])
    k.s.op("activation", wrowd[:], wrowd[:], AF.Identity, bias=RC,
           scale=stt[1][0:HS, 0:1], ins=[B("wrowd"), B("stt1"), B("rc_t")],
           outs=[B("wrowd")])
    k.v.op("tensor_scalar", wrowd[:], wrowd[:], -RC, None, ALU.add,
           ins=[B("wrowd")], outs=[B("wrowd")])
    k.v.op("tensor_scalar", wqd_b[:], wrowd[:], stt[2][0:HS, 0:1], None, ALU.mult,
           ins=[B("wrowd"), B("stt2")], outs=[B("wqd_b")])
    k.qg.dma(fqd_p.ap(), wqd_b[:], ins=[B("wqd_b")], outs=[B("fqd_p")])
    k.qg.wait_all(k.g)
    k.qc.collective("AllGather", ALU.bypass, RG, [fqgT_p.ap()], [fqgT_d.ap()],
                    ins=[B("fqgT_p")], outs=[B("fqgT_d")])
    k.qc.collective("AllGather", ALU.bypass, RG, [fquT_p.ap()], [fquT_d.ap()],
                    ins=[B("fquT_p")], outs=[B("fquT_d")])
    k.qc.collective("AllGather", ALU.bypass, RG, [fqd_p.ap()], [fqd_d.ap()],
                    ins=[B("fqd_p")], outs=[B("fqd_d")])

    wabs_acc = k.sb("wabs_acc", [128, 48], scope=scw)
    for S in SS[:2]:
        S.h["wrow2"] = k.sb(f"wrow2_u{S.si}", [128, IM], scope=scw)
        S.h["wq2"] = k.sb(f"wq2_u{S.si}", [128, IM], BF16, scope=scw)
    n_gp = 2 * IM // 128
    n_dp = H // 128

    def wabs_chain(win, wid, i, S):
        wrow2, wscr2 = S["wrow2"], S["wq2"]
        bw, bs = S.b("wrow2"), S.b("wq2")
        k.qy.dma(wrow2[:, 0:wid], win.ap()[128 * i:128 * (i + 1), :],
                 ins=[B(win.name)], outs=[bw])
        yield
        k.s.op("activation", wscr2[:, 0:wid], wrow2[:, 0:wid], AF.Abs, bias=0.0,
               scale=1.0, accum_out=S["st0"][:],
               ins=[bw], outs=[bs, S.b("st0")])
        yield
        k.v.op("tensor_tensor", wabs_acc[:, 40 + S.si:41 + S.si],
               wabs_acc[:, 40 + S.si:41 + S.si], S["st0"][:], ALU.add,
               ins=[B("wabs_acc"), S.b("st0")], outs=[B("wabs_acc")])
        yield

    def wtern_chain(win, wout, wid, i, S, sc_ptr, sc_buf):
        wrow2, wq2 = S["wrow2"], S["wq2"]
        bw, bq = S.b("wrow2"), S.b("wq2")
        k.qy.dma(wrow2[:, 0:wid], win.ap()[128 * i:128 * (i + 1), :],
                 ins=[B(win.name)], outs=[bw])
        yield
        k.s.op("activation", wrow2[:, 0:wid], wrow2[:, 0:wid], AF.Identity,
               bias=RC, scale=sc_ptr, ins=[bw, sc_buf, B("rc_t")], outs=[bw])
        yield
        k.v.op("tensor_scalar", wrow2[:, 0:wid], wrow2[:, 0:wid], -RC, None,
               ALU.add, ins=[bw], outs=[bw])
        k.v.op("tensor_scalar", wq2[:, 0:wid], wrow2[:, 0:wid], -1.0, 1.0,
               ALU.max, ALU.min, ins=[bw], outs=[bq])
        yield
        k.qg.dma(wout.ap()[128 * i:128 * (i + 1), :], wq2[:, 0:wid],
                 ins=[bq], outs=[B(wout.name)])
        yield

    k.v.op("memset", wabs_acc[:], 0.0, outs=[B("wabs_acc")])
    jobs_a = [(gpw_in, H, i) for i in range(n_gp)]
    interleave([(lambda si: (x for j in range(si, len(jobs_a), 2)
                             for x in wabs_chain(*jobs_a[j], SS[si])))(si)
                for si in range(2)])
    asum = k.sb("asum", [128, 1], scope=scw)
    k.v.op("tensor_tensor", asum[:], wabs_acc[:, 40:41], wabs_acc[:, 41:42],
           ALU.add, ins=[B("wabs_acc")], outs=[B("asum")])
    k.v.op("tensor_tensor", asum[:], asum[:], wabs_acc[:, 42:43],
           ALU.add, ins=[B("asum"), B("wabs_acc")], outs=[B("asum")])
    k.t.op("matmul", pk[0][0:1, 0:1], ones_cf[:], asum[:], start=True,
           stop=True, ins=[B("ones_cf"), B("asum")], outs=[B("pk0")])
    wsgp = k.sb("wsgp", [1, 1], scope=scw)
    k.v.op("tensor_scalar", wsgp[:], pk[0][0:1, 0:1], 1.0 / (2 * IM * H), 1e-5,
           ALU.mult, ALU.max, ins=[B("pk0")], outs=[B("wsgp")])
    wsgp_r = k.sb("wsgp_r", [1, 1], scope=scw)
    k.v.op("reciprocal", wsgp_r[:], wsgp[:], ins=[B("wsgp")], outs=[B("wsgp_r")])
    t1 = bcast_row("wsgp_rb", wsgp_r[:], B("wsgp_r"), 1, scope=scw)
    t2 = bcast_row("wsgp_b", wsgp[:], B("wsgp"), 1, scope=scw)
    k.v.op("tensor_scalar", mgpc_b[:], t2[:], 1.0 / 127.0, None, ALU.mult,
           ins=[B("wsgp_b")], outs=[B("mgpc_b")])
    jobs_q = [(gpw_in, gpq_d, H, i) for i in range(n_gp)]
    interleave([(lambda si: (x for j in range(si, len(jobs_q), 2)
                             for x in wtern_chain(*jobs_q[j], SS[si],
                                                  t1[:, 0:1], B("wsgp_rb"))))(si)
                for si in range(2)])
    k.v.op("memset", wabs_acc[:, 40:43], 0.0, outs=[B("wabs_acc")])
    jobs_a2 = [(dpw_in, IM, i) for i in range(n_dp)]
    interleave([(lambda si: (x for j in range(si, len(jobs_a2), 2)
                             for x in wabs_chain(*jobs_a2[j], SS[si])))(si)
                for si in range(2)])
    k.v.op("tensor_tensor", asum[:], wabs_acc[:, 40:41], wabs_acc[:, 41:42],
           ALU.add, ins=[B("wabs_acc")], outs=[B("asum")])
    k.v.op("tensor_tensor", asum[:], asum[:], wabs_acc[:, 42:43],
           ALU.add, ins=[B("asum"), B("wabs_acc")], outs=[B("asum")])
    k.t.op("matmul", pk[0][0:1, 0:1], ones_cf[:], asum[:], start=True,
           stop=True, ins=[B("ones_cf"), B("asum")], outs=[B("pk0")])
    wsdp = k.sb("wsdp", [1, 1], scope=scw)
    k.v.op("tensor_scalar", wsdp[:], pk[0][0:1, 0:1], 1.0 / (H * IM), 1e-5,
           ALU.mult, ALU.max, ins=[B("pk0")], outs=[B("wsdp")])
    wsdp_r = k.sb("wsdp_r", [1, 1], scope=scw)
    k.v.op("reciprocal", wsdp_r[:], wsdp[:], ins=[B("wsdp")], outs=[B("wsdp_r")])
    t3 = bcast_row("wsdp_rb", wsdp_r[:], B("wsdp_r"), 1, scope=scw)
    t4 = bcast_row("wsdp_b", wsdp[:], B("wsdp"), 1, scope=scw)
    k.v.op("tensor_scalar", mdpc_b[:], t4[:], 1.0 / 127.0, None, ALU.mult,
           ins=[B("wsdp_b")], outs=[B("mdpc_b")])
    jobs_q2 = [(dpw_in, dpq_d, IM, i) for i in range(n_dp)]
    interleave([(lambda si: (x for j in range(si, len(jobs_q2), 2)
                             for x in wtern_chain(*jobs_q2[j], SS[si],
                                                  t3[:, 0:1], B("wsdp_rb"))))(si)
                for si in range(2)])

    sc_ng = ExitStack()
    ngt = k.sb("ngt", [128, 8 * TL], BF16, scope=sc_ng)
    with ExitStack() as sc1:
        p1w = [("p1x", [128, H], F32), ("p1h", [128, H], F32),
               ("p1tg", [128, H], F32)]
        for S in SS:
            for nm, shape, dt in p1w:
                S.h[nm] = k.sb(f"{nm}_u{S.si}", shape, dt, scope=sc1)

        def p1_chain(t, S):
            p1x, p1h, p1tg = (S["p1x"], S["p1h"], S["p1tg"])
            p1v = p1tg
            bx, bh, btg = (S.b("p1x"), S.b("p1h"), S.b("p1tg"))
            bv = btg
            scr_, bscr = S["scr"], S.b("scr")
            st = [S[f"st{i}"] for i in range(8)]
            sb_ = [S.b(f"st{i}") for i in range(8)]
            k.qy.dma(p1x[:], xown.ap()[128 * t:128 * (t + 1), :],
                     ins=[B("xown")], outs=[bx])
            yield
            yield from adaln_tile(S, p1x[:], bx, p1h[:], bh, p1v, bv)
            k.s.op("activation", scr_[:], p1h[:], AF.Square, bias=0.0, scale=1.0,
                   accum_out=st[0][:], ins=[bh], outs=[bscr, sb_[0]])
            yield
            yield from rsqrt_col(S, 1, 0, 1.0 / H, 1e-6)
            k.v.op("tensor_tensor", p1tg[:], p1h[:], gnwb[:], ALU.mult,
                   ins=[bh, B("gnwb")], outs=[btg])
            yield
            k.s.op("activation", scr_[:], p1tg[:], AF.Square, bias=0.0, scale=1.0,
                   accum_out=st[2][:], ins=[btg], outs=[bscr, sb_[2]])
            yield
            k.v.op("tensor_tensor", st[3][:], st[1][:], st[1][:], ALU.mult,
                   ins=[sb_[1]], outs=[sb_[3]])
            k.v.op("tensor_tensor", st[3][:], st[3][:], st[2][:], ALU.mult,
                   ins=[sb_[3], sb_[2]], outs=[sb_[3]])
            yield
            yield from rsqrt_col(S, 4, 3, 1.0 / H, 1e-8)
            k.v.op("tensor_tensor", p1tg[:], p1h[:], ggb[:], ALU.mult,
                   ins=[bh, B("ggb")], outs=[btg])
            yield
            k.v.op("tensor_reduce", st[5][:], p1tg[:], AX.X, ALU.max,
                   apply_absolute_value=True, ins=[btg], outs=[sb_[5]])
            yield
            k.v.op("tensor_scalar", st[6][:], st[5][:], 1e-30, None, ALU.max,
                   ins=[sb_[5]], outs=[sb_[6]])
            k.v.op("reciprocal", st[6][:], st[6][:], ins=[sb_[6]], outs=[sb_[6]])
            k.v.op("tensor_scalar", st[6][:], st[6][:], 127.0, None, ALU.mult,
                   ins=[sb_[6]], outs=[sb_[6]])
            yield
            k.s.op("activation", scr_[:], p1tg[:], AF.Identity, bias=RC,
                   scale=st[6][:, 0:1], ins=[btg, sb_[6], B("rc_t")], outs=[bscr])
            yield
            ngt3 = ngt.ap().rearrange("p (a c) -> p a c", c=TL)
            for kk in range(8):
                j = kk // 4
                pbn = f"pk{(2 * S.si + j) % 6}"
                pbank = pk[(2 * S.si + j) % 6]
                k.t.op("transpose", pbank[:, 128 * (kk % 4):128 * (kk % 4) + 128],
                       scr_[:, 128 * kk:128 * (kk + 1)], id_f[:],
                       ins=[bscr, B("id_f")], outs=[B(pbn)],
                       defer=(kk % 4 != 3))
                if kk % 4 == 3:
                    k.s.op("activation",
                           ngt3[:, 4 * j:4 * j + 4, 128 * t:128 * (t + 1)],
                           pk[(2 * S.si + j) % 6][:, 0:512]
                           .rearrange("p (a c) -> p a c", c=128),
                           AF.Copy, bias=-RC, scale=1.0,
                           ins=[B(pbn), B("mrc_t")], outs=[B("ngt")])
                    yield
            k.v.op("tensor_tensor", st[7][:], st[1][:], st[4][:], ALU.mult,
                   ins=[sb_[1], sb_[4]], outs=[sb_[7]])
            k.v.op("tensor_tensor", st[7][:], st[7][:], st[5][:], ALU.mult,
                   ins=[sb_[7], sb_[5]], outs=[sb_[7]])
            k.v.op("tensor_scalar", st[7][:], st[7][:], 1e-5, None, ALU.max,
                   ins=[sb_[7]], outs=[sb_[7]])
            k.v.op("tensor_tensor", stat_g[:, t:t + 1], st[7][:], gamc_b[:],
                   ALU.mult, ins=[sb_[7], B("gamc_b")], outs=[B("stat_g")])
            yield

        def p1_stream(si):
            for t in range(si, NT, len(SS)):
                yield from p1_chain(t, SS[si])

        interleave([p1_stream(si) for si in range(len(SS))])
    k.fence()

    with ExitStack() as sc2:
        for t in range(NT):
            for kk in range(8):
                k.t.op("matmul", pk[t % 4][:, 0:E],
                       ngt[:, TL * kk + 128 * t:TL * kk + 128 * (t + 1)],
                       mgt[:, E * kk:E * (kk + 1)], start=(kk == 0), stop=(kk == 7),
                       ins=[B("ngt"), B("mgt")], outs=[B(f"pk{t % 4}")],
                       defer=(kk != 7))
            k.v.op("tensor_copy", IL[:, E * t:E * (t + 1)], pk[t % 4][:, 0:E],
                   ins=[B(f"pk{t % 4}")], outs=[B("IL")])
    sc_ng.close()
    k.fence()

    NCL = NT * E
    selP_d = k.dram("selP_d", [128, E * NT], F32)
    selF_d = k.dram("selF_d", [128 * NCORE, E * NT], F32)
    tot_d = k.dram("tot_d", [1, 8], F32)
    totF_d = k.dram("totF_d", [NCORE, 8], F32)
    lcar_d = k.dram("lcar_d", [NT, E], F32)
    ccar_d = k.dram("ccar_d", [NCORE, E], F32)
    cs_d = k.dram("cs_d", [1, NCL], F32)
    with ExitStack() as sc3:
        def sb3(name, dtype=F32):
            tile = k.sb(name, [128, NCL], dtype, scope=sc3)
            return tile, tile.ap().rearrange("p (a e) -> p a e", e=E)

        ioe3 = io_e.ap()[0:128, 0:E].rearrange("p e -> p () e").broadcast_to(
            (128, NT, E))
        IL3 = IL.ap().rearrange("p (a e) -> p a e", e=E)
        keys, keys3 = sb3("keys")
        k.v.op("tensor_scalar", keys[:], IL[:], 16.0, None, ALU.mult,
               ins=[B("IL")], outs=[B("keys")])
        k.v.op("tensor_tensor", keys3, keys3, ioe3, ALU.subtract,
               ins=[B("keys"), B("io_e")], outs=[B("keys")])
        m1 = k.sb("m1", [128, NT], scope=sc3)
        m13 = m1.ap().rearrange("p (a o) -> p a o", o=1)
        k.v.op("tensor_reduce", m13, keys3, AX.X, ALU.max,
               ins=[B("keys")], outs=[B("m1")])
        is1, is13 = sb3("is1")
        k.v.op("tensor_tensor", is13, keys3, m13.broadcast_to((128, NT, E)),
               ALU.is_ge, ins=[B("keys"), B("m1")], outs=[B("is1")])
        keys2, keys23 = sb3("keys2")
        k.v.op("scalar_tensor_tensor", keys2[:], is1[:], -2.0 ** 30, keys[:],
               ALU.mult, ALU.add, ins=[B("is1"), B("keys")], outs=[B("keys2")])
        m2 = k.sb("m2", [128, NT], scope=sc3)
        m23 = m2.ap().rearrange("p (a o) -> p a o", o=1)
        k.v.op("tensor_reduce", m23, keys23, AX.X, ALU.max,
               ins=[B("keys2")], outs=[B("m2")])
        sel, sel3 = sb3("sel")
        k.v.op("tensor_tensor", sel3, keys3, m23.broadcast_to((128, NT, E)),
               ALU.is_ge, ins=[B("keys"), B("m2")], outs=[B("sel")])
        lg, lg3 = sb3("lg")
        g3 = stat_g.ap().rearrange("p (a o) -> p a o", o=1)
        k.v.op("tensor_tensor", lg3, IL3, g3.broadcast_to((128, NT, E)), ALU.mult,
               ins=[B("IL"), B("stat_g")], outs=[B("lg")])
        lm = k.sb("lm", [128, NT], scope=sc3)
        lm3 = lm.ap().rearrange("p (a o) -> p a o", o=1)
        k.v.op("tensor_reduce", lm3, lg3, AX.X, ALU.max, ins=[B("lg")], outs=[B("lm")])
        k.v.op("tensor_tensor", lg3, lg3, lm3.broadcast_to((128, NT, E)),
               ALU.subtract, ins=[B("lg"), B("lm")], outs=[B("lg")])
        k.s.op("activation", lg[:], lg[:], AF.Exp, bias=0.0, scale=1.0,
               ins=[B("lg")], outs=[B("lg")])
        ssum = k.sb("ssum", [128, NT], scope=sc3)
        ssum3 = ssum.ap().rearrange("p (a o) -> p a o", o=1)
        k.v.op("tensor_reduce", ssum3, lg3, AX.X, ALU.add,
               ins=[B("lg")], outs=[B("ssum")])
        k.v.op("reciprocal", ssum[:], ssum[:], ins=[B("ssum")], outs=[B("ssum")])
        wprob, wprob3 = sb3("wprob")
        k.v.op("tensor_tensor", wprob3, lg3, ssum3.broadcast_to((128, NT, E)),
               ALU.mult, ins=[B("lg"), B("ssum")], outs=[B("wprob")])
        k.v.op("tensor_tensor", wprob[:], wprob[:], sel[:], ALU.mult,
               ins=[B("wprob"), B("sel")], outs=[B("wprob")])
        selb = k.sb("selb", [128, NCL], BF16, scope=sc3)
        k.v.op("tensor_copy", selb[:], sel[:], ins=[B("sel")], outs=[B("selb")])
        wpx, _ = sb3("wpx")
        for j in range(cdiv(NCL, 512)):
            w = min(512, NCL - 512 * j)
            p = pk[j % 4]
            k.t.op("matmul", p[:, 0:w], tri_b[:], selb[:, 512 * j:512 * j + w],
                   start=True, stop=True, ins=[B("tri_b"), B("selb")],
                   outs=[B(f"pk{j % 4}")])
            k.v.op("tensor_copy", wpx[:, 512 * j:512 * j + w], p[:, 0:w],
                   ins=[B(f"pk{j % 4}")], outs=[B("wpx")])
        cs_row = k.sb("cs_row", [1, NCL], scope=sc3)
        for j in range(cdiv(NCL, 512)):
            w = min(512, NCL - 512 * j)
            p = pk[4 + (j % 2)]
            k.t.op("matmul", p[0:1, 0:w], ones_c[:], selb[:, 512 * j:512 * j + w],
                   start=True, stop=True, ins=[B("ones_c"), B("selb")],
                   outs=[B(f"pk{4 + (j % 2)}")])
            k.v.op("tensor_copy", cs_row[0:1, 512 * j:512 * j + w], p[0:1, 0:w],
                   ins=[B(f"pk{4 + (j % 2)}")], outs=[B("cs_row")])
        k.qg.dma(cs_d.ap(), cs_row[:], ins=[B("cs_row")], outs=[B("cs_d")])
        csje = k.sb("csje", [NT, E], BF16, scope=sc3)
        csje_f = k.sb("csje_f", [NT, E], scope=sc3)
        k.qy.dma(csje_f[:], cs_d.ap().rearrange("o (a e) -> (o a) e", e=E),
                 ins=[B("cs_d")], outs=[B("csje_f")])
        k.v.op("tensor_copy", csje[:], csje_f[:], ins=[B("csje_f")], outs=[B("csje")])
        k.t.op("matmul", pk[6][0:NT, 0:E], tri_b[0:NT, 0:NT], csje[:],
               start=True, stop=True, ins=[B("tri_b"), B("csje")], outs=[B("pk6")])
        lcar = k.sb("lcar", [NT, E], scope=sc3)
        k.v.op("tensor_copy", lcar[:], pk[6][0:NT, 0:E], ins=[B("pk6")], outs=[B("lcar")])
        k.qg.dma(lcar_d.ap(), lcar[:], ins=[B("lcar")], outs=[B("lcar_d")])
        k.t.op("matmul", pk[6][0:1, 0:E], ones_c[0:NT, :], csje[:],
               start=True, stop=True, ins=[B("ones_c"), B("csje")], outs=[B("pk6")])
        totr = k.sb("totr", [1, E], scope=sc3)
        k.v.op("tensor_copy", totr[:], pk[6][0:1, 0:E], ins=[B("pk6")], outs=[B("totr")])
        k.qg.dma(tot_d.ap(), totr[:], ins=[B("totr")], outs=[B("tot_d")])
        k.qc.collective("AllGather", ALU.bypass, RG, [tot_d.ap()], [totF_d.ap()],
                        ins=[B("tot_d")], outs=[B("totF_d")])
        selx = k.sb("selx", [128, E * NT], scope=sc3)
        k.v.op("tensor_copy", selx.ap().rearrange("p (e a) -> p e a", a=NT),
               sel.ap().rearrange("p (a e) -> p e a", e=E),
               ins=[B("sel")], outs=[B("selx")])
        k.qg.dma(selP_d.ap(), selx[:], ins=[B("selx")], outs=[B("selP_d")])
        k.qg.wait_all(k.g)
        k.qc.collective("AllGather", ALU.bypass, RG, [selP_d.ap()], [selF_d.ap()],
                        ins=[B("selP_d")], outs=[B("selF_d")])
        totF = k.sb("totF", [NCORE, E], scope=sc3)
        k.qy.dma(totF[:], totF_d.ap(), ins=[B("totF_d")], outs=[B("totF")])
        k.t.op("matmul", pk[6][0:NCORE, 0:E], tri_f[0:NCORE, 0:NCORE], totF[:],
               start=True, stop=True, ins=[B("tri_f"), B("totF")], outs=[B("pk6")])
        ccar = k.sb("ccar", [NCORE, E], scope=sc3)
        k.v.op("tensor_copy", ccar[:], pk[6][0:NCORE, 0:E], ins=[B("pk6")],
               outs=[B("ccar")])
        k.qg.dma(ccar_d.ap(), ccar[:], ins=[B("ccar")], outs=[B("ccar_d")])
        ccrow = k.sb("ccrow", [1, E], scope=sc3)
        k.qg.wait_all(k.y)
        k.qy.dma(ccrow[:], ccar_d.ap()[ds(pid["y"], 1), :],
                 ins=[B("ccar_d")], outs=[B("ccrow")])
        ccb = k.sb("ccb", [128, E], scope=sc3)
        k.t.op("matmul", pk[5][:, 0:E], ones_rf[:], ccrow[0:1, :],
               start=True, stop=True, ins=[B("ones_rf"), B("ccrow")], outs=[B("pk5")])
        k.v.op("tensor_copy", ccb[:], pk[5][:, 0:E], ins=[B("pk5")], outs=[B("ccb")])
        lcrow = k.sb("lcrow", [1, NCL], scope=sc3)
        k.qy.dma(lcrow[:], lcar_d.ap().rearrange("a e -> () (a e)"),
                 ins=[B("lcar_d")], outs=[B("lcrow")])
        lcb = k.sb("lcb", [128, NCL], scope=sc3)
        for j in range(cdiv(NCL, 512)):
            w = min(512, NCL - 512 * j)
            p = pk[4 + (j % 2)]
            k.t.op("matmul", p[:, 0:w], ones_rf[:], lcrow[0:1, 512 * j:512 * j + w],
                   start=True, stop=True, ins=[B("ones_rf"), B("lcrow")],
                   outs=[B(f"pk{4 + (j % 2)}")])
            k.v.op("tensor_copy", lcb[:, 512 * j:512 * j + w], p[:, 0:w],
                   ins=[B(f"pk{4 + (j % 2)}")], outs=[B("lcb")])
        pos, pos3 = sb3("pos")
        k.v.op("tensor_tensor", pos[:], wpx[:], lcb[:], ALU.add,
               ins=[B("wpx"), B("lcb")], outs=[B("pos")])
        ccb3 = ccb.ap().rearrange("p e -> p () e").broadcast_to((128, NT, E))
        k.v.op("tensor_tensor", pos3, pos3, ccb3, ALU.add,
               ins=[B("pos"), B("ccb")], outs=[B("pos")])
        keep, keep3 = sb3("keep")
        k.v.op("tensor_scalar", keep[:], pos[:], float(CAP), None, ALU.is_lt,
               ins=[B("pos")], outs=[B("keep")])
        k.v.op("tensor_tensor", keep[:], keep[:], sel[:], ALU.mult,
               ins=[B("keep"), B("sel")], outs=[B("keep")])
        is2 = k.sb("is2", [128, NCL], scope=sc3)
        k.v.op("tensor_tensor", is2[:], sel[:], is1[:], ALU.subtract,
               ins=[B("sel"), B("is1")], outs=[B("is2")])
        tmp3t = k.sb("tmp3t", [128, NCL], scope=sc3)
        tmp3 = tmp3t.ap().rearrange("p (a e) -> p a e", e=E)
        red1 = k.sb("red1", [128, NT], scope=sc3)
        red13 = red1.ap().rearrange("p (a o) -> p a o", o=1)
        cidf = k.sb("cidf", [128, NT], scope=sc3)

        def extract(mask_t, mask_buf, cid_dst, cid_buf, wk_dst, wk_buf):
            m3 = mask_t.ap().rearrange("p (a e) -> p a e", e=E)
            k.v.op("tensor_tensor", tmp3, m3, ioe3, ALU.mult,
                   ins=[mask_buf, B("io_e")], outs=[B("tmp3t")])
            k.v.op("tensor_reduce", red13, tmp3, AX.X, ALU.add,
                   ins=[B("tmp3t")], outs=[B("red1")])
            k.v.op("tensor_scalar", cidf[:], red1[:], float(CHS), None, ALU.mult,
                   ins=[B("red1")], outs=[B("cidf")])
            k.v.op("tensor_tensor", tmp3t[:], mask_t[:], pos[:], ALU.mult,
                   ins=[mask_buf, B("pos")], outs=[B("tmp3t")])
            k.v.op("tensor_reduce", red13, tmp3, AX.X, ALU.add,
                   ins=[B("tmp3t")], outs=[B("red1")])
            k.v.op("tensor_tensor", cidf[:], cidf[:], red1[:], ALU.add,
                   ins=[B("cidf"), B("red1")], outs=[B("cidf")])
            for j in range(1, NCH_AG):
                k.v.op("tensor_scalar", tmp3t[:, 0:NT], red1[:], float(CHS * j),
                       None, ALU.is_ge, ins=[B("red1")], outs=[B("tmp3t")])
                k.v.op("scalar_tensor_tensor", cidf[:], tmp3t[:, 0:NT],
                       float((E - 1) * CHS), cidf[:], ALU.mult, ALU.add,
                       ins=[B("tmp3t"), B("cidf")], outs=[B("cidf")])
            k.v.op("tensor_tensor", tmp3t[:], mask_t[:], keep[:], ALU.mult,
                   ins=[mask_buf, B("keep")], outs=[B("tmp3t")])
            k.v.op("tensor_reduce", red13, tmp3, AX.X, ALU.add,
                   ins=[B("tmp3t")], outs=[B("red1")])
            k.v.op("tensor_scalar", tmp3t[:, 0:NT], red1[:], -1e6, 1e6,
                   ALU.mult, ALU.add, ins=[B("red1")], outs=[B("tmp3t")])
            k.v.op("tensor_tensor", cidf[:], cidf[:], tmp3t[:, 0:NT], ALU.add,
                   ins=[B("cidf"), B("tmp3t")], outs=[B("cidf")])
            k.v.op("tensor_scalar", cidf[:], cidf[:], float(E * CAP - 1), None,
                   ALU.min, ins=[B("cidf")], outs=[B("cidf")])
            k.v.op("tensor_copy", cid_dst[:], cidf[:], ins=[B("cidf")],
                   outs=[cid_buf])
            k.v.op("tensor_tensor", tmp3t[:], mask_t[:], wprob[:], ALU.mult,
                   ins=[mask_buf, B("wprob")], outs=[B("tmp3t")])
            k.v.op("tensor_reduce",
                   wk_dst.ap()[0:128, 0:NT].rearrange("p (a o) -> p a o", o=1),
                   tmp3, AX.X, ALU.add, ins=[B("tmp3t")], outs=[wk_buf])
            k.v.op("tensor_tensor", wk_dst[:], wk_dst[:], red1[:], ALU.mult,
                   ins=[wk_buf, B("red1")], outs=[wk_buf])

        extract(is1, B("is1"), cid1_s, B("cid1_s"), wk1_s, B("wk1_s"))
        extract(is2, B("is2"), cid2_s, B("cid2_s"), wk2_s, B("wk2_s"))
    k.fence()

    scw.close()
    k.fence()

    sc6a = ExitStack()
    gpwt = k.sb("gpwt", [128, 8 * 4096], BF16, scope=sc6a)
    k.qg.wait_all(k.y)
    for kk in range(8):
        k.qy.dma_t(gpwt[:, 4096 * kk:4096 * (kk + 1)],
                   gpq_d.ap()[:, 128 * kk:128 * (kk + 1)],
                   ins=[B("gpq_d")], outs=[B("gpwt")])

    with ExitStack() as sc5:
        selg = k.sb("selg", [128, NJ], scope=sc5)
        ey = pid["y"]
        for c in range(NCORE):
            k.qy.dma(selg[:, NT * c:NT * (c + 1)],
                     selF_d.ap()[128 * c:128 * (c + 1)][:, ds(ey * NT, NT)],
                     ins=[B("selF_d")], outs=[B("selg")])
        selgb = k.sb("selgb", [128, NJ], BF16, scope=sc5)
        k.v.op("tensor_copy", selgb[:], selg[:], ins=[B("selg")], outs=[B("selgb")])
        csg = k.sb("csg", [1, NJ], scope=sc5)
        for j in range(cdiv(NJ, 512)):
            w = min(512, NJ - 512 * j)
            p = pk[j % 2]
            k.t.op("matmul", p[0:1, 0:w], ones_c[:], selgb[:, 512 * j:512 * j + w],
                   start=True, stop=True, ins=[B("ones_c"), B("selgb")],
                   outs=[B(f"pk{j % 2}")])
            k.v.op("tensor_copy", csg[0:1, 512 * j:512 * j + w], p[0:1, 0:w],
                   ins=[B(f"pk{j % 2}")], outs=[B("csg")])
        csgc = k.sb("csgc", [NJ, 1], BF16, scope=sc5)
        k.t.op("transpose", pk[2][0:NJ, 0:1], csg[:], id_f[0:1, 0:1],
               ins=[B("csg"), B("id_f")], outs=[B("pk2")])
        csgc_f = k.sb("csgc_f", [NJ, 1], scope=sc5)
        k.v.op("tensor_copy", csgc_f[:], pk[2][0:NJ, 0:1], ins=[B("pk2")],
               outs=[B("csgc_f")])
        k.v.op("tensor_copy", csgc[:], csgc_f[:], ins=[B("csgc_f")], outs=[B("csgc")])
        k.t.op("matmul", pk[3][0:NJ, 0:1], tri_b[0:NJ, 0:NJ], csgc[:],
               start=True, stop=True, ins=[B("tri_b"), B("csgc")], outs=[B("pk3")])
        carsel = k.sb("carsel", [NJ, 1], scope=sc5)
        k.v.op("tensor_copy", carsel[:], pk[3][0:NJ, 0:1], ins=[B("pk3")],
               outs=[B("carsel")])
        carK = k.sb("carK", [NJ, 1], scope=sc5)
        k.v.op("tensor_scalar", carK[:], carsel[:], float(CAP), None, ALU.min,
               ins=[B("carsel")], outs=[B("carK")])
        k.t.op("transpose", pk[2][0:1, 0:NJ], carsel[:], id_f[0:NJ, 0:NJ],
               ins=[B("carsel"), B("id_f")], outs=[B("pk2")])
        carselr = k.sb("carselr", [1, NJ], scope=sc5)
        k.v.op("tensor_copy", carselr[:], pk[2][0:1, 0:NJ], ins=[B("pk2")],
               outs=[B("carselr")])
        carKr = k.sb("carKr", [1, NJ], scope=sc5)
        k.v.op("tensor_scalar", carKr[:], carselr[:], float(CAP), None, ALU.min,
               ins=[B("carselr")], outs=[B("carKr")])
        wpe = k.sb("wpe", [128, NJ], scope=sc5)
        for j in range(cdiv(NJ, 512)):
            w = min(512, NJ - 512 * j)
            p = pk[j % 2]
            k.t.op("matmul", p[:, 0:w], tri_b[:], selgb[:, 512 * j:512 * j + w],
                   start=True, stop=True, ins=[B("tri_b"), B("selgb")],
                   outs=[B(f"pk{j % 2}")])
            k.v.op("tensor_copy", wpe[:, 512 * j:512 * j + w], p[:, 0:w],
                   ins=[B(f"pk{j % 2}")], outs=[B("wpe")])
        carsb = k.sb("carsb", [128, NJ], scope=sc5)
        for j in range(cdiv(NJ, 512)):
            w = min(512, NJ - 512 * j)
            p = pk[2 + (j % 2)]
            k.t.op("matmul", p[:, 0:w], ones_rf[:], carselr[0:1, 512 * j:512 * j + w],
                   start=True, stop=True, ins=[B("ones_rf"), B("carselr")],
                   outs=[B(f"pk{2 + (j % 2)}")])
            k.v.op("tensor_copy", carsb[:, 512 * j:512 * j + w], p[:, 0:w],
                   ins=[B(f"pk{2 + (j % 2)}")], outs=[B("carsb")])
        pose = k.sb("pose", [128, NJ], scope=sc5)
        k.v.op("tensor_tensor", pose[:], wpe[:], carsb[:], ALU.add,
               ins=[B("wpe"), B("carsb")], outs=[B("pose")])
        keep_e = k.sb("keep_e", [128, NJ], scope=sc5)
        k.v.op("tensor_scalar", keep_e[:], pose[:], float(CAP), None, ALU.is_lt,
               ins=[B("pose")], outs=[B("keep_e")])
        k.v.op("tensor_tensor", keep_e[:], keep_e[:], selg[:], ALU.mult,
               ins=[B("keep_e"), B("selg")], outs=[B("keep_e")])
        keb = k.sb("keb", [128, NJ], BF16, scope=sc5)
        k.v.op("tensor_copy", keb[:], keep_e[:], ins=[B("keep_e")], outs=[B("keb")])
        rw = k.sb("rw", [128, NJ], scope=sc5)
        for j in range(cdiv(NJ, 512)):
            w = min(512, NJ - 512 * j)
            p = pk[j % 2]
            k.t.op("matmul", p[:, 0:w], tri_b[:], keb[:, 512 * j:512 * j + w],
                   start=True, stop=True, ins=[B("tri_b"), B("keb")],
                   outs=[B(f"pk{j % 2}")])
            k.v.op("tensor_copy", rw[:, 512 * j:512 * j + w], p[:, 0:w],
                   ins=[B(f"pk{j % 2}")], outs=[B("rw")])
        k.v.op("tensor_scalar", rw[:], rw[:], 1.0, None, ALU.add,
               ins=[B("rw")], outs=[B("rw")])
        k.v.op("tensor_tensor", rw[:], rw[:], keep_e[:], ALU.mult,
               ins=[B("rw"), B("keep_e")], outs=[B("rw")])
        k.v.op("tensor_scalar", rw[:], rw[:], -1.0, None, ALU.add,
               ins=[B("rw")], outs=[B("rw")])
        NJH = min(64, NJ)
        pall = k.sb("pall", [128, NJH * 128], BF16, scope=sc5)
        pall3 = pall.ap().rearrange("p (a r) -> p a r", r=128)
        ior3 = io_r.ap().rearrange("p r -> p () r").broadcast_to((128, NJH, 128))
        ct = k.sb("ct", [128, NJ], scope=sc5)
        for hh in range(NJ // NJH):
            rw3 = rw.ap()[0:128, NJH * hh:NJH * (hh + 1)].rearrange(
                "p (a o) -> p a o", o=1).broadcast_to((128, NJH, 128))
            k.v.op("tensor_tensor", pall3, rw3, ior3, ALU.is_equal,
                   ins=[B("rw"), B("io_r")], outs=[B("pall")])
            for jj in range(NJH):
                j = NJH * hh + jj
                p = pk[j % 4]
                k.t.op("matmul", p[:, 0:1], pall[:, 128 * jj:128 * (jj + 1)],
                       iop1_b[:], start=True, stop=True,
                       ins=[B("pall"), B("iop1_b")], outs=[B(f"pk{j % 4}")])
                k.v.op("tensor_copy", ct[:, j:j + 1], p[:, 0:1],
                       ins=[B(f"pk{j % 4}")], outs=[B("ct")])
        k.t.op("transpose", pk[4][0:NJ, 0:128], ct[:], id_f[:],
               ins=[B("ct"), B("id_f")], outs=[B("pk4")])
        ctt = k.sb("ctt", [128, 128], scope=sc5)
        k.v.op("tensor_copy", ctt[0:NJ, :], pk[4][0:NJ, 0:128],
               ins=[B("pk4")], outs=[B("ctt")])
        k.qg.dma(ct_d.ap().rearrange("(a r) o -> a (r o)", r=128), ctt[0:NJ, :],
                 ins=[B("ctt")], outs=[B("ct_d")])
        dcar = k.sb("dcar", [1, NJ], scope=sc5)
        k.v.op("memset", dcar[:], 0.0, outs=[B("dcar")])
        k.v.op("tensor_copy", dcar[0:1, 1:NJ], carKr[0:1, 0:NJ - 1],
               ins=[B("carKr")], outs=[B("dcar")])
        k.v.op("tensor_tensor", dcar[:], carKr[:], dcar[:], ALU.subtract,
               ins=[B("carKr"), B("dcar")], outs=[B("dcar")])
        k.t.op("transpose", pk[5][0:NJ, 0:1], dcar[:], id_f[0:1, 0:1],
               ins=[B("dcar"), B("id_f")], outs=[B("pk5")])
        dcarc = k.sb("dcarc", [NJ, 1], BF16, scope=sc5)
        k.v.op("tensor_copy", dcarc[:], pk[5][0:NJ, 0:1], ins=[B("pk5")],
               outs=[B("dcarc")])
        CH5 = min(1024, CAP)
        cm = k.sb("cm", [128, CH5], BF16, scope=sc5)
        j1r = k.sb("j1r", [1, CH5], scope=sc5)
        cofr = k.sb("cofr", [1, CH5], scope=sc5)
        flr = k.sb("flr", [1, CH5], scope=sc5)
        for ph in range(CAP // CH5):
            k.v.op("tensor_scalar", cm[0:NJ, :],
                   io_s[0:NJ, CH5 * ph:CH5 * (ph + 1)], carK[:, 0:1], None,
                   ALU.is_ge, ins=[B("io_s"), B("carK")], outs=[B("cm")])
            for j in range(cdiv(CH5, 512)):
                w = min(512, CH5 - 512 * j)
                c0 = 512 * j
                p0, p1 = pk[j % 2], pk[2 + j % 2]
                k.t.op("matmul", p0[0:1, 0:w], ones_c[0:NJ, :],
                       cm[0:NJ, c0:c0 + w], start=True, stop=True,
                       ins=[B("ones_c"), B("cm")], outs=[B(f"pk{j % 2}")])
                k.v.op("tensor_copy", j1r[0:1, 512 * j:512 * j + w], p0[0:1, 0:w],
                       ins=[B(f"pk{j % 2}")], outs=[B("j1r")])
                k.t.op("matmul", p1[0:1, 0:w], dcarc[:], cm[0:NJ, c0:c0 + w],
                       start=True, stop=True, ins=[B("dcarc"), B("cm")],
                       outs=[B(f"pk{2 + j % 2}")])
                k.v.op("tensor_copy", cofr[0:1, 512 * j:512 * j + w], p1[0:1, 0:w],
                       ins=[B(f"pk{2 + j % 2}")], outs=[B("cofr")])
            k.v.op("tensor_scalar", flr[:], j1r[:], 128.0, -128.0, ALU.mult,
                   ALU.add, ins=[B("j1r")], outs=[B("flr")])
            k.v.op("tensor_tensor", flr[:], flr[:],
                   io_s[0:1, CH5 * ph:CH5 * (ph + 1)], ALU.add,
                   ins=[B("flr"), B("io_s")], outs=[B("flr")])
            k.v.op("tensor_tensor", flr[:], flr[:], cofr[:], ALU.subtract,
                   ins=[B("flr"), B("cofr")], outs=[B("flr")])
            k.v.op("tensor_scalar", flr[:], flr[:], float(T - 1), None, ALU.min,
                   ins=[B("flr")], outs=[B("flr")])
            k.v.op("tensor_scalar", j1r[:], j1r[:], -1.0, None, ALU.add,
                   ins=[B("j1r")], outs=[B("j1r")])
            k.qg.dma(fl_d.ap()[0:1, CH5 * ph:CH5 * (ph + 1)], flr[:],
                     ins=[B("flr")], outs=[B("fl_d")])
            k.qg.dma(jr_d.ap()[0:1, CH5 * ph:CH5 * (ph + 1)], j1r[:],
                     ins=[B("j1r")], outs=[B("jr_d")])
        k.qg.wait_all(k.y)
        flv = k.sb("flv", [NS, 128], scope=sc5)
        jrv = k.sb("jrv", [NS, 128], scope=sc5)
        k.qy.dma(flv[:], fl_d.ap().rearrange("o (a p) -> (o a) p", p=128),
                 ins=[B("fl_d")], outs=[B("flv")])
        k.qy.dma(jrv[:], jr_d.ap().rearrange("o (a p) -> (o a) p", p=128),
                 ins=[B("jr_d")], outs=[B("jrv")])
        k.t.op("transpose", pk[0][:, 0:NS], flv[:], id_f[0:NS, 0:NS],
               ins=[B("flv"), B("id_f")], outs=[B("pk0")])
        flT = k.sb("flT", [128, NS], scope=sc5)
        k.v.op("tensor_copy", flT[:], pk[0][:, 0:NS], ins=[B("pk0")], outs=[B("flT")])
        k.t.op("transpose", pk[1][:, 0:NS], jrv[:], id_f[0:NS, 0:NS],
               ins=[B("jrv"), B("id_f")], outs=[B("pk1")])
        jrT = k.sb("jrT", [128, NS], scope=sc5)
        k.v.op("tensor_copy", jrT[:], pk[1][:, 0:NS], ins=[B("pk1")], outs=[B("jrT")])
        flTi = k.sb("flTi", [128, NS], I32, scope=sc5)
        k.v.op("tensor_copy", flTi[:], flT[:], ins=[B("flT")], outs=[B("flTi")])
        ctg = k.sb("ctg", [128, NS], scope=sc5)
        for s in range(NS):
            k.qg.indirect(ctg[:, s:s + 1], None, ct_d.ap(),
                          IndirectOffsetOnAxis(flTi[:, s:s + 1], 0),
                          ins=[B("flTi"), B("ct_d")], outs=[B("ctg")])
        xf_t = k.sb("xf_t", [128, NS], scope=sc5)
        k.v.op("tensor_scalar", xf_t[:], jrT[:], 128.0, None, ALU.mult,
               ins=[B("jrT")], outs=[B("xf_t")])
        k.v.op("tensor_tensor", xf_t[:], xf_t[:], ctg[:], ALU.add,
               ins=[B("xf_t"), B("ctg")], outs=[B("xf_t")])
        k.v.op("tensor_scalar", xf_t[:], xf_t[:], -1.0, 0.0, ALU.add, ALU.max,
               ins=[B("xf_t")], outs=[B("xf_t")])
        k.v.op("tensor_copy", xidx_s[:], xf_t[:], ins=[B("xf_t")], outs=[B("xidx_s")])
    k.fence()



    NG = min(8, NS, NT)
    with ExitStack() as sc6:
        sc6x = ExitStack()
        SS6 = SS
        p6w = [("xin", [128, H], F32), ("xinB", [128, H], F32),
               ("he", [128, H], F32), ("tgp", [128, H], F32)]
        for S in SS6:
            for nm, shape, dt in p6w:
                S.h[nm] = k.sb(f"{nm}_u{S.si}", shape, dt, scope=sc6x)
            k.v.op("memset", S["xin"][:], 0.0, outs=[S.b("xin")])
            k.v.op("memset", S["xinB"][:], 0.0, outs=[S.b("xinB")])

        def p6_gather(s, S, i):
            nm = "xin" if i % 2 == 0 else "xinB"
            k.qg.indirect(S[nm][:], None, x_full.ap(),
                          IndirectOffsetOnAxis(xidx_s[:, s:s + 1], 0),
                          ins=[B("xidx_s"), B("x_full")], outs=[S.b(nm)])

        def p6_chain(s, S, i):
            xin = S["xin"] if i % 2 == 0 else S["xinB"]
            bxin = S.b("xin") if i % 2 == 0 else S.b("xinB")
            he, tgp = S["he"], S["tgp"]
            hv = tgp
            bhe, btgp = S.b("he"), S.b("tgp")
            bhv = btgp
            st = [S[f"st{i}"] for i in range(8)]
            sb_ = [S.b(f"st{i}") for i in range(8)]
            scr_, bscr = S["scr"], S.b("scr")
            yield
            yield from adaln_tile(S, xin[:], bxin, he[:], bhe, hv, bhv)
            k.s.op("activation", scr_[:], he[:], AF.Square, bias=0.0, scale=1.0,
                   accum_out=st[0][:], ins=[bhe], outs=[bscr, sb_[0]])
            yield
            yield from rsqrt_col(S, 1, 0, 1.0 / H, 1e-8)
            k.v.op("tensor_tensor", tgp[:], he[:], gpnb[:], ALU.mult,
                   ins=[bhe, B("gpnb")], outs=[btgp])
            yield
            k.v.op("tensor_reduce", st[2][:], tgp[:], AX.X, ALU.max,
                   apply_absolute_value=True, ins=[btgp], outs=[sb_[2]])
            yield
            k.v.op("tensor_scalar", st[3][:], st[2][:], 1e-30, None, ALU.max,
                   ins=[sb_[2]], outs=[sb_[3]])
            k.v.op("reciprocal", st[3][:], st[3][:], ins=[sb_[3]], outs=[sb_[3]])
            k.v.op("tensor_scalar", st[3][:], st[3][:], 127.0, None, ALU.mult,
                   ins=[sb_[3]], outs=[sb_[3]])
            yield
            k.s.op("activation", scr_[:], tgp[:], AF.Identity, bias=RC,
                   scale=st[3][:, 0:1], ins=[btgp, sb_[3], B("rc_t")],
                   outs=[bscr])
            yield
            k.qg.dma(ng6_d.ap()[128 * s:128 * (s + 1), :], scr_[:],
                     ins=[bscr], outs=[B("ng6_d")])
            k.v.op("tensor_tensor", st[4][:], st[1][:], st[2][:], ALU.mult,
                   ins=[sb_[1], sb_[2]], outs=[sb_[4]])
            k.v.op("tensor_scalar", st[4][:], st[4][:], 1e-5, None, ALU.max,
                   ins=[sb_[4]], outs=[sb_[4]])
            k.v.op("tensor_tensor", agp_s[:, s:s + 1], st[4][:], mgpc_b[:],
                   ALU.mult, ins=[sb_[4], B("mgpc_b")], outs=[B("agp_s")])
            yield

        def p6_stream(si):
            S = SS6[si]
            tiles = list(range(si, NS, 3))
            p6_gather(tiles[0], S, 0)
            for i, s in enumerate(tiles):
                if i + 1 < len(tiles):
                    p6_gather(tiles[i + 1], S, i + 1)
                yield from p6_chain(s, S, i)

        interleave([p6_stream(si) for si in range(3)])
        sc6x.close()
        k.fence()
        k.qg.wait_all(k.y)
        dpwt = k.sb("dpwt", [128, 16 * 1024], BF16, scope=sc6a)
        for kk in range(16):
            k.qy.dma_t(dpwt[:, 1024 * kk:1024 * (kk + 1)],
                       dpq_d.ap()[:, 128 * kk:128 * (kk + 1)],
                       ins=[B("dpq_d")], outs=[B("dpwt")])
        prod = k.sb("prod", [128, IM], scope=sc6a)
        uu2 = [k.sb(f"uu{i}", [128, IM], scope=sc6a) for i in range(2)]
        n2tl = k.sb("n2tl", [128, 16 * 128], BF16, scope=sc6a)
        ctrb = k.sb("ctrb", [128, H], scope=sc6a)
        ngs2 = [k.sb(f"ngs{i}", [128, H], F32, scope=sc6a) for i in range(2)]
        nmTs = k.sb("nmTs", [128, 8 * 128], BF16, scope=sc6a)

        def load_ngs(s):
            k.qs.dma(ngs2[s % 2][:], ng6_d.ap()[128 * s:128 * (s + 1), :],
                     ins=[B("ng6_d")], outs=[B(f"ngs{s % 2}")])

        def gp_part(s, si):
            bngs = B(f"ngs{s % 2}")
            uu, buu = uu2[s % 2], B(f"uu{s % 2}")
            for kk in range(8):
                tb = kk // 4
                k.t.op("transpose", pk[tb][:, 128 * (kk % 4):128 * (kk % 4) + 128],
                       ngs2[s % 2][:, 128 * kk:128 * (kk + 1)], id_f[:],
                       ins=[bngs, B("id_f")], outs=[B(f"pk{tb}")],
                       defer=(kk % 4 != 3))
                if kk % 4 == 3:
                    k.s.op("activation", nmTs[:, 512 * tb:512 * (tb + 1)],
                           pk[tb][:, 0:512], AF.Copy, bias=-RC, scale=1.0,
                           ins=[B(f"pk{tb}"), B("mrc_t")], outs=[B("nmTs")])
            for half in range(2):
                pb = 4 * half
                for kk in range(8):
                    for nch in range(4):
                        k.t.op("matmul", pk[pb + nch][:, 0:512],
                               nmTs[:, 128 * kk:128 * (kk + 1)],
                               gpwt[:, 4096 * kk + 2048 * half + 512 * nch:
                                    4096 * kk + 2048 * half + 512 * (nch + 1)],
                               start=(kk == 0), stop=(kk == 7),
                               ins=[B("nmTs"), B("gpwt")], outs=[B(f"pk{pb + nch}")],
                               defer=not (kk == 7 and nch == 3))
                if half == 0:
                    for nch in range(4):
                        k.s.op("activation", prod[:, 512 * nch:512 * (nch + 1)],
                               pk[pb + nch][:, 0:512], AF.Silu,
                               bias=0.0, scale=agp_s[:, s:s + 1],
                               ins=[B(f"pk{pb + nch}"), B("agp_s")], outs=[B("prod")])
                else:
                    for nch in range(4):
                        k.v.op("scalar_tensor_tensor", prod[:, 512 * nch:512 * (nch + 1)],
                               pk[pb + nch][:, 0:512], agp_s[:, s:s + 1],
                               prod[:, 512 * nch:512 * (nch + 1)], ALU.mult, ALU.mult,
                               ins=[B(f"pk{pb + nch}"), B("agp_s"), B("prod")],
                               outs=[B("prod")])
            k.s.op("activation", uu[:], prod[:], AF.Square, bias=0.0, scale=1.0,
                   accum_out=stt[0][:], ins=[B("prod")],
                   outs=[buu, B("stt0")])
            drive(rsqrt_col(LS, 1, 0, 1.0 / IM, 1e-8))
            k.v.op("tensor_tensor", uu[:], prod[:], dpnb[:], ALU.mult,
                   ins=[B("prod"), B("dpnb")], outs=[buu])
            k.v.op("tensor_reduce", stt[2][:], uu[:], AX.X, ALU.max,
                   apply_absolute_value=True, ins=[buu], outs=[B("stt2")])
            k.v.op("tensor_scalar", stt[3][:], stt[2][:], 1e-30, None, ALU.max,
                   ins=[B("stt2")], outs=[B("stt3")])
            k.v.op("reciprocal", stt[3][:], stt[3][:], ins=[B("stt3")],
                   outs=[B("stt3")])
            k.v.op("tensor_scalar", stt[3][:], stt[3][:], 127.0, None, ALU.mult,
                   ins=[B("stt3")], outs=[B("stt3")])
            k.s.op("activation", uu[:], uu[:], AF.Identity, bias=RC,
                   scale=stt[3][:, 0:1], ins=[buu, B("stt3"), B("rc_t")],
                   outs=[buu])
            k.v.op("tensor_tensor", stt[4][:], stt[1][:], stt[2][:], ALU.mult,
                   ins=[B("stt1"), B("stt2")], outs=[B("stt4")])
            k.v.op("tensor_scalar", stt[4][:], stt[4][:], 1e-5, None, ALU.max,
                   ins=[B("stt4")], outs=[B("stt4")])
            k.v.op("tensor_tensor", adp_s[:, s:s + 1], stt[4][:], mdpc_b[:],
                   ALU.mult, ins=[B("stt4"), B("mdpc_b")], outs=[B("adp_s")])

        def dp_part(s):
            uu, buu = uu2[s % 2], B(f"uu{s % 2}")
            for kk in range(16):
                tb = 2 + kk // 4
                k.t.op("transpose", pk[tb][:, 128 * (kk % 4):128 * (kk % 4) + 128],
                       uu[:, 128 * kk:128 * (kk + 1)], id_f[:],
                       ins=[buu, B("id_f")], outs=[B(f"pk{tb}")],
                       defer=(kk % 4 != 3))
                if kk % 4 == 3:
                    k.s.op("activation", n2tl[:, 512 * (tb - 2):512 * (tb - 1)],
                           pk[tb][:, 0:512], AF.Copy, bias=-RC, scale=1.0,
                           ins=[B(f"pk{tb}"), B("mrc_t")], outs=[B("n2tl")])
            for kk in range(16):
                for nch in range(2):
                    k.t.op("matmul", pk[nch][:, 0:512],
                           n2tl[:, 128 * kk:128 * (kk + 1)],
                           dpwt[:, 1024 * kk + 512 * nch:1024 * kk + 512 * (nch + 1)],
                           start=(kk == 0), stop=(kk == 15),
                           ins=[B("n2tl"), B("dpwt")], outs=[B(f"pk{nch}")],
                           defer=not (kk == 15 and nch == 1))
            for nch in range(2):
                k.s.op("activation", ctrb[:, 512 * nch:512 * (nch + 1)],
                       pk[nch][:, 0:512], AF.Copy, bias=0.0,
                       scale=adp_s[:, s:s + 1],
                       ins=[B(f"pk{nch}"), B("adp_s")], outs=[B("ctrb")])
            k.qg.dma(contribP.ap()[128 * s:128 * (s + 1), :], ctrb[:],
                     ins=[B("ctrb")], outs=[B("contribP")])
            if (s + 1) % (NS // NCH_AG) == 0:
                c = (s + 1) // (NS // NCH_AG) - 1
                k.qg.wait_all(k.g)
                k.qc.collective(
                    "AllGather", ALU.bypass, RG,
                    [contribP.ap()[CHS * c:CHS * (c + 1), :]],
                    [contribF.ap()[E * CHS * c:E * CHS * (c + 1), :]],
                    ins=[B("contribP")], outs=[B("contribF")])

        load_ngs(0)
        for s in range(NS):
            if s + 1 < NS:
                load_ngs(s + 1)
            gp_part(s, s % NG)
            if s >= 1:
                dp_part(s - 1)
        dp_part(NS - 1)
        sc6a.close()
    k.fence(skip=(k.qc,))

    sc9 = ExitStack()
    wgt = k.sb("wgt", [128, 8 * IF], BF16, scope=sc9)
    wut = k.sb("wut", [128, 8 * IF], BF16, scope=sc9)

    def load_wgut():
        for srcT, srcb, dstt, dstb in ((fqgT_d, "fqgT_d", wgt, "wgt"),
                                       (fquT_d, "fquT_d", wut, "wut")):
            src4 = srcT.ap().rearrange("(c b p) r -> p b c r", c=NCORE, p=128)
            for kk in range(8):
                k.qy.dma(dstt[:, IF * kk:IF * (kk + 1)]
                         .rearrange("p (c r) -> p () c r", r=IFS),
                         src4[:, kk:kk + 1], ins=[B(srcb)], outs=[B(dstb)])

    with ExitStack() as sc7:
        SS7 = SS[:2]
        p7w = [("c1t", [128, H], F32), ("c2t", [128, H], F32),
               ("c1tB", [128, H], F32), ("c2tB", [128, H], F32),
               ("xt7", [128, H], F32)]
        for S in SS7:
            for nm, shape, dt in p7w:
                S.h[nm] = k.sb(f"{nm}_u{S.si}", shape, dt, scope=sc7)
        macc = k.sb("macc", [128, NT], scope=sc7)
        for S in SS7:
            for nm in ("c1t", "c2t", "c1tB", "c2tB"):
                k.v.op("memset", S[nm][:], 0.0, outs=[S.b(nm)])
        k.qc.wait_all(k.g)

        def p7_gather(t, S, i):
            sfx = "" if i % 2 == 0 else "B"
            k.qg.indirect(S["c1t" + sfx][:], None, contribF.ap(),
                          IndirectOffsetOnAxis(cid1_s[:, t:t + 1], 0),
                          ins=[B("cid1_s"), B("contribF")],
                          outs=[S.b("c1t" + sfx)])
            k.qg.indirect(S["c2t" + sfx][:], None, contribF.ap(),
                          IndirectOffsetOnAxis(cid2_s[:, t:t + 1], 0),
                          ins=[B("cid2_s"), B("contribF")],
                          outs=[S.b("c2t" + sfx)])

        def p7_chain(t, S, i):
            sfx = "" if i % 2 == 0 else "B"
            c1t, c2t, xt7 = (S["c1t" + sfx], S["c2t" + sfx], S["xt7"])
            h2t = c2t
            b1, b2, bx = (S.b("c1t" + sfx), S.b("c2t" + sfx), S.b("xt7"))
            bh2 = b2
            st = [S[f"st{i}"] for i in range(4)]
            sb_ = [S.b(f"st{i}") for i in range(4)]
            scr_, bscr = S["scr"], S.b("scr")
            k.qs.dma(xt7[:], xown.ap()[128 * t:128 * (t + 1), :],
                     ins=[B("xown")], outs=[bx])
            yield
            k.s.op("activation", scr_[:], c1t[:], AF.Copy, bias=0.0,
                   scale=wk1_s[:, t:t + 1], ins=[b1, B("wk1_s")], outs=[bscr])
            yield
            k.v.op("scalar_tensor_tensor", scr_[:], c2t[:], wk2_s[:, t:t + 1],
                   scr_[:], ALU.mult, ALU.add, ins=[b2, B("wk2_s"), bscr],
                   outs=[bscr])
            yield
            k.v.op("tensor_tensor", scr_[:], scr_[:], gaxb[:], ALU.mult,
                   ins=[bscr, B("gaxb")], outs=[bscr])
            k.v.op("tensor_tensor", xt7[:], scr_[:], xt7[:], ALU.add,
                   ins=[bscr, bx], outs=[bx])
            yield
            k.qg.dma(x1_d.ap()[128 * t:128 * (t + 1), :], xt7[:],
                     ins=[bx], outs=[B("x1_d")])
            k.v.op("tensor_reduce", st[0][:], xt7[:], AX.X, ALU.add,
                   ins=[bx], outs=[sb_[0]])
            yield
            k.s.op("activation", c1t[:], xt7[:], AF.Square, bias=0.0, scale=1.0,
                   accum_out=st[1][:], ins=[bx], outs=[b1, sb_[1]])
            yield
            k.v.op("tensor_scalar", st[0][:], st[0][:], 1.0 / H, None, ALU.mult,
                   ins=[sb_[0]], outs=[sb_[0]])
            k.v.op("tensor_tensor", st[2][:], st[0][:], st[0][:], ALU.mult,
                   ins=[sb_[0]], outs=[sb_[2]])
            k.v.op("tensor_scalar", st[3][:], st[1][:], 1.0 / H, None, ALU.mult,
                   ins=[sb_[1]], outs=[sb_[3]])
            k.v.op("tensor_tensor", st[3][:], st[3][:], st[2][:], ALU.subtract,
                   ins=[sb_[3], sb_[2]], outs=[sb_[3]])
            k.v.op("tensor_scalar", st[3][:], st[3][:], 1e-5, None, ALU.add,
                   ins=[sb_[3]], outs=[sb_[3]])
            yield
            k.s.op("activation", st[3][:], st[3][:], AF.Sqrt, bias=0.0, scale=1.0,
                   ins=[sb_[3]], outs=[sb_[3]])
            yield
            k.v.op("reciprocal", st[3][:], st[3][:], ins=[sb_[3]], outs=[sb_[3]])
            k.v.op("tensor_tensor", st[2][:], st[0][:], st[3][:], ALU.mult,
                   ins=[sb_[0], sb_[3]], outs=[sb_[2]])
            k.v.op("tensor_scalar", st[2][:], st[2][:], -1.0, None, ALU.mult,
                   ins=[sb_[2]], outs=[sb_[2]])
            yield
            k.s.op("activation", h2t[:], xt7[:], AF.Identity, bias=st[2][:, 0:1],
                   scale=st[3][:, 0:1], ins=[bx, sb_[2], sb_[3]], outs=[bh2])
            yield
            k.v.op("tensor_tensor", h2t[:], h2t[:], cb1b[:], ALU.mult,
                   ins=[bh2, B("cb1b")], outs=[bh2])
            k.v.op("tensor_tensor", h2t[:], h2t[:], sbb[:], ALU.add,
                   ins=[bh2, B("sbb")], outs=[bh2])
            yield
            k.qg.dma(h2_d.ap()[128 * t:128 * (t + 1), :], h2t[:],
                     ins=[bh2], outs=[B("h2_d")])
            k.v.op("tensor_reduce", macc[:, t:t + 1], h2t[:], AX.X, ALU.max,
                   apply_absolute_value=True, ins=[bh2], outs=[B("macc")])
            yield

        def p7_stream(si):
            S = SS7[si]
            tiles = list(range(si, NT, 2))
            p7_gather(tiles[0], S, 0)
            for i, t in enumerate(tiles):
                if i + 1 < len(tiles):
                    p7_gather(tiles[i + 1], S, i + 1)
                yield from p7_chain(t, S, i)

        interleave([p7_stream(si) for si in range(2)])
        load_wgut()
        k.v.op("tensor_reduce", stt[0][:], macc[:], AX.X, ALU.max,
               ins=[B("macc")], outs=[B("stt0")])
        k.t.op("transpose", pk[0][0:1, 0:128], stt[0][:], id_f[:],
               ins=[B("stt0"), B("id_f")], outs=[B("pk0")])
        mrow1 = k.sb("mrow1", [1, 128], scope=sc7)
        k.v.op("tensor_copy", mrow1[:], pk[0][0:1, 0:128], ins=[B("pk0")],
               outs=[B("mrow1")])
        gmx = k.sb("gmx", [1, 8], scope=sc7)
        k.v.op("memset", gmx[:], 0.0, outs=[B("gmx")])
        k.v.op("tensor_reduce", gmx[0:1, 0:1], mrow1[:], AX.X, ALU.max,
               ins=[B("mrow1")], outs=[B("gmx")])
        k.qg.dma(gm1P.ap(), gmx[:], ins=[B("gmx")], outs=[B("gm1P")])
        k.qc.collective("AllGather", ALU.bypass, RG, [gm1P.ap()], [gm1F.ap()],
                        ins=[B("gm1P")], outs=[B("gm1F")])
        gmall = k.sb("gmall", [1, 64], scope=sc7)
        k.qy.dma(gmall[:], gm1F.ap().rearrange("a b -> () (a b)"),
                 ins=[B("gm1F")], outs=[B("gmall")])
        k.v.op("tensor_reduce", gmx[0:1, 0:1], gmall[:], AX.X, ALU.max,
               ins=[B("gmall")], outs=[B("gmx")])
        clx = k.sb("clx", [1, 1], scope=sc7)
        k.qy.dma(clx[:], clipx_in.ap(), ins=[B("clipx_in")], outs=[B("clx")])
        k.s.op("activation", clx[:], clx[:], AF.Sigmoid, bias=0.0, scale=1.0,
               ins=[B("clx")], outs=[B("clx")])
        k.v.op("tensor_tensor", clx[:], clx[:], gmx[0:1, 0:1], ALU.mult,
               ins=[B("clx"), B("gmx")], outs=[B("clx")])
        k.v.op("tensor_scalar", clx[:], clx[:], 1.0 / 127.0, None, ALU.mult,
               ins=[B("clx")], outs=[B("clx")])
        sxv_ = bcast_row("sxv_", clx[:], B("clx"), 1, scope=sc7)
        k.v.op("tensor_copy", sxv[:], sxv_[:], ins=[B("sxv_")], outs=[B("sxv")])
        k.v.op("reciprocal", sxb[:], sxv[:], ins=[B("sxv")], outs=[B("sxb")])
        for S in SS7:
            S.h["n1t_"] = k.sb(f"n1t__u{S.si}", [128, H], BF16, scope=sc7)
        k.qg.wait_all(k.y)
        k.qg.wait_all(k.s)

        def n1_chain(t, S):
            h2t, n1t_, scr_ = S["c2t"], S["n1t_"], S["scr"]
            bh2, bn1, bscr = S.b("c2t"), S.b("n1t_"), S.b("scr")
            k.qs.dma(h2t[:], h2_d.ap()[128 * t:128 * (t + 1), :],
                     ins=[B("h2_d")], outs=[bh2])
            yield
            k.s.op("activation", scr_[:], h2t[:], AF.Identity, bias=RC,
                   scale=sxb[:, 0:1], ins=[bh2, B("sxb"), B("rc_t")],
                   outs=[bscr])
            yield
            k.v.op("tensor_scalar", scr_[:], scr_[:], -RC, None, ALU.add,
                   ins=[bscr], outs=[bscr])
            k.v.op("tensor_scalar", n1t_[:], scr_[:], -128.0, 127.0, ALU.max,
                   ALU.min, ins=[bscr], outs=[bn1])
            yield
            k.qg.dma(n1_d.ap()[128 * t:128 * (t + 1), :], n1t_[:],
                     ins=[bn1], outs=[B("n1_d")])
            yield

        interleave([(lambda si: (x for t in range(si, NT, 2)
                                 for x in n1_chain(t, SS7[si])))(si)
                    for si in range(2)])
    k.fence()

    ich = [(512 * i, min(512, IF - 512 * i)) for i in range(cdiv(IF, 512))]
    NI = len(ich)
    NK2 = cdiv(IF, 128)
    if True:
        n1tt2 = [k.sb(f"n1tt{i}", [128, 8 * 128 * NG], BF16, scope=sc9)
                 for i in range(2)]
        sgf = k.sb("sgf", [128, 512], scope=sc9)
        hyf = k.sb("hyf", [128, 512], scope=sc9)
        prodf = k.sb("prodf", [128, IF], BF16, scope=sc9)
        maccf = k.sb("maccf", [128, NT * NI], scope=sc9)

        def load_n1tt(g):
            for kk in range(8):
                k.qy.dma_t(n1tt2[g % 2][:, 128 * NG * kk:128 * NG * (kk + 1)],
                           n1_d.ap()[128 * NG * g:128 * NG * (g + 1),
                                     128 * kk:128 * (kk + 1)],
                           ins=[B("n1_d")], outs=[B(f"n1tt{g % 2}")])
        load_n1tt(0)
        for g in range(cdiv(NT, NG)):
            if g + 1 < cdiv(NT, NG):
                load_n1tt(g + 1)
            n1tt = n1tt2[g % 2]
            bn1tt = B(f"n1tt{g % 2}")
            for si in range(NG):
                t = NG * g + si
                for c0 in range(0, NI, 2):
                    nch_here = min(2, NI - c0)
                    pb = 4 * ((c0 // 2 + t) % 2)
                    for kk in range(8):
                        for ci in range(nch_here):
                            off, wdt = ich[c0 + ci]
                            k.t.op("matmul", pk[pb + 2 * ci][:, 0:wdt],
                                   n1tt[:, 128 * NG * kk + 128 * si:
                                        128 * NG * kk + 128 * (si + 1)],
                                   wgt[:, IF * kk + off:IF * kk + off + wdt],
                                   start=(kk == 0), stop=(kk == 7),
                                   ins=[bn1tt, B("wgt")],
                                   outs=[B(f"pk{pb + 2 * ci}")], defer=True)
                            k.t.op("matmul", pk[pb + 2 * ci + 1][:, 0:wdt],
                                   n1tt[:, 128 * NG * kk + 128 * si:
                                        128 * NG * kk + 128 * (si + 1)],
                                   wut[:, IF * kk + off:IF * kk + off + wdt],
                                   start=(kk == 0), stop=(kk == 7),
                                   ins=[bn1tt, B("wut")],
                                   outs=[B(f"pk{pb + 2 * ci + 1}")],
                                   defer=not (kk == 7 and ci == nch_here - 1))
                    for ci in range(nch_here):
                        off, wdt = ich[c0 + ci]
                        k.s.op("activation", sgf[:, 0:wdt], pk[pb + 2 * ci][:, 0:wdt],
                               AF.Silu, bias=0.0, scale=sxv[:, 0:1],
                               ins=[B(f"pk{pb + 2 * ci}"), B("sxv")], outs=[B("sgf")])
                        k.s.op("activation", hyf[:, 0:wdt], pk[pb + 2 * ci + 1][:, 0:wdt],
                               AF.Copy, bias=0.0, scale=sxv[:, 0:1],
                               ins=[B(f"pk{pb + 2 * ci + 1}"), B("sxv")], outs=[B("hyf")])
                        k.v.op("tensor_tensor", prodf[:, off:off + wdt],
                               sgf[:, 0:wdt], hyf[:, 0:wdt], ALU.mult,
                               ins=[B("sgf"), B("hyf")], outs=[B("prodf")])
                        k.v.op("tensor_reduce", maccf[:, t * NI + c0 + ci:
                                                      t * NI + c0 + ci + 1],
                               prodf[:, off:off + wdt], AX.X, ALU.max,
                               apply_absolute_value=True,
                               ins=[B("prodf")], outs=[B("maccf")])
                k.qg.dma(prod_d.ap()[128 * t:128 * (t + 1), :], prodf[:],
                         ins=[B("prodf")], outs=[B("prod_d")])
        k.v.op("tensor_reduce", stt[0][:], maccf[:], AX.X, ALU.max,
               ins=[B("maccf")], outs=[B("stt0")])
        k.t.op("transpose", pk[0][0:1, 0:128], stt[0][:], id_f[:],
               ins=[B("stt0"), B("id_f")], outs=[B("pk0")])
        mrow2 = k.sb("mrow2", [1, 128], scope=sc9)
        k.v.op("tensor_copy", mrow2[:], pk[0][0:1, 0:128], ins=[B("pk0")],
               outs=[B("mrow2")])
        gmx2 = k.sb("gmx2", [1, 8], scope=sc9)
        k.v.op("memset", gmx2[:], 0.0, outs=[B("gmx2")])
        k.v.op("tensor_reduce", gmx2[0:1, 0:1], mrow2[:], AX.X, ALU.max,
               ins=[B("mrow2")], outs=[B("gmx2")])
        k.qg.dma(gm2P.ap(), gmx2[:], ins=[B("gmx2")], outs=[B("gm2P")])
        k.qc.collective("AllGather", ALU.bypass, RG, [gm2P.ap()], [gm2F.ap()],
                        ins=[B("gm2P")], outs=[B("gm2F")])
        gmall2 = k.sb("gmall2", [1, 64], scope=sc9)
        k.qy.dma(gmall2[:], gm2F.ap().rearrange("a b -> () (a b)"),
                 ins=[B("gm2F")], outs=[B("gmall2")])
        k.v.op("tensor_reduce", gmx2[0:1, 0:1], gmall2[:], AX.X, ALU.max,
               ins=[B("gmall2")], outs=[B("gmx2")])
        cld = k.sb("cld", [1, 1], scope=sc9)
        k.qy.dma(cld[:], clipd_in.ap(), ins=[B("clipd_in")], outs=[B("cld")])
        k.s.op("activation", cld[:], cld[:], AF.Sigmoid, bias=0.0, scale=1.0,
               ins=[B("cld")], outs=[B("cld")])
        k.v.op("tensor_tensor", cld[:], cld[:], gmx2[0:1, 0:1], ALU.mult,
               ins=[B("cld"), B("gmx2")], outs=[B("cld")])
        k.v.op("tensor_scalar", cld[:], cld[:], 1.0 / 127.0, None, ALU.mult,
               ins=[B("cld")], outs=[B("cld")])
        s2v_ = bcast_row("s2v_", cld[:], B("cld"), 1, scope=sc9)
        k.v.op("tensor_copy", s2v[:], s2v_[:], ins=[B("s2v_")], outs=[B("s2v")])
        k.v.op("reciprocal", s2b[:], s2v[:], ins=[B("s2v")], outs=[B("s2b")])
        sc9.close()
        k.fence()
        scd = ExitStack()
        wdt_t = k.sb("wdt_t", [128, NK2 * H], BF16, scope=scd)
        wdsg2 = [k.sb(f"wdsg{i}", [128, IF], BF16, scope=scd) for i in range(2)]
        wdsf2 = [k.sb(f"wdsf{i}", [128, IF], F32, scope=scd) for i in range(2)]
        for r in range(H // 128):
            wdsg, bwdsg = wdsg2[r % 2], B(f"wdsg{r % 2}")
            wdsf, bwdsf = wdsf2[r % 2], B(f"wdsf{r % 2}")
            k.qy.dma(wdsg[:], fqd_d.ap()[128 * r:128 * (r + 1), :],
                     ins=[B("fqd_d")], outs=[bwdsg])
            k.v.op("tensor_copy", wdsf[:], wdsg[:], ins=[bwdsg],
                   outs=[bwdsf])
            wdt3 = wdt_t.ap().rearrange("p (a c) -> p a c", c=H)
            for kk in range(NK2):
                j, jj = kk // 4, kk % 4
                gw = min(4, NK2 - 4 * j)
                tb = (j + 3 * r) % 6
                k.t.op("transpose", pk[tb][:, 128 * jj:128 * jj + 128],
                       wdsf[:, 128 * kk:128 * (kk + 1)], id_f[:],
                       ins=[bwdsf, B("id_f")], outs=[B(f"pk{tb}")],
                       defer=(jj != gw - 1))
                if jj == gw - 1:
                    k.s.op("activation",
                           wdt3[:, 4 * j:4 * j + gw, 128 * r:128 * (r + 1)],
                           pk[tb][:, 0:128 * gw]
                           .rearrange("p (a c) -> p a c", c=128),
                           AF.Copy, bias=0.0, scale=1.0,
                           ins=[B(f"pk{tb}")], outs=[B("wdt_t")])
        scq = ExitStack()
        for S in SS[:2]:
            S.h["prt"] = k.sb(f"prt_u{S.si}", [128, IF], BF16, scope=scq)
            S.h["prtf"] = k.sb(f"prtf_u{S.si}", [128, IF], F32, scope=scq)
            S.h["n2ft"] = k.sb(f"n2ft_u{S.si}", [128, IF], BF16, scope=scq)
        k.qg.wait_all(k.y)
        k.qg.wait_all(k.s)

        def n2f_chain(t, S):
            prt, prtf, n2ft = S["prt"], S["prtf"], S["n2ft"]
            bp, bpf, bn2 = S.b("prt"), S.b("prtf"), S.b("n2ft")
            k.qs.dma(prt[:], prod_d.ap()[128 * t:128 * (t + 1), :],
                     ins=[B("prod_d")], outs=[bp])
            yield
            k.s.op("activation", prtf[:], prt[:], AF.Identity, bias=RC,
                   scale=s2b[:, 0:1], ins=[bp, B("s2b"), B("rc_t")], outs=[bpf])
            yield
            k.v.op("tensor_scalar", prtf[:], prtf[:], -RC, None, ALU.add,
                   ins=[bpf], outs=[bpf])
            k.v.op("tensor_scalar", n2ft[:], prtf[:], -128.0, 127.0, ALU.max,
                   ALU.min, ins=[bpf], outs=[bn2])
            yield
            k.qg.dma(n2f_d.ap()[128 * t:128 * (t + 1), :], n2ft[:],
                     ins=[bn2], outs=[B("n2f_d")])
            yield

        interleave([(lambda si: (x for t in range(si, NT, 2)
                                 for x in n2f_chain(t, SS[si])))(si)
                    for si in range(2)])
    scq.close()
    k.fence()

    NGd = min(4, NG)
    if True:
        n2tt2 = [k.sb(f"n2tt{i}", [128, NK2 * 128 * NGd], BF16, scope=scd)
                 for i in range(2)]
        x1r2 = [k.sb(f"x1r{i}", [128, H], scope=scd) for i in range(2)]
        fot2 = [k.sb(f"fot{i}", [128, H], scope=scd) for i in range(2)]

        def load_n2tt(g):
            for kk in range(NK2):
                k.qy.dma_t(n2tt2[g % 2][:, 128 * NGd * kk:128 * NGd * (kk + 1)],
                           n2f_d.ap()[128 * NGd * g:128 * NGd * (g + 1),
                                      128 * kk:128 * (kk + 1)],
                           ins=[B("n2f_d")], outs=[B(f"n2tt{g % 2}")])
        load_n2tt(0)
        for g in range(NT // NGd):
            if g + 1 < NT // NGd:
                load_n2tt(g + 1)
            n2tt = n2tt2[g % 2]
            bn2tt = B(f"n2tt{g % 2}")
            for si in range(NGd):
                t = NGd * g + si
                pb = 2 * (si % 4)
                for kk in range(NK2):
                    for nch in range(2):
                        k.t.op("matmul", pk[pb + nch][:, 0:512],
                               n2tt[:, 128 * NGd * kk + 128 * si:
                                    128 * NGd * kk + 128 * (si + 1)],
                               wdt_t[:, H * kk + 512 * nch:H * kk + 512 * (nch + 1)],
                               start=(kk == 0), stop=(kk == NK2 - 1),
                               ins=[bn2tt, B("wdt_t")], outs=[B(f"pk{pb + nch}")],
                               defer=not (kk == NK2 - 1 and nch == 1))
                x1r = x1r2[si % 2]
                fot = fot2[si % 2]
                bx1r, bfot = B(f"x1r{si % 2}"), B(f"fot{si % 2}")
                k.qs.dma(x1r[:], x1_d.ap()[128 * t:128 * (t + 1), :],
                         ins=[B("x1_d")], outs=[bx1r])
                for nch in range(2):
                    k.s.op("activation", fot[:, 512 * nch:512 * (nch + 1)],
                           pk[pb + nch][:, 0:512], AF.Copy, bias=0.0,
                           scale=s2v[:, 0:1],
                           ins=[B(f"pk{pb + nch}"), B("s2v")], outs=[bfot])
                k.v.op("tensor_tensor", fot[:], fot[:], gbxb[:], ALU.mult,
                       ins=[bfot, B("gbxb")], outs=[bfot])
                k.v.op("tensor_tensor", fot[:], fot[:], x1r[:], ALU.add,
                       ins=[bfot, bx1r], outs=[bfot])
                k.qo.dma(out.ap()[128 * t:128 * (t + 1), :], fot[:],
                         ins=[bfot], outs=[B("out")])
        k.qo.wait_all(k.g)
        k.qo.wait_all(k.y)
        scd.close()
    return k


def _shard_inputs(inputs, T):
    x = np.ascontiguousarray(np.asarray(inputs["L0_attn_x"],
                                        dtype=np.float32).reshape(T, H))
    wmod = np.asarray(inputs["L0_wmod"], dtype=np.float32)
    W6 = 6 * H // NCORE
    gpw = np.asarray(inputs["L0_attn_gp_w"], dtype=np.float32)
    dpw = np.asarray(inputs["L0_attn_dp_w"], dtype=np.float32)
    wg = np.asarray(inputs["L0_ffn_w_gate"], dtype=np.float32)
    wu = np.asarray(inputs["L0_ffn_w_up"], dtype=np.float32)
    wd = np.asarray(inputs["L0_ffn_w_down"], dtype=np.float32)
    TL = T // NCORE
    IFS = IF // NCORE
    HS = H // NCORE
    in_maps = []
    for c in range(NCORE):
        in_maps.append({
            "x_full": x,
            "xown": np.ascontiguousarray(x[TL * c:TL * (c + 1)]),
            "c_vec": np.asarray(inputs["L0_c"], dtype=np.float32).reshape(1, H),
            "wmod_sl": np.ascontiguousarray(wmod[:, W6 * c:W6 * (c + 1)]),
            "gnw_in": np.asarray(inputs["L0_attn_gate_norm_w"],
                                 dtype=np.float32).reshape(1, H),
            "ginw_in": np.asarray(inputs["L0_attn_gate_in_norm_w"],
                                  dtype=np.float32).reshape(1, H),
            "gatew_in": np.asarray(inputs["L0_attn_gate_w"], dtype=np.float32),
            "gpn_in": np.ascontiguousarray(
                np.asarray(inputs["L0_attn_gp_norm_w"],
                           dtype=np.float32)[c].reshape(1, H)),
            "gpw_in": np.ascontiguousarray(gpw[c]),
            "dpn_in": np.ascontiguousarray(
                np.asarray(inputs["L0_attn_dp_norm_w"],
                           dtype=np.float32)[c].reshape(1, IM)),
            "dpw_in": np.ascontiguousarray(dpw[c]),
            "clipx_in": np.asarray(inputs["L0_ffn_clip_x"],
                                   dtype=np.float32).reshape(1, 1),
            "clipd_in": np.asarray(inputs["L0_ffn_clip_down"],
                                   dtype=np.float32).reshape(1, 1),
            "wg_sl": np.ascontiguousarray(wg[IFS * c:IFS * (c + 1)]),
            "wu_sl": np.ascontiguousarray(wu[IFS * c:IFS * (c + 1)]),
            "wd_sl": np.ascontiguousarray(wd[HS * c:HS * (c + 1)]),
        })
    return in_maps


def kernel(**inputs):
    T = int(np.asarray(inputs["L0_attn_x"]).shape[1])
    nc = bass.Bass("TRN2", target_bir_lowering=False)
    with ExitStack() as st:
        build(nc, st, T)
    in_maps = _shard_inputs(inputs, T)
    res = run_bass_kernel_spmd(nc, in_maps, core_ids=list(range(NCORE)))
    TL = T // NCORE
    full = np.concatenate([res.results[c]["out"] for c in range(NCORE)], axis=0)
    return full.reshape(1, T, H).astype(np.float32)
